# version 01
# baseline (speedup 1.0000x reference)
import contextlib
import math
import numpy as np
import concourse.bass as bass
import concourse.mybir as mybir
from concourse.bass_utils import run_bass_kernel_spmd

F32 = mybir.dt.float32
BF16 = mybir.dt.bfloat16
AF = mybir.ActivationFunctionType
ALU = mybir.AluOpType
AX = mybir.AxisListType

B, S, D, NH = 4, 4096, 2048, 16
ROPE, NOPE, VD = 64, 64, 128
QH = ROPE + NOPE
QL = D // 3
KVL = D // 8
THETA = 128000.0
SCALE = 1.0 / math.sqrt(QH)
EPS_LN = 1e-5
EPS_RMS = 1e-6
E = 16
I_MOE, I_SH = 1408, 5632
NT = 16
TOK = NT * 128
NKT = 32
IT_MOE = I_MOE // 128
IT_SH = I_SH // 128
KC = D // 128
QLC = [128, 128, 128, 128, 128, 42]
NEG = -30000.0

ENGS = ("pe", "dve", "act", "pool", "sp")


def _rope_tables():
    inv = 1.0 / THETA ** (np.arange(0, ROPE, 2, dtype=np.float64) / ROPE)
    pos = np.arange(S, dtype=np.float64)
    fr = np.outer(inv, pos)
    cos = np.cos(np.concatenate([fr, fr], 0))
    sin = np.sin(np.concatenate([fr, fr], 0))
    sin_signed = sin.copy()
    sin_signed[:32] *= -1.0
    return cos.astype(np.float32), sin_signed.astype(np.float32)


_PERM = np.concatenate([np.arange(0, 64, 2), np.arange(1, 64, 2)])


def _tile_k(w):
    k, n = w.shape
    return np.ascontiguousarray(w.reshape(k // 128, 128, n).transpose(1, 0, 2))


def _tile_cols(w, ct):
    k, n = w.shape
    return np.ascontiguousarray(
        w.reshape(k // 128, 128, n // ct, ct).transpose(2, 1, 0, 3))


def _host_prepare(inp):
    cos, sin = _rope_tables()
    pr = {}
    pr["qaw_t"] = _tile_k(inp["L0_attn_q_a_w"])
    pr["lnq"] = inp["L0_attn_q_a_ln_w"].reshape(1, QL)
    qbw = np.array(inp["L0_attn_q_b_w"], dtype=np.float32)
    for h in range(NH):
        base = h * QH + NOPE
        qbw[:, base:base + ROPE] = qbw[:, base + _PERM]
    qbh = np.zeros((NH, 128, 6, 128), np.float32)
    for h in range(NH):
        for c in range(6):
            w = QLC[c]
            qbh[h, :w, c, :] = qbw[c * 128:c * 128 + w, h * QH:(h + 1) * QH]
    pr["qbw_h"] = qbh
    kvaw = np.array(inp["L0_attn_kv_a_w"], dtype=np.float32)
    kvaw[:, KVL:] = kvaw[:, KVL + _PERM]
    pr["kvaw_t"] = _tile_k(kvaw)
    pr["lnkv"] = inp["L0_attn_kv_a_ln_w"].reshape(1, KVL)
    kvb3 = np.asarray(inp["L0_attn_kv_b_w"]).reshape(KVL, NH, NOPE + VD)
    pr["kvb_nope"] = np.ascontiguousarray(kvb3[:, :, :NOPE].reshape(KVL, NH * NOPE))
    pr["kvb_v"] = np.ascontiguousarray(kvb3[:, :, NOPE:].reshape(KVL, NH * VD))
    pr["ow_t"] = _tile_k(inp["L0_attn_o_w"])
    pr["router_t"] = _tile_k(np.ascontiguousarray(np.asarray(inp["L0_ffn_router_w"]).T))
    pr["wg_t"] = np.stack([_tile_cols(inp["L0_ffn_w_gate"][e], 128) for e in range(E)])
    pr["wu_t"] = np.stack([_tile_cols(inp["L0_ffn_w_up"][e], 128) for e in range(E)])
    pr["wd_t"] = np.stack([_tile_k(inp["L0_ffn_w_down"][e]) for e in range(E)])
    pr["shg_t"] = _tile_cols(inp["L0_ffn_sh_gate"], 128)
    pr["shu_t"] = _tile_cols(inp["L0_ffn_sh_up"], 128)
    pr["shd_t"] = _tile_k(inp["L0_ffn_sh_down"])
    pr["shgw_t"] = _tile_k(inp["L0_ffn_sh_gate_w"])
    pr["id128"] = np.eye(128, dtype=np.float32)
    pr["ones_row"] = np.ones((1, 128), np.float32)
    mswap = np.zeros((64, 128), np.float32)
    for m in range(32):
        mswap[m, 96 + m] = 1.0
        mswap[32 + m, 64 + m] = 1.0
    pr["mswapT"] = np.ascontiguousarray(mswap.T)
    mshift = np.zeros((64, 128), np.float32)
    for m in range(64):
        mshift[m, 64 + m] = 1.0
    pr["mshiftT"] = np.ascontiguousarray(mshift.T)
    r64 = np.zeros((64, 64), np.float32)
    for m in range(32):
        r64[m, 32 + m] = 1.0
        r64[32 + m, m] = 1.0
    pr["r64T"] = np.ascontiguousarray(r64.T)
    pr["ck"] = np.ascontiguousarray(cos)
    pr["sk"] = np.ascontiguousarray(sin)
    oneh = np.zeros((E, E * 128), np.float32)
    for e in range(E):
        oneh[e, e * 128:(e + 1) * 128] = 1.0
    pr["onehot"] = oneh
    percore = []
    x = np.ascontiguousarray(np.asarray(inp["L0_attn_x"], dtype=np.float32))
    for c in range(8):
        b, p = c // 2, c % 2
        d = {}
        slots = np.array([2 * i + p for i in range(NT)])
        rows = (slots[:, None] * 128 + np.arange(128)[None, :]).reshape(-1)
        d["xq"] = np.ascontiguousarray(x[b][rows])
        d["xk"] = x[b]
        d["cq"] = np.ascontiguousarray(cos[:, rows])
        d["sq"] = np.ascontiguousarray(sin[:, rows])
        for W in (256, 512):
            cc, rr = np.meshgrid(np.arange(W), np.arange(128))
            d[f"mask1_{W}"] = np.where(cc - rr > W - 256 + 128 * p, NEG, 0.0
                                       ).astype(np.float32)
        m2 = np.zeros((8, 128, 512), np.float32)
        for dd in range(8):
            for t in range(4):
                cc, rr = np.meshgrid(np.arange(128), np.arange(128))
                m2[dd][:, t * 128:(t + 1) * 128] = np.where(
                    (dd - 2 * t - p) * 128 + rr > cc, NEG, 0.0)
        d["mask2"] = m2
        percore.append(d)
    return pr, percore


class Prog:
    def __init__(self, nc):
        self.nc = nc
        self.items = {e: [] for e in ENGS}
        self.sem = {}
        self.cnt = {}
        self.water = {e: {} for e in ENGS}

    def new_sem(self, name, es):
        self.sem[name] = es.enter_context(self.nc.semaphore(name))
        self.cnt[name] = 0

    def tok(self, name):
        return (name, self.cnt[name])

    paranoid = ()

    def emit(self, eng, fn, inc=None, inck=1, waits=()):
        if self.paranoid:
            waits = list(waits) + [self.tok(s) for s in self.paranoid]
        wl = []
        for tkn in waits:
            if tkn is None:
                continue
            s, v = tkn
            if v <= 0:
                continue
            w = self.water[eng]
            if w.get(s, 0) >= v:
                continue
            w[s] = v
            wl.append((self.sem[s], v))
        tk = None
        sem_h = None
        if inc is not None:
            self.cnt[inc] += inck
            tk = (inc, self.cnt[inc])
            sem_h = self.sem[inc]

        def wrapped(h, fn=fn, wl=wl, sem_h=sem_h, inck=inck):
            for (sh, v) in wl:
                h.wait_ge(sh, v)
            inst = fn(h)
            if sem_h is not None:
                inst.then_inc(sem_h, inck)
        self.items[eng].append(wrapped)
        return tk

    def realize(self, es):
        with self.nc.Block() as block:
            for eng, deco in (("pe", block.tensor), ("dve", block.vector),
                              ("act", block.scalar), ("pool", block.gpsimd),
                              ("sp", block.sync)):
                fns = self.items[eng]
                if fns:
                    deco(lambda h, fns=list(fns): [fn(h) for fn in fns])
        self.items = {e: [] for e in ENGS}
        for e in ENGS:
            for s in self.sem:
                if not s.startswith("q"):
                    self.water[e][s] = self.cnt[s]


def build_nc(upto="full", nheads=NH):
    nc = bass.Bass("TRN2", target_bir_lowering=False)
    dr = {}

    def din(name, shape):
        dr[name] = nc.dram_tensor(name, list(shape), F32, kind="ExternalInput")

    din("xq", (TOK, D)); din("xk", (S, D))
    din("qaw_t", (128, KC, QL)); din("lnq", (1, QL)); din("qbw_h", (NH, 128, 6, 128))
    din("kvaw_t", (128, KC, KVL + ROPE)); din("lnkv", (1, KVL))
    din("kvb_nope", (KVL, NH * NOPE)); din("kvb_v", (KVL, NH * VD))
    din("ow_t", (128, KC, D)); din("router_t", (128, KC, E))
    din("wg_t", (E, IT_MOE, 128, KC, 128)); din("wu_t", (E, IT_MOE, 128, KC, 128))
    din("wd_t", (E, 128, IT_MOE, D))
    din("shg_t", (IT_SH, 128, KC, 128)); din("shu_t", (IT_SH, 128, KC, 128))
    din("shd_t", (128, IT_SH, D)); din("shgw_t", (128, KC, 1))
    din("id128", (128, 128)); din("ones_row", (1, 128))
    din("mswapT", (128, 64)); din("mshiftT", (128, 64)); din("r64T", (64, 64))
    din("ck", (ROPE, S)); din("sk", (ROPE, S))
    din("cq", (ROPE, TOK)); din("sq", (ROPE, TOK))
    din("mask1_256", (128, 256)); din("mask1_512", (128, 512))
    din("mask2", (8, 128, 512)); din("onehot", (E, E * 128))
    out = nc.dram_tensor("out", [TOK, D], F32, kind="ExternalOutput")
    x1d = nc.dram_tensor("x1d", [TOK, D], F32, kind="Internal")
    ap = lambda n: dr[n].ap()

    P = Prog(nc)
    esG = contextlib.ExitStack()
    with esG:
        for s in ("pe", "dve", "act", "pool"):
            P.new_sem(s, esG)
        QROT = 8
        for q in ("qg", "qs", "qo"):
            for i in range(QROT):
                P.new_sem(f"{q}{i}", esG)
        qrot = {"qg": 0, "qs": 0, "qo": 0}

        def qall(q):
            return [P.tok(f"{q}{i}") for i in range(QROT)]
        ps = [esG.enter_context(nc.psum_tensor(f"ps{i}", [128, 512], F32))
              for i in range(8)]
        rel = {}

        def sbuf(es, name, shp, dt=F32, side="right"):
            return es.enter_context(nc.sbuf_tensor(name, list(shp), dt, side=side))

        def dma(eng, q, dst, src, waits=()):
            s_name = f"{q}{qrot[q]}"
            qrot[q] = (qrot[q] + 1) % QROT
            return P.emit(eng, lambda h, d=dst, s=src: h.dma_start(d, s),
                          inc=s_name, inck=16, waits=waits)

        L = lambda name, shp, dt=F32: sbuf(esG, name, shp, dt, side="left")
        id_bf = L("id_bf", (128, 128), BF16)
        id_f32 = L("id_f32", (128, 128), F32)
        ones_bf = L("ones_bf", (1, 128), BF16)
        router_sb = L("router_sb", (128, KC * E), BF16)
        shgw_sb = L("shgw_sb", (128, KC), BF16)
        xn = L("xn", (128, D), BF16)
        xnT = L("xnT", (128, KC * 128), BF16)
        stats = L("stats", (128, 24), F32)
        mv = L("mv", (128, 2), F32)
        sd_t = L("sd_t", (128, 1), F32)
        rstd = L("rstd", (128, 1), F32)
        sq1 = L("sq1", (128, 1), F32)
        sq2 = L("sq2", (128, 1), F32)
        sqs = L("sqs", (128, 1), F32)

        esR1 = contextlib.ExitStack()
        o_all = sbuf(esR1, "o_all", (128, NH * TOK), BF16)
        esR2 = contextlib.ExitStack()
        qaT = sbuf(esR2, "qaT", (128, 6 * TOK), BF16)
        ckvT = sbuf(esR2, "ckvT", (128, 2 * S), BF16)
        kpe_aug = sbuf(esR2, "kpe_aug", (65, S), BF16)
        kvbn_sb = sbuf(esR2, "kvbn_sb", (128, 2 * NH * NOPE), BF16)
        kvbv_sb = sbuf(esR2, "kvbv_sb", (128, 2 * NH * VD), BF16)

        esR3 = contextlib.ExitStack()
        qaw_sb = sbuf(esR3, "qaw_sb", (128, KC * QL), BF16)
        kva_sb = sbuf(esR3, "kva_sb", (128, KC * (KVL + ROPE)), BF16)
        lnq_b = sbuf(esR3, "lnq_b", (128, QL), BF16)
        lnkv_b = sbuf(esR3, "lnkv_b", (128, KVL), BF16)
        lnrow = sbuf(esR3, "lnrow", (1, QL), BF16)
        ck_sb = sbuf(esR3, "ck_sb", (ROPE, S), BF16)
        sk_sb = sbuf(esR3, "sk_sb", (ROPE, S), BF16)
        r64_sb = sbuf(esR3, "r64_sb", (64, 64), BF16)
        xtile = [sbuf(esR3, f"xtileA{i}", (128, D), F32) for i in range(2)]
        qa_n = sbuf(esR3, "qa_n", (128, QL), BF16)
        ckv_n = sbuf(esR3, "ckv_n", (128, KVL), BF16)
        kpe_tm = sbuf(esR3, "kpe_tm", (128, ROPE), BF16)
        sqscr = sbuf(esR3, "sqscr", (128, 512), BF16)
        tp1 = sbuf(esR3, "tp1", (64, 512), F32)
        tp2 = sbuf(esR3, "tp2", (64, 512), F32)

        dma("pool", "qg", id_bf[:], ap("id128"))
        dma("sp", "qs", id_f32[:], ap("id128"))
        dma("pool", "qg", ones_bf[:], ap("ones_row"))
        dma("pool", "qg", router_sb[:].rearrange("p (c n) -> p c n", c=KC), ap("router_t"))
        dma("pool", "qg", shgw_sb[:].rearrange("p (c n) -> p c n", c=KC), ap("shgw_t"))
        dma("pool", "qg", kvbn_sb[:].rearrange("p (c n) -> p c n", c=2),
            ap("kvb_nope").rearrange("(c p) n -> p c n", p=128))
        dma("pool", "qg", kvbv_sb[:].rearrange("p (c n) -> p c n", c=2),
            ap("kvb_v").rearrange("(c p) n -> p c n", p=128))
        dma("pool", "qg", ck_sb[:], ap("ck"))
        dma("pool", "qg", sk_sb[:], ap("sk"))
        dma("pool", "qg", r64_sb[:], ap("r64T"))
        t_wconst = qall("qg")
        t_qaw = dma("pool", "qg", qaw_sb[:].rearrange("p (c n) -> p c n", c=KC), ap("qaw_t"))
        t_kva = dma("pool", "qg", kva_sb[:].rearrange("p (c n) -> p c n", c=KC), ap("kvaw_t"))
        t_l1 = dma("pool", "qg", lnrow[0:1, 0:QL], ap("lnq"))

        def bcast_row(width, dst, t_row):
            t_cp = None
            for j in range((width + 511) // 512):
                w = min(512, width - j * 512)
                t_mm = P.emit("pe", lambda h, j=j, w=w: h.matmul(
                    ps[5][:, 0:w], ones_bf[0:1, :], lnrow[0:1, j * 512:j * 512 + w],
                    start=True, stop=True), inc="pe", waits=[t_row, rel.get(5)])
                t_cp = P.emit("act", lambda h, j=j, w=w, dst=dst: h.copy(
                    dst[:, j * 512:j * 512 + w], ps[5][:, 0:w]), inc="act",
                    waits=[t_mm])
                rel[5] = t_cp
            return t_cp

        t_lnqb = bcast_row(QL, lnq_b, t_l1)
        t_l2 = dma("pool", "qg", lnrow[0:1, 0:KVL], ap("lnkv"), waits=[rel[5]])
        t_lnkb = bcast_row(KVL, lnkv_b, t_l2)

        def emit_ln(xbuf, t_x, extra=()):
            for sl in range(4):
                P.emit("dve", lambda h, sl=sl, xb=xbuf: h.bn_stats(
                    stats[:, sl * 6:(sl + 1) * 6], xb[:, sl * 512:(sl + 1) * 512]),
                    inc="dve", waits=[t_x] if sl == 0 else ())
            P.emit("dve", lambda h: h.bn_aggr(mv[:], stats[:]), inc="dve",
                   waits=[P.tok("dve")])
            P.emit("dve", lambda h: h.tensor_scalar(sq1[:], mv[:, 1:2], EPS_LN,
                                                    None, ALU.add), inc="dve",
                   waits=[P.tok("dve")])
            P.emit("act", lambda h: h.activation(sd_t[:], sq1[:], AF.Sqrt), inc="act",
                   waits=[P.tok("dve")])
            P.emit("dve", lambda h: h.reciprocal(rstd[:], sd_t[:]), inc="dve",
                   waits=[P.tok("act")])
            P.emit("dve", lambda h, xb=xbuf: h.tensor_scalar(
                xn[:], xb[:], mv[:, 0:1], rstd[:], ALU.subtract, ALU.mult),
                inc="dve", waits=[P.tok("dve")] + list(extra))
            return P.tok("dve")

        def emit_transposes(src, n, widths, dst_fn, t_src):
            t_last = None
            for c in range(n):
                w = widths[c] if widths else 128
                bank = 6 + (c % 2)
                t_t = P.emit("pe", lambda h, c=c, w=w, bank=bank, src=src: h.transpose(
                    ps[bank].bitcast(BF16)[0:w, 0:128], src[:, c * 128:c * 128 + w],
                    id_bf[:]), inc="pe", waits=[t_src, rel.get(bank)])
                t_last = P.emit("act", lambda h, d=dst_fn(c), w=w, bank=bank: h.copy(
                    d, ps[bank].bitcast(BF16)[0:w, 0:128]), inc="act", waits=[t_t])
                rel[bank] = t_last
            return t_last

        xrel = [None, None]
        for tt in range(NT):
            xb = xtile[tt % 2]
            t_x = dma("sp", "qs", xb[:], ap("xq")[tt * 128:(tt + 1) * 128, :],
                      waits=[xrel[tt % 2]])
            t_xn = emit_ln(xb, t_x, extra=[P.tok("pe")])
            xrel[tt % 2] = t_xn
            t_xT = emit_transposes(xn, KC, None,
                                   lambda c: xnT[:, c * 128:(c + 1) * 128], t_xn)
            for c in range(KC):
                last = c == KC - 1
                P.emit("pe", lambda h, c=c: h.matmul(
                    ps[0][:, :], xnT[:, c * 128:(c + 1) * 128],
                    qaw_sb[:, c * QL:c * QL + 512], start=(c == 0), stop=last),
                    waits=[t_xT, t_qaw, rel.get(0), rel.get(1)] if c == 0 else ())
                P.emit("pe", lambda h, c=c: h.matmul(
                    ps[1][:, 0:170], xnT[:, c * 128:(c + 1) * 128],
                    qaw_sb[:, c * QL + 512:(c + 1) * QL], start=(c == 0), stop=last),
                    inc="pe" if last else None)
            t_qa = P.tok("pe")
            P.emit("act", lambda h: h.activation(sqscr[:, 0:512], ps[0][:, :],
                                                 AF.Square, accum_out=sq1[:]),
                   inc="act", waits=[t_qa, P.tok("dve")])
            P.emit("act", lambda h: h.activation(sqscr[:, 0:170], ps[1][:, 0:170],
                                                 AF.Square, accum_out=sq2[:]),
                   inc="act")
            P.emit("dve", lambda h: h.tensor_add(sqs[:], sq1[:], sq2[:]), inc="dve",
                   waits=[P.tok("act")])
            P.emit("dve", lambda h: h.tensor_scalar(sqs[:], sqs[:], 1.0 / QL, EPS_RMS,
                                                    ALU.mult, ALU.add), inc="dve",
                   waits=[P.tok("dve")])
            P.emit("act", lambda h: h.activation(sd_t[:], sqs[:], AF.Sqrt), inc="act",
                   waits=[P.tok("dve")])
            P.emit("dve", lambda h: h.reciprocal(rstd[:], sd_t[:]), inc="dve",
                   waits=[P.tok("act")])
            P.emit("dve", lambda h: h.tensor_scalar(qa_n[:, 0:512], ps[0][:, :],
                                                    rstd[:], None, ALU.mult),
                   inc="dve", waits=[P.tok("dve")])
            t_ev = P.emit("dve", lambda h: h.tensor_scalar(
                qa_n[:, 512:QL], ps[1][:, 0:170], rstd[:], None, ALU.mult), inc="dve")
            rel[0] = t_ev
            rel[1] = t_ev
            t_qan = P.emit("dve", lambda h: h.tensor_mul(qa_n[:], qa_n[:], lnq_b[:]),
                           inc="dve", waits=[t_ev, t_lnqb])
            emit_transposes(qa_n, 6, QLC,
                            lambda c, tt=tt: qaT[0:QLC[c],
                                                 c * TOK + tt * 128:c * TOK + (tt + 1) * 128],
                            t_qan)
        P.realize(esG)

        for tk in range(NKT):
            xb = xtile[tk % 2]
            t_x = dma("sp", "qs", xb[:], ap("xk")[tk * 128:(tk + 1) * 128, :],
                      waits=[xrel[tk % 2]])
            t_xn = emit_ln(xb, t_x, extra=[P.tok("pe")])
            xrel[tk % 2] = t_xn
            t_xT = emit_transposes(xn, KC, None,
                                   lambda c: xnT[:, c * 128:(c + 1) * 128], t_xn)
            for c in range(KC):
                last = c == KC - 1
                P.emit("pe", lambda h, c=c: h.matmul(
                    ps[0][:, 0:KVL + ROPE], xnT[:, c * 128:(c + 1) * 128],
                    kva_sb[:, c * 320:(c + 1) * 320], start=(c == 0), stop=last),
                    inc="pe" if last else None,
                    waits=[t_xT, t_kva, rel.get(0)] if c == 0 else ())
            t_kv = P.tok("pe")
            P.emit("act", lambda h: h.activation(sqscr[:, 0:KVL], ps[0][:, 0:KVL],
                                                 AF.Square, accum_out=sqs[:]),
                   inc="act", waits=[t_kv, P.tok("dve")])
            P.emit("dve", lambda h: h.tensor_scalar(sqs[:], sqs[:], 1.0 / KVL,
                                                    EPS_RMS, ALU.mult, ALU.add),
                   inc="dve", waits=[P.tok("act")])
            P.emit("act", lambda h: h.activation(sd_t[:], sqs[:], AF.Sqrt), inc="act",
                   waits=[P.tok("dve")])
            P.emit("dve", lambda h: h.reciprocal(rstd[:], sd_t[:]), inc="dve",
                   waits=[P.tok("act")])
            t_craw = P.emit("dve", lambda h: h.tensor_scalar(
                ckv_n[:], ps[0][:, 0:KVL], rstd[:], None, ALU.mult),
                inc="dve", waits=[P.tok("dve")])
            t_cn = P.emit("dve", lambda h: h.tensor_mul(ckv_n[:], ckv_n[:], lnkv_b[:]),
                          inc="dve", waits=[P.tok("dve"), t_lnkb])
            t_kp = P.emit("act", lambda h: h.copy(kpe_tm[:], ps[0][:, KVL:KVL + ROPE]),
                          inc="act", waits=[t_kv, t_craw])
            rel[0] = t_kp
            emit_transposes(ckv_n, 2, None,
                            lambda c, tk=tk: ckvT[:, c * S + tk * 128:c * S + (tk + 1) * 128],
                            t_cn)
            bank = 6
            t_t = P.emit("pe", lambda h: h.transpose(
                ps[bank].bitcast(BF16)[0:ROPE, 0:128], kpe_tm[:], id_bf[:]),
                inc="pe", waits=[t_kp, rel.get(bank)])
            t_cpy = P.emit("act", lambda h, tk=tk: h.copy(
                kpe_aug[0:64, tk * 128:(tk + 1) * 128],
                ps[bank].bitcast(BF16)[0:ROPE, 0:128]), inc="act", waits=[t_t])
            rel[bank] = t_cpy
        t_kperaw = P.tok("act")

        P.emit("dve", lambda h: h.memset(kpe_aug[64:65, :], 1.0), inc="dve",
               waits=[t_kperaw])
        for j in range(8):
            slc = slice(j * 512, (j + 1) * 512)
            bank = j % 2
            t_m = P.emit("pe", lambda h, slc=slc, bank=bank: h.matmul(
                ps[bank][0:64, :], r64_sb[:], kpe_aug[0:64, slc],
                start=True, stop=True), inc="pe",
                waits=[t_kperaw, rel.get(bank), P.tok("dve")] + t_wconst)
            P.emit("dve", lambda h, slc=slc, bank=bank: h.tensor_mul(
                tp1[:], ps[bank][0:64, :], sk_sb[:, slc]), inc="dve", waits=[t_m])
            P.emit("dve", lambda h, slc=slc: h.tensor_mul(
                tp2[:], kpe_aug[0:64, slc], ck_sb[:, slc]), inc="dve")
            t_add = P.emit("dve", lambda h, slc=slc: h.tensor_add(
                kpe_aug[0:64, slc], tp1[:], tp2[:]), inc="dve", waits=[P.tok("dve")])
            rel[bank] = t_add
        t_kpe = P.tok("dve")
        P.realize(esG)
        esR3.close()

        if upto == "A":
            t1 = dma("pool", "qo", out.ap()[0:128, :], qaT[:, 0:TOK])
            t2 = dma("pool", "qo", out.ap()[128:256, :], ckvT[:, 0:2048])
            t3 = dma("pool", "qo", out.ap()[256:321, :], kpe_aug[0:65, 0:2048])
            P.emit("sp", lambda h: h.nop(), waits=qall("qo"))
            P.realize(esG)
            return nc

        esR4 = contextlib.ExitStack()
        qbw_h = [sbuf(esR4, f"qbw_h{i}", (128, 6 * 128), BF16) for i in range(2)]
        m1_256 = sbuf(esR4, "m1_256", (128, 256), BF16)
        m1_512 = sbuf(esR4, "m1_512", (128, 512), BF16)
        m2_sb = sbuf(esR4, "m2_sb", (128, 8 * 512), BF16)
        cq_sb = sbuf(esR4, "cq_sb", (ROPE, TOK), BF16)
        sq_sb = sbuf(esR4, "sq_sb", (ROPE, TOK), BF16)
        mswap_sb = sbuf(esR4, "mswap_sb", (128, 64), BF16)
        mshift_sb = sbuf(esR4, "mshift_sb", (128, 64), BF16)
        qraw = sbuf(esR4, "qraw", (128, TOK), BF16)
        qpe_aug = sbuf(esR4, "qpe_aug", (65, TOK), BF16)
        knope_h = [sbuf(esR4, f"knope_h{i}", (64, S), BF16) for i in range(2)]
        v2 = sbuf(esR4, "v2", (128, NKT * VD), BF16)
        tb1 = sbuf(esR4, "tb1", (64, 512), F32)
        tb2 = sbuf(esR4, "tb2", (64, 512), F32)
        mstats = sbuf(esR4, "mstats", (128, 8), F32)
        zstats = sbuf(esR4, "zstats", (128, 8), F32)
        mcol = sbuf(esR4, "mcol", (128, 1), F32)
        mbf = sbuf(esR4, "mbf", (128, 1), BF16)
        mr32 = sbuf(esR4, "mr32", (128, 1), F32)
        nmsc = sbuf(esR4, "nmsc", (128, 1), F32)
        t8 = sbuf(esR4, "t8", (128, 8), F32)
        zw8 = sbuf(esR4, "zw8", (128, 8), F32)
        zcol = sbuf(esR4, "zcol", (128, 1), F32)
        zinv = sbuf(esR4, "zinv", (128, 1), F32)
        zrow = sbuf(esR4, "zrow", (1, TOK), BF16)
        scr = [sbuf(esR4, f"scrB{i}", (128, 512), F32) for i in range(2)]
        attnT = [sbuf(esR4, f"attnT{i}", (128, 512), BF16) for i in range(2)]
        escr = sbuf(esR4, "escr", (128, 512), BF16)
        zbc = sbuf(esR4, "zbc", (128, 512), BF16)

        dma("pool", "qg", m1_256[:], ap("mask1_256"))
        dma("pool", "qg", m1_512[:], ap("mask1_512"))
        dma("pool", "qg", m2_sb[:].rearrange("p (c n) -> p c n", c=8),
            ap("mask2").rearrange("c p n -> p c n"))
        dma("pool", "qg", cq_sb[:], ap("cq"))
        dma("pool", "qg", sq_sb[:], ap("sq"))
        dma("pool", "qg", mswap_sb[:], ap("mswapT"))
        dma("pool", "qg", mshift_sb[:], ap("mshiftT"))
        t_bconst = qall("qg")

        t_qbw = []
        for h_ in range(2):
            t_qbw.append(dma("pool", "qg",
                             qbw_h[h_][:].rearrange("p (c n) -> p c n", c=6),
                             ap("qbw_h")[h_]))

        def p1_tiles(i):
            ext = (2 * i + 2) * 128
            full, remn = divmod(ext, 512)
            tl = [(j * 512, 512) for j in range(full)]
            if remn:
                tl.append((full * 512, remn))
            return tl

        att_rel = [None, None]
        scr_rel = [None, None]
        for h_ in range(nheads):
            hh = h_ % 2
            buf = qbw_h[h_ % 2]
            P.emit("dve", lambda h: h.memset(qpe_aug[64:65, :], 0.0), inc="dve",
                   waits=[P.tok("pe")])
            t_qpz = P.tok("dve")
            t_qlast = None
            for nb in range(4):
                qsl = slice(nb * 512, (nb + 1) * 512)
                pq = ps[nb % 2]
                for c in range(6):
                    last = c == 5
                    P.emit("pe", lambda h, c=c, nb=nb, pq=pq, buf=buf: h.matmul(
                        pq[:, :], buf[:, c * 128:c * 128 + QLC[c]].transpose([0, 1]) if False
                        else buf[0:QLC[c], c * 128:(c + 1) * 128],
                        qaT[0:QLC[c], c * TOK + nb * 512:c * TOK + (nb + 1) * 512],
                        start=(c == 0), stop=last),
                        inc="pe" if last else None,
                        waits=([t_qbw[h_], rel.get(nb % 2)] + t_bconst) if c == 0 else ())
                t_q = P.tok("pe")
                t_qlast = t_q
                t_qr = P.emit("act", lambda h, qsl=qsl, pq=pq: h.copy(
                    qraw[:, qsl], pq[:, :]), inc="act", waits=[t_q, P.tok("dve")])
                P.emit("pe", lambda h, qsl=qsl: h.matmul(
                    ps[2][0:64, :], mswap_sb[:], qraw[:, qsl], start=True, stop=True),
                    inc="pe", waits=[t_qr, rel.get(2)])
                P.emit("pe", lambda h, qsl=qsl: h.matmul(
                    ps[3][0:64, :], mshift_sb[:], qraw[:, qsl], start=True, stop=True),
                    inc="pe", waits=[rel.get(3)])
                t_rot = P.tok("pe")
                rel[nb % 2] = t_qr
                P.emit("dve", lambda h, qsl=qsl: h.tensor_mul(
                    tb1[:], ps[2][0:64, :], sq_sb[:, qsl]), inc="dve",
                    waits=[t_rot] + t_bconst)
                t_d1 = P.tok("dve")
                rel[2] = t_d1
                P.emit("dve", lambda h, qsl=qsl: h.tensor_mul(
                    tb2[:], ps[3][0:64, :], cq_sb[:, qsl]), inc="dve")
                t_d2 = P.tok("dve")
                rel[3] = t_d2
                P.emit("dve", lambda h, qsl=qsl: h.tensor_add(
                    qpe_aug[0:64, qsl], tb1[:], tb2[:]), inc="dve",
                    waits=[t_d2, t_qpz])
            t_qf = P.tok("dve")
            if upto == "QF":
                dma("pool", "qo", out.ap()[256:384, :], qraw[:, :], waits=[t_qf])
                dma("pool", "qo", out.ap()[384:449, :], qpe_aug[0:65, :], waits=[t_qf])
                P.emit("sp", lambda h: h.nop(), waits=[P.tok("qo")])
                P.realize(esG)
                return nc
            if h_ + 2 < nheads:
                t_qbw.append(dma("pool", "qg",
                                 qbw_h[h_ % 2][:].rearrange("p (c n) -> p c n", c=6),
                                 ap("qbw_h")[h_ + 2], waits=[t_qlast]))

            if hh == 0:
                for j in range(8):
                    ksl = slice(j * 512, (j + 1) * 512)
                    for hp in range(2):
                        pk = ps[4]
                        for c in range(2):
                            last = c == 1
                            P.emit("pe", lambda h, c=c, j=j, pk=pk, h_=h_, hp=hp:
                                   h.matmul(
                                pk[0:64, :],
                                kvbn_sb[:, c * NH * NOPE + (h_ + hp) * NOPE:
                                        c * NH * NOPE + (h_ + hp + 1) * NOPE],
                                ckvT[:, c * S + j * 512:c * S + (j + 1) * 512],
                                start=(c == 0), stop=last),
                                inc="pe" if last else None,
                                waits=([rel.get(4)] + t_bconst) if c == 0 else ())
                        t_k = P.tok("pe")
                        t_cp = P.emit("act", lambda h, ksl=ksl, pk=pk, hp=hp: h.copy(
                            knope_h[hp][:, ksl], pk[0:64, :]), inc="act", waits=[t_k])
                        rel[4] = t_cp
            for tk in range(NKT):
                pk = ps[4]
                for c in range(2):
                    last = c == 1
                    P.emit("pe", lambda h, c=c, tk=tk, pk=pk, h_=h_: h.matmul(
                        pk[:, 0:VD],
                        ckvT[:, c * S + tk * 128:c * S + (tk + 1) * 128],
                        kvbv_sb[:, c * NH * VD + h_ * VD:c * NH * VD + (h_ + 1) * VD],
                        start=(c == 0), stop=last),
                        inc="pe" if last else None,
                        waits=([rel.get(4)] + t_bconst) if c == 0 else ())
                t_v = P.tok("pe")
                t_cp = P.emit("act", lambda h, tk=tk, pk=pk: h.copy(
                    v2[:, tk * VD:(tk + 1) * VD], pk[:, 0:VD]),
                    inc="act", waits=[t_v])
                rel[4] = t_cp
            t_kvb = P.tok("act")

            for qt in range(NT):
                tiles = p1_tiles(qt)
                nE = len(tiles)
                for ti, (k0, w) in enumerate(tiles):
                    bank = ti % 2
                    pq = ps[bank]
                    P.emit("pe", lambda h, k0=k0, w=w, pq=pq, qt=qt, hh=hh: h.matmul(
                        pq[:, 0:w], qraw[0:64, qt * 128:(qt + 1) * 128],
                        knope_h[hh][:, k0:k0 + w],
                        start=True, stop=False),
                        waits=[t_qf, t_kvb, t_kpe, rel.get(bank)])
                    P.emit("pe", lambda h, k0=k0, w=w, pq=pq, qt=qt: h.matmul(
                        pq[:, 0:w], qpe_aug[0:65, qt * 128:(qt + 1) * 128],
                        kpe_aug[0:65, k0:k0 + w], start=False, stop=True), inc="pe")
                    t_s = P.tok("pe")
                    if ti == nE - 1:
                        msk = m1_256 if w == 256 else m1_512
                        sc = scr[ti % 2]
                        t_s = P.emit("dve", lambda h, w=w, pq=pq, msk=msk, sc=sc:
                                     h.tensor_add(sc[:, 0:w], pq[:, 0:w], msk[:, 0:w]),
                                     inc="dve", waits=[t_s, scr_rel[ti % 2]] + t_bconst)
                        scr_rel[ti % 2] = None
                        src = sc
                        rel[bank] = t_s
                        t_mk = P.emit("dve", lambda h, w=w, src=src, ti=ti:
                                      h.reduce_max(mstats[:, ti:ti + 1], src[:, 0:w],
                                                   axis=AX.X), inc="dve")
                    else:
                        src = pq
                        t_mk = P.emit("dve", lambda h, w=w, src=src, ti=ti:
                                      h.reduce_max(mstats[:, ti:ti + 1], src[:, 0:w],
                                                   axis=AX.X), inc="dve", waits=[t_s])
                    t_nm = P.emit("dve", lambda h, ti=ti: h.tensor_scalar(
                        nmsc[:], mstats[:, ti:ti + 1], -SCALE, None, ALU.mult),
                        inc="dve", waits=[t_mk, P.tok("act")])
                    t_e = P.emit("act", lambda h, w=w, src=src, ti=ti: h.activation(
                        escr[:, 0:w], src[:, 0:w], AF.Exp, bias=nmsc[:], scale=SCALE,
                        accum_out=zstats[:, ti:ti + 1]), inc="act", waits=[t_nm])
                    if ti == nE - 1:
                        scr_rel[ti % 2] = t_e
                    else:
                        rel[bank] = t_e
                t_z = P.tok("act")
                P.emit("dve", lambda h, nE=nE: h.reduce_max(
                    mcol[:], mstats[:, 0:nE], axis=AX.X), inc="dve", waits=[t_z])
                P.emit("dve", lambda h: h.tensor_copy(mbf[:], mcol[:]), inc="dve",
                       waits=[P.tok("dve")])
                P.emit("dve", lambda h: h.tensor_copy(mr32[:], mbf[:]), inc="dve",
                       waits=[P.tok("dve")])
                t_nm2 = P.emit("dve", lambda h: h.tensor_scalar(
                    nmsc[:], mr32[:], -SCALE, None, ALU.mult), inc="dve",
                    waits=[P.tok("dve")])
                t_t8 = P.emit("act", lambda h, nE=nE: h.activation(
                    t8[:, 0:nE], mstats[:, 0:nE], AF.Exp, bias=nmsc[:], scale=SCALE),
                    inc="act", waits=[t_nm2])
                P.emit("dve", lambda h, nE=nE: h.tensor_mul(
                    zw8[:, 0:nE], t8[:, 0:nE], zstats[:, 0:nE]), inc="dve",
                    waits=[t_t8])
                P.emit("dve", lambda h, nE=nE: h.reduce_sum(
                    zcol[:], zw8[:, 0:nE], axis=AX.X), inc="dve", waits=[P.tok("dve")])
                t_zi = P.emit("dve", lambda h: h.reciprocal(zinv[:], zcol[:]),
                              inc="dve", waits=[P.tok("dve")])
                t_tr = P.emit("pe", lambda h: h.transpose(
                    ps[6][0:1, 0:128], mr32[:], id_f32[:]), inc="pe",
                    waits=[t_zi, rel.get(6)])
                t_mrow = P.emit("act", lambda h, qt=qt: h.mul(
                    qpe_aug[64:65, qt * 128:(qt + 1) * 128], ps[6][0:1, 0:128], -1.0),
                    inc="act", waits=[t_tr])
                rel[6] = t_mrow
                t_tr = P.emit("pe", lambda h: h.transpose(
                    ps[6][0:1, 128:256], zinv[:], id_f32[:]), inc="pe",
                    waits=[t_zi, rel.get(6)])
                t_zrow = P.emit("act", lambda h, qt=qt: h.copy(
                    zrow[0:1, qt * 128:(qt + 1) * 128], ps[6][0:1, 128:256]),
                    inc="act", waits=[t_tr])
                rel[6] = t_zrow
            t_p1 = P.tok("act")

            for qb in range(4):
                nkt = 8 * qb + 8
                for kt in range(nkt):
                    bank = 2 + (kt % 2)
                    pq = ps[bank]
                    P.emit("pe", lambda h, kt=kt, pq=pq, qb=qb, hh=hh: h.matmul(
                        pq[:, :], knope_h[hh][:, kt * 128:(kt + 1) * 128],
                        qraw[0:64, qb * 512:(qb + 1) * 512], start=True, stop=False),
                        waits=[t_p1, rel.get(bank)])
                    P.emit("pe", lambda h, kt=kt, pq=pq, qb=qb: h.matmul(
                        pq[:, :], kpe_aug[0:65, kt * 128:(kt + 1) * 128],
                        qpe_aug[0:65, qb * 512:(qb + 1) * 512],
                        start=False, stop=True), inc="pe")
                    t_s = P.tok("pe")
                    dd = kt - 8 * qb
                    if dd >= 0:
                        sc = scr[kt % 2]
                        t_s = P.emit("dve", lambda h, pq=pq, dd=dd, sc=sc: h.tensor_add(
                            sc[:], pq[:, :], m2_sb[:, dd * 512:(dd + 1) * 512]),
                            inc="dve", waits=[t_s, scr_rel[kt % 2]] + t_bconst)
                        src = sc
                        rel[bank] = t_s
                    else:
                        src = pq
                    at = attnT[kt % 2]
                    t_e = P.emit("act", lambda h, src=src, at=at: h.activation(
                        at[:], src[:, 0:512], AF.Exp, scale=SCALE), inc="act",
                        waits=[t_s, att_rel[kt % 2]])
                    if dd >= 0:
                        scr_rel[kt % 2] = t_e
                    else:
                        rel[bank] = t_e
                    t_av = P.emit("pe", lambda h, kt=kt, at=at, nkt=nkt: h.matmul(
                        ps[4][:, :], v2[:, kt * VD:(kt + 1) * VD],
                        at[:], start=(kt == 0), stop=(kt == nkt - 1)),
                        inc="pe", waits=[t_e] + ([rel.get(4)] if kt == 0 else []))
                    att_rel[kt % 2] = t_av
                t_zb = P.emit("pe", lambda h, qb=qb: h.matmul(
                    ps[5][:, :], ones_bf[0:1, :], zrow[0:1, qb * 512:(qb + 1) * 512],
                    start=True, stop=True), inc="pe", waits=[rel.get(5)])
                t_zc = P.emit("act", lambda h: h.copy(zbc[:], ps[5][:, :]), inc="act",
                              waits=[t_zb])
                rel[5] = t_zc
                t_o = P.emit("dve", lambda h, h_=h_, qb=qb: h.tensor_mul(
                    o_all[:, h_ * TOK + qb * 512:h_ * TOK + (qb + 1) * 512],
                    ps[4][:, :], zbc[:]), inc="dve", waits=[P.tok("pe"), t_zc])
                rel[4] = t_o
        P.realize(esG)
        esR4.close()
        esR2.close()

        if upto == "B":
            for h_ in range(nheads):
                dma("pool", "qo", out.ap()[h_ * 128:(h_ + 1) * 128, :],
                    o_all[:, h_ * TOK:(h_ + 1) * TOK])
            if nheads == 1:
                dma("pool", "qo", out.ap()[256:384, :], qraw[:, :])
                dma("pool", "qo", out.ap()[384:449, :], qpe_aug[0:65, :])
                dma("pool", "qo", out.ap()[512:576, :], knope_h[0][:, 0:2048])
                dma("pool", "qo", out.ap()[640:768, :], v2[:, 0:2048])
                dma("pool", "qo", out.ap()[800:801, :], zrow[0:1, :])
                dma("pool", "qo", out.ap()[832:960, :], qaT[:, TOK:2 * TOK])
                dma("pool", "qo", out.ap()[960:1002, :], qaT[0:42, 5 * TOK:6 * TOK])
                dma("pool", "qo", out.ap()[1024:1152, 0:768], qbw_h[0][:, :])
            P.emit("sp", lambda h: h.nop(), waits=qall("qo"))
            P.realize(esG)
            return nc

        esR5 = contextlib.ExitStack()
        ow_sb = sbuf(esR5, "ow_sb", (128, KC * D), BF16)
        xtC = [sbuf(esR5, f"xtC{i}", (128, D), F32) for i in range(2)]
        x1t = [sbuf(esR5, f"x1t{i}", (128, D), F32) for i in range(2)]
        for c in range(KC):
            dma("pool", "qg",
                ow_sb[:].rearrange("p (c n) -> p c n", c=KC)[:, c:c + 1, :],
                ap("ow_t")[:, c:c + 1, :])
        t_ow = qall("qg")
        xrelC = [None, None]
        x1rel = [None, None]
        for tt in range(NT):
            xb = xtC[tt % 2]
            t_x = dma("sp", "qs", xb[:], ap("xq")[tt * 128:(tt + 1) * 128, :],
                      waits=[xrelC[tt % 2]])
            for h_ in range(NH):
                for hcg in range(4):
                    P.emit("pe", lambda h, h_=h_, hcg=hcg, tt=tt: h.matmul(
                        ps[hcg][:, :],
                        o_all[:, h_ * TOK + tt * 128:h_ * TOK + (tt + 1) * 128],
                        ow_sb[:, h_ * D + hcg * 512:h_ * D + (hcg + 1) * 512],
                        start=(h_ == 0), stop=(h_ == NH - 1)),
                        inc="pe" if (h_ == NH - 1 and hcg == 3) else None,
                        waits=(t_ow + [rel.get(i) for i in range(4)])
                        if (h_ == 0 and hcg == 0) else ())
            t_o = P.tok("pe")
            x1b = x1t[tt % 2]
            t_a = None
            for hcg in range(4):
                t_a = P.emit("dve", lambda h, hcg=hcg, x1b=x1b, xb=xb: h.tensor_add(
                    x1b[:, hcg * 512:(hcg + 1) * 512], ps[hcg][:, :],
                    xb[:, hcg * 512:(hcg + 1) * 512]), inc="dve",
                    waits=[t_o, t_x, x1rel[tt % 2]])
            for i in range(4):
                rel[i] = t_a
            xrelC[tt % 2] = t_a
            t_st = dma("sp", "qs", x1d.ap()[tt * 128:(tt + 1) * 128, :], x1b[:],
                       waits=[t_a])
            x1rel[tt % 2] = t_st
        t_x1d = qall("qs")
        P.realize(esG)
        esR5.close()
        esR1.close()

        if upto == "C":
            for tt in range(NT):
                dma("sp", "qo", out.ap()[tt * 128:(tt + 1) * 128, :],
                    x1d.ap()[tt * 128:(tt + 1) * 128, :], waits=t_x1d)
            P.emit("sp", lambda h: h.nop(), waits=qall("qo"))
            P.realize(esG)
            return nc

        esR6 = contextlib.ExitStack()
        x2 = sbuf(esR6, "x2", (128, 8 * D), F32)
        x1nT = sbuf(esR6, "x1nT", (128, KC * 1024), BF16)
        z_e = sbuf(esR6, "z_e", (128, IT_MOE * 1024), BF16)
        wgl = [sbuf(esR6, f"wgl{i}", (128, KC * 128), BF16) for i in range(2)]
        wul = [sbuf(esR6, f"wul{i}", (128, KC * 128), BF16) for i in range(2)]
        wdr = [sbuf(esR6, f"wdr{i}", (128, IT_MOE * 1024), BF16) for i in range(2)]
        onehot_sb = sbuf(esR6, "onehot_sb", (16, E * 128), BF16)
        combT = sbuf(esR6, "combT", (16, 1024), BF16)
        comb_b = sbuf(esR6, "comb_b", (128, 1024), BF16)
        lgmx = sbuf(esR6, "lgmx", (128, 1), F32)
        nlgmx = sbuf(esR6, "nlgmx", (128, 1), F32)
        pun = sbuf(esR6, "pun", (128, 16), F32)
        rank = sbuf(esR6, "rank", (128, 16), F32)
        tcmp = sbuf(esR6, "tcmp", (128, 16), F32)
        comb = sbuf(esR6, "comb", (128, 16), F32)
        den = sbuf(esR6, "den", (128, 1), F32)
        dinv = sbuf(esR6, "dinv", (128, 1), F32)
        sg = sbuf(esR6, "sg", (128, 8), F32)
        silu_s = sbuf(esR6, "silu_s", (128, 512), BF16)
        tmul = sbuf(esR6, "tmul", (128, 512), BF16)
        t_oneh = dma("pool", "qg", onehot_sb[:], ap("onehot"))

        x2rel = None
        for hf in range(2):
            toff = hf * 8
            for tt8 in range(8):
                tt = toff + tt8
                xsl = slice(tt8 * D, (tt8 + 1) * D)
                t_x = dma("sp", "qs", x2[:, xsl],
                          x1d.ap()[tt * 128:(tt + 1) * 128, :],
                          waits=t_x1d + ([] if x2rel is None else x2rel))
                t_xn = emit_ln(x2[:, xsl], t_x, extra=[P.tok("pe")])
                emit_transposes(
                    xn, KC, None,
                    lambda c, tt8=tt8: x1nT[:, c * 1024 + tt8 * 128:
                                            c * 1024 + (tt8 + 1) * 128], t_xn)
            t_nT = P.tok("act")

            for tt8 in range(8):
                for c in range(KC):
                    P.emit("pe", lambda h, c=c, tt8=tt8: h.matmul(
                        ps[0][:, 0:E],
                        x1nT[:, c * 1024 + tt8 * 128:c * 1024 + (tt8 + 1) * 128],
                        router_sb[:, c * E:(c + 1) * E],
                        start=(c == 0), stop=(c == KC - 1)),
                        inc="pe" if c == KC - 1 else None,
                        waits=[t_nT, rel.get(0)] if c == 0 else ())
                t_l = P.tok("pe")
                P.emit("dve", lambda h: h.reduce_max(lgmx[:], ps[0][:, 0:E],
                                                     axis=AX.X), inc="dve",
                       waits=[t_l])
                t_nm = P.emit("dve", lambda h: h.tensor_scalar(
                    nlgmx[:], lgmx[:], -1.0, None, ALU.mult), inc="dve",
                    waits=[P.tok("dve"), P.tok("act")])
                t_p = P.emit("act", lambda h: h.activation(
                    pun[:], ps[0][:, 0:E], AF.Exp, bias=nlgmx[:]), inc="act",
                    waits=[t_nm])
                rel[0] = t_p
                P.emit("dve", lambda h: h.memset(rank[:], 0.0), inc="dve",
                       waits=[t_p])
                for e2 in range(E):
                    P.emit("dve", lambda h, e2=e2: h.tensor_scalar(
                        tcmp[:], pun[:], pun[:, e2:e2 + 1], None, ALU.is_lt),
                        inc="dve", waits=[P.tok("dve")])
                    P.emit("dve", lambda h: h.tensor_add(rank[:], rank[:], tcmp[:]),
                           inc="dve", waits=[P.tok("dve")])
                P.emit("dve", lambda h: h.tensor_scalar(tcmp[:], rank[:], 3.0, None,
                                                        ALU.is_le), inc="dve",
                       waits=[P.tok("dve")])
                P.emit("dve", lambda h: h.tensor_mul(comb[:], pun[:], tcmp[:]),
                       inc="dve", waits=[P.tok("dve")])
                P.emit("dve", lambda h: h.reduce_sum(den[:], comb[:], axis=AX.X),
                       inc="dve", waits=[P.tok("dve")])
                P.emit("dve", lambda h: h.reciprocal(dinv[:], den[:]), inc="dve",
                       waits=[P.tok("dve")])
                t_cb = P.emit("dve", lambda h: h.tensor_scalar(
                    comb[:], comb[:], dinv[:], None, ALU.mult), inc="dve",
                    waits=[P.tok("dve")])
                t_tr = P.emit("pe", lambda h: h.transpose(
                    ps[6].bitcast(F32)[0:16, 0:128] if False else ps[6][0:16, 0:128],
                    comb[:, 0:16], id_f32[:]), inc="pe", waits=[t_cb, rel.get(6)])
                t_ct = P.emit("act", lambda h, tt8=tt8: h.copy(
                    combT[0:16, tt8 * 128:(tt8 + 1) * 128], ps[6][0:16, 0:128]),
                    inc="act", waits=[t_tr])
                rel[6] = t_ct
                for c in range(KC):
                    P.emit("pe", lambda h, c=c, tt8=tt8: h.matmul(
                        ps[1][:, 0:1],
                        x1nT[:, c * 1024 + tt8 * 128:c * 1024 + (tt8 + 1) * 128],
                        shgw_sb[:, c:c + 1], start=(c == 0), stop=(c == KC - 1)),
                        inc="pe" if c == KC - 1 else None,
                        waits=[rel.get(1)] if c == 0 else ())
                t_sgm = P.tok("pe")
                t_sg = P.emit("act", lambda h, tt8=tt8: h.activation(
                    sg[:, tt8:tt8 + 1], ps[1][:, 0:1], AF.Sigmoid), inc="act",
                    waits=[t_sgm])
                rel[1] = t_sg
            t_route = P.tok("act")
            t_combT = rel[6]

            wg_rel = [None, None]
            wd_rel = [None, None]
            t_wg = {}
            t_wd = {}

            def load_gu(kind, idx, buf, wtok):
                if kind == "moe":
                    e_, it = idx
                    ta = dma("pool", "qg",
                             wgl[buf][:].rearrange("p (c n) -> p c n", c=KC),
                             ap("wg_t")[e_, it], waits=[wtok])
                    tb = dma("pool", "qg",
                             wul[buf][:].rearrange("p (c n) -> p c n", c=KC),
                             ap("wu_t")[e_, it])
                else:
                    it = idx
                    ta = dma("pool", "qg",
                             wgl[buf][:].rearrange("p (c n) -> p c n", c=KC),
                             ap("shg_t")[it], waits=[wtok])
                    tb = dma("pool", "qg",
                             wul[buf][:].rearrange("p (c n) -> p c n", c=KC),
                             ap("shu_t")[it])
                return [ta, tb]

            def load_wd(kind, idx, buf, wtok):
                if kind == "moe":
                    e_, hh2 = idx
                    ta = dma("pool", "qg",
                             wdr[buf][:].rearrange("p (c n) -> p c n", c=IT_MOE),
                             ap("wd_t")[e_][:, :, hh2 * 1024:(hh2 + 1) * 1024],
                             waits=[wtok])
                else:
                    grp, hh2 = idx
                    ta = dma("pool", "qg",
                             wdr[buf][:].rearrange("p (c n) -> p c n", c=IT_MOE),
                             ap("shd_t")[:, grp * IT_MOE:(grp + 1) * IT_MOE,
                                         hh2 * 1024:(hh2 + 1) * 1024], waits=[wtok])
                return [ta]

            groups = [("moe", e_) for e_ in range(E)] + [("sh", g) for g in range(4)]
            gu_list = [(k, (g, it) if k == "moe" else g * IT_MOE + it)
                       for (k, g) in groups for it in range(IT_MOE)]
            for gi in range(2):
                k, idx = gu_list[gi]
                t_wg[gi] = load_gu(k, idx, gi % 2, None)
            wd_list = [(k, (g, hh2) if k == "moe" else (g, hh2))
                       for (k, g) in groups for hh2 in range(2)]
            for di in range(2):
                k, idx = wd_list[di]
                t_wd[di] = load_wd(wd_list[di][0], idx, di % 2, None)

            gi_count = [0]
            di_count = [0]

            for (kind, gidx) in groups:
                is_moe = kind == "moe"
                if is_moe:
                    e_ = gidx
                    for nb in range(2):
                        t_b = P.emit("pe", lambda h, e_=e_, nb=nb: h.matmul(
                            ps[4][:, :], onehot_sb[0:16, e_ * 128:(e_ + 1) * 128],
                            combT[0:16, nb * 512:(nb + 1) * 512],
                            start=True, stop=True), inc="pe",
                            waits=[t_combT, t_oneh, rel.get(4)])
                        t_cpb = P.emit("act", lambda h, nb=nb: h.copy(
                            comb_b[:, nb * 512:(nb + 1) * 512], ps[4][:, :]),
                            inc="act", waits=[t_b, P.tok("dve")])
                        rel[4] = t_cpb
                    t_cbb = P.tok("act")
                for it in range(IT_MOE):
                    gi = gi_count[0]
                    buf = gi % 2
                    t_w = t_wg[gi]
                    t_gu = None
                    for c in range(KC):
                        for bi, (pp, wsb, xo) in enumerate((
                                (ps[0], wgl[buf], 0), (ps[1], wgl[buf], 512),
                                (ps[2], wul[buf], 0), (ps[3], wul[buf], 512))):
                            P.emit("pe", lambda h, c=c, pp=pp, wsb=wsb, xo=xo:
                                   h.matmul(pp[:, :], wsb[:, c * 128:(c + 1) * 128],
                                            x1nT[:, c * 1024 + xo:c * 1024 + xo + 512],
                                            start=(c == 0), stop=(c == KC - 1)),
                                   inc="pe" if (c == KC - 1 and bi == 3) else None,
                                   waits=(t_w + [t_nT] + [rel.get(i) for i in range(4)])
                                   if (c == 0 and bi == 0) else ())
                    t_gu = P.tok("pe")
                    if gi + 2 < len(gu_list):
                        k2, idx2 = gu_list[gi + 2]
                        t_wg[gi + 2] = load_gu(k2, idx2, buf, t_gu)
                    for nb in range(2):
                        zsl = slice(it * 1024 + nb * 512, it * 1024 + (nb + 1) * 512)
                        pg = (ps[0], ps[1])[nb]
                        pu = (ps[2], ps[3])[nb]
                        t_si = P.emit("act", lambda h, pg=pg: h.activation(
                            silu_s[:], pg[:, :], AF.Silu), inc="act",
                            waits=[t_gu, P.tok("dve")])
                        rel[nb] = t_si
                        if is_moe:
                            P.emit("dve", lambda h, pu=pu, nb=nb: h.tensor_mul(
                                tmul[:], pu[:, :], comb_b[:, nb * 512:(nb + 1) * 512]),
                                inc="dve", waits=[t_gu, t_cbb, P.tok("act")])
                        else:
                            P.emit("dve", lambda h, pu=pu: h.tensor_copy(
                                tmul[:], pu[:, :]), inc="dve",
                                waits=[t_gu, P.tok("act")])
                        t_tm = P.tok("dve")
                        rel[2 + nb] = t_tm
                        t_zw = P.emit("dve", lambda h, zsl=zsl: h.tensor_mul(
                            z_e[:, zsl], tmul[:], silu_s[:]), inc="dve",
                            waits=[t_tm, t_si])
                    gi_count[0] += 1
                t_z = P.tok("dve")
                for hh2 in range(2):
                    di = di_count[0]
                    dbuf = di % 2
                    t_wdt = t_wd[di]
                    t_dlast = None
                    for pair in range(4):
                        for it in range(IT_MOE):
                            for sub in range(2):
                                tt8 = pair * 2 + sub
                                for hq in range(2):
                                    bank = 4 + sub * 2 + hq
                                    P.emit("pe", lambda h, it=it, tt8=tt8, hq=hq,
                                           hh2=hh2, dbuf=dbuf, bank=bank: h.matmul(
                                        ps[bank][:, :],
                                        z_e[:, it * 1024 + tt8 * 128:
                                            it * 1024 + (tt8 + 1) * 128],
                                        wdr[dbuf][:, it * 1024 + hq * 512:
                                                  it * 1024 + hq * 512 + 512],
                                        start=(it == 0), stop=(it == IT_MOE - 1)),
                                        inc="pe" if (it == IT_MOE - 1 and sub == 1
                                                     and hq == 1) else None,
                                        waits=([t_z] + t_wdt +
                                        [rel.get(4 + i) for i in range(4)])
                                        if (it == 0 and sub == 0 and hq == 0) else ())
                        t_d = P.tok("pe")
                        t_dlast = t_d
                        for sub in range(2):
                            tt8 = pair * 2 + sub
                            for hq in range(2):
                                bank = 4 + sub * 2 + hq
                                col = tt8 * D + hh2 * 1024 + hq * 512
                                if is_moe:
                                    t_ad = P.emit("dve", lambda h, col=col, bank=bank:
                                                  h.tensor_add(
                                        x2[:, col:col + 512], x2[:, col:col + 512],
                                        ps[bank][:, :]), inc="dve", waits=[t_d])
                                else:
                                    t_ad = P.emit("dve", lambda h, col=col, bank=bank,
                                                  tt8=tt8: h.scalar_tensor_tensor(
                                        x2[:, col:col + 512], ps[bank][:, :],
                                        sg[:, tt8:tt8 + 1], x2[:, col:col + 512],
                                        ALU.mult, ALU.add), inc="dve",
                                        waits=[t_d, t_route])
                                rel[bank] = t_ad
                    di_count[0] += 1
                    if di + 2 < len(wd_list):
                        k2, idx2 = wd_list[di + 2]
                        t_wd[di + 2] = load_wd(k2, idx2, dbuf, t_dlast)
            t_ffn = P.tok("dve")
            for tt8 in range(8):
                tt = toff + tt8
                dma("sp", "qo", out.ap()[tt * 128:(tt + 1) * 128, :],
                    x2[:, tt8 * D:(tt8 + 1) * D], waits=[t_ffn])
            x2rel = qall("qo")
            P.realize(esG)
        esR6.close()

        P.emit("sp", lambda h: h.nop(), waits=qall("qo"))
        P.realize(esG)
    return nc


def kernel(**inputs):
    pr, percore = _host_prepare(inputs)
    nc = build_nc()
    shared = {k: np.ascontiguousarray(v, dtype=np.float32) for k, v in pr.items()}
    in_maps = []
    for c in range(8):
        m = dict(shared)
        for k, v in percore[c].items():
            m[k] = np.ascontiguousarray(v, dtype=np.float32)
        in_maps.append(m)
    res = run_bass_kernel_spmd(nc, in_maps, core_ids=list(range(8)))
    full = np.empty((B, S, D), dtype=np.float32)
    for c in range(8):
        b, p = c // 2, c % 2
        o = np.asarray(res.results[c]["out"])
        for i in range(NT):
            g = 2 * i + p
            full[b, g * 128:(g + 1) * 128, :] = o[i * 128:(i + 1) * 128, :]
    return full


# version 02
# speedup vs baseline: 1.1570x; 1.1570x over previous
import contextlib
import math
import numpy as np
import concourse.bass as bass
import concourse.mybir as mybir
from concourse.bass_utils import run_bass_kernel_spmd

F32 = mybir.dt.float32
BF16 = mybir.dt.bfloat16
AF = mybir.ActivationFunctionType
ALU = mybir.AluOpType
AX = mybir.AxisListType

B, S, D, NH = 4, 4096, 2048, 16
ROPE, NOPE, VD = 64, 64, 128
QH = ROPE + NOPE
QL = D // 3
KVL = D // 8
THETA = 128000.0
SCALE = 1.0 / math.sqrt(QH)
EPS_LN = 1e-5
EPS_RMS = 1e-6
E = 16
I_MOE, I_SH = 1408, 5632
NT = 16
TOK = NT * 128
NKT = 32
IT_MOE = I_MOE // 128
IT_SH = I_SH // 128
KC = D // 128
QLC = [128, 128, 128, 128, 128, 42]
NEG = -30000.0

ENGS = ("pe", "dve", "act", "pool", "sp")


def _rope_tables():
    inv = 1.0 / THETA ** (np.arange(0, ROPE, 2, dtype=np.float64) / ROPE)
    pos = np.arange(S, dtype=np.float64)
    fr = np.outer(inv, pos)
    cos = np.cos(np.concatenate([fr, fr], 0))
    sin = np.sin(np.concatenate([fr, fr], 0))
    sin_signed = sin.copy()
    sin_signed[:32] *= -1.0
    return cos.astype(np.float32), sin_signed.astype(np.float32)


_PERM = np.concatenate([np.arange(0, 64, 2), np.arange(1, 64, 2)])


def _tile_k(w):
    k, n = w.shape
    return np.ascontiguousarray(w.reshape(k // 128, 128, n).transpose(1, 0, 2))


def _tile_cols(w, ct):
    k, n = w.shape
    return np.ascontiguousarray(
        w.reshape(k // 128, 128, n // ct, ct).transpose(2, 1, 0, 3))


def _host_prepare(inp):
    cos, sin = _rope_tables()
    pr = {}
    pr["qaw_t"] = _tile_k(inp["L0_attn_q_a_w"])
    pr["lnq"] = inp["L0_attn_q_a_ln_w"].reshape(1, QL)
    qbw = np.array(inp["L0_attn_q_b_w"], dtype=np.float32)
    for h in range(NH):
        base = h * QH + NOPE
        qbw[:, base:base + ROPE] = qbw[:, base + _PERM]
    qbh = np.zeros((NH, 128, 6, 128), np.float32)
    for h in range(NH):
        for c in range(6):
            w = QLC[c]
            qbh[h, :w, c, :] = qbw[c * 128:c * 128 + w, h * QH:(h + 1) * QH]
    pr["qbw_h"] = qbh
    kvaw = np.array(inp["L0_attn_kv_a_w"], dtype=np.float32)
    kvaw[:, KVL:] = kvaw[:, KVL + _PERM]
    pr["kvaw_t"] = _tile_k(kvaw)
    pr["lnkv"] = inp["L0_attn_kv_a_ln_w"].reshape(1, KVL)
    kvb3 = np.asarray(inp["L0_attn_kv_b_w"]).reshape(KVL, NH, NOPE + VD)
    pr["kvb_nope"] = np.ascontiguousarray(kvb3[:, :, :NOPE].reshape(KVL, NH * NOPE))
    pr["kvb_v"] = np.ascontiguousarray(kvb3[:, :, NOPE:].reshape(KVL, NH * VD))
    pr["ow_t"] = _tile_k(inp["L0_attn_o_w"])
    pr["router_t"] = _tile_k(np.ascontiguousarray(np.asarray(inp["L0_ffn_router_w"]).T))
    pr["wg_t"] = np.stack([_tile_cols(inp["L0_ffn_w_gate"][e], 128) for e in range(E)])
    pr["wu_t"] = np.stack([_tile_cols(inp["L0_ffn_w_up"][e], 128) for e in range(E)])
    pr["wd_t"] = np.stack([
        np.ascontiguousarray(np.asarray(inp["L0_ffn_w_down"][e], np.float32)
                             .reshape(IT_MOE, 128, 2, 1024).transpose(2, 1, 0, 3)
                             .reshape(2, 128, IT_MOE * 1024)) for e in range(E)])
    pr["shg_t"] = _tile_cols(inp["L0_ffn_sh_gate"], 128)
    pr["shu_t"] = _tile_cols(inp["L0_ffn_sh_up"], 128)
    pr["shd_t"] = np.ascontiguousarray(
        np.asarray(inp["L0_ffn_sh_down"], np.float32)
        .reshape(4, IT_MOE, 128, 2, 1024).transpose(0, 3, 2, 1, 4)
        .reshape(4, 2, 128, IT_MOE * 1024))
    pr["shgw_t"] = _tile_k(inp["L0_ffn_sh_gate_w"])
    pr["id128"] = np.eye(128, dtype=np.float32)
    pr["ones_row"] = np.ones((1, 128), np.float32)
    mswap = np.zeros((64, 128), np.float32)
    for m in range(32):
        mswap[m, 96 + m] = 1.0
        mswap[32 + m, 64 + m] = 1.0
    pr["mswapT"] = np.ascontiguousarray(mswap.T)
    mshift = np.zeros((64, 128), np.float32)
    for m in range(64):
        mshift[m, 64 + m] = 1.0
    pr["mshiftT"] = np.ascontiguousarray(mshift.T)
    r64 = np.zeros((64, 64), np.float32)
    for m in range(32):
        r64[m, 32 + m] = 1.0
        r64[32 + m, m] = 1.0
    pr["r64T"] = np.ascontiguousarray(r64.T)
    pr["ck"] = np.ascontiguousarray(cos)
    pr["sk"] = np.ascontiguousarray(sin)
    oneh = np.zeros((E, E * 128), np.float32)
    for e in range(E):
        oneh[e, e * 128:(e + 1) * 128] = 1.0
    pr["onehot"] = oneh
    percore = []
    x = np.ascontiguousarray(np.asarray(inp["L0_attn_x"], dtype=np.float32))
    for c in range(8):
        b, p = c // 2, c % 2
        d = {}
        slots = np.array([2 * i + p for i in range(NT)])
        rows = (slots[:, None] * 128 + np.arange(128)[None, :]).reshape(-1)
        d["xq"] = np.ascontiguousarray(x[b][rows])
        d["xk"] = x[b]
        d["cq"] = np.ascontiguousarray(cos[:, rows])
        d["sq"] = np.ascontiguousarray(sin[:, rows])
        for W in (256, 512):
            cc, rr = np.meshgrid(np.arange(W), np.arange(128))
            d[f"mask1_{W}"] = np.where(cc - rr > W - 256 + 128 * p, NEG, 0.0
                                       ).astype(np.float32)
        m2 = np.zeros((8, 128, 512), np.float32)
        for dd in range(8):
            for t in range(4):
                cc, rr = np.meshgrid(np.arange(128), np.arange(128))
                m2[dd][:, t * 128:(t + 1) * 128] = np.where(
                    (dd - 2 * t - p) * 128 + rr > cc, NEG, 0.0)
        d["mask2"] = m2
        percore.append(d)
    return pr, percore


class Prog:
    def __init__(self, nc):
        self.nc = nc
        self.items = {e: [] for e in ENGS}
        self.sem = {}
        self.cnt = {}
        self.water = {e: {} for e in ENGS}

    def new_sem(self, name, es):
        self.sem[name] = es.enter_context(self.nc.semaphore(name))
        self.cnt[name] = 0

    def tok(self, name):
        return (name, self.cnt[name])

    paranoid = ()

    def emit(self, eng, fn, inc=None, inck=1, waits=()):
        if self.paranoid:
            waits = list(waits) + [self.tok(s) for s in self.paranoid]
        wl = []
        for tkn in waits:
            if tkn is None:
                continue
            s, v = tkn
            if v <= 0:
                continue
            w = self.water[eng]
            if w.get(s, 0) >= v:
                continue
            w[s] = v
            wl.append((self.sem[s], v))
        tk = None
        sem_h = None
        if inc is not None:
            self.cnt[inc] += inck
            tk = (inc, self.cnt[inc])
            sem_h = self.sem[inc]

        def wrapped(h, fn=fn, wl=wl, sem_h=sem_h, inck=inck):
            for (sh, v) in wl:
                h.wait_ge(sh, v)
            inst = fn(h)
            if sem_h is not None:
                inst.then_inc(sem_h, inck)
        self.items[eng].append(wrapped)
        return tk

    def realize(self, es):
        with self.nc.Block() as block:
            for eng, deco in (("pe", block.tensor), ("dve", block.vector),
                              ("act", block.scalar), ("pool", block.gpsimd),
                              ("sp", block.sync)):
                fns = self.items[eng]
                if fns:
                    deco(lambda h, fns=list(fns): [fn(h) for fn in fns])
        self.items = {e: [] for e in ENGS}
        for e in ENGS:
            for s in self.sem:
                if not s.startswith("q"):
                    self.water[e][s] = self.cnt[s]


def build_nc(upto="full", nheads=NH):
    nc = bass.Bass("TRN2", target_bir_lowering=False)
    dr = {}

    def din(name, shape):
        dr[name] = nc.dram_tensor(name, list(shape), F32, kind="ExternalInput")

    din("xq", (TOK, D)); din("xk", (S, D))
    din("qaw_t", (128, KC, QL)); din("lnq", (1, QL)); din("qbw_h", (NH, 128, 6, 128))
    din("kvaw_t", (128, KC, KVL + ROPE)); din("lnkv", (1, KVL))
    din("kvb_nope", (KVL, NH * NOPE)); din("kvb_v", (KVL, NH * VD))
    din("ow_t", (128, KC, D)); din("router_t", (128, KC, E))
    din("wg_t", (E, IT_MOE, 128, KC, 128)); din("wu_t", (E, IT_MOE, 128, KC, 128))
    din("wd_t", (E, 2, 128, IT_MOE * 1024))
    din("shg_t", (IT_SH, 128, KC, 128)); din("shu_t", (IT_SH, 128, KC, 128))
    din("shd_t", (4, 2, 128, IT_MOE * 1024)); din("shgw_t", (128, KC, 1))
    din("id128", (128, 128)); din("ones_row", (1, 128))
    din("mswapT", (128, 64)); din("mshiftT", (128, 64)); din("r64T", (64, 64))
    din("ck", (ROPE, S)); din("sk", (ROPE, S))
    din("cq", (ROPE, TOK)); din("sq", (ROPE, TOK))
    din("mask1_256", (128, 256)); din("mask1_512", (128, 512))
    din("mask2", (8, 128, 512)); din("onehot", (E, E * 128))
    out = nc.dram_tensor("out", [TOK, D], F32, kind="ExternalOutput")
    x1d = nc.dram_tensor("x1d", [TOK, D], F32, kind="Internal")
    ap = lambda n: dr[n].ap()

    P = Prog(nc)
    esG = contextlib.ExitStack()
    with esG:
        for s in ("pe", "dve", "act", "pool"):
            P.new_sem(s, esG)
        QROT = 8
        for q in ("qg", "qs", "qo"):
            for i in range(QROT):
                P.new_sem(f"{q}{i}", esG)
        qrot = {"qg": 0, "qs": 0, "qo": 0}

        def qall(q):
            return [P.tok(f"{q}{i}") for i in range(QROT)]
        ps = [esG.enter_context(nc.psum_tensor(f"ps{i}", [128, 512], F32))
              for i in range(8)]
        rel = {}

        def sbuf(es, name, shp, dt=F32, side="right"):
            return es.enter_context(nc.sbuf_tensor(name, list(shp), dt, side=side))

        def dma(eng, q, dst, src, waits=()):
            s_name = f"{q}{qrot[q]}"
            qrot[q] = (qrot[q] + 1) % QROT
            return P.emit(eng, lambda h, d=dst, s=src: h.dma_start(d, s),
                          inc=s_name, inck=16, waits=waits)

        L = lambda name, shp, dt=F32: sbuf(esG, name, shp, dt, side="left")
        id_bf = L("id_bf", (128, 128), BF16)
        id_f32 = L("id_f32", (128, 128), F32)
        ones_bf = L("ones_bf", (1, 128), BF16)
        router_sb = L("router_sb", (128, KC * E), BF16)
        shgw_sb = L("shgw_sb", (128, KC), BF16)
        xn = L("xn", (128, D), BF16)
        xnT = L("xnT", (128, KC * 128), BF16)
        stats = L("stats", (128, 24), F32)
        mv = L("mv", (128, 2), F32)
        sd_t = L("sd_t", (128, 1), F32)
        rstd = L("rstd", (128, 1), F32)
        sq1 = L("sq1", (128, 1), F32)
        sq2 = L("sq2", (128, 1), F32)
        sqs = L("sqs", (128, 1), F32)

        esR1 = contextlib.ExitStack()
        o_all = sbuf(esR1, "o_all", (128, NH * TOK), BF16)
        esR2 = contextlib.ExitStack()
        qaT = sbuf(esR2, "qaT", (128, 6 * TOK), BF16)
        ckvT = sbuf(esR2, "ckvT", (128, 2 * S), BF16)
        kpe_aug = sbuf(esR2, "kpe_aug", (65, S), BF16)
        kvbn_sb = sbuf(esR2, "kvbn_sb", (128, 2 * NH * NOPE), BF16)
        kvbv_sb = sbuf(esR2, "kvbv_sb", (128, 2 * NH * VD), BF16)

        esR3 = contextlib.ExitStack()
        qaw_sb = sbuf(esR3, "qaw_sb", (128, KC * QL), BF16)
        kva_sb = sbuf(esR3, "kva_sb", (128, KC * (KVL + ROPE)), BF16)
        lnq_b = sbuf(esR3, "lnq_b", (128, QL), BF16)
        lnkv_b = sbuf(esR3, "lnkv_b", (128, KVL), BF16)
        lnrow = sbuf(esR3, "lnrow", (1, QL), BF16)
        ck_sb = sbuf(esR3, "ck_sb", (ROPE, S), BF16)
        sk_sb = sbuf(esR3, "sk_sb", (ROPE, S), BF16)
        r64_sb = sbuf(esR3, "r64_sb", (64, 64), BF16)
        xtile = [sbuf(esR3, f"xtileA{i}", (128, D), F32) for i in range(2)]
        qa_n = sbuf(esR3, "qa_n", (128, QL), BF16)
        ckv_n = sbuf(esR3, "ckv_n", (128, KVL), BF16)
        kpe_tm = sbuf(esR3, "kpe_tm", (128, ROPE), BF16)
        sqscr = sbuf(esR3, "sqscr", (128, 512), BF16)
        tp1 = sbuf(esR3, "tp1", (64, 512), F32)
        tp2 = sbuf(esR3, "tp2", (64, 512), F32)

        dma("pool", "qg", id_bf[:], ap("id128"))
        dma("sp", "qs", id_f32[:], ap("id128"))
        dma("pool", "qg", ones_bf[:], ap("ones_row"))
        dma("pool", "qg", router_sb[:].rearrange("p (c n) -> p c n", c=KC), ap("router_t"))
        dma("pool", "qg", shgw_sb[:].rearrange("p (c n) -> p c n", c=KC), ap("shgw_t"))
        dma("pool", "qg", kvbn_sb[:].rearrange("p (c n) -> p c n", c=2),
            ap("kvb_nope").rearrange("(c p) n -> p c n", p=128))
        dma("pool", "qg", kvbv_sb[:].rearrange("p (c n) -> p c n", c=2),
            ap("kvb_v").rearrange("(c p) n -> p c n", p=128))
        dma("pool", "qg", ck_sb[:], ap("ck"))
        dma("pool", "qg", sk_sb[:], ap("sk"))
        dma("pool", "qg", r64_sb[:], ap("r64T"))
        t_wconst = qall("qg")
        t_qaw = dma("pool", "qg", qaw_sb[:].rearrange("p (c n) -> p c n", c=KC), ap("qaw_t"))
        t_kva = dma("pool", "qg", kva_sb[:].rearrange("p (c n) -> p c n", c=KC), ap("kvaw_t"))
        t_l1 = dma("pool", "qg", lnrow[0:1, 0:QL], ap("lnq"))

        def bcast_row(width, dst, t_row):
            t_cp = None
            for j in range((width + 511) // 512):
                w = min(512, width - j * 512)
                t_mm = P.emit("pe", lambda h, j=j, w=w: h.matmul(
                    ps[5][:, 0:w], ones_bf[0:1, :], lnrow[0:1, j * 512:j * 512 + w],
                    start=True, stop=True), inc="pe", waits=[t_row, rel.get(5)])
                t_cp = P.emit("act", lambda h, j=j, w=w, dst=dst: h.copy(
                    dst[:, j * 512:j * 512 + w], ps[5][:, 0:w]), inc="act",
                    waits=[t_mm])
                rel[5] = t_cp
            return t_cp

        t_lnqb = bcast_row(QL, lnq_b, t_l1)
        t_l2 = dma("pool", "qg", lnrow[0:1, 0:KVL], ap("lnkv"), waits=[rel[5]])
        t_lnkb = bcast_row(KVL, lnkv_b, t_l2)

        def emit_ln(xbuf, t_x, extra=()):
            for sl in range(4):
                P.emit("dve", lambda h, sl=sl, xb=xbuf: h.bn_stats(
                    stats[:, sl * 6:(sl + 1) * 6], xb[:, sl * 512:(sl + 1) * 512]),
                    inc="dve", waits=[t_x] if sl == 0 else ())
            P.emit("dve", lambda h: h.bn_aggr(mv[:], stats[:]), inc="dve",
                   waits=[P.tok("dve")])
            P.emit("dve", lambda h: h.tensor_scalar(sq1[:], mv[:, 1:2], EPS_LN,
                                                    None, ALU.add), inc="dve",
                   waits=[P.tok("dve")])
            P.emit("act", lambda h: h.activation(sd_t[:], sq1[:], AF.Sqrt), inc="act",
                   waits=[P.tok("dve")])
            P.emit("dve", lambda h: h.reciprocal(rstd[:], sd_t[:]), inc="dve",
                   waits=[P.tok("act")])
            P.emit("dve", lambda h, xb=xbuf: h.tensor_scalar(
                xn[:], xb[:], mv[:, 0:1], rstd[:], ALU.subtract, ALU.mult),
                inc="dve", waits=[P.tok("dve")] + list(extra))
            return P.tok("dve")

        def emit_transposes(src, n, widths, dst_fn, t_src):
            t_last = None
            for c in range(n):
                w = widths[c] if widths else 128
                bank = 6 + (c % 2)
                t_t = P.emit("pe", lambda h, c=c, w=w, bank=bank, src=src: h.transpose(
                    ps[bank].bitcast(BF16)[0:w, 0:128], src[:, c * 128:c * 128 + w],
                    id_bf[:]), inc="pe", waits=[t_src, rel.get(bank)])
                t_last = P.emit("act", lambda h, d=dst_fn(c), w=w, bank=bank: h.copy(
                    d, ps[bank].bitcast(BF16)[0:w, 0:128]), inc="act", waits=[t_t])
                rel[bank] = t_last
            return t_last

        xrel = [None, None]
        for tt in range(NT):
            xb = xtile[tt % 2]
            t_x = dma("sp", "qs", xb[:], ap("xq")[tt * 128:(tt + 1) * 128, :],
                      waits=[xrel[tt % 2]])
            t_xn = emit_ln(xb, t_x, extra=[P.tok("pe")])
            xrel[tt % 2] = t_xn
            t_xT = emit_transposes(xn, KC, None,
                                   lambda c: xnT[:, c * 128:(c + 1) * 128], t_xn)
            for c in range(KC):
                last = c == KC - 1
                P.emit("pe", lambda h, c=c: h.matmul(
                    ps[0][:, :], xnT[:, c * 128:(c + 1) * 128],
                    qaw_sb[:, c * QL:c * QL + 512], start=(c == 0), stop=last),
                    waits=[t_xT, t_qaw, rel.get(0), rel.get(1)] if c == 0 else ())
                P.emit("pe", lambda h, c=c: h.matmul(
                    ps[1][:, 0:170], xnT[:, c * 128:(c + 1) * 128],
                    qaw_sb[:, c * QL + 512:(c + 1) * QL], start=(c == 0), stop=last),
                    inc="pe" if last else None)
            t_qa = P.tok("pe")
            P.emit("act", lambda h: h.activation(sqscr[:, 0:512], ps[0][:, :],
                                                 AF.Square, accum_out=sq1[:]),
                   inc="act", waits=[t_qa, P.tok("dve")])
            P.emit("act", lambda h: h.activation(sqscr[:, 0:170], ps[1][:, 0:170],
                                                 AF.Square, accum_out=sq2[:]),
                   inc="act")
            P.emit("dve", lambda h: h.tensor_add(sqs[:], sq1[:], sq2[:]), inc="dve",
                   waits=[P.tok("act")])
            P.emit("dve", lambda h: h.tensor_scalar(sqs[:], sqs[:], 1.0 / QL, EPS_RMS,
                                                    ALU.mult, ALU.add), inc="dve",
                   waits=[P.tok("dve")])
            P.emit("act", lambda h: h.activation(sd_t[:], sqs[:], AF.Sqrt), inc="act",
                   waits=[P.tok("dve")])
            P.emit("dve", lambda h: h.reciprocal(rstd[:], sd_t[:]), inc="dve",
                   waits=[P.tok("act")])
            P.emit("dve", lambda h: h.tensor_scalar(qa_n[:, 0:512], ps[0][:, :],
                                                    rstd[:], None, ALU.mult),
                   inc="dve", waits=[P.tok("dve")])
            t_ev = P.emit("dve", lambda h: h.tensor_scalar(
                qa_n[:, 512:QL], ps[1][:, 0:170], rstd[:], None, ALU.mult), inc="dve")
            rel[0] = t_ev
            rel[1] = t_ev
            t_qan = P.emit("dve", lambda h: h.tensor_mul(qa_n[:], qa_n[:], lnq_b[:]),
                           inc="dve", waits=[t_ev, t_lnqb])
            emit_transposes(qa_n, 6, QLC,
                            lambda c, tt=tt: qaT[0:QLC[c],
                                                 c * TOK + tt * 128:c * TOK + (tt + 1) * 128],
                            t_qan)
        P.realize(esG)

        for tk in range(NKT):
            xb = xtile[tk % 2]
            t_x = dma("sp", "qs", xb[:], ap("xk")[tk * 128:(tk + 1) * 128, :],
                      waits=[xrel[tk % 2]])
            t_xn = emit_ln(xb, t_x, extra=[P.tok("pe")])
            xrel[tk % 2] = t_xn
            t_xT = emit_transposes(xn, KC, None,
                                   lambda c: xnT[:, c * 128:(c + 1) * 128], t_xn)
            for c in range(KC):
                last = c == KC - 1
                P.emit("pe", lambda h, c=c: h.matmul(
                    ps[0][:, 0:KVL + ROPE], xnT[:, c * 128:(c + 1) * 128],
                    kva_sb[:, c * 320:(c + 1) * 320], start=(c == 0), stop=last),
                    inc="pe" if last else None,
                    waits=[t_xT, t_kva, rel.get(0)] if c == 0 else ())
            t_kv = P.tok("pe")
            P.emit("act", lambda h: h.activation(sqscr[:, 0:KVL], ps[0][:, 0:KVL],
                                                 AF.Square, accum_out=sqs[:]),
                   inc="act", waits=[t_kv, P.tok("dve")])
            P.emit("dve", lambda h: h.tensor_scalar(sqs[:], sqs[:], 1.0 / KVL,
                                                    EPS_RMS, ALU.mult, ALU.add),
                   inc="dve", waits=[P.tok("act")])
            P.emit("act", lambda h: h.activation(sd_t[:], sqs[:], AF.Sqrt), inc="act",
                   waits=[P.tok("dve")])
            P.emit("dve", lambda h: h.reciprocal(rstd[:], sd_t[:]), inc="dve",
                   waits=[P.tok("act")])
            t_craw = P.emit("dve", lambda h: h.tensor_scalar(
                ckv_n[:], ps[0][:, 0:KVL], rstd[:], None, ALU.mult),
                inc="dve", waits=[P.tok("dve")])
            t_cn = P.emit("dve", lambda h: h.tensor_mul(ckv_n[:], ckv_n[:], lnkv_b[:]),
                          inc="dve", waits=[P.tok("dve"), t_lnkb])
            t_kp = P.emit("act", lambda h: h.copy(kpe_tm[:], ps[0][:, KVL:KVL + ROPE]),
                          inc="act", waits=[t_kv, t_craw])
            rel[0] = t_kp
            emit_transposes(ckv_n, 2, None,
                            lambda c, tk=tk: ckvT[:, c * S + tk * 128:c * S + (tk + 1) * 128],
                            t_cn)
            bank = 6
            t_t = P.emit("pe", lambda h: h.transpose(
                ps[bank].bitcast(BF16)[0:ROPE, 0:128], kpe_tm[:], id_bf[:]),
                inc="pe", waits=[t_kp, rel.get(bank)])
            t_cpy = P.emit("act", lambda h, tk=tk: h.copy(
                kpe_aug[0:64, tk * 128:(tk + 1) * 128],
                ps[bank].bitcast(BF16)[0:ROPE, 0:128]), inc="act", waits=[t_t])
            rel[bank] = t_cpy
        t_kperaw = P.tok("act")

        P.emit("dve", lambda h: h.memset(kpe_aug[64:65, :], 1.0), inc="dve",
               waits=[t_kperaw])
        for j in range(8):
            slc = slice(j * 512, (j + 1) * 512)
            bank = j % 2
            t_m = P.emit("pe", lambda h, slc=slc, bank=bank: h.matmul(
                ps[bank][0:64, :], r64_sb[:], kpe_aug[0:64, slc],
                start=True, stop=True), inc="pe",
                waits=[t_kperaw, rel.get(bank), P.tok("dve")] + t_wconst)
            P.emit("dve", lambda h, slc=slc, bank=bank: h.tensor_mul(
                tp1[:], ps[bank][0:64, :], sk_sb[:, slc]), inc="dve", waits=[t_m])
            P.emit("dve", lambda h, slc=slc: h.tensor_mul(
                tp2[:], kpe_aug[0:64, slc], ck_sb[:, slc]), inc="dve")
            t_add = P.emit("dve", lambda h, slc=slc: h.tensor_add(
                kpe_aug[0:64, slc], tp1[:], tp2[:]), inc="dve", waits=[P.tok("dve")])
            rel[bank] = t_add
        t_kpe = P.tok("dve")
        P.realize(esG)
        esR3.close()

        if upto == "A":
            t1 = dma("pool", "qo", out.ap()[0:128, :], qaT[:, 0:TOK])
            t2 = dma("pool", "qo", out.ap()[128:256, :], ckvT[:, 0:2048])
            t3 = dma("pool", "qo", out.ap()[256:321, :], kpe_aug[0:65, 0:2048])
            P.emit("sp", lambda h: h.nop(), waits=qall("qo"))
            P.realize(esG)
            return nc

        esR4 = contextlib.ExitStack()
        qbw_h = [sbuf(esR4, f"qbw_h{i}", (128, 6 * 128), BF16) for i in range(2)]
        m1_256 = sbuf(esR4, "m1_256", (128, 256), BF16)
        m1_512 = sbuf(esR4, "m1_512", (128, 512), BF16)
        m2_sb = sbuf(esR4, "m2_sb", (128, 8 * 512), BF16)
        cq_sb = sbuf(esR4, "cq_sb", (ROPE, TOK), BF16)
        sq_sb = sbuf(esR4, "sq_sb", (ROPE, TOK), BF16)
        mswap_sb = sbuf(esR4, "mswap_sb", (128, 64), BF16)
        mshift_sb = sbuf(esR4, "mshift_sb", (128, 64), BF16)
        qraw = sbuf(esR4, "qraw", (128, TOK), BF16)
        qpe_aug = sbuf(esR4, "qpe_aug", (65, TOK), BF16)
        knope_h = [sbuf(esR4, f"knope_h{i}", (64, S), BF16) for i in range(2)]
        v2 = sbuf(esR4, "v2", (128, NKT * VD), BF16)
        tb1 = sbuf(esR4, "tb1", (64, 512), F32)
        tb2 = sbuf(esR4, "tb2", (64, 512), F32)
        mstats = sbuf(esR4, "mstats", (128, NT * 8), F32)
        zstats = sbuf(esR4, "zstats", (128, NT * 8), F32)
        mins = sbuf(esR4, "mins", (128, NT), F32)
        minbf = sbuf(esR4, "minbf", (128, NT), BF16)
        minr = sbuf(esR4, "minr", (128, NT), F32)
        t8a = sbuf(esR4, "t8a", (128, NT * 8), F32)
        zw8a = sbuf(esR4, "zw8a", (128, NT * 8), F32)
        zca = sbuf(esR4, "zca", (128, NT), F32)
        zia = sbuf(esR4, "zia", (128, NT), F32)
        zrow = sbuf(esR4, "zrow", (1, TOK), BF16)
        attnT = [sbuf(esR4, f"attnT{i}", (128, 512), BF16) for i in range(4)]
        escr = sbuf(esR4, "escr", (128, 512), BF16)
        zbc = sbuf(esR4, "zbc", (128, 512), BF16)

        dma("pool", "qg", m1_256[:], ap("mask1_256"))
        dma("pool", "qg", m1_512[:], ap("mask1_512"))
        dma("pool", "qg", m2_sb[:].rearrange("p (c n) -> p c n", c=8),
            ap("mask2").rearrange("c p n -> p c n"))
        dma("pool", "qg", cq_sb[:], ap("cq"))
        dma("pool", "qg", sq_sb[:], ap("sq"))
        dma("pool", "qg", mswap_sb[:], ap("mswapT"))
        dma("pool", "qg", mshift_sb[:], ap("mshiftT"))
        t_bconst = qall("qg")

        t_qbw = []
        for h_ in range(2):
            t_qbw.append(dma("pool", "qg",
                             qbw_h[h_][:].rearrange("p (c n) -> p c n", c=6),
                             ap("qbw_h")[h_]))

        def p1_tiles(i):
            ext = (2 * i + 2) * 128
            full, remn = divmod(ext, 512)
            tl = [(j * 512, 512) for j in range(full)]
            if remn:
                tl.append((full * 512, remn))
            return tl

        att_rel = [None, None, None, None]
        for h_ in range(nheads):
            hh = h_ % 2
            buf = qbw_h[h_ % 2]
            P.emit("dve", lambda h: h.memset(qpe_aug[64:65, :], 0.0), inc="dve",
                   waits=[P.tok("pe")])
            t_qpz = P.tok("dve")
            t_qlast = None
            for nb in range(4):
                qsl = slice(nb * 512, (nb + 1) * 512)
                pq = ps[nb % 2]
                for c in range(6):
                    last = c == 5
                    P.emit("pe", lambda h, c=c, nb=nb, pq=pq, buf=buf: h.matmul(
                        pq[:, :], buf[0:QLC[c], c * 128:(c + 1) * 128],
                        qaT[0:QLC[c], c * TOK + nb * 512:c * TOK + (nb + 1) * 512],
                        start=(c == 0), stop=last),
                        inc="pe" if last else None,
                        waits=([t_qbw[h_], rel.get(nb % 2)] + t_bconst)
                        if c == 0 else ())
                t_q = P.tok("pe")
                t_qlast = t_q
                t_qr = P.emit("act", lambda h, qsl=qsl, pq=pq: h.mul(
                    qraw[:, qsl], pq[:, :], SCALE), inc="act",
                    waits=[t_q, P.tok("dve")])
                rel[nb % 2] = t_qr
                P.emit("pe", lambda h, qsl=qsl: h.matmul(
                    ps[2][0:64, :], mswap_sb[:], qraw[:, qsl], start=True, stop=True),
                    inc="pe", waits=[t_qr, rel.get(2)])
                P.emit("pe", lambda h, qsl=qsl: h.matmul(
                    ps[3][0:64, :], mshift_sb[:], qraw[:, qsl], start=True, stop=True),
                    inc="pe", waits=[rel.get(3)])
                t_rot = P.tok("pe")
                t_d1 = P.emit("dve", lambda h, qsl=qsl: h.tensor_mul(
                    tb1[:], ps[2][0:64, :], sq_sb[:, qsl]), inc="dve",
                    waits=[t_rot] + t_bconst)
                rel[2] = t_d1
                t_d2 = P.emit("dve", lambda h, qsl=qsl: h.tensor_mul(
                    tb2[:], ps[3][0:64, :], cq_sb[:, qsl]), inc="dve")
                rel[3] = t_d2
                P.emit("dve", lambda h, qsl=qsl: h.tensor_add(
                    qpe_aug[0:64, qsl], tb1[:], tb2[:]), inc="dve",
                    waits=[t_d2, t_qpz])
            t_qf = P.tok("dve")
            if upto == "QF":
                dma("pool", "qo", out.ap()[256:384, :], qraw[:, :], waits=[t_qf])
                dma("pool", "qo", out.ap()[384:449, :], qpe_aug[0:65, :], waits=[t_qf])
                P.emit("sp", lambda h: h.nop(), waits=qall("qo"))
                P.realize(esG)
                return nc
            if h_ + 2 < nheads:
                t_qbw.append(dma("pool", "qg",
                                 qbw_h[h_ % 2][:].rearrange("p (c n) -> p c n", c=6),
                                 ap("qbw_h")[h_ + 2], waits=[t_qlast]))

            if hh == 0:
                for j in range(8):
                    ksl = slice(j * 512, (j + 1) * 512)
                    for hp in range(2):
                        bank = 4 + ((2 * j + hp) % 2)
                        pk = ps[bank]
                        for c in range(2):
                            last = c == 1
                            P.emit("pe", lambda h, c=c, j=j, pk=pk, h_=h_, hp=hp:
                                   h.matmul(
                                pk[0:64, :],
                                kvbn_sb[:, c * NH * NOPE + (h_ + hp) * NOPE:
                                        c * NH * NOPE + (h_ + hp + 1) * NOPE],
                                ckvT[:, c * S + j * 512:c * S + (j + 1) * 512],
                                start=(c == 0), stop=last),
                                inc="pe" if last else None,
                                waits=([rel.get(bank)] + t_bconst) if c == 0 else ())
                        t_k = P.tok("pe")
                        t_cp = P.emit("act", lambda h, ksl=ksl, pk=pk, hp=hp: h.copy(
                            knope_h[hp][:, ksl], pk[0:64, :]), inc="act", waits=[t_k])
                        rel[bank] = t_cp
            for tk in range(NKT):
                bank = 4 + (tk % 2)
                pk = ps[bank]
                for c in range(2):
                    last = c == 1
                    P.emit("pe", lambda h, c=c, tk=tk, pk=pk, h_=h_: h.matmul(
                        pk[:, 0:VD],
                        ckvT[:, c * S + tk * 128:c * S + (tk + 1) * 128],
                        kvbv_sb[:, c * NH * VD + h_ * VD:c * NH * VD + (h_ + 1) * VD],
                        start=(c == 0), stop=last),
                        inc="pe" if last else None,
                        waits=([rel.get(bank)] + t_bconst) if c == 0 else ())
                t_v = P.tok("pe")
                t_cp = P.emit("act", lambda h, tk=tk, pk=pk: h.copy(
                    v2[:, tk * VD:(tk + 1) * VD], pk[:, 0:VD]),
                    inc="act", waits=[t_v])
                rel[bank] = t_cp
            t_kvb = P.tok("act")

            for qt in range(NT):
                tiles = p1_tiles(qt)
                nE = len(tiles)
                c0 = qt * 8
                for ti, (k0, w) in enumerate(tiles):
                    bank = ti % 2
                    pq = ps[bank]
                    P.emit("pe", lambda h, k0=k0, w=w, pq=pq, qt=qt, hh=hh: h.matmul(
                        pq[:, 0:w], qraw[0:64, qt * 128:(qt + 1) * 128],
                        knope_h[hh][:, k0:k0 + w],
                        start=True, stop=False),
                        waits=[t_qf, t_kvb, t_kpe, rel.get(bank)])
                    P.emit("pe", lambda h, k0=k0, w=w, pq=pq, qt=qt: h.matmul(
                        pq[:, 0:w], qpe_aug[0:65, qt * 128:(qt + 1) * 128],
                        kpe_aug[0:65, k0:k0 + w], start=False, stop=True), inc="pe")
                    t_s = P.tok("pe")
                    if ti == nE - 1:
                        msk = m1_256 if w == 256 else m1_512
                        t_s = P.emit("dve", lambda h, w=w, pq=pq, msk=msk:
                                     h.tensor_add(pq[:, 0:w], pq[:, 0:w], msk[:, 0:w]),
                                     inc="dve", waits=[t_s] + t_bconst)
                    t_mk = P.emit("dve", lambda h, w=w, pq=pq, ci=c0 + ti:
                                  h.reduce_max(mstats[:, ci:ci + 1], pq[:, 0:w],
                                               axis=AX.X, negate=True), inc="dve",
                                  waits=[t_s])
                    t_e = P.emit("act", lambda h, w=w, pq=pq, ci=c0 + ti: h.activation(
                        escr[:, 0:w], pq[:, 0:w], AF.Exp,
                        bias=mstats[:, ci:ci + 1],
                        accum_out=zstats[:, ci:ci + 1]), inc="act", waits=[t_mk])
                    rel[bank] = t_e
                t_z = P.tok("act")
                P.emit("dve", lambda h, nE=nE, c0=c0, qt=qt: h.tensor_reduce(
                    mins[:, qt:qt + 1], mstats[:, c0:c0 + nE], AX.X, ALU.min),
                    inc="dve", waits=[t_z])
                P.emit("dve", lambda h, qt=qt: h.tensor_copy(
                    minbf[:, qt:qt + 1], mins[:, qt:qt + 1]), inc="dve",
                    waits=[P.tok("dve")])
                t_mr = P.emit("dve", lambda h, qt=qt: h.tensor_copy(
                    minr[:, qt:qt + 1], minbf[:, qt:qt + 1]), inc="dve",
                    waits=[P.tok("dve")])
                t_t8 = P.emit("act", lambda h, nE=nE, c0=c0, qt=qt: h.activation(
                    t8a[:, c0:c0 + nE], mstats[:, c0:c0 + nE], AF.Exp,
                    bias=minr[:, qt:qt + 1], scale=-1.0), inc="act", waits=[t_mr])
                P.emit("dve", lambda h, nE=nE, c0=c0: h.tensor_mul(
                    zw8a[:, c0:c0 + nE], t8a[:, c0:c0 + nE], zstats[:, c0:c0 + nE]),
                    inc="dve", waits=[t_t8])
                P.emit("dve", lambda h, nE=nE, c0=c0, qt=qt: h.reduce_sum(
                    zca[:, qt:qt + 1], zw8a[:, c0:c0 + nE], axis=AX.X), inc="dve",
                    waits=[P.tok("dve")])
                t_zi = P.emit("dve", lambda h, qt=qt: h.reciprocal(
                    zia[:, qt:qt + 1], zca[:, qt:qt + 1]), inc="dve",
                    waits=[P.tok("dve")])
                bank = 6 + (qt % 2)
                t_tr = P.emit("pe", lambda h, qt=qt, bank=bank: h.transpose(
                    ps[bank][0:1, 0:128], minr[:, qt:qt + 1], id_f32[:]), inc="pe",
                    waits=[t_mr, rel.get(bank)])
                t_mrow = P.emit("act", lambda h, qt=qt, bank=bank: h.copy(
                    qpe_aug[64:65, qt * 128:(qt + 1) * 128], ps[bank][0:1, 0:128]),
                    inc="act", waits=[t_tr])
                t_tr2 = P.emit("pe", lambda h, qt=qt, bank=bank: h.transpose(
                    ps[bank][0:1, 128:256], zia[:, qt:qt + 1], id_f32[:]), inc="pe",
                    waits=[t_zi, t_mrow])
                t_zrow = P.emit("act", lambda h, qt=qt, bank=bank: h.copy(
                    zrow[0:1, qt * 128:(qt + 1) * 128], ps[bank][0:1, 128:256]),
                    inc="act", waits=[t_tr2])
                rel[bank] = t_zrow
            t_p1 = P.tok("act")

            for qb in range(4):
                nkt = 8 * qb + 8
                pend = []
                t_av = None

                def emit_av(kt, t_e, nkt=None):
                    return P.emit("pe", lambda h, kt=kt, nkt=nkt: h.matmul(
                        ps[4][:, :], v2[:, kt * VD:(kt + 1) * VD],
                        attnT[kt % 4][:], start=(kt == 0), stop=(kt == nkt - 1)),
                        inc="pe", waits=[t_e] + ([rel.get(4)] if kt == 0 else []))

                for kt in range(nkt):
                    bank = kt % 4
                    pq = ps[bank]
                    P.emit("pe", lambda h, kt=kt, pq=pq, qb=qb, hh=hh: h.matmul(
                        pq[:, :], knope_h[hh][:, kt * 128:(kt + 1) * 128],
                        qraw[0:64, qb * 512:(qb + 1) * 512], start=True, stop=False),
                        waits=[t_p1, rel.get(bank)])
                    P.emit("pe", lambda h, kt=kt, pq=pq, qb=qb: h.matmul(
                        pq[:, :], kpe_aug[0:65, kt * 128:(kt + 1) * 128],
                        qpe_aug[0:65, qb * 512:(qb + 1) * 512],
                        start=False, stop=True), inc="pe")
                    t_s = P.tok("pe")
                    dd = kt - 8 * qb
                    if dd >= 0:
                        t_s = P.emit("dve", lambda h, pq=pq, dd=dd: h.tensor_add(
                            pq[:, :], pq[:, :], m2_sb[:, dd * 512:(dd + 1) * 512]),
                            inc="dve", waits=[t_s] + t_bconst)
                    t_e = P.emit("act", lambda h, pq=pq, kt=kt: h.activation(
                        attnT[kt % 4][:], pq[:, :], AF.Exp), inc="act",
                        waits=[t_s, att_rel[kt % 4]])
                    rel[bank] = t_e
                    pend.append((kt, t_e))
                    if len(pend) > 2:
                        kt0, te0 = pend.pop(0)
                        att_rel[kt0 % 4] = emit_av(kt0, te0, nkt)
                for kt0, te0 in pend:
                    att_rel[kt0 % 4] = emit_av(kt0, te0, nkt)
                t_av = P.tok("pe")
                t_zb = P.emit("pe", lambda h, qb=qb: h.matmul(
                    ps[5][:, :], ones_bf[0:1, :], zrow[0:1, qb * 512:(qb + 1) * 512],
                    start=True, stop=True), inc="pe", waits=[rel.get(5)])
                t_zc = P.emit("act", lambda h: h.copy(zbc[:], ps[5][:, :]), inc="act",
                              waits=[t_zb])
                rel[5] = t_zc
                t_o = P.emit("dve", lambda h, h_=h_, qb=qb: h.tensor_mul(
                    o_all[:, h_ * TOK + qb * 512:h_ * TOK + (qb + 1) * 512],
                    ps[4][:, :], zbc[:]), inc="dve", waits=[t_av, t_zc])
                rel[4] = t_o
        P.realize(esG)
        esR4.close()
        esR2.close()

        if upto == "B":
            for h_ in range(nheads):
                dma("pool", "qo", out.ap()[h_ * 128:(h_ + 1) * 128, :],
                    o_all[:, h_ * TOK:(h_ + 1) * TOK])
            if nheads == 1:
                dma("pool", "qo", out.ap()[256:384, :], qraw[:, :])
                dma("pool", "qo", out.ap()[384:449, :], qpe_aug[0:65, :])
                dma("pool", "qo", out.ap()[512:576, :], knope_h[0][:, 0:2048])
                dma("pool", "qo", out.ap()[640:768, :], v2[:, 0:2048])
                dma("pool", "qo", out.ap()[800:801, :], zrow[0:1, :])
                dma("pool", "qo", out.ap()[832:960, :], qaT[:, TOK:2 * TOK])
                dma("pool", "qo", out.ap()[960:1002, :], qaT[0:42, 5 * TOK:6 * TOK])
                dma("pool", "qo", out.ap()[1024:1152, 0:768], qbw_h[0][:, :])
            P.emit("sp", lambda h: h.nop(), waits=qall("qo"))
            P.realize(esG)
            return nc

        esR5 = contextlib.ExitStack()
        ow_sb = sbuf(esR5, "ow_sb", (128, KC * D), BF16)
        xtC = [sbuf(esR5, f"xtC{i}", (128, D), F32) for i in range(2)]
        x1t = [sbuf(esR5, f"x1t{i}", (128, D), F32) for i in range(2)]
        for c in range(KC):
            dma("pool", "qg",
                ow_sb[:].rearrange("p (c n) -> p c n", c=KC)[:, c:c + 1, :],
                ap("ow_t")[:, c:c + 1, :])
        t_ow = qall("qg")
        xrelC = [None, None]
        x1rel = [None, None]
        for tt in range(NT):
            xb = xtC[tt % 2]
            t_x = dma("sp", "qs", xb[:], ap("xq")[tt * 128:(tt + 1) * 128, :],
                      waits=[xrelC[tt % 2]])
            for h_ in range(NH):
                for hcg in range(4):
                    P.emit("pe", lambda h, h_=h_, hcg=hcg, tt=tt: h.matmul(
                        ps[hcg][:, :],
                        o_all[:, h_ * TOK + tt * 128:h_ * TOK + (tt + 1) * 128],
                        ow_sb[:, h_ * D + hcg * 512:h_ * D + (hcg + 1) * 512],
                        start=(h_ == 0), stop=(h_ == NH - 1)),
                        inc="pe" if (h_ == NH - 1 and hcg == 3) else None,
                        waits=(t_ow + [rel.get(i) for i in range(4)])
                        if (h_ == 0 and hcg == 0) else ())
            t_o = P.tok("pe")
            x1b = x1t[tt % 2]
            t_a = None
            for hcg in range(4):
                t_a = P.emit("dve", lambda h, hcg=hcg, x1b=x1b, xb=xb: h.tensor_add(
                    x1b[:, hcg * 512:(hcg + 1) * 512], ps[hcg][:, :],
                    xb[:, hcg * 512:(hcg + 1) * 512]), inc="dve",
                    waits=[t_o, t_x, x1rel[tt % 2]])
            for i in range(4):
                rel[i] = t_a
            xrelC[tt % 2] = t_a
            t_st = dma("sp", "qs", x1d.ap()[tt * 128:(tt + 1) * 128, :], x1b[:],
                       waits=[t_a])
            x1rel[tt % 2] = t_st
        t_x1d = qall("qs")
        P.realize(esG)
        esR5.close()
        esR1.close()

        if upto == "C":
            for tt in range(NT):
                dma("sp", "qo", out.ap()[tt * 128:(tt + 1) * 128, :],
                    x1d.ap()[tt * 128:(tt + 1) * 128, :], waits=t_x1d)
            P.emit("sp", lambda h: h.nop(), waits=qall("qo"))
            P.realize(esG)
            return nc

        esR6 = contextlib.ExitStack()
        x2 = sbuf(esR6, "x2", (128, 8 * D), F32)
        x1nT = sbuf(esR6, "x1nT", (128, KC * 1024), BF16)
        z_e = sbuf(esR6, "z_e", (128, IT_MOE * 1024), BF16)
        wgl = [sbuf(esR6, f"wgl{i}", (128, KC * 128), BF16) for i in range(2)]
        wul = [sbuf(esR6, f"wul{i}", (128, KC * 128), BF16) for i in range(2)]
        wdr = [sbuf(esR6, f"wdr{i}", (128, IT_MOE * 1024), BF16) for i in range(2)]
        onehot_sb = sbuf(esR6, "onehot_sb", (16, E * 128), BF16)
        combT = sbuf(esR6, "combT", (16, 1024), BF16)
        comb_b = sbuf(esR6, "comb_b", (128, 1024), BF16)
        lgmx = sbuf(esR6, "lgmx", (128, 1), F32)
        nlgmx = sbuf(esR6, "nlgmx", (128, 1), F32)
        pun = sbuf(esR6, "pun", (128, 16), F32)
        rank = sbuf(esR6, "rank", (128, 16), F32)
        tcmp = sbuf(esR6, "tcmp", (128, 16), F32)
        comb = sbuf(esR6, "comb", (128, 16), F32)
        den = sbuf(esR6, "den", (128, 1), F32)
        dinv = sbuf(esR6, "dinv", (128, 1), F32)
        sg = sbuf(esR6, "sg", (128, 8), F32)
        silu_s = sbuf(esR6, "silu_s", (128, 512), BF16)
        tmul = sbuf(esR6, "tmul", (128, 512), BF16)
        t_oneh = dma("pool", "qg", onehot_sb[:], ap("onehot"))

        x2rel = None
        for hf in range(2):
            toff = hf * 8
            for tt8 in range(8):
                tt = toff + tt8
                xsl = slice(tt8 * D, (tt8 + 1) * D)
                t_x = dma("sp", "qs", x2[:, xsl],
                          x1d.ap()[tt * 128:(tt + 1) * 128, :],
                          waits=t_x1d + ([] if x2rel is None else x2rel))
                t_xn = emit_ln(x2[:, xsl], t_x, extra=[P.tok("pe")])
                emit_transposes(
                    xn, KC, None,
                    lambda c, tt8=tt8: x1nT[:, c * 1024 + tt8 * 128:
                                            c * 1024 + (tt8 + 1) * 128], t_xn)
            t_nT = P.tok("act")

            for tt8 in range(8):
                for c in range(KC):
                    P.emit("pe", lambda h, c=c, tt8=tt8: h.matmul(
                        ps[0][:, 0:E],
                        x1nT[:, c * 1024 + tt8 * 128:c * 1024 + (tt8 + 1) * 128],
                        router_sb[:, c * E:(c + 1) * E],
                        start=(c == 0), stop=(c == KC - 1)),
                        inc="pe" if c == KC - 1 else None,
                        waits=[t_nT, rel.get(0)] if c == 0 else ())
                t_l = P.tok("pe")
                P.emit("dve", lambda h: h.reduce_max(lgmx[:], ps[0][:, 0:E],
                                                     axis=AX.X), inc="dve",
                       waits=[t_l])
                t_nm = P.emit("dve", lambda h: h.tensor_scalar(
                    nlgmx[:], lgmx[:], -1.0, None, ALU.mult), inc="dve",
                    waits=[P.tok("dve"), P.tok("act")])
                t_p = P.emit("act", lambda h: h.activation(
                    pun[:], ps[0][:, 0:E], AF.Exp, bias=nlgmx[:]), inc="act",
                    waits=[t_nm])
                rel[0] = t_p
                P.emit("dve", lambda h: h.memset(rank[:], 0.0), inc="dve",
                       waits=[t_p])
                for e2 in range(E):
                    P.emit("dve", lambda h, e2=e2: h.tensor_scalar(
                        tcmp[:], pun[:], pun[:, e2:e2 + 1], None, ALU.is_lt),
                        inc="dve", waits=[P.tok("dve")])
                    P.emit("dve", lambda h: h.tensor_add(rank[:], rank[:], tcmp[:]),
                           inc="dve", waits=[P.tok("dve")])
                P.emit("dve", lambda h: h.tensor_scalar(tcmp[:], rank[:], 3.0, None,
                                                        ALU.is_le), inc="dve",
                       waits=[P.tok("dve")])
                P.emit("dve", lambda h: h.tensor_mul(comb[:], pun[:], tcmp[:]),
                       inc="dve", waits=[P.tok("dve")])
                P.emit("dve", lambda h: h.reduce_sum(den[:], comb[:], axis=AX.X),
                       inc="dve", waits=[P.tok("dve")])
                P.emit("dve", lambda h: h.reciprocal(dinv[:], den[:]), inc="dve",
                       waits=[P.tok("dve")])
                t_cb = P.emit("dve", lambda h: h.tensor_scalar(
                    comb[:], comb[:], dinv[:], None, ALU.mult), inc="dve",
                    waits=[P.tok("dve")])
                t_tr = P.emit("pe", lambda h: h.transpose(
                    ps[6].bitcast(F32)[0:16, 0:128] if False else ps[6][0:16, 0:128],
                    comb[:, 0:16], id_f32[:]), inc="pe", waits=[t_cb, rel.get(6)])
                t_ct = P.emit("act", lambda h, tt8=tt8: h.copy(
                    combT[0:16, tt8 * 128:(tt8 + 1) * 128], ps[6][0:16, 0:128]),
                    inc="act", waits=[t_tr])
                rel[6] = t_ct
                for c in range(KC):
                    P.emit("pe", lambda h, c=c, tt8=tt8: h.matmul(
                        ps[1][:, 0:1],
                        x1nT[:, c * 1024 + tt8 * 128:c * 1024 + (tt8 + 1) * 128],
                        shgw_sb[:, c:c + 1], start=(c == 0), stop=(c == KC - 1)),
                        inc="pe" if c == KC - 1 else None,
                        waits=[rel.get(1)] if c == 0 else ())
                t_sgm = P.tok("pe")
                t_sg = P.emit("act", lambda h, tt8=tt8: h.activation(
                    sg[:, tt8:tt8 + 1], ps[1][:, 0:1], AF.Sigmoid), inc="act",
                    waits=[t_sgm])
                rel[1] = t_sg
            t_route = P.tok("act")
            t_combT = rel[6]

            wg_rel = [None, None]
            wd_rel = [None, None]
            t_wg = {}
            t_wd = {}

            def load_gu(kind, idx, buf, wtok):
                if kind == "moe":
                    e_, it = idx
                    ta = dma("pool", "qg",
                             wgl[buf][:].rearrange("p (c n) -> p c n", c=KC),
                             ap("wg_t")[e_, it], waits=[wtok])
                    tb = dma("pool", "qg",
                             wul[buf][:].rearrange("p (c n) -> p c n", c=KC),
                             ap("wu_t")[e_, it])
                else:
                    it = idx
                    ta = dma("pool", "qg",
                             wgl[buf][:].rearrange("p (c n) -> p c n", c=KC),
                             ap("shg_t")[it], waits=[wtok])
                    tb = dma("pool", "qg",
                             wul[buf][:].rearrange("p (c n) -> p c n", c=KC),
                             ap("shu_t")[it])
                return [ta, tb]

            def load_wd(kind, idx, buf, wtok):
                if kind == "moe":
                    e_, hh2 = idx
                    ta = dma("pool", "qg", wdr[buf][:], ap("wd_t")[e_, hh2],
                             waits=[wtok])
                else:
                    grp, hh2 = idx
                    ta = dma("pool", "qg", wdr[buf][:], ap("shd_t")[grp, hh2],
                             waits=[wtok])
                return [ta]

            groups = [("moe", e_) for e_ in range(E)] + [("sh", g) for g in range(4)]
            gu_list = [(k, (g, it) if k == "moe" else g * IT_MOE + it)
                       for (k, g) in groups for it in range(IT_MOE)]
            for gi in range(2):
                k, idx = gu_list[gi]
                t_wg[gi] = load_gu(k, idx, gi % 2, None)
            wd_list = [(k, (g, hh2) if k == "moe" else (g, hh2))
                       for (k, g) in groups for hh2 in range(2)]
            for di in range(2):
                k, idx = wd_list[di]
                t_wd[di] = load_wd(wd_list[di][0], idx, di % 2, None)

            gi_count = [0]
            di_count = [0]

            for (kind, gidx) in groups:
                is_moe = kind == "moe"
                if is_moe:
                    e_ = gidx
                    for nb in range(2):
                        t_b = P.emit("pe", lambda h, e_=e_, nb=nb: h.matmul(
                            ps[4][:, :], onehot_sb[0:16, e_ * 128:(e_ + 1) * 128],
                            combT[0:16, nb * 512:(nb + 1) * 512],
                            start=True, stop=True), inc="pe",
                            waits=[t_combT, t_oneh, rel.get(4)])
                        t_cpb = P.emit("act", lambda h, nb=nb: h.copy(
                            comb_b[:, nb * 512:(nb + 1) * 512], ps[4][:, :]),
                            inc="act", waits=[t_b, P.tok("dve")])
                        rel[4] = t_cpb
                    t_cbb = P.tok("act")
                for it in range(IT_MOE):
                    gi = gi_count[0]
                    buf = gi % 2
                    t_w = t_wg[gi]
                    t_gu = None
                    for c in range(KC):
                        for bi, (pp, wsb, xo) in enumerate((
                                (ps[0], wgl[buf], 0), (ps[1], wgl[buf], 512),
                                (ps[2], wul[buf], 0), (ps[3], wul[buf], 512))):
                            P.emit("pe", lambda h, c=c, pp=pp, wsb=wsb, xo=xo:
                                   h.matmul(pp[:, :], wsb[:, c * 128:(c + 1) * 128],
                                            x1nT[:, c * 1024 + xo:c * 1024 + xo + 512],
                                            start=(c == 0), stop=(c == KC - 1)),
                                   inc="pe" if (c == KC - 1 and bi == 3) else None,
                                   waits=(t_w + [t_nT] + [rel.get(i) for i in range(4)])
                                   if (c == 0 and bi == 0) else ())
                    t_gu = P.tok("pe")
                    if gi + 2 < len(gu_list):
                        k2, idx2 = gu_list[gi + 2]
                        t_wg[gi + 2] = load_gu(k2, idx2, buf, t_gu)
                    for nb in range(2):
                        zsl = slice(it * 1024 + nb * 512, it * 1024 + (nb + 1) * 512)
                        pg = (ps[0], ps[1])[nb]
                        pu = (ps[2], ps[3])[nb]
                        t_si = P.emit("act", lambda h, pg=pg: h.activation(
                            silu_s[:], pg[:, :], AF.Silu), inc="act",
                            waits=[t_gu, P.tok("dve")])
                        rel[nb] = t_si
                        if is_moe:
                            P.emit("dve", lambda h, pu=pu, nb=nb: h.tensor_mul(
                                tmul[:], pu[:, :], comb_b[:, nb * 512:(nb + 1) * 512]),
                                inc="dve", waits=[t_gu, t_cbb, P.tok("act")])
                        else:
                            P.emit("dve", lambda h, pu=pu: h.tensor_copy(
                                tmul[:], pu[:, :]), inc="dve",
                                waits=[t_gu, P.tok("act")])
                        t_tm = P.tok("dve")
                        rel[2 + nb] = t_tm
                        t_zw = P.emit("dve", lambda h, zsl=zsl: h.tensor_mul(
                            z_e[:, zsl], tmul[:], silu_s[:]), inc="dve",
                            waits=[t_tm, t_si])
                    gi_count[0] += 1
                t_z = P.tok("dve")
                for hh2 in range(2):
                    di = di_count[0]
                    dbuf = di % 2
                    t_wdt = t_wd[di]
                    t_dlast = None
                    for pair in range(4):
                        for it in range(IT_MOE):
                            for sub in range(2):
                                tt8 = pair * 2 + sub
                                for hq in range(2):
                                    bank = 4 + sub * 2 + hq
                                    P.emit("pe", lambda h, it=it, tt8=tt8, hq=hq,
                                           hh2=hh2, dbuf=dbuf, bank=bank: h.matmul(
                                        ps[bank][:, :],
                                        z_e[:, it * 1024 + tt8 * 128:
                                            it * 1024 + (tt8 + 1) * 128],
                                        wdr[dbuf][:, it * 1024 + hq * 512:
                                                  it * 1024 + hq * 512 + 512],
                                        start=(it == 0), stop=(it == IT_MOE - 1)),
                                        inc="pe" if (it == IT_MOE - 1 and sub == 1
                                                     and hq == 1) else None,
                                        waits=([t_z] + t_wdt +
                                        [rel.get(4 + i) for i in range(4)])
                                        if (it == 0 and sub == 0 and hq == 0) else ())
                        t_d = P.tok("pe")
                        t_dlast = t_d
                        for sub in range(2):
                            tt8 = pair * 2 + sub
                            for hq in range(2):
                                bank = 4 + sub * 2 + hq
                                col = tt8 * D + hh2 * 1024 + hq * 512
                                if is_moe:
                                    t_ad = P.emit("dve", lambda h, col=col, bank=bank:
                                                  h.tensor_add(
                                        x2[:, col:col + 512], x2[:, col:col + 512],
                                        ps[bank][:, :]), inc="dve", waits=[t_d])
                                else:
                                    t_ad = P.emit("dve", lambda h, col=col, bank=bank,
                                                  tt8=tt8: h.scalar_tensor_tensor(
                                        x2[:, col:col + 512], ps[bank][:, :],
                                        sg[:, tt8:tt8 + 1], x2[:, col:col + 512],
                                        ALU.mult, ALU.add), inc="dve",
                                        waits=[t_d, t_route])
                                rel[bank] = t_ad
                    di_count[0] += 1
                    if di + 2 < len(wd_list):
                        k2, idx2 = wd_list[di + 2]
                        t_wd[di + 2] = load_wd(k2, idx2, dbuf, t_dlast)
            t_ffn = P.tok("dve")
            for tt8 in range(8):
                tt = toff + tt8
                dma("sp", "qo", out.ap()[tt * 128:(tt + 1) * 128, :],
                    x2[:, tt8 * D:(tt8 + 1) * D], waits=[t_ffn])
            x2rel = qall("qo")
            P.realize(esG)
        esR6.close()

        P.emit("sp", lambda h: h.nop(), waits=qall("qo"))
        P.realize(esG)
    return nc


def kernel(**inputs):
    pr, percore = _host_prepare(inputs)
    nc = build_nc()
    shared = {k: np.ascontiguousarray(v, dtype=np.float32) for k, v in pr.items()}
    in_maps = []
    for c in range(8):
        m = dict(shared)
        for k, v in percore[c].items():
            m[k] = np.ascontiguousarray(v, dtype=np.float32)
        in_maps.append(m)
    res = run_bass_kernel_spmd(nc, in_maps, core_ids=list(range(8)))
    full = np.empty((B, S, D), dtype=np.float32)
    for c in range(8):
        b, p = c // 2, c % 2
        o = np.asarray(res.results[c]["out"])
        for i in range(NT):
            g = 2 * i + p
            full[b, g * 128:(g + 1) * 128, :] = o[i * 128:(i + 1) * 128, :]
    return full


# version 03
# speedup vs baseline: 1.3277x; 1.1475x over previous
import contextlib
import math
import numpy as np
import concourse.bass as bass
import concourse.mybir as mybir
from concourse.bass_utils import run_bass_kernel_spmd

F32 = mybir.dt.float32
BF16 = mybir.dt.bfloat16
AF = mybir.ActivationFunctionType
ALU = mybir.AluOpType
AX = mybir.AxisListType

B, S, D, NH = 4, 4096, 2048, 16
ROPE, NOPE, VD = 64, 64, 128
QH = ROPE + NOPE
QL = D // 3
KVL = D // 8
THETA = 128000.0
SCALE = 1.0 / math.sqrt(QH)
EPS_LN = 1e-5
EPS_RMS = 1e-6
E = 16
I_MOE, I_SH = 1408, 5632
NT = 16
TOK = NT * 128
NKT = 32
IT_MOE = I_MOE // 128
IT_SH = I_SH // 128
KC = D // 128
QLC = [128, 128, 128, 128, 128, 42]
NEG = -30000.0

ENGS = ("pe", "dve", "act", "pool", "sp")


def _rope_tables():
    inv = 1.0 / THETA ** (np.arange(0, ROPE, 2, dtype=np.float64) / ROPE)
    pos = np.arange(S, dtype=np.float64)
    fr = np.outer(inv, pos)
    cos = np.cos(np.concatenate([fr, fr], 0))
    sin = np.sin(np.concatenate([fr, fr], 0))
    sin_signed = sin.copy()
    sin_signed[:32] *= -1.0
    return cos.astype(np.float32), sin_signed.astype(np.float32)


_PERM = np.concatenate([np.arange(0, 64, 2), np.arange(1, 64, 2)])


def _tile_k(w):
    k, n = w.shape
    return np.ascontiguousarray(w.reshape(k // 128, 128, n).transpose(1, 0, 2))


def _tile_cols(w, ct):
    k, n = w.shape
    return np.ascontiguousarray(
        w.reshape(k // 128, 128, n // ct, ct).transpose(2, 1, 0, 3))


def _host_prepare(inp):
    cos, sin = _rope_tables()
    pr = {}
    pr["qaw_t"] = _tile_k(inp["L0_attn_q_a_w"])
    pr["lnq"] = inp["L0_attn_q_a_ln_w"].reshape(1, QL)
    qbw = np.array(inp["L0_attn_q_b_w"], dtype=np.float32)
    for h in range(NH):
        base = h * QH + NOPE
        qbw[:, base:base + ROPE] = qbw[:, base + _PERM]
    qbh = np.zeros((NH, 128, 6, 128), np.float32)
    for h in range(NH):
        for c in range(6):
            w = QLC[c]
            qbh[h, :w, c, :] = qbw[c * 128:c * 128 + w, h * QH:(h + 1) * QH]
    pr["qbw_h"] = qbh
    kvaw = np.array(inp["L0_attn_kv_a_w"], dtype=np.float32)
    kvaw[:, KVL:] = kvaw[:, KVL + _PERM]
    pr["kvaw_t"] = _tile_k(kvaw)
    pr["lnkv"] = inp["L0_attn_kv_a_ln_w"].reshape(1, KVL)
    kvb3 = np.asarray(inp["L0_attn_kv_b_w"]).reshape(KVL, NH, NOPE + VD)
    pr["kvb_nope"] = np.ascontiguousarray(kvb3[:, :, :NOPE].reshape(KVL, NH * NOPE))
    pr["kvb_v"] = np.ascontiguousarray(kvb3[:, :, NOPE:].reshape(KVL, NH * VD))
    pr["ow_t"] = _tile_k(inp["L0_attn_o_w"])
    pr["router_t"] = _tile_k(np.ascontiguousarray(np.asarray(inp["L0_ffn_router_w"]).T))
    pr["wg_t"] = np.stack([_tile_cols(inp["L0_ffn_w_gate"][e], 128) for e in range(E)])
    pr["wu_t"] = np.stack([_tile_cols(inp["L0_ffn_w_up"][e], 128) for e in range(E)])
    pr["wd_t"] = np.stack([
        np.ascontiguousarray(np.asarray(inp["L0_ffn_w_down"][e], np.float32)
                             .reshape(IT_MOE, 128, 2, 1024).transpose(2, 1, 0, 3)
                             .reshape(2, 128, IT_MOE * 1024)) for e in range(E)])
    pr["shg_t"] = _tile_cols(inp["L0_ffn_sh_gate"], 128)
    pr["shu_t"] = _tile_cols(inp["L0_ffn_sh_up"], 128)
    pr["shd_t"] = np.ascontiguousarray(
        np.asarray(inp["L0_ffn_sh_down"], np.float32)
        .reshape(4, IT_MOE, 128, 2, 1024).transpose(0, 3, 2, 1, 4)
        .reshape(4, 2, 128, IT_MOE * 1024))
    pr["shgw_t"] = _tile_k(inp["L0_ffn_sh_gate_w"])
    pr["id128"] = np.eye(128, dtype=np.float32)
    pr["ones_row"] = np.ones((1, 128), np.float32)
    mswap = np.zeros((64, 128), np.float32)
    for m in range(32):
        mswap[m, 96 + m] = 1.0
        mswap[32 + m, 64 + m] = 1.0
    pr["mswapT"] = np.ascontiguousarray(mswap.T)
    mshift = np.zeros((64, 128), np.float32)
    for m in range(64):
        mshift[m, 64 + m] = 1.0
    pr["mshiftT"] = np.ascontiguousarray(mshift.T)
    r64 = np.zeros((64, 64), np.float32)
    for m in range(32):
        r64[m, 32 + m] = 1.0
        r64[32 + m, m] = 1.0
    pr["r64T"] = np.ascontiguousarray(r64.T)
    pr["ck"] = np.ascontiguousarray(cos)
    pr["sk"] = np.ascontiguousarray(sin)
    oneh = np.zeros((E, E * 128), np.float32)
    for e in range(E):
        oneh[e, e * 128:(e + 1) * 128] = 1.0
    pr["onehot"] = oneh
    percore = []
    x = np.ascontiguousarray(np.asarray(inp["L0_attn_x"], dtype=np.float32))
    for c in range(8):
        b, p = c // 2, c % 2
        d = {}
        slots = np.array([2 * i + p for i in range(NT)])
        rows = (slots[:, None] * 128 + np.arange(128)[None, :]).reshape(-1)
        d["xq"] = np.ascontiguousarray(x[b][rows])
        d["xk"] = x[b]
        d["cq"] = np.ascontiguousarray(cos[:, rows])
        d["sq"] = np.ascontiguousarray(sin[:, rows])
        for W in (256, 512):
            cc, rr = np.meshgrid(np.arange(W), np.arange(128))
            d[f"mask1_{W}"] = np.where(cc - rr > W - 256 + 128 * p, NEG, 0.0
                                       ).astype(np.float32)
        m2 = np.zeros((8, 128, 512), np.float32)
        for dd in range(8):
            for t in range(4):
                cc, rr = np.meshgrid(np.arange(128), np.arange(128))
                m2[dd][:, t * 128:(t + 1) * 128] = np.where(
                    (dd - 2 * t - p) * 128 + rr > cc, NEG, 0.0)
        d["mask2"] = m2
        percore.append(d)
    return pr, percore


class Prog:
    def __init__(self, nc):
        self.nc = nc
        self.items = {e: [] for e in ENGS}
        self.sem = {}
        self.cnt = {}
        self.water = {e: {} for e in ENGS}

    def new_sem(self, name, es):
        self.sem[name] = es.enter_context(self.nc.semaphore(name))
        self.cnt[name] = 0

    def tok(self, name):
        return (name, self.cnt[name])

    paranoid = ()

    def emit(self, eng, fn, inc=None, inck=1, waits=()):
        if self.paranoid:
            waits = list(waits) + [self.tok(s) for s in self.paranoid]
        wl = []
        for tkn in waits:
            if tkn is None:
                continue
            s, v = tkn
            if v <= 0:
                continue
            w = self.water[eng]
            if w.get(s, 0) >= v:
                continue
            w[s] = v
            wl.append((self.sem[s], v))
        tk = None
        sem_h = None
        if inc is not None:
            self.cnt[inc] += inck
            tk = (inc, self.cnt[inc])
            sem_h = self.sem[inc]

        def wrapped(h, fn=fn, wl=wl, sem_h=sem_h, inck=inck):
            for (sh, v) in wl:
                h.wait_ge(sh, v)
            inst = fn(h)
            if sem_h is not None:
                inst.then_inc(sem_h, inck)
        self.items[eng].append(wrapped)
        return tk

    def realize(self, es):
        with self.nc.Block() as block:
            for eng, deco in (("pe", block.tensor), ("dve", block.vector),
                              ("act", block.scalar), ("pool", block.gpsimd),
                              ("sp", block.sync)):
                fns = self.items[eng]
                if fns:
                    deco(lambda h, fns=list(fns): [fn(h) for fn in fns])
        self.items = {e: [] for e in ENGS}
        for e in ENGS:
            for s in self.sem:
                if not s.startswith("q"):
                    self.water[e][s] = self.cnt[s]


def build_nc(upto="full", nheads=NH):
    nc = bass.Bass("TRN2", target_bir_lowering=False)
    dr = {}

    def din(name, shape):
        dr[name] = nc.dram_tensor(name, list(shape), F32, kind="ExternalInput")

    din("xq", (TOK, D)); din("xk", (S, D))
    din("qaw_t", (128, KC, QL)); din("lnq", (1, QL)); din("qbw_h", (NH, 128, 6, 128))
    din("kvaw_t", (128, KC, KVL + ROPE)); din("lnkv", (1, KVL))
    din("kvb_nope", (KVL, NH * NOPE)); din("kvb_v", (KVL, NH * VD))
    din("ow_t", (128, KC, D)); din("router_t", (128, KC, E))
    din("wg_t", (E, IT_MOE, 128, KC, 128)); din("wu_t", (E, IT_MOE, 128, KC, 128))
    din("wd_t", (E, 2, 128, IT_MOE * 1024))
    din("shg_t", (IT_SH, 128, KC, 128)); din("shu_t", (IT_SH, 128, KC, 128))
    din("shd_t", (4, 2, 128, IT_MOE * 1024)); din("shgw_t", (128, KC, 1))
    din("id128", (128, 128)); din("ones_row", (1, 128))
    din("mswapT", (128, 64)); din("mshiftT", (128, 64)); din("r64T", (64, 64))
    din("ck", (ROPE, S)); din("sk", (ROPE, S))
    din("cq", (ROPE, TOK)); din("sq", (ROPE, TOK))
    din("mask1_256", (128, 256)); din("mask1_512", (128, 512))
    din("mask2", (8, 128, 512)); din("onehot", (E, E * 128))
    out = nc.dram_tensor("out", [TOK, D], F32, kind="ExternalOutput")
    x1d = nc.dram_tensor("x1d", [TOK, D], F32, kind="Internal")
    wg_bf = nc.dram_tensor("wg_bf", [E, IT_MOE, 128, KC * 128], BF16, kind="Internal")
    wu_bf = nc.dram_tensor("wu_bf", [E, IT_MOE, 128, KC * 128], BF16, kind="Internal")
    wd_bf = nc.dram_tensor("wd_bf", [E, 2, 128, IT_MOE * 1024], BF16, kind="Internal")
    shg_bf = nc.dram_tensor("shg_bf", [IT_SH, 128, KC * 128], BF16, kind="Internal")
    shu_bf = nc.dram_tensor("shu_bf", [IT_SH, 128, KC * 128], BF16, kind="Internal")
    shd_bf = nc.dram_tensor("shd_bf", [4, 2, 128, IT_MOE * 1024], BF16, kind="Internal")
    ap = lambda n: dr[n].ap()

    P = Prog(nc)
    esG = contextlib.ExitStack()
    with esG:
        for s in ("pe", "dve", "act", "pool"):
            P.new_sem(s, esG)
        QROT = 8
        for q in ("qg", "qs", "qo", "qa"):
            for i in range(QROT):
                P.new_sem(f"{q}{i}", esG)
        qrot = {"qg": 0, "qs": 0, "qo": 0, "qa": 0}

        def qall(q):
            return [P.tok(f"{q}{i}") for i in range(QROT)]
        ps = [esG.enter_context(nc.psum_tensor(f"ps{i}", [128, 512], F32))
              for i in range(8)]
        rel = {}

        def sbuf(es, name, shp, dt=F32, side="right"):
            return es.enter_context(nc.sbuf_tensor(name, list(shp), dt, side=side))

        def dma(eng, q, dst, src, waits=()):
            s_name = f"{q}{qrot[q]}"
            qrot[q] = (qrot[q] + 1) % QROT
            return P.emit(eng, lambda h, d=dst, s=src: h.dma_start(d, s),
                          inc=s_name, inck=16, waits=waits)

        cast_jobs = []
        cast_tok = {}
        for e_ in range(E):
            cast_jobs.append((("g", e_), wg_bf.ap()[e_], ap("wg_t")[e_]))
            cast_jobs.append((("u", e_), wu_bf.ap()[e_], ap("wu_t")[e_]))
            cast_jobs.append((("d", e_), wd_bf.ap()[e_], ap("wd_t")[e_]))
        for g_ in range(4):
            cast_jobs.append((("shg", g_), shg_bf.ap()[g_ * IT_MOE:(g_ + 1) * IT_MOE],
                              ap("shg_t")[g_ * IT_MOE:(g_ + 1) * IT_MOE]))
            cast_jobs.append((("shu", g_), shu_bf.ap()[g_ * IT_MOE:(g_ + 1) * IT_MOE],
                              ap("shu_t")[g_ * IT_MOE:(g_ + 1) * IT_MOE]))
            cast_jobs.append((("shd", g_), shd_bf.ap()[g_], ap("shd_t")[g_]))

        def emit_casts(n):
            for _ in range(n):
                if not cast_jobs:
                    return
                key, dst, srcap = cast_jobs.pop(0)
                cast_tok[key] = dma("pool", "qg", dst, srcap)

        L = lambda name, shp, dt=F32: sbuf(esG, name, shp, dt, side="left")
        id_bf = L("id_bf", (128, 128), BF16)
        id_f32 = L("id_f32", (128, 128), F32)
        ones_bf = L("ones_bf", (1, 128), BF16)
        router_sb = L("router_sb", (128, KC * E), BF16)
        shgw_sb = L("shgw_sb", (128, KC), BF16)
        xn = L("xn", (128, D), BF16)
        xnT = L("xnT", (128, KC * 128), BF16)
        stats = L("stats", (128, 24), F32)
        mv = L("mv", (128, 2), F32)
        sd_t = L("sd_t", (128, 1), F32)
        rstd = L("rstd", (128, 1), F32)
        sq1 = L("sq1", (128, 1), F32)
        sq2 = L("sq2", (128, 1), F32)
        sqs = L("sqs", (128, 1), F32)

        esR1 = contextlib.ExitStack()
        o_all = sbuf(esR1, "o_all", (128, NH * TOK), BF16)
        esR2 = contextlib.ExitStack()
        qaT = sbuf(esR2, "qaT", (128, 6 * TOK), BF16)
        ckvT = sbuf(esR2, "ckvT", (128, 2 * S), BF16)
        kpe_aug = sbuf(esR2, "kpe_aug", (65, S), BF16)
        kvbn_sb = sbuf(esR2, "kvbn_sb", (128, 2 * NH * NOPE), BF16)
        kvbv_sb = sbuf(esR2, "kvbv_sb", (128, 2 * NH * VD), BF16)

        esR3 = contextlib.ExitStack()
        qaw_sb = sbuf(esR3, "qaw_sb", (128, KC * QL), BF16)
        kva_sb = sbuf(esR3, "kva_sb", (128, KC * (KVL + ROPE)), BF16)
        lnq_b = sbuf(esR3, "lnq_b", (128, QL), BF16)
        lnkv_b = sbuf(esR3, "lnkv_b", (128, KVL), BF16)
        lnrow = sbuf(esR3, "lnrow", (1, QL), BF16)
        ck_sb = sbuf(esR3, "ck_sb", (ROPE, S), BF16)
        sk_sb = sbuf(esR3, "sk_sb", (ROPE, S), BF16)
        r64_sb = sbuf(esR3, "r64_sb", (64, 64), BF16)
        xtile = [sbuf(esR3, f"xtileA{i}", (128, D), F32) for i in range(2)]
        qa_n = sbuf(esR3, "qa_n", (128, QL), BF16)
        ckv_n = sbuf(esR3, "ckv_n", (128, KVL), BF16)
        kpe_tm = sbuf(esR3, "kpe_tm", (128, ROPE), BF16)
        sqscr = sbuf(esR3, "sqscr", (128, 512), BF16)
        tp1 = sbuf(esR3, "tp1", (64, 512), F32)
        tp2 = sbuf(esR3, "tp2", (64, 512), F32)

        dma("pool", "qg", id_bf[:], ap("id128"))
        dma("sp", "qs", id_f32[:], ap("id128"))
        dma("pool", "qg", ones_bf[:], ap("ones_row"))
        dma("pool", "qg", router_sb[:].rearrange("p (c n) -> p c n", c=KC), ap("router_t"))
        dma("pool", "qg", shgw_sb[:].rearrange("p (c n) -> p c n", c=KC), ap("shgw_t"))
        dma("pool", "qg", kvbn_sb[:].rearrange("p (c n) -> p c n", c=2),
            ap("kvb_nope").rearrange("(c p) n -> p c n", p=128))
        dma("pool", "qg", kvbv_sb[:].rearrange("p (c n) -> p c n", c=2),
            ap("kvb_v").rearrange("(c p) n -> p c n", p=128))
        dma("pool", "qg", ck_sb[:], ap("ck"))
        dma("pool", "qg", sk_sb[:], ap("sk"))
        dma("pool", "qg", r64_sb[:], ap("r64T"))
        t_wconst = qall("qg")
        t_qaw = dma("pool", "qg", qaw_sb[:].rearrange("p (c n) -> p c n", c=KC), ap("qaw_t"))
        t_kva = dma("pool", "qg", kva_sb[:].rearrange("p (c n) -> p c n", c=KC), ap("kvaw_t"))
        t_l1 = dma("pool", "qg", lnrow[0:1, 0:QL], ap("lnq"))

        def bcast_row(width, dst, t_row):
            t_cp = None
            for j in range((width + 511) // 512):
                w = min(512, width - j * 512)
                t_mm = P.emit("pe", lambda h, j=j, w=w: h.matmul(
                    ps[5][:, 0:w], ones_bf[0:1, :], lnrow[0:1, j * 512:j * 512 + w],
                    start=True, stop=True), inc="pe", waits=[t_row, rel.get(5)])
                t_cp = P.emit("act", lambda h, j=j, w=w, dst=dst: h.copy(
                    dst[:, j * 512:j * 512 + w], ps[5][:, 0:w]), inc="act",
                    waits=[t_mm])
                rel[5] = t_cp
            return t_cp

        t_lnqb = bcast_row(QL, lnq_b, t_l1)
        t_l2 = dma("pool", "qg", lnrow[0:1, 0:KVL], ap("lnkv"), waits=[rel[5]])
        t_lnkb = bcast_row(KVL, lnkv_b, t_l2)

        def emit_ln(xbuf, t_x, extra=()):
            for sl in range(4):
                P.emit("dve", lambda h, sl=sl, xb=xbuf: h.bn_stats(
                    stats[:, sl * 6:(sl + 1) * 6], xb[:, sl * 512:(sl + 1) * 512]),
                    inc="dve", waits=[t_x] if sl == 0 else ())
            P.emit("dve", lambda h: h.bn_aggr(mv[:], stats[:]), inc="dve",
                   waits=[P.tok("dve")])
            P.emit("dve", lambda h: h.tensor_scalar(sq1[:], mv[:, 1:2], EPS_LN,
                                                    None, ALU.add), inc="dve",
                   waits=[P.tok("dve")])
            P.emit("act", lambda h: h.activation(sd_t[:], sq1[:], AF.Sqrt), inc="act",
                   waits=[P.tok("dve")])
            P.emit("dve", lambda h: h.reciprocal(rstd[:], sd_t[:]), inc="dve",
                   waits=[P.tok("act")])
            P.emit("dve", lambda h, xb=xbuf: h.tensor_scalar(
                xn[:], xb[:], mv[:, 0:1], rstd[:], ALU.subtract, ALU.mult),
                inc="dve", waits=[P.tok("dve")] + list(extra))
            return P.tok("dve")

        def emit_transposes(src, n, widths, dst_fn, t_src):
            t_last = None
            for c in range(n):
                w = widths[c] if widths else 128
                bank = 6 + (c % 2)
                t_t = P.emit("pe", lambda h, c=c, w=w, bank=bank, src=src: h.transpose(
                    ps[bank].bitcast(BF16)[0:w, 0:128], src[:, c * 128:c * 128 + w],
                    id_bf[:]), inc="pe", waits=[t_src, rel.get(bank)])
                t_last = P.emit("act", lambda h, d=dst_fn(c), w=w, bank=bank: h.copy(
                    d, ps[bank].bitcast(BF16)[0:w, 0:128]), inc="act", waits=[t_t])
                rel[bank] = t_last
            return t_last

        xrel = [None, None]
        for tt in range(NT):
            xb = xtile[tt % 2]
            t_x = dma("sp", "qs", xb[:], ap("xq")[tt * 128:(tt + 1) * 128, :],
                      waits=[xrel[tt % 2]])
            t_xn = emit_ln(xb, t_x, extra=[P.tok("pe")])
            xrel[tt % 2] = t_xn
            t_xT = emit_transposes(xn, KC, None,
                                   lambda c: xnT[:, c * 128:(c + 1) * 128], t_xn)
            for c in range(KC):
                last = c == KC - 1
                P.emit("pe", lambda h, c=c: h.matmul(
                    ps[0][:, :], xnT[:, c * 128:(c + 1) * 128],
                    qaw_sb[:, c * QL:c * QL + 512], start=(c == 0), stop=last),
                    waits=[t_xT, t_qaw, rel.get(0), rel.get(1)] if c == 0 else ())
                P.emit("pe", lambda h, c=c: h.matmul(
                    ps[1][:, 0:170], xnT[:, c * 128:(c + 1) * 128],
                    qaw_sb[:, c * QL + 512:(c + 1) * QL], start=(c == 0), stop=last),
                    inc="pe" if last else None)
            t_qa = P.tok("pe")
            P.emit("act", lambda h: h.activation(sqscr[:, 0:512], ps[0][:, :],
                                                 AF.Square, accum_out=sq1[:]),
                   inc="act", waits=[t_qa, P.tok("dve")])
            P.emit("act", lambda h: h.activation(sqscr[:, 0:170], ps[1][:, 0:170],
                                                 AF.Square, accum_out=sq2[:]),
                   inc="act")
            P.emit("dve", lambda h: h.tensor_add(sqs[:], sq1[:], sq2[:]), inc="dve",
                   waits=[P.tok("act")])
            P.emit("dve", lambda h: h.tensor_scalar(sqs[:], sqs[:], 1.0 / QL, EPS_RMS,
                                                    ALU.mult, ALU.add), inc="dve",
                   waits=[P.tok("dve")])
            P.emit("act", lambda h: h.activation(sd_t[:], sqs[:], AF.Sqrt), inc="act",
                   waits=[P.tok("dve")])
            P.emit("dve", lambda h: h.reciprocal(rstd[:], sd_t[:]), inc="dve",
                   waits=[P.tok("act")])
            P.emit("dve", lambda h: h.tensor_scalar(qa_n[:, 0:512], ps[0][:, :],
                                                    rstd[:], None, ALU.mult),
                   inc="dve", waits=[P.tok("dve")])
            t_ev = P.emit("dve", lambda h: h.tensor_scalar(
                qa_n[:, 512:QL], ps[1][:, 0:170], rstd[:], None, ALU.mult), inc="dve")
            rel[0] = t_ev
            rel[1] = t_ev
            t_qan = P.emit("dve", lambda h: h.tensor_mul(qa_n[:], qa_n[:], lnq_b[:]),
                           inc="dve", waits=[t_ev, t_lnqb])
            emit_transposes(qa_n, 6, QLC,
                            lambda c, tt=tt: qaT[0:QLC[c],
                                                 c * TOK + tt * 128:c * TOK + (tt + 1) * 128],
                            t_qan)
        P.realize(esG)

        for tk in range(NKT):
            xb = xtile[tk % 2]
            t_x = dma("sp", "qs", xb[:], ap("xk")[tk * 128:(tk + 1) * 128, :],
                      waits=[xrel[tk % 2]])
            t_xn = emit_ln(xb, t_x, extra=[P.tok("pe")])
            xrel[tk % 2] = t_xn
            t_xT = emit_transposes(xn, KC, None,
                                   lambda c: xnT[:, c * 128:(c + 1) * 128], t_xn)
            for c in range(KC):
                last = c == KC - 1
                P.emit("pe", lambda h, c=c: h.matmul(
                    ps[0][:, 0:KVL + ROPE], xnT[:, c * 128:(c + 1) * 128],
                    kva_sb[:, c * 320:(c + 1) * 320], start=(c == 0), stop=last),
                    inc="pe" if last else None,
                    waits=[t_xT, t_kva, rel.get(0)] if c == 0 else ())
            t_kv = P.tok("pe")
            P.emit("act", lambda h: h.activation(sqscr[:, 0:KVL], ps[0][:, 0:KVL],
                                                 AF.Square, accum_out=sqs[:]),
                   inc="act", waits=[t_kv, P.tok("dve")])
            P.emit("dve", lambda h: h.tensor_scalar(sqs[:], sqs[:], 1.0 / KVL,
                                                    EPS_RMS, ALU.mult, ALU.add),
                   inc="dve", waits=[P.tok("act")])
            P.emit("act", lambda h: h.activation(sd_t[:], sqs[:], AF.Sqrt), inc="act",
                   waits=[P.tok("dve")])
            P.emit("dve", lambda h: h.reciprocal(rstd[:], sd_t[:]), inc="dve",
                   waits=[P.tok("act")])
            t_craw = P.emit("dve", lambda h: h.tensor_scalar(
                ckv_n[:], ps[0][:, 0:KVL], rstd[:], None, ALU.mult),
                inc="dve", waits=[P.tok("dve")])
            t_cn = P.emit("dve", lambda h: h.tensor_mul(ckv_n[:], ckv_n[:], lnkv_b[:]),
                          inc="dve", waits=[P.tok("dve"), t_lnkb])
            t_kp = P.emit("act", lambda h: h.copy(kpe_tm[:], ps[0][:, KVL:KVL + ROPE]),
                          inc="act", waits=[t_kv, t_craw])
            rel[0] = t_kp
            emit_transposes(ckv_n, 2, None,
                            lambda c, tk=tk: ckvT[:, c * S + tk * 128:c * S + (tk + 1) * 128],
                            t_cn)
            bank = 6
            t_t = P.emit("pe", lambda h: h.transpose(
                ps[bank].bitcast(BF16)[0:ROPE, 0:128], kpe_tm[:], id_bf[:]),
                inc="pe", waits=[t_kp, rel.get(bank)])
            t_cpy = P.emit("act", lambda h, tk=tk: h.copy(
                kpe_aug[0:64, tk * 128:(tk + 1) * 128],
                ps[bank].bitcast(BF16)[0:ROPE, 0:128]), inc="act", waits=[t_t])
            rel[bank] = t_cpy
        t_kperaw = P.tok("act")

        P.emit("dve", lambda h: h.memset(kpe_aug[64:65, :], 1.0), inc="dve",
               waits=[t_kperaw])
        for j in range(8):
            slc = slice(j * 512, (j + 1) * 512)
            bank = j % 2
            t_m = P.emit("pe", lambda h, slc=slc, bank=bank: h.matmul(
                ps[bank][0:64, :], r64_sb[:], kpe_aug[0:64, slc],
                start=True, stop=True), inc="pe",
                waits=[t_kperaw, rel.get(bank), P.tok("dve")] + t_wconst)
            P.emit("dve", lambda h, slc=slc, bank=bank: h.tensor_mul(
                tp1[:], ps[bank][0:64, :], sk_sb[:, slc]), inc="dve", waits=[t_m])
            P.emit("dve", lambda h, slc=slc: h.tensor_mul(
                tp2[:], kpe_aug[0:64, slc], ck_sb[:, slc]), inc="dve")
            t_add = P.emit("dve", lambda h, slc=slc: h.tensor_add(
                kpe_aug[0:64, slc], tp1[:], tp2[:]), inc="dve", waits=[P.tok("dve")])
            rel[bank] = t_add
        t_kpe = P.tok("dve")
        emit_casts(12)
        P.realize(esG)
        esR3.close()

        if upto == "A":
            t1 = dma("pool", "qo", out.ap()[0:128, :], qaT[:, 0:TOK])
            t2 = dma("pool", "qo", out.ap()[128:256, :], ckvT[:, 0:2048])
            t3 = dma("pool", "qo", out.ap()[256:321, :], kpe_aug[0:65, 0:2048])
            P.emit("sp", lambda h: h.nop(), waits=qall("qo"))
            P.realize(esG)
            return nc

        esR4 = contextlib.ExitStack()
        qbw_h = [sbuf(esR4, f"qbw_h{i}", (128, 6 * 128), BF16) for i in range(2)]
        m1_256 = sbuf(esR4, "m1_256", (128, 256), BF16)
        m1_512 = sbuf(esR4, "m1_512", (128, 512), BF16)
        m2_sb = sbuf(esR4, "m2_sb", (128, 8 * 512), BF16)
        cq_sb = sbuf(esR4, "cq_sb", (ROPE, TOK), BF16)
        sq_sb = sbuf(esR4, "sq_sb", (ROPE, TOK), BF16)
        mswap_sb = sbuf(esR4, "mswap_sb", (128, 64), BF16)
        mshift_sb = sbuf(esR4, "mshift_sb", (128, 64), BF16)
        qraw = sbuf(esR4, "qraw", (128, TOK), BF16)
        qpe_aug = sbuf(esR4, "qpe_aug", (65, TOK), BF16)
        knope_h = [sbuf(esR4, f"knope_h{i}", (64, S), BF16) for i in range(2)]
        v2 = sbuf(esR4, "v2", (128, NKT * VD), BF16)
        tb1 = sbuf(esR4, "tb1", (64, 512), F32)
        tb2 = sbuf(esR4, "tb2", (64, 512), F32)
        mstats = sbuf(esR4, "mstats", (128, NT * 8), F32)
        zstats = sbuf(esR4, "zstats", (128, NT * 8), F32)
        mins = sbuf(esR4, "mins", (128, NT), F32)
        minbf = sbuf(esR4, "minbf", (128, NT), BF16)
        minr = sbuf(esR4, "minr", (128, NT), F32)
        t8a = sbuf(esR4, "t8a", (128, NT * 8), F32)
        zw8a = sbuf(esR4, "zw8a", (128, NT * 8), F32)
        zca = sbuf(esR4, "zca", (128, NT), F32)
        zia = sbuf(esR4, "zia", (128, NT), F32)
        zrow = sbuf(esR4, "zrow", (1, TOK), BF16)
        attnT = [sbuf(esR4, f"attnT{i}", (128, 512), BF16) for i in range(4)]
        escr = sbuf(esR4, "escr", (128, 512), BF16)
        zbc = sbuf(esR4, "zbc", (128, 512), BF16)

        dma("pool", "qg", m1_256[:], ap("mask1_256"))
        dma("pool", "qg", m1_512[:], ap("mask1_512"))
        dma("pool", "qg", m2_sb[:].rearrange("p (c n) -> p c n", c=8),
            ap("mask2").rearrange("c p n -> p c n"))
        dma("pool", "qg", cq_sb[:], ap("cq"))
        dma("pool", "qg", sq_sb[:], ap("sq"))
        dma("pool", "qg", mswap_sb[:], ap("mswapT"))
        dma("pool", "qg", mshift_sb[:], ap("mshiftT"))
        t_bconst = qall("qg")

        t_qbw = []
        for h_ in range(2):
            t_qbw.append(dma("pool", "qg",
                             qbw_h[h_][:].rearrange("p (c n) -> p c n", c=6),
                             ap("qbw_h")[h_]))

        def p1_tiles(i):
            ext = (2 * i + 2) * 128
            full, remn = divmod(ext, 512)
            tl = [(j * 512, 512) for j in range(full)]
            if remn:
                tl.append((full * 512, remn))
            return tl

        att_rel = [None, None, None, None]
        for h_ in range(nheads):
            hh = h_ % 2
            buf = qbw_h[h_ % 2]
            emit_casts(4)
            P.emit("dve", lambda h: h.memset(qpe_aug[64:65, :], 0.0), inc="dve",
                   waits=[P.tok("pe")])
            t_qpz = P.tok("dve")
            t_qlast = None
            for nb in range(4):
                qsl = slice(nb * 512, (nb + 1) * 512)
                pq = ps[nb % 2]
                for c in range(6):
                    last = c == 5
                    P.emit("pe", lambda h, c=c, nb=nb, pq=pq, buf=buf: h.matmul(
                        pq[:, :], buf[0:QLC[c], c * 128:(c + 1) * 128],
                        qaT[0:QLC[c], c * TOK + nb * 512:c * TOK + (nb + 1) * 512],
                        start=(c == 0), stop=last),
                        inc="pe" if last else None,
                        waits=([t_qbw[h_], rel.get(nb % 2)] + t_bconst)
                        if c == 0 else ())
                t_q = P.tok("pe")
                t_qlast = t_q
                t_qr = P.emit("act", lambda h, qsl=qsl, pq=pq: h.mul(
                    qraw[:, qsl], pq[:, :], SCALE), inc="act",
                    waits=[t_q, P.tok("dve")])
                rel[nb % 2] = t_qr
                P.emit("pe", lambda h, qsl=qsl: h.matmul(
                    ps[2][0:64, :], mswap_sb[:], qraw[:, qsl], start=True, stop=True),
                    inc="pe", waits=[t_qr, rel.get(2)])
                P.emit("pe", lambda h, qsl=qsl: h.matmul(
                    ps[3][0:64, :], mshift_sb[:], qraw[:, qsl], start=True, stop=True),
                    inc="pe", waits=[rel.get(3)])
                t_rot = P.tok("pe")
                t_d1 = P.emit("dve", lambda h, qsl=qsl: h.tensor_mul(
                    tb1[:], ps[2][0:64, :], sq_sb[:, qsl]), inc="dve",
                    waits=[t_rot] + t_bconst)
                rel[2] = t_d1
                t_d2 = P.emit("dve", lambda h, qsl=qsl: h.tensor_mul(
                    tb2[:], ps[3][0:64, :], cq_sb[:, qsl]), inc="dve")
                rel[3] = t_d2
                P.emit("dve", lambda h, qsl=qsl: h.tensor_add(
                    qpe_aug[0:64, qsl], tb1[:], tb2[:]), inc="dve",
                    waits=[t_d2, t_qpz])
            t_qf = P.tok("dve")
            if upto == "QF":
                dma("pool", "qo", out.ap()[256:384, :], qraw[:, :], waits=[t_qf])
                dma("pool", "qo", out.ap()[384:449, :], qpe_aug[0:65, :], waits=[t_qf])
                P.emit("sp", lambda h: h.nop(), waits=qall("qo"))
                P.realize(esG)
                return nc
            if h_ + 2 < nheads:
                t_qbw.append(dma("pool", "qg",
                                 qbw_h[h_ % 2][:].rearrange("p (c n) -> p c n", c=6),
                                 ap("qbw_h")[h_ + 2], waits=[t_qlast]))

            if hh == 0:
                for j in range(8):
                    ksl = slice(j * 512, (j + 1) * 512)
                    for hp in range(2):
                        bank = 4 + ((2 * j + hp) % 2)
                        pk = ps[bank]
                        for c in range(2):
                            last = c == 1
                            P.emit("pe", lambda h, c=c, j=j, pk=pk, h_=h_, hp=hp:
                                   h.matmul(
                                pk[0:64, :],
                                kvbn_sb[:, c * NH * NOPE + (h_ + hp) * NOPE:
                                        c * NH * NOPE + (h_ + hp + 1) * NOPE],
                                ckvT[:, c * S + j * 512:c * S + (j + 1) * 512],
                                start=(c == 0), stop=last),
                                inc="pe" if last else None,
                                waits=([rel.get(bank)] + t_bconst) if c == 0 else ())
                        t_k = P.tok("pe")
                        t_cp = P.emit("act", lambda h, ksl=ksl, pk=pk, hp=hp: h.copy(
                            knope_h[hp][:, ksl], pk[0:64, :]), inc="act", waits=[t_k])
                        rel[bank] = t_cp
            for tk in range(NKT):
                bank = 4 + (tk % 2)
                pk = ps[bank]
                for c in range(2):
                    last = c == 1
                    P.emit("pe", lambda h, c=c, tk=tk, pk=pk, h_=h_: h.matmul(
                        pk[:, 0:VD],
                        ckvT[:, c * S + tk * 128:c * S + (tk + 1) * 128],
                        kvbv_sb[:, c * NH * VD + h_ * VD:c * NH * VD + (h_ + 1) * VD],
                        start=(c == 0), stop=last),
                        inc="pe" if last else None,
                        waits=([rel.get(bank)] + t_bconst) if c == 0 else ())
                t_v = P.tok("pe")
                t_cp = P.emit("act", lambda h, tk=tk, pk=pk: h.copy(
                    v2[:, tk * VD:(tk + 1) * VD], pk[:, 0:VD]),
                    inc="act", waits=[t_v])
                rel[bank] = t_cp
            t_kvb = P.tok("act")

            for qt in range(NT):
                tiles = p1_tiles(qt)
                nE = len(tiles)
                c0 = qt * 8
                for ti, (k0, w) in enumerate(tiles):
                    bank = ti % 4
                    pq = ps[bank]
                    P.emit("pe", lambda h, k0=k0, w=w, pq=pq, qt=qt, hh=hh: h.matmul(
                        pq[:, 0:w], qraw[0:64, qt * 128:(qt + 1) * 128],
                        knope_h[hh][:, k0:k0 + w],
                        start=True, stop=False),
                        waits=[t_qf, t_kvb, t_kpe, rel.get(bank)])
                    P.emit("pe", lambda h, k0=k0, w=w, pq=pq, qt=qt: h.matmul(
                        pq[:, 0:w], qpe_aug[0:65, qt * 128:(qt + 1) * 128],
                        kpe_aug[0:65, k0:k0 + w], start=False, stop=True), inc="pe")
                    t_s = P.tok("pe")
                    if ti == nE - 1:
                        msk = m1_256 if w == 256 else m1_512
                        t_s = P.emit("dve", lambda h, w=w, pq=pq, msk=msk:
                                     h.tensor_add(pq[:, 0:w], pq[:, 0:w], msk[:, 0:w]),
                                     inc="dve", waits=[t_s] + t_bconst)
                    t_mk = P.emit("dve", lambda h, w=w, pq=pq, ci=c0 + ti:
                                  h.reduce_max(mstats[:, ci:ci + 1], pq[:, 0:w],
                                               axis=AX.X, negate=True), inc="dve",
                                  waits=[t_s])
                    t_e = P.emit("act", lambda h, w=w, pq=pq, ci=c0 + ti: h.activation(
                        escr[:, 0:w], pq[:, 0:w], AF.Exp,
                        bias=mstats[:, ci:ci + 1],
                        accum_out=zstats[:, ci:ci + 1]), inc="act", waits=[t_mk])
                    rel[bank] = t_e
                t_z = P.tok("act")
                P.emit("dve", lambda h, nE=nE, c0=c0, qt=qt: h.tensor_reduce(
                    mins[:, qt:qt + 1], mstats[:, c0:c0 + nE], AX.X, ALU.min),
                    inc="dve", waits=[t_z])
                P.emit("dve", lambda h, qt=qt: h.tensor_copy(
                    minbf[:, qt:qt + 1], mins[:, qt:qt + 1]), inc="dve",
                    waits=[P.tok("dve")])
                t_mr = P.emit("dve", lambda h, qt=qt: h.tensor_copy(
                    minr[:, qt:qt + 1], minbf[:, qt:qt + 1]), inc="dve",
                    waits=[P.tok("dve")])
                t_t8 = P.emit("act", lambda h, nE=nE, c0=c0, qt=qt: h.activation(
                    t8a[:, c0:c0 + nE], mstats[:, c0:c0 + nE], AF.Exp,
                    bias=minr[:, qt:qt + 1], scale=-1.0), inc="act", waits=[t_mr])
                P.emit("dve", lambda h, nE=nE, c0=c0: h.tensor_mul(
                    zw8a[:, c0:c0 + nE], t8a[:, c0:c0 + nE], zstats[:, c0:c0 + nE]),
                    inc="dve", waits=[t_t8])
                P.emit("dve", lambda h, nE=nE, c0=c0, qt=qt: h.reduce_sum(
                    zca[:, qt:qt + 1], zw8a[:, c0:c0 + nE], axis=AX.X), inc="dve",
                    waits=[P.tok("dve")])
                t_zi = P.emit("dve", lambda h, qt=qt: h.reciprocal(
                    zia[:, qt:qt + 1], zca[:, qt:qt + 1]), inc="dve",
                    waits=[P.tok("dve")])
                bank = 6 + (qt % 2)
                t_tr = P.emit("pe", lambda h, qt=qt, bank=bank: h.transpose(
                    ps[bank][0:1, 0:128], minr[:, qt:qt + 1], id_f32[:]), inc="pe",
                    waits=[t_mr, rel.get(bank)])
                t_mrow = P.emit("act", lambda h, qt=qt, bank=bank: h.copy(
                    qpe_aug[64:65, qt * 128:(qt + 1) * 128], ps[bank][0:1, 0:128]),
                    inc="act", waits=[t_tr])
                t_tr2 = P.emit("pe", lambda h, qt=qt, bank=bank: h.transpose(
                    ps[bank][0:1, 128:256], zia[:, qt:qt + 1], id_f32[:]), inc="pe",
                    waits=[t_zi, t_mrow])
                t_zrow = P.emit("act", lambda h, qt=qt, bank=bank: h.copy(
                    zrow[0:1, qt * 128:(qt + 1) * 128], ps[bank][0:1, 128:256]),
                    inc="act", waits=[t_tr2])
                rel[bank] = t_zrow
            t_p1 = P.tok("act")

            for qb in range(4):
                nkt = 8 * qb + 8
                pend = []
                t_av = None

                def emit_av(kt, t_e, nkt=None):
                    return P.emit("pe", lambda h, kt=kt, nkt=nkt: h.matmul(
                        ps[4][:, :], v2[:, kt * VD:(kt + 1) * VD],
                        attnT[kt % 4][:], start=(kt == 0), stop=(kt == nkt - 1)),
                        inc="pe", waits=[t_e] + ([rel.get(4)] if kt == 0 else []))

                for kt in range(nkt):
                    bank = kt % 4
                    pq = ps[bank]
                    P.emit("pe", lambda h, kt=kt, pq=pq, qb=qb, hh=hh: h.matmul(
                        pq[:, :], knope_h[hh][:, kt * 128:(kt + 1) * 128],
                        qraw[0:64, qb * 512:(qb + 1) * 512], start=True, stop=False),
                        waits=[t_p1, rel.get(bank)])
                    P.emit("pe", lambda h, kt=kt, pq=pq, qb=qb: h.matmul(
                        pq[:, :], kpe_aug[0:65, kt * 128:(kt + 1) * 128],
                        qpe_aug[0:65, qb * 512:(qb + 1) * 512],
                        start=False, stop=True), inc="pe")
                    t_s = P.tok("pe")
                    dd = kt - 8 * qb
                    if dd >= 0:
                        t_s = P.emit("dve", lambda h, pq=pq, dd=dd: h.tensor_add(
                            pq[:, :], pq[:, :], m2_sb[:, dd * 512:(dd + 1) * 512]),
                            inc="dve", waits=[t_s] + t_bconst)
                    t_e = P.emit("act", lambda h, pq=pq, kt=kt: h.activation(
                        attnT[kt % 4][:], pq[:, :], AF.Exp), inc="act",
                        waits=[t_s, att_rel[kt % 4]])
                    rel[bank] = t_e
                    pend.append((kt, t_e))
                    if len(pend) > 2:
                        kt0, te0 = pend.pop(0)
                        att_rel[kt0 % 4] = emit_av(kt0, te0, nkt)
                for kt0, te0 in pend:
                    att_rel[kt0 % 4] = emit_av(kt0, te0, nkt)
                t_av = P.tok("pe")
                t_zb = P.emit("pe", lambda h, qb=qb: h.matmul(
                    ps[5][:, :], ones_bf[0:1, :], zrow[0:1, qb * 512:(qb + 1) * 512],
                    start=True, stop=True), inc="pe", waits=[rel.get(5)])
                t_zc = P.emit("act", lambda h: h.copy(zbc[:], ps[5][:, :]), inc="act",
                              waits=[t_zb])
                rel[5] = t_zc
                t_o = P.emit("dve", lambda h, h_=h_, qb=qb: h.tensor_mul(
                    o_all[:, h_ * TOK + qb * 512:h_ * TOK + (qb + 1) * 512],
                    ps[4][:, :], zbc[:]), inc="dve", waits=[t_av, t_zc])
                rel[4] = t_o
        P.realize(esG)
        esR4.close()
        esR2.close()

        if upto == "B":
            for h_ in range(nheads):
                dma("pool", "qo", out.ap()[h_ * 128:(h_ + 1) * 128, :],
                    o_all[:, h_ * TOK:(h_ + 1) * TOK])
            if nheads == 1:
                dma("pool", "qo", out.ap()[256:384, :], qraw[:, :])
                dma("pool", "qo", out.ap()[384:449, :], qpe_aug[0:65, :])
                dma("pool", "qo", out.ap()[512:576, :], knope_h[0][:, 0:2048])
                dma("pool", "qo", out.ap()[640:768, :], v2[:, 0:2048])
                dma("pool", "qo", out.ap()[800:801, :], zrow[0:1, :])
                dma("pool", "qo", out.ap()[832:960, :], qaT[:, TOK:2 * TOK])
                dma("pool", "qo", out.ap()[960:1002, :], qaT[0:42, 5 * TOK:6 * TOK])
                dma("pool", "qo", out.ap()[1024:1152, 0:768], qbw_h[0][:, :])
            P.emit("sp", lambda h: h.nop(), waits=qall("qo"))
            P.realize(esG)
            return nc

        esR5 = contextlib.ExitStack()
        ow_sb = sbuf(esR5, "ow_sb", (128, KC * D), BF16)
        xtC = [sbuf(esR5, f"xtC{i}", (128, D), F32) for i in range(2)]
        x1t = [sbuf(esR5, f"x1t{i}", (128, D), F32) for i in range(2)]
        for c in range(KC):
            dma("pool", "qg",
                ow_sb[:].rearrange("p (c n) -> p c n", c=KC)[:, c:c + 1, :],
                ap("ow_t")[:, c:c + 1, :])
        t_ow = qall("qg")
        xrelC = [None, None]
        x1rel = [None, None]
        for tt in range(NT):
            xb = xtC[tt % 2]
            t_x = dma("sp", "qs", xb[:], ap("xq")[tt * 128:(tt + 1) * 128, :],
                      waits=[xrelC[tt % 2]])
            for h_ in range(NH):
                for hcg in range(4):
                    P.emit("pe", lambda h, h_=h_, hcg=hcg, tt=tt: h.matmul(
                        ps[hcg][:, :],
                        o_all[:, h_ * TOK + tt * 128:h_ * TOK + (tt + 1) * 128],
                        ow_sb[:, h_ * D + hcg * 512:h_ * D + (hcg + 1) * 512],
                        start=(h_ == 0), stop=(h_ == NH - 1)),
                        inc="pe" if (h_ == NH - 1 and hcg == 3) else None,
                        waits=(t_ow + [rel.get(i) for i in range(4)])
                        if (h_ == 0 and hcg == 0) else ())
            t_o = P.tok("pe")
            x1b = x1t[tt % 2]
            t_a = None
            for hcg in range(4):
                t_a = P.emit("dve", lambda h, hcg=hcg, x1b=x1b, xb=xb: h.tensor_add(
                    x1b[:, hcg * 512:(hcg + 1) * 512], ps[hcg][:, :],
                    xb[:, hcg * 512:(hcg + 1) * 512]), inc="dve",
                    waits=[t_o, t_x, x1rel[tt % 2]])
            for i in range(4):
                rel[i] = t_a
            xrelC[tt % 2] = t_a
            t_st = dma("sp", "qs", x1d.ap()[tt * 128:(tt + 1) * 128, :], x1b[:],
                       waits=[t_a])
            x1rel[tt % 2] = t_st
        t_x1d = qall("qs")
        P.realize(esG)
        esR5.close()
        esR1.close()

        if upto == "C":
            for tt in range(NT):
                dma("sp", "qo", out.ap()[tt * 128:(tt + 1) * 128, :],
                    x1d.ap()[tt * 128:(tt + 1) * 128, :], waits=t_x1d)
            P.emit("sp", lambda h: h.nop(), waits=qall("qo"))
            P.realize(esG)
            return nc

        esR6 = contextlib.ExitStack()
        x2 = sbuf(esR6, "x2", (128, 8 * D), F32)
        x1nT = sbuf(esR6, "x1nT", (128, KC * 1024), BF16)
        z_e = sbuf(esR6, "z_e", (128, IT_MOE * 1024), BF16)
        wgl = [sbuf(esR6, f"wgl{i}", (128, KC * 128), BF16) for i in range(2)]
        wul = [sbuf(esR6, f"wul{i}", (128, KC * 128), BF16) for i in range(2)]
        wdr = [sbuf(esR6, f"wdr{i}", (128, IT_MOE * 1024), BF16) for i in range(2)]
        onehot_sb = sbuf(esR6, "onehot_sb", (16, E * 128), BF16)
        combT = sbuf(esR6, "combT", (16, 1024), BF16)
        comb_b = sbuf(esR6, "comb_b", (128, 1024), BF16)
        lgmx = sbuf(esR6, "lgmx", (128, 1), F32)
        nlgmx = sbuf(esR6, "nlgmx", (128, 1), F32)
        pun = sbuf(esR6, "pun", (128, 16), F32)
        rank = sbuf(esR6, "rank", (128, 16), F32)
        tcmp = sbuf(esR6, "tcmp", (128, 16), F32)
        comb = sbuf(esR6, "comb", (128, 16), F32)
        den = sbuf(esR6, "den", (128, 1), F32)
        dinv = sbuf(esR6, "dinv", (128, 1), F32)
        sg = sbuf(esR6, "sg", (128, 8), F32)
        silu_s = sbuf(esR6, "silu_s", (128, 512), BF16)
        tmul = sbuf(esR6, "tmul", (128, 512), BF16)
        t_oneh = dma("pool", "qg", onehot_sb[:], ap("onehot"))

        x2rel = None
        for hf in range(2):
            toff = hf * 8
            for tt8 in range(8):
                tt = toff + tt8
                xsl = slice(tt8 * D, (tt8 + 1) * D)
                t_x = dma("sp", "qs", x2[:, xsl],
                          x1d.ap()[tt * 128:(tt + 1) * 128, :],
                          waits=t_x1d + ([] if x2rel is None else x2rel))
                t_xn = emit_ln(x2[:, xsl], t_x, extra=[P.tok("pe")])
                emit_transposes(
                    xn, KC, None,
                    lambda c, tt8=tt8: x1nT[:, c * 1024 + tt8 * 128:
                                            c * 1024 + (tt8 + 1) * 128], t_xn)
            t_nT = P.tok("act")

            for tt8 in range(8):
                for c in range(KC):
                    P.emit("pe", lambda h, c=c, tt8=tt8: h.matmul(
                        ps[0][:, 0:E],
                        x1nT[:, c * 1024 + tt8 * 128:c * 1024 + (tt8 + 1) * 128],
                        router_sb[:, c * E:(c + 1) * E],
                        start=(c == 0), stop=(c == KC - 1)),
                        inc="pe" if c == KC - 1 else None,
                        waits=[t_nT, rel.get(0)] if c == 0 else ())
                t_l = P.tok("pe")
                P.emit("dve", lambda h: h.reduce_max(lgmx[:], ps[0][:, 0:E],
                                                     axis=AX.X), inc="dve",
                       waits=[t_l])
                t_nm = P.emit("dve", lambda h: h.tensor_scalar(
                    nlgmx[:], lgmx[:], -1.0, None, ALU.mult), inc="dve",
                    waits=[P.tok("dve"), P.tok("act")])
                t_p = P.emit("act", lambda h: h.activation(
                    pun[:], ps[0][:, 0:E], AF.Exp, bias=nlgmx[:]), inc="act",
                    waits=[t_nm])
                rel[0] = t_p
                P.emit("dve", lambda h: h.memset(rank[:], 0.0), inc="dve",
                       waits=[t_p])
                for e2 in range(E):
                    P.emit("dve", lambda h, e2=e2: h.tensor_scalar(
                        tcmp[:], pun[:], pun[:, e2:e2 + 1], None, ALU.is_lt),
                        inc="dve", waits=[P.tok("dve")])
                    P.emit("dve", lambda h: h.tensor_add(rank[:], rank[:], tcmp[:]),
                           inc="dve", waits=[P.tok("dve")])
                P.emit("dve", lambda h: h.tensor_scalar(tcmp[:], rank[:], 3.0, None,
                                                        ALU.is_le), inc="dve",
                       waits=[P.tok("dve")])
                P.emit("dve", lambda h: h.tensor_mul(comb[:], pun[:], tcmp[:]),
                       inc="dve", waits=[P.tok("dve")])
                P.emit("dve", lambda h: h.reduce_sum(den[:], comb[:], axis=AX.X),
                       inc="dve", waits=[P.tok("dve")])
                P.emit("dve", lambda h: h.reciprocal(dinv[:], den[:]), inc="dve",
                       waits=[P.tok("dve")])
                t_cb = P.emit("dve", lambda h: h.tensor_scalar(
                    comb[:], comb[:], dinv[:], None, ALU.mult), inc="dve",
                    waits=[P.tok("dve")])
                t_tr = P.emit("pe", lambda h: h.transpose(
                    ps[6].bitcast(F32)[0:16, 0:128] if False else ps[6][0:16, 0:128],
                    comb[:, 0:16], id_f32[:]), inc="pe", waits=[t_cb, rel.get(6)])
                t_ct = P.emit("act", lambda h, tt8=tt8: h.copy(
                    combT[0:16, tt8 * 128:(tt8 + 1) * 128], ps[6][0:16, 0:128]),
                    inc="act", waits=[t_tr])
                rel[6] = t_ct
                for c in range(KC):
                    P.emit("pe", lambda h, c=c, tt8=tt8: h.matmul(
                        ps[1][:, 0:1],
                        x1nT[:, c * 1024 + tt8 * 128:c * 1024 + (tt8 + 1) * 128],
                        shgw_sb[:, c:c + 1], start=(c == 0), stop=(c == KC - 1)),
                        inc="pe" if c == KC - 1 else None,
                        waits=[rel.get(1)] if c == 0 else ())
                t_sgm = P.tok("pe")
                t_sg = P.emit("act", lambda h, tt8=tt8: h.activation(
                    sg[:, tt8:tt8 + 1], ps[1][:, 0:1], AF.Sigmoid), inc="act",
                    waits=[t_sgm])
                rel[1] = t_sg
            t_route = P.tok("act")
            t_combT = rel[6]

            wg_rel = [None, None]
            wd_rel = [None, None]
            t_wg = {}
            t_wd = {}

            def load_gu(kind, idx, buf, wtok):
                if kind == "moe":
                    e_, it = idx
                    ta = dma("sp", "qs", wgl[buf][:], wg_bf.ap()[e_, it],
                             waits=[wtok, cast_tok[("g", e_)]])
                    tb = dma("sp", "qs", wul[buf][:], wu_bf.ap()[e_, it],
                             waits=[cast_tok[("u", e_)]])
                else:
                    it = idx
                    ta = dma("sp", "qs", wgl[buf][:], shg_bf.ap()[it],
                             waits=[wtok, cast_tok[("shg", it // IT_MOE)]])
                    tb = dma("sp", "qs", wul[buf][:], shu_bf.ap()[it],
                             waits=[cast_tok[("shu", it // IT_MOE)]])
                return [ta, tb]

            def load_wd(kind, idx, buf, wtok):
                if kind == "moe":
                    e_, hh2 = idx
                    ta = dma("act", "qa", wdr[buf][:], wd_bf.ap()[e_, hh2],
                             waits=[wtok, cast_tok[("d", e_)]])
                else:
                    grp, hh2 = idx
                    ta = dma("act", "qa", wdr[buf][:], shd_bf.ap()[grp, hh2],
                             waits=[wtok, cast_tok[("shd", grp)]])
                return [ta]

            groups = [("moe", e_) for e_ in range(E)] + [("sh", g) for g in range(4)]
            gu_list = [(k, (g, it) if k == "moe" else g * IT_MOE + it)
                       for (k, g) in groups for it in range(IT_MOE)]
            for gi in range(2):
                k, idx = gu_list[gi]
                t_wg[gi] = load_gu(k, idx, gi % 2, None)
            wd_list = [(k, (g, hh2) if k == "moe" else (g, hh2))
                       for (k, g) in groups for hh2 in range(2)]
            for di in range(2):
                k, idx = wd_list[di]
                t_wd[di] = load_wd(wd_list[di][0], idx, di % 2, None)

            gi_count = [0]
            di_count = [0]

            for (kind, gidx) in groups:
                is_moe = kind == "moe"
                if is_moe:
                    e_ = gidx
                    for nb in range(2):
                        t_b = P.emit("pe", lambda h, e_=e_, nb=nb: h.matmul(
                            ps[4][:, :], onehot_sb[0:16, e_ * 128:(e_ + 1) * 128],
                            combT[0:16, nb * 512:(nb + 1) * 512],
                            start=True, stop=True), inc="pe",
                            waits=[t_combT, t_oneh, rel.get(4)])
                        t_cpb = P.emit("act", lambda h, nb=nb: h.copy(
                            comb_b[:, nb * 512:(nb + 1) * 512], ps[4][:, :]),
                            inc="act", waits=[t_b, P.tok("dve")])
                        rel[4] = t_cpb
                    t_cbb = P.tok("act")
                for it in range(IT_MOE):
                    gi = gi_count[0]
                    buf = gi % 2
                    bs = [0, 1, 2, 3] if (gi % 2 == 0) else [4, 5, 6, 7]
                    t_w = t_wg[gi]
                    t_gu = None
                    for c in range(KC):
                        for bi, (pp, wsb, xo) in enumerate((
                                (ps[bs[0]], wgl[buf], 0), (ps[bs[1]], wgl[buf], 512),
                                (ps[bs[2]], wul[buf], 0), (ps[bs[3]], wul[buf], 512))):
                            P.emit("pe", lambda h, c=c, pp=pp, wsb=wsb, xo=xo:
                                   h.matmul(pp[:, :], wsb[:, c * 128:(c + 1) * 128],
                                            x1nT[:, c * 1024 + xo:c * 1024 + xo + 512],
                                            start=(c == 0), stop=(c == KC - 1)),
                                   inc="pe" if (c == KC - 1 and bi == 3) else None,
                                   waits=(t_w + [t_nT] + [rel.get(i) for i in bs])
                                   if (c == 0 and bi == 0) else ())
                    t_gu = P.tok("pe")
                    if gi + 2 < len(gu_list):
                        k2, idx2 = gu_list[gi + 2]
                        t_wg[gi + 2] = load_gu(k2, idx2, buf, t_gu)
                    for nb in range(2):
                        zsl = slice(it * 1024 + nb * 512, it * 1024 + (nb + 1) * 512)
                        pg = (ps[bs[0]], ps[bs[1]])[nb]
                        pu = (ps[bs[2]], ps[bs[3]])[nb]
                        t_si = P.emit("act", lambda h, pg=pg: h.activation(
                            silu_s[:], pg[:, :], AF.Silu), inc="act",
                            waits=[t_gu, P.tok("dve")])
                        rel[bs[nb]] = t_si
                        if is_moe:
                            P.emit("dve", lambda h, pu=pu, nb=nb: h.tensor_mul(
                                tmul[:], pu[:, :], comb_b[:, nb * 512:(nb + 1) * 512]),
                                inc="dve", waits=[t_gu, t_cbb, P.tok("act")])
                        else:
                            P.emit("dve", lambda h, pu=pu: h.tensor_copy(
                                tmul[:], pu[:, :]), inc="dve",
                                waits=[t_gu, P.tok("act")])
                        t_tm = P.tok("dve")
                        rel[bs[2 + nb]] = t_tm
                        t_zw = P.emit("dve", lambda h, zsl=zsl: h.tensor_mul(
                            z_e[:, zsl], tmul[:], silu_s[:]), inc="dve",
                            waits=[t_tm, t_si])
                    gi_count[0] += 1
                t_z = P.tok("dve")
                for hh2 in range(2):
                    di = di_count[0]
                    dbuf = di % 2
                    t_wdt = t_wd[di]
                    t_dlast = None
                    for pair in range(4):
                        dbs = [0, 1, 2, 3] if (pair % 2 == (di % 2)) else [4, 5, 6, 7]
                        for it in range(IT_MOE):
                            for sub in range(2):
                                tt8 = pair * 2 + sub
                                for hq in range(2):
                                    bank = dbs[sub * 2 + hq]
                                    P.emit("pe", lambda h, it=it, tt8=tt8, hq=hq,
                                           hh2=hh2, dbuf=dbuf, bank=bank: h.matmul(
                                        ps[bank][:, :],
                                        z_e[:, it * 1024 + tt8 * 128:
                                            it * 1024 + (tt8 + 1) * 128],
                                        wdr[dbuf][:, it * 1024 + hq * 512:
                                                  it * 1024 + hq * 512 + 512],
                                        start=(it == 0), stop=(it == IT_MOE - 1)),
                                        inc="pe" if (it == IT_MOE - 1 and sub == 1
                                                     and hq == 1) else None,
                                        waits=([t_z] + t_wdt +
                                        [rel.get(i) for i in dbs])
                                        if (it == 0 and sub == 0 and hq == 0) else ())
                        t_d = P.tok("pe")
                        t_dlast = t_d
                        for sub in range(2):
                            tt8 = pair * 2 + sub
                            for hq in range(2):
                                bank = dbs[sub * 2 + hq]
                                col = tt8 * D + hh2 * 1024 + hq * 512
                                if is_moe:
                                    t_ad = P.emit("dve", lambda h, col=col, bank=bank:
                                                  h.tensor_add(
                                        x2[:, col:col + 512], x2[:, col:col + 512],
                                        ps[bank][:, :]), inc="dve", waits=[t_d])
                                else:
                                    t_ad = P.emit("dve", lambda h, col=col, bank=bank,
                                                  tt8=tt8: h.scalar_tensor_tensor(
                                        x2[:, col:col + 512], ps[bank][:, :],
                                        sg[:, tt8:tt8 + 1], x2[:, col:col + 512],
                                        ALU.mult, ALU.add), inc="dve",
                                        waits=[t_d, t_route])
                                rel[bank] = t_ad
                    di_count[0] += 1
                    if di + 2 < len(wd_list):
                        k2, idx2 = wd_list[di + 2]
                        t_wd[di + 2] = load_wd(k2, idx2, dbuf, t_dlast)
            t_ffn = P.tok("dve")
            for tt8 in range(8):
                tt = toff + tt8
                dma("sp", "qo", out.ap()[tt * 128:(tt + 1) * 128, :],
                    x2[:, tt8 * D:(tt8 + 1) * D], waits=[t_ffn])
            x2rel = qall("qo")
            P.realize(esG)
        esR6.close()

        P.emit("sp", lambda h: h.nop(), waits=qall("qo"))
        P.realize(esG)
    return nc


def kernel(**inputs):
    pr, percore = _host_prepare(inputs)
    nc = build_nc()
    shared = {k: np.ascontiguousarray(v, dtype=np.float32) for k, v in pr.items()}
    in_maps = []
    for c in range(8):
        m = dict(shared)
        for k, v in percore[c].items():
            m[k] = np.ascontiguousarray(v, dtype=np.float32)
        in_maps.append(m)
    res = run_bass_kernel_spmd(nc, in_maps, core_ids=list(range(8)))
    full = np.empty((B, S, D), dtype=np.float32)
    for c in range(8):
        b, p = c // 2, c % 2
        o = np.asarray(res.results[c]["out"])
        for i in range(NT):
            g = 2 * i + p
            full[b, g * 128:(g + 1) * 128, :] = o[i * 128:(i + 1) * 128, :]
    return full


# version 04
# speedup vs baseline: 1.4278x; 1.0754x over previous
import contextlib
import math
import numpy as np
import concourse.bass as bass
import concourse.mybir as mybir
from concourse.bass_utils import run_bass_kernel_spmd

F32 = mybir.dt.float32
BF16 = mybir.dt.bfloat16
AF = mybir.ActivationFunctionType
ALU = mybir.AluOpType
AX = mybir.AxisListType

B, S, D, NH = 4, 4096, 2048, 16
ROPE, NOPE, VD = 64, 64, 128
QH = ROPE + NOPE
QL = D // 3
KVL = D // 8
THETA = 128000.0
SCALE = 1.0 / math.sqrt(QH)
EPS_LN = 1e-5
EPS_RMS = 1e-6
E = 16
I_MOE, I_SH = 1408, 5632
NT = 16
TOK = NT * 128
NKT = 32
IT_MOE = I_MOE // 128
IT_SH = I_SH // 128
KC = D // 128
QLC = [128, 128, 128, 128, 128, 42]
NEG = -30000.0

ENGS = ("pe", "dve", "act", "pool", "sp")


def _rope_tables():
    inv = 1.0 / THETA ** (np.arange(0, ROPE, 2, dtype=np.float64) / ROPE)
    pos = np.arange(S, dtype=np.float64)
    fr = np.outer(inv, pos)
    cos = np.cos(np.concatenate([fr, fr], 0))
    sin = np.sin(np.concatenate([fr, fr], 0))
    sin_signed = sin.copy()
    sin_signed[:32] *= -1.0
    return cos.astype(np.float32), sin_signed.astype(np.float32)


_PERM = np.concatenate([np.arange(0, 64, 2), np.arange(1, 64, 2)])


def _tile_k(w):
    k, n = w.shape
    return np.ascontiguousarray(w.reshape(k // 128, 128, n).transpose(1, 0, 2))


def _tile_cols(w, ct):
    k, n = w.shape
    return np.ascontiguousarray(
        w.reshape(k // 128, 128, n // ct, ct).transpose(2, 1, 0, 3))


def _host_prepare(inp):
    cos, sin = _rope_tables()
    pr = {}
    pr["qaw_t"] = _tile_k(inp["L0_attn_q_a_w"])
    pr["lnq"] = inp["L0_attn_q_a_ln_w"].reshape(1, QL)
    qbw = np.array(inp["L0_attn_q_b_w"], dtype=np.float32)
    for h in range(NH):
        base = h * QH + NOPE
        qbw[:, base:base + ROPE] = qbw[:, base + _PERM]
    qbh = np.zeros((NH, 128, 6, 128), np.float32)
    for h in range(NH):
        for c in range(6):
            w = QLC[c]
            qbh[h, :w, c, :] = qbw[c * 128:c * 128 + w, h * QH:(h + 1) * QH]
    pr["qbw_h"] = qbh
    kvaw = np.array(inp["L0_attn_kv_a_w"], dtype=np.float32)
    kvaw[:, KVL:] = kvaw[:, KVL + _PERM]
    pr["kvaw_t"] = _tile_k(kvaw)
    pr["lnkv"] = inp["L0_attn_kv_a_ln_w"].reshape(1, KVL)
    kvb3 = np.asarray(inp["L0_attn_kv_b_w"]).reshape(KVL, NH, NOPE + VD)
    pr["kvb_nope"] = np.ascontiguousarray(kvb3[:, :, :NOPE].reshape(KVL, NH * NOPE))
    pr["kvb_v"] = np.ascontiguousarray(kvb3[:, :, NOPE:].reshape(KVL, NH * VD))
    pr["ow_t"] = _tile_k(inp["L0_attn_o_w"])
    pr["router_t"] = _tile_k(np.ascontiguousarray(np.asarray(inp["L0_ffn_router_w"]).T))
    pr["wg_t"] = np.stack([_tile_cols(inp["L0_ffn_w_gate"][e], 128) for e in range(E)])
    pr["wu_t"] = np.stack([_tile_cols(inp["L0_ffn_w_up"][e], 128) for e in range(E)])
    pr["wd_t"] = np.stack([
        np.ascontiguousarray(np.asarray(inp["L0_ffn_w_down"][e], np.float32)
                             .reshape(IT_MOE, 128, 2, 1024).transpose(2, 1, 0, 3)
                             .reshape(2, 128, IT_MOE * 1024)) for e in range(E)])
    pr["shg_t"] = _tile_cols(inp["L0_ffn_sh_gate"], 128)
    pr["shu_t"] = _tile_cols(inp["L0_ffn_sh_up"], 128)
    pr["shd_t"] = np.ascontiguousarray(
        np.asarray(inp["L0_ffn_sh_down"], np.float32)
        .reshape(4, IT_MOE, 128, 2, 1024).transpose(0, 3, 2, 1, 4)
        .reshape(4, 2, 128, IT_MOE * 1024))
    pr["shgw_t"] = _tile_k(inp["L0_ffn_sh_gate_w"])
    pr["id128"] = np.eye(128, dtype=np.float32)
    pr["ones_row"] = np.ones((1, 128), np.float32)
    mswap = np.zeros((64, 128), np.float32)
    for m in range(32):
        mswap[m, 96 + m] = 1.0
        mswap[32 + m, 64 + m] = 1.0
    pr["mswapT"] = np.ascontiguousarray(mswap.T)
    mshift = np.zeros((64, 128), np.float32)
    for m in range(64):
        mshift[m, 64 + m] = 1.0
    pr["mshiftT"] = np.ascontiguousarray(mshift.T)
    r64 = np.zeros((64, 64), np.float32)
    for m in range(32):
        r64[m, 32 + m] = 1.0
        r64[32 + m, m] = 1.0
    pr["r64T"] = np.ascontiguousarray(r64.T)
    pr["ck"] = np.ascontiguousarray(cos)
    pr["sk"] = np.ascontiguousarray(sin)
    oneh = np.zeros((E, E * 128), np.float32)
    for e in range(E):
        oneh[e, e * 128:(e + 1) * 128] = 1.0
    pr["onehot"] = oneh
    percore = []
    x = np.ascontiguousarray(np.asarray(inp["L0_attn_x"], dtype=np.float32))
    for c in range(8):
        b, p = c // 2, c % 2
        d = {}
        slots = np.array([2 * i + p for i in range(NT)])
        rows = (slots[:, None] * 128 + np.arange(128)[None, :]).reshape(-1)
        d["xq"] = np.ascontiguousarray(x[b][rows])
        d["xk"] = x[b]
        d["cq"] = np.ascontiguousarray(cos[:, rows])
        d["sq"] = np.ascontiguousarray(sin[:, rows])
        for W in (256, 512):
            cc, rr = np.meshgrid(np.arange(W), np.arange(128))
            d[f"mask1_{W}"] = np.where(cc - rr > W - 256 + 128 * p, NEG, 0.0
                                       ).astype(np.float32)
        m2 = np.zeros((8, 128, 512), np.float32)
        for dd in range(8):
            for t in range(4):
                cc, rr = np.meshgrid(np.arange(128), np.arange(128))
                m2[dd][:, t * 128:(t + 1) * 128] = np.where(
                    (dd - 2 * t - p) * 128 + rr > cc, NEG, 0.0)
        d["mask2"] = m2
        percore.append(d)
    return pr, percore


class Prog:
    def __init__(self, nc):
        self.nc = nc
        self.items = {e: [] for e in ENGS}
        self.sem = {}
        self.cnt = {}
        self.water = {e: {} for e in ENGS}

    def new_sem(self, name, es):
        self.sem[name] = es.enter_context(self.nc.semaphore(name))
        self.cnt[name] = 0

    def tok(self, name):
        return (name, self.cnt[name])

    paranoid = ()

    def emit(self, eng, fn, inc=None, inck=1, waits=()):
        if self.paranoid:
            waits = list(waits) + [self.tok(s) for s in self.paranoid]
        wl = []
        for tkn in waits:
            if tkn is None:
                continue
            s, v = tkn
            if v <= 0:
                continue
            w = self.water[eng]
            if w.get(s, 0) >= v:
                continue
            w[s] = v
            wl.append((self.sem[s], v))
        tk = None
        sem_h = None
        if inc is not None:
            self.cnt[inc] += inck
            tk = (inc, self.cnt[inc])
            sem_h = self.sem[inc]

        def wrapped(h, fn=fn, wl=wl, sem_h=sem_h, inck=inck):
            for (sh, v) in wl:
                h.wait_ge(sh, v)
            inst = fn(h)
            if sem_h is not None:
                inst.then_inc(sem_h, inck)
        self.items[eng].append(wrapped)
        return tk

    def realize(self, es):
        with self.nc.Block() as block:
            for eng, deco in (("pe", block.tensor), ("dve", block.vector),
                              ("act", block.scalar), ("pool", block.gpsimd),
                              ("sp", block.sync)):
                fns = self.items[eng]
                if fns:
                    deco(lambda h, fns=list(fns): [fn(h) for fn in fns])
        self.items = {e: [] for e in ENGS}
        for e in ENGS:
            for s in self.sem:
                if not s.startswith("q"):
                    self.water[e][s] = self.cnt[s]


def build_nc(upto="full", nheads=NH):
    nc = bass.Bass("TRN2", target_bir_lowering=False)
    dr = {}

    def din(name, shape):
        dr[name] = nc.dram_tensor(name, list(shape), F32, kind="ExternalInput")

    din("xq", (TOK, D)); din("xk", (S, D))
    din("qaw_t", (128, KC, QL)); din("lnq", (1, QL)); din("qbw_h", (NH, 128, 6, 128))
    din("kvaw_t", (128, KC, KVL + ROPE)); din("lnkv", (1, KVL))
    din("kvb_nope", (KVL, NH * NOPE)); din("kvb_v", (KVL, NH * VD))
    din("ow_t", (128, KC, D)); din("router_t", (128, KC, E))
    din("wg_t", (E, IT_MOE, 128, KC, 128)); din("wu_t", (E, IT_MOE, 128, KC, 128))
    din("wd_t", (E, 2, 128, IT_MOE * 1024))
    din("shg_t", (IT_SH, 128, KC, 128)); din("shu_t", (IT_SH, 128, KC, 128))
    din("shd_t", (4, 2, 128, IT_MOE * 1024)); din("shgw_t", (128, KC, 1))
    din("id128", (128, 128)); din("ones_row", (1, 128))
    din("mswapT", (128, 64)); din("mshiftT", (128, 64)); din("r64T", (64, 64))
    din("ck", (ROPE, S)); din("sk", (ROPE, S))
    din("cq", (ROPE, TOK)); din("sq", (ROPE, TOK))
    din("mask1_256", (128, 256)); din("mask1_512", (128, 512))
    din("mask2", (8, 128, 512)); din("onehot", (E, E * 128))
    out = nc.dram_tensor("out", [TOK, D], F32, kind="ExternalOutput")
    x1d = nc.dram_tensor("x1d", [TOK, D], F32, kind="Internal")
    wg_bf = nc.dram_tensor("wg_bf", [E, IT_MOE, 128, KC * 128], BF16, kind="Internal")
    wu_bf = nc.dram_tensor("wu_bf", [E, IT_MOE, 128, KC * 128], BF16, kind="Internal")
    wd_bf = nc.dram_tensor("wd_bf", [E, 2, 128, IT_MOE * 1024], BF16, kind="Internal")
    shg_bf = nc.dram_tensor("shg_bf", [IT_SH, 128, KC * 128], BF16, kind="Internal")
    shu_bf = nc.dram_tensor("shu_bf", [IT_SH, 128, KC * 128], BF16, kind="Internal")
    shd_bf = nc.dram_tensor("shd_bf", [4, 2, 128, IT_MOE * 1024], BF16, kind="Internal")
    ap = lambda n: dr[n].ap()

    P = Prog(nc)
    esG = contextlib.ExitStack()
    with esG:
        for s in ("pe", "dve", "act", "pool"):
            P.new_sem(s, esG)
        QROT = 8
        for q in ("qg", "qs", "qo", "qa"):
            for i in range(QROT):
                P.new_sem(f"{q}{i}", esG)
        qrot = {"qg": 0, "qs": 0, "qo": 0, "qa": 0}

        def qall(q):
            return [P.tok(f"{q}{i}") for i in range(QROT)]
        ps = [esG.enter_context(nc.psum_tensor(f"ps{i}", [128, 512], F32))
              for i in range(8)]
        rel = {}

        def sbuf(es, name, shp, dt=F32, side="right"):
            return es.enter_context(nc.sbuf_tensor(name, list(shp), dt, side=side))

        def dma(eng, q, dst, src, waits=()):
            s_name = f"{q}{qrot[q]}"
            qrot[q] = (qrot[q] + 1) % QROT
            return P.emit(eng, lambda h, d=dst, s=src: h.dma_start(d, s),
                          inc=s_name, inck=16, waits=waits)

        cast_jobs = []
        cast_tok = {}
        for e_ in range(E):
            cast_jobs.append((("g", e_), wg_bf.ap()[e_], ap("wg_t")[e_]))
            cast_jobs.append((("u", e_), wu_bf.ap()[e_], ap("wu_t")[e_]))
            cast_jobs.append((("d", e_), wd_bf.ap()[e_], ap("wd_t")[e_]))
        for g_ in range(4):
            cast_jobs.append((("shg", g_), shg_bf.ap()[g_ * IT_MOE:(g_ + 1) * IT_MOE],
                              ap("shg_t")[g_ * IT_MOE:(g_ + 1) * IT_MOE]))
            cast_jobs.append((("shu", g_), shu_bf.ap()[g_ * IT_MOE:(g_ + 1) * IT_MOE],
                              ap("shu_t")[g_ * IT_MOE:(g_ + 1) * IT_MOE]))
            cast_jobs.append((("shd", g_), shd_bf.ap()[g_], ap("shd_t")[g_]))

        def emit_casts(n):
            for _ in range(n):
                if not cast_jobs:
                    return
                key, dst, srcap = cast_jobs.pop(0)
                cast_tok[key] = dma("pool", "qg", dst, srcap)

        L = lambda name, shp, dt=F32: sbuf(esG, name, shp, dt, side="left")
        id_bf = L("id_bf", (128, 128), BF16)
        id_f32 = L("id_f32", (128, 128), F32)
        ones_bf = L("ones_bf", (1, 128), BF16)
        router_sb = L("router_sb", (128, KC * E), BF16)
        shgw_sb = L("shgw_sb", (128, KC), BF16)
        xn = L("xn", (128, D), BF16)
        xnT = L("xnT", (128, KC * 128), BF16)
        stats = L("stats", (128, 24), F32)
        mv = L("mv", (128, 2), F32)
        sd_t = L("sd_t", (128, 1), F32)
        rstd = L("rstd", (128, 1), F32)
        sq1 = L("sq1", (128, 1), F32)
        sq2 = L("sq2", (128, 1), F32)
        sqs = L("sqs", (128, 1), F32)

        esR1 = contextlib.ExitStack()
        o_all = sbuf(esR1, "o_all", (128, NH * TOK), BF16)
        esR2 = contextlib.ExitStack()
        qaT = sbuf(esR2, "qaT", (128, 6 * TOK), BF16)
        ckvT = sbuf(esR2, "ckvT", (128, 2 * S), BF16)
        kpe_aug = sbuf(esR2, "kpe_aug", (65, S), BF16)
        kvbn_sb = sbuf(esR2, "kvbn_sb", (128, 2 * NH * NOPE), BF16)
        kvbv_sb = sbuf(esR2, "kvbv_sb", (128, 2 * NH * VD), BF16)

        esR3 = contextlib.ExitStack()
        qaw_sb = sbuf(esR3, "qaw_sb", (128, KC * QL), BF16)
        kva_sb = sbuf(esR3, "kva_sb", (128, KC * (KVL + ROPE)), BF16)
        lnq_b = sbuf(esR3, "lnq_b", (128, QL), BF16)
        lnkv_b = sbuf(esR3, "lnkv_b", (128, KVL), BF16)
        lnrow = sbuf(esR3, "lnrow", (1, QL), BF16)
        ck_sb = sbuf(esR3, "ck_sb", (ROPE, S), BF16)
        sk_sb = sbuf(esR3, "sk_sb", (ROPE, S), BF16)
        r64_sb = sbuf(esR3, "r64_sb", (64, 64), BF16)
        xtile = [sbuf(esR3, f"xtileA{i}", (128, D), F32) for i in range(2)]
        qa_n = sbuf(esR3, "qa_n", (128, QL), BF16)
        ckv_n = sbuf(esR3, "ckv_n", (128, KVL), BF16)
        kpe_tm = sbuf(esR3, "kpe_tm", (128, ROPE), BF16)
        sqscr = sbuf(esR3, "sqscr", (128, 512), BF16)
        tp1 = sbuf(esR3, "tp1", (64, 512), F32)
        tp2 = sbuf(esR3, "tp2", (64, 512), F32)

        dma("pool", "qg", id_bf[:], ap("id128"))
        dma("sp", "qs", id_f32[:], ap("id128"))
        dma("pool", "qg", ones_bf[:], ap("ones_row"))
        dma("pool", "qg", router_sb[:].rearrange("p (c n) -> p c n", c=KC), ap("router_t"))
        dma("pool", "qg", shgw_sb[:].rearrange("p (c n) -> p c n", c=KC), ap("shgw_t"))
        dma("pool", "qg", kvbn_sb[:].rearrange("p (c n) -> p c n", c=2),
            ap("kvb_nope").rearrange("(c p) n -> p c n", p=128))
        dma("pool", "qg", kvbv_sb[:].rearrange("p (c n) -> p c n", c=2),
            ap("kvb_v").rearrange("(c p) n -> p c n", p=128))
        dma("pool", "qg", ck_sb[:], ap("ck"))
        dma("pool", "qg", sk_sb[:], ap("sk"))
        dma("pool", "qg", r64_sb[:], ap("r64T"))
        t_wconst = qall("qg")
        t_qaw = dma("pool", "qg", qaw_sb[:].rearrange("p (c n) -> p c n", c=KC), ap("qaw_t"))
        t_kva = dma("pool", "qg", kva_sb[:].rearrange("p (c n) -> p c n", c=KC), ap("kvaw_t"))
        t_l1 = dma("pool", "qg", lnrow[0:1, 0:QL], ap("lnq"))

        def bcast_row(width, dst, t_row):
            t_cp = None
            for j in range((width + 511) // 512):
                w = min(512, width - j * 512)
                t_mm = P.emit("pe", lambda h, j=j, w=w: h.matmul(
                    ps[5][:, 0:w], ones_bf[0:1, :], lnrow[0:1, j * 512:j * 512 + w],
                    start=True, stop=True), inc="pe", waits=[t_row, rel.get(5)])
                t_cp = P.emit("act", lambda h, j=j, w=w, dst=dst: h.copy(
                    dst[:, j * 512:j * 512 + w], ps[5][:, 0:w]), inc="act",
                    waits=[t_mm])
                rel[5] = t_cp
            return t_cp

        t_lnqb = bcast_row(QL, lnq_b, t_l1)
        t_l2 = dma("pool", "qg", lnrow[0:1, 0:KVL], ap("lnkv"), waits=[rel[5]])
        t_lnkb = bcast_row(KVL, lnkv_b, t_l2)

        def emit_ln(xbuf, t_x, extra=()):
            for sl in range(4):
                P.emit("dve", lambda h, sl=sl, xb=xbuf: h.bn_stats(
                    stats[:, sl * 6:(sl + 1) * 6], xb[:, sl * 512:(sl + 1) * 512]),
                    inc="dve", waits=[t_x] if sl == 0 else ())
            P.emit("dve", lambda h: h.bn_aggr(mv[:], stats[:]), inc="dve",
                   waits=[P.tok("dve")])
            P.emit("dve", lambda h: h.tensor_scalar(sq1[:], mv[:, 1:2], EPS_LN,
                                                    None, ALU.add), inc="dve",
                   waits=[P.tok("dve")])
            P.emit("act", lambda h: h.activation(sd_t[:], sq1[:], AF.Sqrt), inc="act",
                   waits=[P.tok("dve")])
            P.emit("dve", lambda h: h.reciprocal(rstd[:], sd_t[:]), inc="dve",
                   waits=[P.tok("act")])
            P.emit("dve", lambda h, xb=xbuf: h.tensor_scalar(
                xn[:], xb[:], mv[:, 0:1], rstd[:], ALU.subtract, ALU.mult),
                inc="dve", waits=[P.tok("dve")] + list(extra))
            return P.tok("dve")

        def emit_transposes(src, n, widths, dst_fn, t_src):
            t_last = None
            for c in range(n):
                w = widths[c] if widths else 128
                bank = 6 + (c % 2)
                t_t = P.emit("pe", lambda h, c=c, w=w, bank=bank, src=src: h.transpose(
                    ps[bank].bitcast(BF16)[0:w, 0:128], src[:, c * 128:c * 128 + w],
                    id_bf[:]), inc="pe", waits=[t_src, rel.get(bank)])
                t_last = P.emit("act", lambda h, d=dst_fn(c), w=w, bank=bank: h.copy(
                    d, ps[bank].bitcast(BF16)[0:w, 0:128]), inc="act", waits=[t_t])
                rel[bank] = t_last
            return t_last

        xrel = [None, None]
        for tt in range(NT):
            xb = xtile[tt % 2]
            t_x = dma("sp", "qs", xb[:], ap("xq")[tt * 128:(tt + 1) * 128, :],
                      waits=[xrel[tt % 2]])
            t_xn = emit_ln(xb, t_x, extra=[P.tok("pe")])
            xrel[tt % 2] = t_xn
            t_xT = emit_transposes(xn, KC, None,
                                   lambda c: xnT[:, c * 128:(c + 1) * 128], t_xn)
            for c in range(KC):
                last = c == KC - 1
                P.emit("pe", lambda h, c=c: h.matmul(
                    ps[0][:, :], xnT[:, c * 128:(c + 1) * 128],
                    qaw_sb[:, c * QL:c * QL + 512], start=(c == 0), stop=last),
                    waits=[t_xT, t_qaw, rel.get(0), rel.get(1)] if c == 0 else ())
                P.emit("pe", lambda h, c=c: h.matmul(
                    ps[1][:, 0:170], xnT[:, c * 128:(c + 1) * 128],
                    qaw_sb[:, c * QL + 512:(c + 1) * QL], start=(c == 0), stop=last),
                    inc="pe" if last else None)
            t_qa = P.tok("pe")
            P.emit("act", lambda h: h.activation(sqscr[:, 0:512], ps[0][:, :],
                                                 AF.Square, accum_out=sq1[:]),
                   inc="act", waits=[t_qa, P.tok("dve")])
            P.emit("act", lambda h: h.activation(sqscr[:, 0:170], ps[1][:, 0:170],
                                                 AF.Square, accum_out=sq2[:]),
                   inc="act")
            P.emit("dve", lambda h: h.tensor_add(sqs[:], sq1[:], sq2[:]), inc="dve",
                   waits=[P.tok("act")])
            P.emit("dve", lambda h: h.tensor_scalar(sqs[:], sqs[:], 1.0 / QL, EPS_RMS,
                                                    ALU.mult, ALU.add), inc="dve",
                   waits=[P.tok("dve")])
            P.emit("act", lambda h: h.activation(sd_t[:], sqs[:], AF.Sqrt), inc="act",
                   waits=[P.tok("dve")])
            P.emit("dve", lambda h: h.reciprocal(rstd[:], sd_t[:]), inc="dve",
                   waits=[P.tok("act")])
            P.emit("dve", lambda h: h.tensor_scalar(qa_n[:, 0:512], ps[0][:, :],
                                                    rstd[:], None, ALU.mult),
                   inc="dve", waits=[P.tok("dve")])
            t_ev = P.emit("dve", lambda h: h.tensor_scalar(
                qa_n[:, 512:QL], ps[1][:, 0:170], rstd[:], None, ALU.mult), inc="dve")
            rel[0] = t_ev
            rel[1] = t_ev
            t_qan = P.emit("dve", lambda h: h.tensor_mul(qa_n[:], qa_n[:], lnq_b[:]),
                           inc="dve", waits=[t_ev, t_lnqb])
            emit_transposes(qa_n, 6, QLC,
                            lambda c, tt=tt: qaT[0:QLC[c],
                                                 c * TOK + tt * 128:c * TOK + (tt + 1) * 128],
                            t_qan)
        P.realize(esG)

        for tk in range(NKT):
            xb = xtile[tk % 2]
            t_x = dma("sp", "qs", xb[:], ap("xk")[tk * 128:(tk + 1) * 128, :],
                      waits=[xrel[tk % 2]])
            t_xn = emit_ln(xb, t_x, extra=[P.tok("pe")])
            xrel[tk % 2] = t_xn
            t_xT = emit_transposes(xn, KC, None,
                                   lambda c: xnT[:, c * 128:(c + 1) * 128], t_xn)
            for c in range(KC):
                last = c == KC - 1
                P.emit("pe", lambda h, c=c: h.matmul(
                    ps[0][:, 0:KVL + ROPE], xnT[:, c * 128:(c + 1) * 128],
                    kva_sb[:, c * 320:(c + 1) * 320], start=(c == 0), stop=last),
                    inc="pe" if last else None,
                    waits=[t_xT, t_kva, rel.get(0)] if c == 0 else ())
            t_kv = P.tok("pe")
            P.emit("act", lambda h: h.activation(sqscr[:, 0:KVL], ps[0][:, 0:KVL],
                                                 AF.Square, accum_out=sqs[:]),
                   inc="act", waits=[t_kv, P.tok("dve")])
            P.emit("dve", lambda h: h.tensor_scalar(sqs[:], sqs[:], 1.0 / KVL,
                                                    EPS_RMS, ALU.mult, ALU.add),
                   inc="dve", waits=[P.tok("act")])
            P.emit("act", lambda h: h.activation(sd_t[:], sqs[:], AF.Sqrt), inc="act",
                   waits=[P.tok("dve")])
            P.emit("dve", lambda h: h.reciprocal(rstd[:], sd_t[:]), inc="dve",
                   waits=[P.tok("act")])
            t_craw = P.emit("dve", lambda h: h.tensor_scalar(
                ckv_n[:], ps[0][:, 0:KVL], rstd[:], None, ALU.mult),
                inc="dve", waits=[P.tok("dve")])
            t_cn = P.emit("dve", lambda h: h.tensor_mul(ckv_n[:], ckv_n[:], lnkv_b[:]),
                          inc="dve", waits=[P.tok("dve"), t_lnkb])
            t_kp = P.emit("act", lambda h: h.copy(kpe_tm[:], ps[0][:, KVL:KVL + ROPE]),
                          inc="act", waits=[t_kv, t_craw])
            rel[0] = t_kp
            emit_transposes(ckv_n, 2, None,
                            lambda c, tk=tk: ckvT[:, c * S + tk * 128:c * S + (tk + 1) * 128],
                            t_cn)
            bank = 6
            t_t = P.emit("pe", lambda h: h.transpose(
                ps[bank].bitcast(BF16)[0:ROPE, 0:128], kpe_tm[:], id_bf[:]),
                inc="pe", waits=[t_kp, rel.get(bank)])
            t_cpy = P.emit("act", lambda h, tk=tk: h.copy(
                kpe_aug[0:64, tk * 128:(tk + 1) * 128],
                ps[bank].bitcast(BF16)[0:ROPE, 0:128]), inc="act", waits=[t_t])
            rel[bank] = t_cpy
        t_kperaw = P.tok("act")

        P.emit("dve", lambda h: h.memset(kpe_aug[64:65, :], 1.0), inc="dve",
               waits=[t_kperaw])
        for j in range(8):
            slc = slice(j * 512, (j + 1) * 512)
            bank = j % 2
            t_m = P.emit("pe", lambda h, slc=slc, bank=bank: h.matmul(
                ps[bank][0:64, :], r64_sb[:], kpe_aug[0:64, slc],
                start=True, stop=True), inc="pe",
                waits=[t_kperaw, rel.get(bank), P.tok("dve")] + t_wconst)
            P.emit("dve", lambda h, slc=slc, bank=bank: h.tensor_mul(
                tp1[:], ps[bank][0:64, :], sk_sb[:, slc]), inc="dve", waits=[t_m])
            P.emit("dve", lambda h, slc=slc: h.tensor_mul(
                tp2[:], kpe_aug[0:64, slc], ck_sb[:, slc]), inc="dve")
            t_add = P.emit("dve", lambda h, slc=slc: h.tensor_add(
                kpe_aug[0:64, slc], tp1[:], tp2[:]), inc="dve", waits=[P.tok("dve")])
            rel[bank] = t_add
        t_kpe = P.tok("dve")
        emit_casts(12)
        P.realize(esG)
        esR3.close()

        if upto == "A":
            t1 = dma("pool", "qo", out.ap()[0:128, :], qaT[:, 0:TOK])
            t2 = dma("pool", "qo", out.ap()[128:256, :], ckvT[:, 0:2048])
            t3 = dma("pool", "qo", out.ap()[256:321, :], kpe_aug[0:65, 0:2048])
            P.emit("sp", lambda h: h.nop(), waits=qall("qo"))
            P.realize(esG)
            return nc

        esR4 = contextlib.ExitStack()
        qbw_h = [sbuf(esR4, f"qbw_h{i}", (128, 6 * 128), BF16) for i in range(2)]
        m1_256 = sbuf(esR4, "m1_256", (128, 256), BF16)
        m1_512 = sbuf(esR4, "m1_512", (128, 512), BF16)
        m2_sb = sbuf(esR4, "m2_sb", (128, 8 * 512), BF16)
        cq_sb = sbuf(esR4, "cq_sb", (ROPE, TOK), BF16)
        sq_sb = sbuf(esR4, "sq_sb", (ROPE, TOK), BF16)
        mswap_sb = sbuf(esR4, "mswap_sb", (128, 64), BF16)
        mshift_sb = sbuf(esR4, "mshift_sb", (128, 64), BF16)
        qraw = sbuf(esR4, "qraw", (128, TOK), BF16)
        qpe_aug = sbuf(esR4, "qpe_aug", (65, TOK), BF16)
        knope_h = [sbuf(esR4, f"knope_h{i}", (64, S), BF16) for i in range(2)]
        v2 = sbuf(esR4, "v2", (128, NKT * 2 * VD), BF16)
        tb1 = sbuf(esR4, "tb1", (64, 512), F32)
        tb2 = sbuf(esR4, "tb2", (64, 512), F32)
        mstats = sbuf(esR4, "mstats", (128, NT * 8), F32)
        zstats = sbuf(esR4, "zstats", (128, NT * 8), F32)
        mins = sbuf(esR4, "mins", (128, NT), F32)
        minbf = sbuf(esR4, "minbf", (128, NT), BF16)
        minr = sbuf(esR4, "minr", (128, NT), F32)
        t8a = sbuf(esR4, "t8a", (128, NT * 8), F32)
        zw8a = sbuf(esR4, "zw8a", (128, NT * 8), F32)
        zca = sbuf(esR4, "zca", (128, NT), F32)
        zia = sbuf(esR4, "zia", (128, NT), F32)
        zrow = xnT[0:1, 0:TOK]
        attnT = [sbuf(esR4, f"attnT{i}", (128, 512), BF16) for i in range(4)]
        escr = attnT[0]
        zbc = xn[:, 0:512]

        dma("pool", "qg", m1_256[:], ap("mask1_256"))
        dma("pool", "qg", m1_512[:], ap("mask1_512"))
        dma("pool", "qg", m2_sb[:].rearrange("p (c n) -> p c n", c=8),
            ap("mask2").rearrange("c p n -> p c n"))
        dma("pool", "qg", cq_sb[:], ap("cq"))
        dma("pool", "qg", sq_sb[:], ap("sq"))
        dma("pool", "qg", mswap_sb[:], ap("mswapT"))
        dma("pool", "qg", mshift_sb[:], ap("mshiftT"))
        t_bconst = qall("qg")

        t_qbw = []
        for h_ in range(2):
            t_qbw.append(dma("pool", "qg",
                             qbw_h[h_][:].rearrange("p (c n) -> p c n", c=6),
                             ap("qbw_h")[h_]))

        def p1_tiles(i):
            ext = (2 * i + 2) * 128
            full, remn = divmod(ext, 512)
            tl = [(j * 512, 512) for j in range(full)]
            if remn:
                tl.append((full * 512, remn))
            return tl

        att_rel = [None, None, None, None]
        for h_ in range(nheads):
            hh = h_ % 2
            buf = qbw_h[h_ % 2]
            emit_casts(4)
            P.emit("dve", lambda h: h.memset(qpe_aug[64:65, :], 0.0), inc="dve",
                   waits=[P.tok("pe")])
            t_qpz = P.tok("dve")
            t_qlast = None
            for nb in range(4):
                qsl = slice(nb * 512, (nb + 1) * 512)
                pq = ps[nb]
                for c in range(6):
                    last = c == 5
                    P.emit("pe", lambda h, c=c, nb=nb, pq=pq, buf=buf: h.matmul(
                        pq[:, :], buf[0:QLC[c], c * 128:(c + 1) * 128],
                        qaT[0:QLC[c], c * TOK + nb * 512:c * TOK + (nb + 1) * 512],
                        start=(c == 0), stop=last),
                        inc="pe" if last else None,
                        waits=([t_qbw[h_], rel.get(nb)] + t_bconst)
                        if c == 0 else ())
                t_q = P.tok("pe")
                t_qlast = t_q
                t_qr = P.emit("act", lambda h, qsl=qsl, pq=pq: h.mul(
                    qraw[:, qsl], pq[:, :], SCALE), inc="act",
                    waits=[t_q, P.tok("dve")])
                rel[nb] = t_qr
                P.emit("pe", lambda h, qsl=qsl: h.matmul(
                    ps[4][0:64, :], mswap_sb[:], qraw[:, qsl], start=True, stop=True),
                    inc="pe", waits=[t_qr, rel.get(4)])
                P.emit("pe", lambda h, qsl=qsl: h.matmul(
                    ps[5][0:64, :], mshift_sb[:], qraw[:, qsl], start=True, stop=True),
                    inc="pe", waits=[rel.get(5)])
                t_rot = P.tok("pe")
                t_d1 = P.emit("dve", lambda h, qsl=qsl: h.tensor_mul(
                    tb1[:], ps[4][0:64, :], sq_sb[:, qsl]), inc="dve",
                    waits=[t_rot] + t_bconst)
                rel[4] = t_d1
                t_d2 = P.emit("dve", lambda h, qsl=qsl: h.tensor_mul(
                    tb2[:], ps[5][0:64, :], cq_sb[:, qsl]), inc="dve")
                rel[5] = t_d2
                P.emit("dve", lambda h, qsl=qsl: h.tensor_add(
                    qpe_aug[0:64, qsl], tb1[:], tb2[:]), inc="dve",
                    waits=[t_d2, t_qpz])
            t_qf = P.tok("dve")
            if upto == "QF":
                dma("pool", "qo", out.ap()[256:384, :], qraw[:, :], waits=[t_qf])
                dma("pool", "qo", out.ap()[384:449, :], qpe_aug[0:65, :], waits=[t_qf])
                P.emit("sp", lambda h: h.nop(), waits=qall("qo"))
                P.realize(esG)
                return nc
            if h_ + 2 < nheads:
                t_qbw.append(dma("pool", "qg",
                                 qbw_h[h_ % 2][:].rearrange("p (c n) -> p c n", c=6),
                                 ap("qbw_h")[h_ + 2], waits=[t_qlast]))

            if hh == 0:
                for j in range(8):
                    ksl = slice(j * 512, (j + 1) * 512)
                    for hp in range(2):
                        bank = 4 + ((2 * j + hp) % 2)
                        pk = ps[bank]
                        for c in range(2):
                            last = c == 1
                            P.emit("pe", lambda h, c=c, j=j, pk=pk, h_=h_, hp=hp:
                                   h.matmul(
                                pk[0:64, :],
                                kvbn_sb[:, c * NH * NOPE + (h_ + hp) * NOPE:
                                        c * NH * NOPE + (h_ + hp + 1) * NOPE],
                                ckvT[:, c * S + j * 512:c * S + (j + 1) * 512],
                                start=(c == 0), stop=last),
                                inc="pe" if last else None,
                                waits=([rel.get(bank)] + t_bconst) if c == 0 else ())
                        t_k = P.tok("pe")
                        t_cp = P.emit("act", lambda h, ksl=ksl, pk=pk, hp=hp: h.copy(
                            knope_h[hp][:, ksl], pk[0:64, :]), inc="act", waits=[t_k])
                        rel[bank] = t_cp
            if hh == 0:
                for tk in range(NKT):
                    bank = 4 + (tk % 2)
                    pk = ps[bank]
                    for c in range(2):
                        last = c == 1
                        P.emit("pe", lambda h, c=c, tk=tk, pk=pk, h_=h_: h.matmul(
                            pk[:, 0:2 * VD],
                            ckvT[:, c * S + tk * 128:c * S + (tk + 1) * 128],
                            kvbv_sb[:, c * NH * VD + h_ * VD:
                                    c * NH * VD + (h_ + 2) * VD],
                            start=(c == 0), stop=last),
                            inc="pe" if last else None,
                            waits=([rel.get(bank)] + t_bconst) if c == 0 else ())
                    t_v = P.tok("pe")
                    t_cp = P.emit("act", lambda h, tk=tk, pk=pk: h.copy(
                        v2[:, tk * 2 * VD:(tk + 1) * 2 * VD], pk[:, 0:2 * VD]),
                        inc="act", waits=[t_v])
                    rel[bank] = t_cp
            t_kvb = P.tok("act")

            for qt in range(NT):
                tiles = p1_tiles(qt)
                nE = len(tiles)
                c0 = qt * 8
                for ti, (k0, w) in enumerate(tiles):
                    bank = ti % 4
                    pq = ps[bank]
                    P.emit("pe", lambda h, k0=k0, w=w, pq=pq, qt=qt, hh=hh: h.matmul(
                        pq[:, 0:w], qraw[0:64, qt * 128:(qt + 1) * 128],
                        knope_h[hh][:, k0:k0 + w],
                        start=True, stop=False),
                        waits=[t_qf, t_kvb, t_kpe, rel.get(bank)])
                    P.emit("pe", lambda h, k0=k0, w=w, pq=pq, qt=qt: h.matmul(
                        pq[:, 0:w], qpe_aug[0:65, qt * 128:(qt + 1) * 128],
                        kpe_aug[0:65, k0:k0 + w], start=False, stop=True), inc="pe")
                    t_s = P.tok("pe")
                    if ti == nE - 1:
                        msk = m1_256 if w == 256 else m1_512
                        t_s = P.emit("dve", lambda h, w=w, pq=pq, msk=msk:
                                     h.tensor_add(pq[:, 0:w], pq[:, 0:w], msk[:, 0:w]),
                                     inc="dve", waits=[t_s] + t_bconst)
                    t_mk = P.emit("dve", lambda h, w=w, pq=pq, ci=c0 + ti:
                                  h.reduce_max(mstats[:, ci:ci + 1], pq[:, 0:w],
                                               axis=AX.X, negate=True), inc="dve",
                                  waits=[t_s])
                    t_e = P.emit("act", lambda h, w=w, pq=pq, ci=c0 + ti: h.activation(
                        escr[:, 0:w], pq[:, 0:w], AF.Exp,
                        bias=mstats[:, ci:ci + 1],
                        accum_out=zstats[:, ci:ci + 1]), inc="act",
                        waits=[t_mk, att_rel[0]])
                    rel[bank] = t_e
                t_z = P.tok("act")
                P.emit("dve", lambda h, nE=nE, c0=c0, qt=qt: h.tensor_reduce(
                    mins[:, qt:qt + 1], mstats[:, c0:c0 + nE], AX.X, ALU.min),
                    inc="dve", waits=[t_z])
                P.emit("dve", lambda h, qt=qt: h.tensor_copy(
                    minbf[:, qt:qt + 1], mins[:, qt:qt + 1]), inc="dve",
                    waits=[P.tok("dve")])
                t_mr = P.emit("dve", lambda h, qt=qt: h.tensor_copy(
                    minr[:, qt:qt + 1], minbf[:, qt:qt + 1]), inc="dve",
                    waits=[P.tok("dve")])
                t_t8 = P.emit("act", lambda h, nE=nE, c0=c0, qt=qt: h.activation(
                    t8a[:, c0:c0 + nE], mstats[:, c0:c0 + nE], AF.Exp,
                    bias=minr[:, qt:qt + 1], scale=-1.0), inc="act", waits=[t_mr])
                P.emit("dve", lambda h, nE=nE, c0=c0: h.tensor_mul(
                    zw8a[:, c0:c0 + nE], t8a[:, c0:c0 + nE], zstats[:, c0:c0 + nE]),
                    inc="dve", waits=[t_t8])
                P.emit("dve", lambda h, nE=nE, c0=c0, qt=qt: h.reduce_sum(
                    zca[:, qt:qt + 1], zw8a[:, c0:c0 + nE], axis=AX.X), inc="dve",
                    waits=[P.tok("dve")])
                t_zi = P.emit("dve", lambda h, qt=qt: h.reciprocal(
                    zia[:, qt:qt + 1], zca[:, qt:qt + 1]), inc="dve",
                    waits=[P.tok("dve")])
                bank = 6 + (qt % 2)
                t_tr = P.emit("pe", lambda h, qt=qt, bank=bank: h.transpose(
                    ps[bank][0:1, 0:128], minr[:, qt:qt + 1], id_f32[:]), inc="pe",
                    waits=[t_mr, rel.get(bank)])
                t_mrow = P.emit("act", lambda h, qt=qt, bank=bank: h.copy(
                    qpe_aug[64:65, qt * 128:(qt + 1) * 128], ps[bank][0:1, 0:128]),
                    inc="act", waits=[t_tr])
                t_tr2 = P.emit("pe", lambda h, qt=qt, bank=bank: h.transpose(
                    ps[bank][0:1, 128:256], zia[:, qt:qt + 1], id_f32[:]), inc="pe",
                    waits=[t_zi, t_mrow])
                t_zrow = P.emit("act", lambda h, qt=qt, bank=bank: h.copy(
                    zrow[0:1, qt * 128:(qt + 1) * 128], ps[bank][0:1, 128:256]),
                    inc="act", waits=[t_tr2])
                rel[bank] = t_zrow
            t_p1 = P.tok("act")

            for qb in range(4):
                nkt = 8 * qb + 8
                pend = []
                t_av = None

                def emit_av(kt, t_e, nkt=None):
                    return P.emit("pe", lambda h, kt=kt, nkt=nkt, hh=hh: h.matmul(
                        ps[4][:, :],
                        v2[:, kt * 2 * VD + hh * VD:kt * 2 * VD + (hh + 1) * VD],
                        attnT[kt % 4][:], start=(kt == 0), stop=(kt == nkt - 1)),
                        inc="pe", waits=[t_e] + ([rel.get(4)] if kt == 0 else []))

                for kt in range(nkt):
                    bank = kt % 4
                    pq = ps[bank]
                    P.emit("pe", lambda h, kt=kt, pq=pq, qb=qb, hh=hh: h.matmul(
                        pq[:, :], knope_h[hh][:, kt * 128:(kt + 1) * 128],
                        qraw[0:64, qb * 512:(qb + 1) * 512], start=True, stop=False),
                        waits=[t_p1, rel.get(bank)])
                    P.emit("pe", lambda h, kt=kt, pq=pq, qb=qb: h.matmul(
                        pq[:, :], kpe_aug[0:65, kt * 128:(kt + 1) * 128],
                        qpe_aug[0:65, qb * 512:(qb + 1) * 512],
                        start=False, stop=True), inc="pe")
                    t_s = P.tok("pe")
                    dd = kt - 8 * qb
                    if dd >= 0:
                        t_s = P.emit("dve", lambda h, pq=pq, dd=dd: h.tensor_add(
                            pq[:, :], pq[:, :], m2_sb[:, dd * 512:(dd + 1) * 512]),
                            inc="dve", waits=[t_s] + t_bconst)
                    t_e = P.emit("act", lambda h, pq=pq, kt=kt: h.activation(
                        attnT[kt % 4][:], pq[:, :], AF.Exp), inc="act",
                        waits=[t_s, att_rel[kt % 4]])
                    rel[bank] = t_e
                    pend.append((kt, t_e))
                    if len(pend) > 2:
                        kt0, te0 = pend.pop(0)
                        att_rel[kt0 % 4] = emit_av(kt0, te0, nkt)
                for kt0, te0 in pend:
                    att_rel[kt0 % 4] = emit_av(kt0, te0, nkt)
                t_av = P.tok("pe")
                t_zb = P.emit("pe", lambda h, qb=qb: h.matmul(
                    ps[5][:, :], ones_bf[0:1, :], zrow[0:1, qb * 512:(qb + 1) * 512],
                    start=True, stop=True), inc="pe", waits=[rel.get(5)])
                t_zc = P.emit("act", lambda h: h.copy(zbc[:], ps[5][:, :]), inc="act",
                              waits=[t_zb])
                rel[5] = t_zc
                t_o = P.emit("dve", lambda h, h_=h_, qb=qb: h.tensor_mul(
                    o_all[:, h_ * TOK + qb * 512:h_ * TOK + (qb + 1) * 512],
                    ps[4][:, :], zbc[:]), inc="dve", waits=[t_av, t_zc])
                rel[4] = t_o
        P.realize(esG)
        esR4.close()
        esR2.close()

        if upto == "B":
            for h_ in range(nheads):
                dma("pool", "qo", out.ap()[h_ * 128:(h_ + 1) * 128, :],
                    o_all[:, h_ * TOK:(h_ + 1) * TOK])
            if nheads == 1:
                dma("pool", "qo", out.ap()[256:384, :], qraw[:, :])
                dma("pool", "qo", out.ap()[384:449, :], qpe_aug[0:65, :])
                dma("pool", "qo", out.ap()[512:576, :], knope_h[0][:, 0:2048])
                dma("pool", "qo", out.ap()[640:768, :], v2[:, 0:2048])
                dma("pool", "qo", out.ap()[800:801, :], zrow[0:1, :])
                dma("pool", "qo", out.ap()[832:960, :], qaT[:, TOK:2 * TOK])
                dma("pool", "qo", out.ap()[960:1002, :], qaT[0:42, 5 * TOK:6 * TOK])
                dma("pool", "qo", out.ap()[1024:1152, 0:768], qbw_h[0][:, :])
            P.emit("sp", lambda h: h.nop(), waits=qall("qo"))
            P.realize(esG)
            return nc

        esR5 = contextlib.ExitStack()
        ow_sb = sbuf(esR5, "ow_sb", (128, KC * D), BF16)
        xtC = [sbuf(esR5, f"xtC{i}", (128, D), F32) for i in range(2)]
        x1t = [sbuf(esR5, f"x1t{i}", (128, D), F32) for i in range(2)]
        for c in range(KC):
            dma("pool", "qg",
                ow_sb[:].rearrange("p (c n) -> p c n", c=KC)[:, c:c + 1, :],
                ap("ow_t")[:, c:c + 1, :])
        t_ow = qall("qg")
        xrelC = [None, None]
        x1rel = [None, None]
        for tt in range(NT):
            xb = xtC[tt % 2]
            t_x = dma("sp", "qs", xb[:], ap("xq")[tt * 128:(tt + 1) * 128, :],
                      waits=[xrelC[tt % 2]])
            for h_ in range(NH):
                for hcg in range(4):
                    P.emit("pe", lambda h, h_=h_, hcg=hcg, tt=tt: h.matmul(
                        ps[hcg][:, :],
                        o_all[:, h_ * TOK + tt * 128:h_ * TOK + (tt + 1) * 128],
                        ow_sb[:, h_ * D + hcg * 512:h_ * D + (hcg + 1) * 512],
                        start=(h_ == 0), stop=(h_ == NH - 1)),
                        inc="pe" if (h_ == NH - 1 and hcg == 3) else None,
                        waits=(t_ow + [rel.get(i) for i in range(4)])
                        if (h_ == 0 and hcg == 0) else ())
            t_o = P.tok("pe")
            x1b = x1t[tt % 2]
            t_a = None
            for hcg in range(4):
                t_a = P.emit("dve", lambda h, hcg=hcg, x1b=x1b, xb=xb: h.tensor_add(
                    x1b[:, hcg * 512:(hcg + 1) * 512], ps[hcg][:, :],
                    xb[:, hcg * 512:(hcg + 1) * 512]), inc="dve",
                    waits=[t_o, t_x, x1rel[tt % 2]])
            for i in range(4):
                rel[i] = t_a
            xrelC[tt % 2] = t_a
            t_st = dma("sp", "qs", x1d.ap()[tt * 128:(tt + 1) * 128, :], x1b[:],
                       waits=[t_a])
            x1rel[tt % 2] = t_st
        t_x1d = qall("qs")
        P.realize(esG)
        esR5.close()
        esR1.close()

        if upto == "C":
            for tt in range(NT):
                dma("sp", "qo", out.ap()[tt * 128:(tt + 1) * 128, :],
                    x1d.ap()[tt * 128:(tt + 1) * 128, :], waits=t_x1d)
            P.emit("sp", lambda h: h.nop(), waits=qall("qo"))
            P.realize(esG)
            return nc

        esR6 = contextlib.ExitStack()
        x2 = sbuf(esR6, "x2", (128, 8 * D), F32)
        x1nT = sbuf(esR6, "x1nT", (128, KC * 1024), BF16)
        z_e = sbuf(esR6, "z_e", (128, IT_MOE * 1024), BF16)
        wgl = [sbuf(esR6, f"wgl{i}", (128, KC * 128), BF16) for i in range(2)]
        wul = [sbuf(esR6, f"wul{i}", (128, KC * 128), BF16) for i in range(2)]
        wdr = [sbuf(esR6, f"wdr{i}", (128, IT_MOE * 1024), BF16) for i in range(2)]
        onehot_sb = sbuf(esR6, "onehot_sb", (16, E * 128), BF16)
        combT = sbuf(esR6, "combT", (16, 1024), BF16)
        comb_b = sbuf(esR6, "comb_b", (128, 1024), BF16)
        lgmx = sbuf(esR6, "lgmx", (128, 1), F32)
        nlgmx = sbuf(esR6, "nlgmx", (128, 1), F32)
        pun = sbuf(esR6, "pun", (128, 16), F32)
        rank = sbuf(esR6, "rank", (128, 16), F32)
        tcmp = sbuf(esR6, "tcmp", (128, 16), F32)
        comb = sbuf(esR6, "comb", (128, 16), F32)
        den = sbuf(esR6, "den", (128, 1), F32)
        dinv = sbuf(esR6, "dinv", (128, 1), F32)
        sg = sbuf(esR6, "sg", (128, 8), F32)
        silu_s = sbuf(esR6, "silu_s", (128, 512), BF16)
        tmul = sbuf(esR6, "tmul", (128, 512), BF16)
        t_oneh = dma("pool", "qg", onehot_sb[:], ap("onehot"))

        x2rel = None
        for hf in range(2):
            toff = hf * 8
            for tt8 in range(8):
                tt = toff + tt8
                xsl = slice(tt8 * D, (tt8 + 1) * D)
                t_x = dma("sp", "qs", x2[:, xsl],
                          x1d.ap()[tt * 128:(tt + 1) * 128, :],
                          waits=t_x1d + ([] if x2rel is None else x2rel))
                t_xn = emit_ln(x2[:, xsl], t_x, extra=[P.tok("pe")])
                emit_transposes(
                    xn, KC, None,
                    lambda c, tt8=tt8: x1nT[:, c * 1024 + tt8 * 128:
                                            c * 1024 + (tt8 + 1) * 128], t_xn)
            t_nT = P.tok("act")

            for tt8 in range(8):
                for c in range(KC):
                    P.emit("pe", lambda h, c=c, tt8=tt8: h.matmul(
                        ps[0][:, 0:E],
                        x1nT[:, c * 1024 + tt8 * 128:c * 1024 + (tt8 + 1) * 128],
                        router_sb[:, c * E:(c + 1) * E],
                        start=(c == 0), stop=(c == KC - 1)),
                        inc="pe" if c == KC - 1 else None,
                        waits=[t_nT, rel.get(0)] if c == 0 else ())
                t_l = P.tok("pe")
                P.emit("dve", lambda h: h.reduce_max(lgmx[:], ps[0][:, 0:E],
                                                     axis=AX.X), inc="dve",
                       waits=[t_l])
                t_nm = P.emit("dve", lambda h: h.tensor_scalar(
                    nlgmx[:], lgmx[:], -1.0, None, ALU.mult), inc="dve",
                    waits=[P.tok("dve"), P.tok("act")])
                t_p = P.emit("act", lambda h: h.activation(
                    pun[:], ps[0][:, 0:E], AF.Exp, bias=nlgmx[:]), inc="act",
                    waits=[t_nm])
                rel[0] = t_p
                P.emit("dve", lambda h: h.memset(rank[:], 0.0), inc="dve",
                       waits=[t_p])
                for e2 in range(E):
                    P.emit("dve", lambda h, e2=e2: h.tensor_scalar(
                        tcmp[:], pun[:], pun[:, e2:e2 + 1], None, ALU.is_lt),
                        inc="dve", waits=[P.tok("dve")])
                    P.emit("dve", lambda h: h.tensor_add(rank[:], rank[:], tcmp[:]),
                           inc="dve", waits=[P.tok("dve")])
                P.emit("dve", lambda h: h.tensor_scalar(tcmp[:], rank[:], 3.0, None,
                                                        ALU.is_le), inc="dve",
                       waits=[P.tok("dve")])
                P.emit("dve", lambda h: h.tensor_mul(comb[:], pun[:], tcmp[:]),
                       inc="dve", waits=[P.tok("dve")])
                P.emit("dve", lambda h: h.reduce_sum(den[:], comb[:], axis=AX.X),
                       inc="dve", waits=[P.tok("dve")])
                P.emit("dve", lambda h: h.reciprocal(dinv[:], den[:]), inc="dve",
                       waits=[P.tok("dve")])
                t_cb = P.emit("dve", lambda h: h.tensor_scalar(
                    comb[:], comb[:], dinv[:], None, ALU.mult), inc="dve",
                    waits=[P.tok("dve")])
                t_tr = P.emit("pe", lambda h: h.transpose(
                    ps[6].bitcast(F32)[0:16, 0:128] if False else ps[6][0:16, 0:128],
                    comb[:, 0:16], id_f32[:]), inc="pe", waits=[t_cb, rel.get(6)])
                t_ct = P.emit("act", lambda h, tt8=tt8: h.copy(
                    combT[0:16, tt8 * 128:(tt8 + 1) * 128], ps[6][0:16, 0:128]),
                    inc="act", waits=[t_tr])
                rel[6] = t_ct
            for tt8 in range(8):
                for c in range(KC):
                    P.emit("pe", lambda h, c=c, tt8=tt8: h.matmul(
                        ps[1][:, 0:1],
                        x1nT[:, c * 1024 + tt8 * 128:c * 1024 + (tt8 + 1) * 128],
                        shgw_sb[:, c:c + 1], start=(c == 0), stop=(c == KC - 1)),
                        inc="pe" if c == KC - 1 else None,
                        waits=[rel.get(1)] if c == 0 else ())
                t_sgm = P.tok("pe")
                t_sg = P.emit("act", lambda h, tt8=tt8: h.activation(
                    sg[:, tt8:tt8 + 1], ps[1][:, 0:1], AF.Sigmoid), inc="act",
                    waits=[t_sgm])
                rel[1] = t_sg
            t_route = P.tok("act")
            t_combT = rel[6]

            wg_rel = [None, None]
            wd_rel = [None, None]
            t_wg = {}
            t_wd = {}

            def load_gu(kind, idx, buf, wtok):
                if kind == "moe":
                    e_, it = idx
                    ta = dma("sp", "qs", wgl[buf][:], wg_bf.ap()[e_, it],
                             waits=[wtok, cast_tok[("g", e_)]])
                    tb = dma("sp", "qs", wul[buf][:], wu_bf.ap()[e_, it],
                             waits=[cast_tok[("u", e_)]])
                else:
                    it = idx
                    ta = dma("sp", "qs", wgl[buf][:], shg_bf.ap()[it],
                             waits=[wtok, cast_tok[("shg", it // IT_MOE)]])
                    tb = dma("sp", "qs", wul[buf][:], shu_bf.ap()[it],
                             waits=[cast_tok[("shu", it // IT_MOE)]])
                return [ta, tb]

            def load_wd(kind, idx, buf, wtok):
                if kind == "moe":
                    e_, hh2 = idx
                    ta = dma("act", "qa", wdr[buf][:], wd_bf.ap()[e_, hh2],
                             waits=[wtok, cast_tok[("d", e_)]])
                else:
                    grp, hh2 = idx
                    ta = dma("act", "qa", wdr[buf][:], shd_bf.ap()[grp, hh2],
                             waits=[wtok, cast_tok[("shd", grp)]])
                return [ta]

            groups = [("moe", e_) for e_ in range(E)] + [("sh", g) for g in range(4)]
            gu_list = [(k, (g, it) if k == "moe" else g * IT_MOE + it)
                       for (k, g) in groups for it in range(IT_MOE)]
            for gi in range(2):
                k, idx = gu_list[gi]
                t_wg[gi] = load_gu(k, idx, gi % 2, None)
            wd_list = [(k, (g, hh2) if k == "moe" else (g, hh2))
                       for (k, g) in groups for hh2 in range(2)]
            for di in range(2):
                k, idx = wd_list[di]
                t_wd[di] = load_wd(wd_list[di][0], idx, di % 2, None)

            gi_count = [0]
            di_count = [0]

            for (kind, gidx) in groups:
                is_moe = kind == "moe"
                if is_moe:
                    e_ = gidx
                    for nb in range(2):
                        t_b = P.emit("pe", lambda h, e_=e_, nb=nb: h.matmul(
                            ps[4][:, :], onehot_sb[0:16, e_ * 128:(e_ + 1) * 128],
                            combT[0:16, nb * 512:(nb + 1) * 512],
                            start=True, stop=True), inc="pe",
                            waits=[t_combT, t_oneh, rel.get(4)])
                        t_cpb = P.emit("act", lambda h, nb=nb: h.copy(
                            comb_b[:, nb * 512:(nb + 1) * 512], ps[4][:, :]),
                            inc="act", waits=[t_b, P.tok("dve")])
                        rel[4] = t_cpb
                    t_cbb = P.tok("act")
                for it in range(IT_MOE):
                    gi = gi_count[0]
                    buf = gi % 2
                    bs = [0, 1, 2, 3] if (gi % 2 == 0) else [4, 5, 6, 7]
                    t_w = t_wg[gi]
                    t_gu = None
                    for c in range(KC):
                        for bi, (pp, wsb, xo) in enumerate((
                                (ps[bs[0]], wgl[buf], 0), (ps[bs[1]], wgl[buf], 512),
                                (ps[bs[2]], wul[buf], 0), (ps[bs[3]], wul[buf], 512))):
                            P.emit("pe", lambda h, c=c, pp=pp, wsb=wsb, xo=xo:
                                   h.matmul(pp[:, :], wsb[:, c * 128:(c + 1) * 128],
                                            x1nT[:, c * 1024 + xo:c * 1024 + xo + 512],
                                            start=(c == 0), stop=(c == KC - 1)),
                                   inc="pe" if (c == KC - 1 and bi == 3) else None,
                                   waits=(t_w + [t_nT] + [rel.get(i) for i in bs])
                                   if (c == 0 and bi == 0) else ())
                    t_gu = P.tok("pe")
                    if gi + 2 < len(gu_list):
                        k2, idx2 = gu_list[gi + 2]
                        t_wg[gi + 2] = load_gu(k2, idx2, buf, t_gu)
                    for nb in range(2):
                        zsl = slice(it * 1024 + nb * 512, it * 1024 + (nb + 1) * 512)
                        pg = (ps[bs[0]], ps[bs[1]])[nb]
                        pu = (ps[bs[2]], ps[bs[3]])[nb]
                        t_si = P.emit("act", lambda h, pg=pg: h.activation(
                            silu_s[:], pg[:, :], AF.Silu), inc="act",
                            waits=[t_gu, P.tok("dve")])
                        rel[bs[nb]] = t_si
                        if is_moe:
                            P.emit("dve", lambda h, pu=pu, nb=nb: h.tensor_mul(
                                tmul[:], pu[:, :], comb_b[:, nb * 512:(nb + 1) * 512]),
                                inc="dve", waits=[t_gu, t_cbb, P.tok("act")])
                        else:
                            P.emit("dve", lambda h, pu=pu: h.tensor_copy(
                                tmul[:], pu[:, :]), inc="dve",
                                waits=[t_gu, P.tok("act")])
                        t_tm = P.tok("dve")
                        rel[bs[2 + nb]] = t_tm
                        t_zw = P.emit("dve", lambda h, zsl=zsl: h.tensor_mul(
                            z_e[:, zsl], tmul[:], silu_s[:]), inc="dve",
                            waits=[t_tm, t_si])
                    gi_count[0] += 1
                t_z = P.tok("dve")
                for hh2 in range(2):
                    di = di_count[0]
                    dbuf = di % 2
                    t_wdt = t_wd[di]
                    t_dlast = None
                    for pair in range(4):
                        dbs = [0, 1, 2, 3] if (pair % 2 == (di % 2)) else [4, 5, 6, 7]
                        for it in range(IT_MOE):
                            for sub in range(2):
                                tt8 = pair * 2 + sub
                                for hq in range(2):
                                    bank = dbs[sub * 2 + hq]
                                    P.emit("pe", lambda h, it=it, tt8=tt8, hq=hq,
                                           hh2=hh2, dbuf=dbuf, bank=bank: h.matmul(
                                        ps[bank][:, :],
                                        z_e[:, it * 1024 + tt8 * 128:
                                            it * 1024 + (tt8 + 1) * 128],
                                        wdr[dbuf][:, it * 1024 + hq * 512:
                                                  it * 1024 + hq * 512 + 512],
                                        start=(it == 0), stop=(it == IT_MOE - 1)),
                                        inc="pe" if (it == IT_MOE - 1 and sub == 1
                                                     and hq == 1) else None,
                                        waits=([t_z] + t_wdt +
                                        [rel.get(i) for i in dbs])
                                        if (it == 0 and sub == 0 and hq == 0) else ())
                        t_d = P.tok("pe")
                        t_dlast = t_d
                        for sub in range(2):
                            tt8 = pair * 2 + sub
                            for hq in range(2):
                                bank = dbs[sub * 2 + hq]
                                col = tt8 * D + hh2 * 1024 + hq * 512
                                if is_moe:
                                    t_ad = P.emit("dve", lambda h, col=col, bank=bank:
                                                  h.tensor_add(
                                        x2[:, col:col + 512], x2[:, col:col + 512],
                                        ps[bank][:, :]), inc="dve", waits=[t_d])
                                else:
                                    t_ad = P.emit("dve", lambda h, col=col, bank=bank,
                                                  tt8=tt8: h.scalar_tensor_tensor(
                                        x2[:, col:col + 512], ps[bank][:, :],
                                        sg[:, tt8:tt8 + 1], x2[:, col:col + 512],
                                        ALU.mult, ALU.add), inc="dve",
                                        waits=[t_d, t_route])
                                rel[bank] = t_ad
                    di_count[0] += 1
                    if di + 2 < len(wd_list):
                        k2, idx2 = wd_list[di + 2]
                        t_wd[di + 2] = load_wd(k2, idx2, dbuf, t_dlast)
            t_ffn = P.tok("dve")
            for tt8 in range(8):
                tt = toff + tt8
                dma("sp", "qo", out.ap()[tt * 128:(tt + 1) * 128, :],
                    x2[:, tt8 * D:(tt8 + 1) * D], waits=[t_ffn])
            x2rel = qall("qo")
            P.realize(esG)
        esR6.close()

        P.emit("sp", lambda h: h.nop(), waits=qall("qo"))
        P.realize(esG)
    return nc


def kernel(**inputs):
    pr, percore = _host_prepare(inputs)
    nc = build_nc()
    shared = {k: np.ascontiguousarray(v, dtype=np.float32) for k, v in pr.items()}
    in_maps = []
    for c in range(8):
        m = dict(shared)
        for k, v in percore[c].items():
            m[k] = np.ascontiguousarray(v, dtype=np.float32)
        in_maps.append(m)
    res = run_bass_kernel_spmd(nc, in_maps, core_ids=list(range(8)))
    full = np.empty((B, S, D), dtype=np.float32)
    for c in range(8):
        b, p = c // 2, c % 2
        o = np.asarray(res.results[c]["out"])
        for i in range(NT):
            g = 2 * i + p
            full[b, g * 128:(g + 1) * 128, :] = o[i * 128:(i + 1) * 128, :]
    return full


# version 07
# speedup vs baseline: 1.5992x; 1.1201x over previous
import contextlib
import math
import numpy as np
import concourse.bass as bass
import concourse.mybir as mybir
from concourse.bass import IndirectOffsetOnAxis
from concourse.bass_utils import run_bass_kernel_spmd

F32 = mybir.dt.float32
BF16 = mybir.dt.bfloat16
I32 = mybir.dt.int32
AF = mybir.ActivationFunctionType
ALU = mybir.AluOpType
AX = mybir.AxisListType

B, S, D, NH = 4, 4096, 2048, 16
ROPE, NOPE, VD = 64, 64, 128
QH = ROPE + NOPE
QL = D // 3
KVL = D // 8
THETA = 128000.0
SCALE = 1.0 / math.sqrt(QH)
EPS_LN = 1e-5
EPS_RMS = 1e-6
E = 16
I_MOE, I_SH = 1408, 5632
NT = 16
TOK = NT * 128
NKT = 32
IT_MOE = I_MOE // 128
IT_SH = I_SH // 128
KC = D // 128
QLC = [128, 128, 128, 128, 128, 42]
NEG = -30000.0

ENGS = ("pe", "dve", "act", "pool", "sp")


def _rope_tables():
    inv = 1.0 / THETA ** (np.arange(0, ROPE, 2, dtype=np.float64) / ROPE)
    pos = np.arange(S, dtype=np.float64)
    fr = np.outer(inv, pos)
    cos = np.cos(np.concatenate([fr, fr], 0))
    sin = np.sin(np.concatenate([fr, fr], 0))
    sin_signed = sin.copy()
    sin_signed[:32] *= -1.0
    return cos.astype(np.float32), sin_signed.astype(np.float32)


_PERM = np.concatenate([np.arange(0, 64, 2), np.arange(1, 64, 2)])


def _tile_k(w):
    k, n = w.shape
    return np.ascontiguousarray(w.reshape(k // 128, 128, n).transpose(1, 0, 2))


def _tile_cols(w, ct):
    k, n = w.shape
    return np.ascontiguousarray(
        w.reshape(k // 128, 128, n // ct, ct).transpose(2, 1, 0, 3))


def _host_prepare(inp):
    cos, sin = _rope_tables()
    pr = {}
    pr["qaw_t"] = _tile_k(inp["L0_attn_q_a_w"])
    pr["lnq"] = inp["L0_attn_q_a_ln_w"].reshape(1, QL)
    qbw = np.array(inp["L0_attn_q_b_w"], dtype=np.float32)
    for h in range(NH):
        base = h * QH + NOPE
        qbw[:, base:base + ROPE] = qbw[:, base + _PERM]
    qbh = np.zeros((NH, 128, 6, 128), np.float32)
    for h in range(NH):
        for c in range(6):
            w = QLC[c]
            qbh[h, :w, c, :] = qbw[c * 128:c * 128 + w, h * QH:(h + 1) * QH]
    pr["qbw_h"] = qbh
    kvaw = np.array(inp["L0_attn_kv_a_w"], dtype=np.float32)
    kvaw[:, KVL:] = kvaw[:, KVL + _PERM]
    pr["kvaw_t"] = _tile_k(kvaw)
    pr["lnkv"] = inp["L0_attn_kv_a_ln_w"].reshape(1, KVL)
    kvb3 = np.asarray(inp["L0_attn_kv_b_w"]).reshape(KVL, NH, NOPE + VD)
    pr["kvb_nope"] = np.ascontiguousarray(kvb3[:, :, :NOPE].reshape(KVL, NH * NOPE))
    pr["kvb_v"] = np.ascontiguousarray(kvb3[:, :, NOPE:].reshape(KVL, NH * VD))
    pr["ow_t"] = _tile_k(inp["L0_attn_o_w"])
    pr["router_t"] = _tile_k(np.ascontiguousarray(np.asarray(inp["L0_ffn_router_w"]).T))
    pr["wg_t"] = np.stack([_tile_cols(inp["L0_ffn_w_gate"][e], 128) for e in range(E)])
    pr["wu_t"] = np.stack([_tile_cols(inp["L0_ffn_w_up"][e], 128) for e in range(E)])
    pr["wd_t"] = np.stack([
        np.ascontiguousarray(np.asarray(inp["L0_ffn_w_down"][e], np.float32)
                             .reshape(IT_MOE, 128, 2, 1024).transpose(2, 1, 0, 3)
                             .reshape(2, 128, IT_MOE * 1024)) for e in range(E)])
    pr["shg_t"] = _tile_cols(inp["L0_ffn_sh_gate"], 128)
    pr["shu_t"] = _tile_cols(inp["L0_ffn_sh_up"], 128)
    pr["shd_t"] = np.ascontiguousarray(
        np.asarray(inp["L0_ffn_sh_down"], np.float32)
        .reshape(4, IT_MOE, 128, 2, 1024).transpose(0, 3, 2, 1, 4)
        .reshape(4, 2, 128, IT_MOE * 1024))
    pr["shgw_t"] = _tile_k(inp["L0_ffn_sh_gate_w"])
    pr["id128"] = np.eye(128, dtype=np.float32)
    pr["ones_row"] = np.ones((1, 128), np.float32)
    mswap = np.zeros((64, 128), np.float32)
    for m in range(32):
        mswap[m, 96 + m] = 1.0
        mswap[32 + m, 64 + m] = 1.0
    pr["mswapT"] = np.ascontiguousarray(mswap.T)
    mshift = np.zeros((64, 128), np.float32)
    for m in range(64):
        mshift[m, 64 + m] = 1.0
    pr["mshiftT"] = np.ascontiguousarray(mshift.T)
    r64 = np.zeros((64, 64), np.float32)
    for m in range(32):
        r64[m, 32 + m] = 1.0
        r64[32 + m, m] = 1.0
    pr["r64T"] = np.ascontiguousarray(r64.T)
    pr["ck"] = np.ascontiguousarray(cos)
    pr["sk"] = np.ascontiguousarray(sin)
    pr["lx"] = np.triu(np.ones((128, 128), np.float32), k=1)
    oneh = np.zeros((E, E * 128), np.float32)
    for e in range(E):
        oneh[e, e * 128:(e + 1) * 128] = 1.0
    pr["onehot"] = oneh
    percore = []
    x = np.ascontiguousarray(np.asarray(inp["L0_attn_x"], dtype=np.float32))
    for c in range(8):
        b, p = c // 2, c % 2
        d = {}
        slots = np.array([2 * i + p for i in range(NT)])
        rows = (slots[:, None] * 128 + np.arange(128)[None, :]).reshape(-1)
        d["xq"] = np.ascontiguousarray(x[b][rows])
        d["xk"] = x[b]
        d["cq"] = np.ascontiguousarray(cos[:, rows])
        d["sq"] = np.ascontiguousarray(sin[:, rows])
        for W in (256, 512):
            cc, rr = np.meshgrid(np.arange(W), np.arange(128))
            d[f"mask1_{W}"] = np.where(cc - rr > W - 256 + 128 * p, NEG, 0.0
                                       ).astype(np.float32)
        m2 = np.zeros((8, 128, 512), np.float32)
        for dd in range(8):
            for t in range(4):
                cc, rr = np.meshgrid(np.arange(128), np.arange(128))
                m2[dd][:, t * 128:(t + 1) * 128] = np.where(
                    (dd - 2 * t - p) * 128 + rr > cc, NEG, 0.0)
        d["mask2"] = m2
        percore.append(d)
    return pr, percore


class Prog:
    def __init__(self, nc):
        self.nc = nc
        self.items = {e: [] for e in ENGS}
        self.sem = {}
        self.cnt = {}
        self.water = {e: {} for e in ENGS}

    def new_sem(self, name, es):
        self.sem[name] = es.enter_context(self.nc.semaphore(name))
        self.cnt[name] = 0

    def tok(self, name):
        return (name, self.cnt[name])

    paranoid = ()

    def emit(self, eng, fn, inc=None, inck=1, waits=()):
        if self.paranoid:
            waits = list(waits) + [self.tok(s) for s in self.paranoid]
        wl = []
        for tkn in waits:
            if tkn is None:
                continue
            s, v = tkn
            if v <= 0:
                continue
            w = self.water[eng]
            if w.get(s, 0) >= v:
                continue
            w[s] = v
            wl.append((self.sem[s], v))
        tk = None
        sem_h = None
        if inc is not None:
            self.cnt[inc] += inck
            tk = (inc, self.cnt[inc])
            sem_h = self.sem[inc]

        def wrapped(h, fn=fn, wl=wl, sem_h=sem_h, inck=inck):
            for (sh, v) in wl:
                h.wait_ge(sh, v)
            inst = fn(h)
            if sem_h is not None:
                inst.then_inc(sem_h, inck)
        self.items[eng].append(wrapped)
        return tk

    def realize(self, es):
        with self.nc.Block() as block:
            for eng, deco in (("pe", block.tensor), ("dve", block.vector),
                              ("act", block.scalar), ("pool", block.gpsimd),
                              ("sp", block.sync)):
                fns = self.items[eng]
                if fns:
                    deco(lambda h, fns=list(fns): [fn(h) for fn in fns])
        self.items = {e: [] for e in ENGS}
        for e in ENGS:
            for s in self.sem:
                if not s.startswith("q"):
                    self.water[e][s] = self.cnt[s]


def build_nc(upto="full", nheads=NH):
    nc = bass.Bass("TRN2", target_bir_lowering=False)
    dr = {}

    def din(name, shape):
        dr[name] = nc.dram_tensor(name, list(shape), F32, kind="ExternalInput")

    din("xq", (TOK, D)); din("xk", (S, D))
    din("qaw_t", (128, KC, QL)); din("lnq", (1, QL)); din("qbw_h", (NH, 128, 6, 128))
    din("kvaw_t", (128, KC, KVL + ROPE)); din("lnkv", (1, KVL))
    din("kvb_nope", (KVL, NH * NOPE)); din("kvb_v", (KVL, NH * VD))
    din("ow_t", (128, KC, D)); din("router_t", (128, KC, E))
    din("wg_t", (E, IT_MOE, 128, KC, 128)); din("wu_t", (E, IT_MOE, 128, KC, 128))
    din("wd_t", (E, 2, 128, IT_MOE * 1024))
    din("shg_t", (IT_SH, 128, KC, 128)); din("shu_t", (IT_SH, 128, KC, 128))
    din("shd_t", (4, 2, 128, IT_MOE * 1024)); din("shgw_t", (128, KC, 1))
    din("id128", (128, 128)); din("ones_row", (1, 128))
    din("mswapT", (128, 64)); din("mshiftT", (128, 64)); din("r64T", (64, 64))
    din("ck", (ROPE, S)); din("sk", (ROPE, S))
    din("cq", (ROPE, TOK)); din("sq", (ROPE, TOK))
    din("mask1_256", (128, 256)); din("mask1_512", (128, 512))
    din("mask2", (8, 128, 512)); din("onehot", (E, E * 128)); din("lx", (128, 128))
    out = nc.dram_tensor("out", [TOK, D], F32, kind="ExternalOutput")
    x1d = nc.dram_tensor("x1d", [TOK, D], F32, kind="Internal")
    wg_bf = nc.dram_tensor("wg_bf", [E, IT_MOE, 128, KC * 128], BF16, kind="Internal")
    wu_bf = nc.dram_tensor("wu_bf", [E, IT_MOE, 128, KC * 128], BF16, kind="Internal")
    wd_bf = nc.dram_tensor("wd_bf", [E, 2, 128, IT_MOE * 1024], BF16, kind="Internal")
    shg_bf = nc.dram_tensor("shg_bf", [IT_SH, 128, KC * 128], BF16, kind="Internal")
    shu_bf = nc.dram_tensor("shu_bf", [IT_SH, 128, KC * 128], BF16, kind="Internal")
    shd_bf = nc.dram_tensor("shd_bf", [4, 2, 128, IT_MOE * 1024], BF16, kind="Internal")
    x1nd = nc.dram_tensor("x1nd", [TOK, D], BF16, kind="Internal")
    combd = nc.dram_tensor("combd", [TOK, E], F32, kind="Internal")
    idxd = nc.dram_tensor("idxd", [2, E, 512], I32, kind="Internal")
    ap = lambda n: dr[n].ap()

    P = Prog(nc)
    esG = contextlib.ExitStack()
    with esG:
        for s in ("pe", "dve", "act", "pool"):
            P.new_sem(s, esG)
        QROT = 8
        for q in ("qg", "qs", "qo", "qa"):
            for i in range(QROT):
                P.new_sem(f"{q}{i}", esG)
        qrot = {"qg": 0, "qs": 0, "qo": 0, "qa": 0}

        def qall(q):
            return [P.tok(f"{q}{i}") for i in range(QROT)]
        ps = [esG.enter_context(nc.psum_tensor(f"ps{i}", [128, 512], F32))
              for i in range(8)]
        rel = {}

        def sbuf(es, name, shp, dt=F32, side="right"):
            return es.enter_context(nc.sbuf_tensor(name, list(shp), dt, side=side))

        def dma(eng, q, dst, src, waits=()):
            s_name = f"{q}{qrot[q]}"
            qrot[q] = (qrot[q] + 1) % QROT
            return P.emit(eng, lambda h, d=dst, s=src: h.dma_start(d, s),
                          inc=s_name, inck=16, waits=waits)

        cast_jobs = []
        cast_tok = {}
        for e_ in range(E):
            cast_jobs.append((("g", e_), wg_bf.ap()[e_], ap("wg_t")[e_]))
            cast_jobs.append((("u", e_), wu_bf.ap()[e_], ap("wu_t")[e_]))
            cast_jobs.append((("d", e_), wd_bf.ap()[e_], ap("wd_t")[e_]))
        for g_ in range(4):
            cast_jobs.append((("shg", g_), shg_bf.ap()[g_ * IT_MOE:(g_ + 1) * IT_MOE],
                              ap("shg_t")[g_ * IT_MOE:(g_ + 1) * IT_MOE]))
            cast_jobs.append((("shu", g_), shu_bf.ap()[g_ * IT_MOE:(g_ + 1) * IT_MOE],
                              ap("shu_t")[g_ * IT_MOE:(g_ + 1) * IT_MOE]))
            cast_jobs.append((("shd", g_), shd_bf.ap()[g_], ap("shd_t")[g_]))

        def emit_casts(n):
            for _ in range(n):
                if not cast_jobs:
                    return
                key, dst, srcap = cast_jobs.pop(0)
                cast_tok[key] = dma("pool", "qg", dst, srcap)

        L = lambda name, shp, dt=F32: sbuf(esG, name, shp, dt, side="left")
        id_bf = L("id_bf", (128, 128), BF16)
        id_f32 = L("id_f32", (128, 128), F32)
        ones_bf = L("ones_bf", (1, 128), BF16)
        router_sb = L("router_sb", (128, KC * E), BF16)
        shgw_sb = L("shgw_sb", (128, KC), BF16)
        xn = L("xn", (128, D), BF16)
        xnT = L("xnT", (128, KC * 128), BF16)
        stats = L("stats", (128, 24), F32)
        mv = L("mv", (128, 2), F32)
        sd_t = L("sd_t", (128, 1), F32)
        rstd = L("rstd", (128, 1), F32)
        sq1 = L("sq1", (128, 1), F32)
        sq2 = L("sq2", (128, 1), F32)
        sqs = L("sqs", (128, 1), F32)

        esR1 = contextlib.ExitStack()
        o_all = sbuf(esR1, "o_all", (128, NH * TOK), BF16)
        esR2 = contextlib.ExitStack()
        qaT = sbuf(esR2, "qaT", (128, 6 * TOK), BF16)
        ckvT = sbuf(esR2, "ckvT", (128, 2 * S), BF16)
        kpe_aug = sbuf(esR2, "kpe_aug", (65, S), BF16)
        kvbn_sb = sbuf(esR2, "kvbn_sb", (128, 2 * NH * NOPE), BF16)
        kvbv_sb = sbuf(esR2, "kvbv_sb", (128, 2 * NH * VD), BF16)

        esR3 = contextlib.ExitStack()
        qaw_sb = sbuf(esR3, "qaw_sb", (128, KC * QL), BF16)
        kva_sb = sbuf(esR3, "kva_sb", (128, KC * (KVL + ROPE)), BF16)
        lnq_b = sbuf(esR3, "lnq_b", (128, QL), BF16)
        lnkv_b = sbuf(esR3, "lnkv_b", (128, KVL), BF16)
        lnrow = sbuf(esR3, "lnrow", (1, QL), BF16)
        ck_sb = sbuf(esR3, "ck_sb", (ROPE, S), BF16)
        sk_sb = sbuf(esR3, "sk_sb", (ROPE, S), BF16)
        r64_sb = sbuf(esR3, "r64_sb", (64, 64), BF16)
        xtile = [sbuf(esR3, f"xtileA{i}", (128, D), F32) for i in range(2)]
        qa_n = sbuf(esR3, "qa_n", (128, QL), BF16)
        ckv_n = sbuf(esR3, "ckv_n", (128, KVL), BF16)
        kpe_tm = sbuf(esR3, "kpe_tm", (128, ROPE), BF16)
        sqscr = sbuf(esR3, "sqscr", (128, 512), BF16)
        tp1 = sbuf(esR3, "tp1", (64, 512), F32)
        tp2 = sbuf(esR3, "tp2", (64, 512), F32)

        dma("pool", "qg", id_bf[:], ap("id128"))
        dma("sp", "qs", id_f32[:], ap("id128"))
        dma("pool", "qg", ones_bf[:], ap("ones_row"))
        dma("pool", "qg", router_sb[:].rearrange("p (c n) -> p c n", c=KC), ap("router_t"))
        dma("pool", "qg", shgw_sb[:].rearrange("p (c n) -> p c n", c=KC), ap("shgw_t"))
        dma("pool", "qg", kvbn_sb[:].rearrange("p (c n) -> p c n", c=2),
            ap("kvb_nope").rearrange("(c p) n -> p c n", p=128))
        dma("pool", "qg", kvbv_sb[:].rearrange("p (c n) -> p c n", c=2),
            ap("kvb_v").rearrange("(c p) n -> p c n", p=128))
        dma("pool", "qg", ck_sb[:], ap("ck"))
        dma("pool", "qg", sk_sb[:], ap("sk"))
        dma("pool", "qg", r64_sb[:], ap("r64T"))
        t_wconst = qall("qg")
        t_qaw = dma("pool", "qg", qaw_sb[:].rearrange("p (c n) -> p c n", c=KC), ap("qaw_t"))
        t_kva = dma("pool", "qg", kva_sb[:].rearrange("p (c n) -> p c n", c=KC), ap("kvaw_t"))
        t_l1 = dma("pool", "qg", lnrow[0:1, 0:QL], ap("lnq"))

        def bcast_row(width, dst, t_row):
            t_cp = None
            for j in range((width + 511) // 512):
                w = min(512, width - j * 512)
                t_mm = P.emit("pe", lambda h, j=j, w=w: h.matmul(
                    ps[5][:, 0:w], ones_bf[0:1, :], lnrow[0:1, j * 512:j * 512 + w],
                    start=True, stop=True), inc="pe", waits=[t_row, rel.get(5)])
                t_cp = P.emit("act", lambda h, j=j, w=w, dst=dst: h.copy(
                    dst[:, j * 512:j * 512 + w], ps[5][:, 0:w]), inc="act",
                    waits=[t_mm])
                rel[5] = t_cp
            return t_cp

        t_lnqb = bcast_row(QL, lnq_b, t_l1)
        t_l2 = dma("pool", "qg", lnrow[0:1, 0:KVL], ap("lnkv"), waits=[rel[5]])
        t_lnkb = bcast_row(KVL, lnkv_b, t_l2)

        def emit_ln(xbuf, t_x, extra=()):
            for sl in range(4):
                P.emit("dve", lambda h, sl=sl, xb=xbuf: h.bn_stats(
                    stats[:, sl * 6:(sl + 1) * 6], xb[:, sl * 512:(sl + 1) * 512]),
                    inc="dve", waits=[t_x] if sl == 0 else ())
            P.emit("dve", lambda h: h.bn_aggr(mv[:], stats[:]), inc="dve",
                   waits=[P.tok("dve")])
            P.emit("dve", lambda h: h.tensor_scalar(sq1[:], mv[:, 1:2], EPS_LN,
                                                    None, ALU.add), inc="dve",
                   waits=[P.tok("dve")])
            P.emit("act", lambda h: h.activation(sd_t[:], sq1[:], AF.Sqrt), inc="act",
                   waits=[P.tok("dve")])
            P.emit("dve", lambda h: h.reciprocal(rstd[:], sd_t[:]), inc="dve",
                   waits=[P.tok("act")])
            P.emit("dve", lambda h, xb=xbuf: h.tensor_scalar(
                xn[:], xb[:], mv[:, 0:1], rstd[:], ALU.subtract, ALU.mult),
                inc="dve", waits=[P.tok("dve")] + list(extra))
            return P.tok("dve")

        def emit_transposes(src, n, widths, dst_fn, t_src):
            t_last = None
            for c in range(n):
                w = widths[c] if widths else 128
                bank = 6 + (c % 2)
                t_t = P.emit("pe", lambda h, c=c, w=w, bank=bank, src=src: h.transpose(
                    ps[bank].bitcast(BF16)[0:w, 0:128], src[:, c * 128:c * 128 + w],
                    id_bf[:]), inc="pe", waits=[t_src, rel.get(bank)])
                t_last = P.emit("act", lambda h, d=dst_fn(c), w=w, bank=bank: h.copy(
                    d, ps[bank].bitcast(BF16)[0:w, 0:128]), inc="act", waits=[t_t])
                rel[bank] = t_last
            return t_last

        xrel = [None, None]
        for tt in range(NT):
            xb = xtile[tt % 2]
            t_x = dma("sp", "qs", xb[:], ap("xq")[tt * 128:(tt + 1) * 128, :],
                      waits=[xrel[tt % 2]])
            t_xn = emit_ln(xb, t_x, extra=[P.tok("pe")])
            xrel[tt % 2] = t_xn
            t_xT = emit_transposes(xn, KC, None,
                                   lambda c: xnT[:, c * 128:(c + 1) * 128], t_xn)
            for c in range(KC):
                last = c == KC - 1
                P.emit("pe", lambda h, c=c: h.matmul(
                    ps[0][:, :], xnT[:, c * 128:(c + 1) * 128],
                    qaw_sb[:, c * QL:c * QL + 512], start=(c == 0), stop=last),
                    waits=[t_xT, t_qaw, rel.get(0), rel.get(1)] if c == 0 else ())
                P.emit("pe", lambda h, c=c: h.matmul(
                    ps[1][:, 0:170], xnT[:, c * 128:(c + 1) * 128],
                    qaw_sb[:, c * QL + 512:(c + 1) * QL], start=(c == 0), stop=last),
                    inc="pe" if last else None)
            t_qa = P.tok("pe")
            P.emit("act", lambda h: h.activation(sqscr[:, 0:512], ps[0][:, :],
                                                 AF.Square, accum_out=sq1[:]),
                   inc="act", waits=[t_qa, P.tok("dve")])
            P.emit("act", lambda h: h.activation(sqscr[:, 0:170], ps[1][:, 0:170],
                                                 AF.Square, accum_out=sq2[:]),
                   inc="act")
            P.emit("dve", lambda h: h.tensor_add(sqs[:], sq1[:], sq2[:]), inc="dve",
                   waits=[P.tok("act")])
            P.emit("dve", lambda h: h.tensor_scalar(sqs[:], sqs[:], 1.0 / QL, EPS_RMS,
                                                    ALU.mult, ALU.add), inc="dve",
                   waits=[P.tok("dve")])
            P.emit("act", lambda h: h.activation(sd_t[:], sqs[:], AF.Sqrt), inc="act",
                   waits=[P.tok("dve")])
            P.emit("dve", lambda h: h.reciprocal(rstd[:], sd_t[:]), inc="dve",
                   waits=[P.tok("act")])
            P.emit("dve", lambda h: h.tensor_scalar(qa_n[:, 0:512], ps[0][:, :],
                                                    rstd[:], None, ALU.mult),
                   inc="dve", waits=[P.tok("dve")])
            t_ev = P.emit("dve", lambda h: h.tensor_scalar(
                qa_n[:, 512:QL], ps[1][:, 0:170], rstd[:], None, ALU.mult), inc="dve")
            rel[0] = t_ev
            rel[1] = t_ev
            t_qan = P.emit("dve", lambda h: h.tensor_mul(qa_n[:], qa_n[:], lnq_b[:]),
                           inc="dve", waits=[t_ev, t_lnqb])
            emit_transposes(qa_n, 6, QLC,
                            lambda c, tt=tt: qaT[0:QLC[c],
                                                 c * TOK + tt * 128:c * TOK + (tt + 1) * 128],
                            t_qan)
        P.realize(esG)

        for tk in range(NKT):
            xb = xtile[tk % 2]
            t_x = dma("sp", "qs", xb[:], ap("xk")[tk * 128:(tk + 1) * 128, :],
                      waits=[xrel[tk % 2]])
            t_xn = emit_ln(xb, t_x, extra=[P.tok("pe")])
            xrel[tk % 2] = t_xn
            t_xT = emit_transposes(xn, KC, None,
                                   lambda c: xnT[:, c * 128:(c + 1) * 128], t_xn)
            for c in range(KC):
                last = c == KC - 1
                P.emit("pe", lambda h, c=c: h.matmul(
                    ps[0][:, 0:KVL + ROPE], xnT[:, c * 128:(c + 1) * 128],
                    kva_sb[:, c * 320:(c + 1) * 320], start=(c == 0), stop=last),
                    inc="pe" if last else None,
                    waits=[t_xT, t_kva, rel.get(0)] if c == 0 else ())
            t_kv = P.tok("pe")
            P.emit("act", lambda h: h.activation(sqscr[:, 0:KVL], ps[0][:, 0:KVL],
                                                 AF.Square, accum_out=sqs[:]),
                   inc="act", waits=[t_kv, P.tok("dve")])
            P.emit("dve", lambda h: h.tensor_scalar(sqs[:], sqs[:], 1.0 / KVL,
                                                    EPS_RMS, ALU.mult, ALU.add),
                   inc="dve", waits=[P.tok("act")])
            P.emit("act", lambda h: h.activation(sd_t[:], sqs[:], AF.Sqrt), inc="act",
                   waits=[P.tok("dve")])
            P.emit("dve", lambda h: h.reciprocal(rstd[:], sd_t[:]), inc="dve",
                   waits=[P.tok("act")])
            t_craw = P.emit("dve", lambda h: h.tensor_scalar(
                ckv_n[:], ps[0][:, 0:KVL], rstd[:], None, ALU.mult),
                inc="dve", waits=[P.tok("dve")])
            t_cn = P.emit("dve", lambda h: h.tensor_mul(ckv_n[:], ckv_n[:], lnkv_b[:]),
                          inc="dve", waits=[P.tok("dve"), t_lnkb])
            t_kp = P.emit("act", lambda h: h.copy(kpe_tm[:], ps[0][:, KVL:KVL + ROPE]),
                          inc="act", waits=[t_kv, t_craw])
            rel[0] = t_kp
            emit_transposes(ckv_n, 2, None,
                            lambda c, tk=tk: ckvT[:, c * S + tk * 128:c * S + (tk + 1) * 128],
                            t_cn)
            bank = 6
            t_t = P.emit("pe", lambda h: h.transpose(
                ps[bank].bitcast(BF16)[0:ROPE, 0:128], kpe_tm[:], id_bf[:]),
                inc="pe", waits=[t_kp, rel.get(bank)])
            t_cpy = P.emit("act", lambda h, tk=tk: h.copy(
                kpe_aug[0:64, tk * 128:(tk + 1) * 128],
                ps[bank].bitcast(BF16)[0:ROPE, 0:128]), inc="act", waits=[t_t])
            rel[bank] = t_cpy
        t_kperaw = P.tok("act")

        P.emit("dve", lambda h: h.memset(kpe_aug[64:65, :], 1.0), inc="dve",
               waits=[t_kperaw])
        for j in range(8):
            slc = slice(j * 512, (j + 1) * 512)
            bank = j % 2
            t_m = P.emit("pe", lambda h, slc=slc, bank=bank: h.matmul(
                ps[bank][0:64, :], r64_sb[:], kpe_aug[0:64, slc],
                start=True, stop=True), inc="pe",
                waits=[t_kperaw, rel.get(bank), P.tok("dve")] + t_wconst)
            P.emit("dve", lambda h, slc=slc, bank=bank: h.tensor_mul(
                tp1[:], ps[bank][0:64, :], sk_sb[:, slc]), inc="dve", waits=[t_m])
            P.emit("dve", lambda h, slc=slc: h.tensor_mul(
                tp2[:], kpe_aug[0:64, slc], ck_sb[:, slc]), inc="dve")
            t_add = P.emit("dve", lambda h, slc=slc: h.tensor_add(
                kpe_aug[0:64, slc], tp1[:], tp2[:]), inc="dve", waits=[P.tok("dve")])
            rel[bank] = t_add
        t_kpe = P.tok("dve")
        emit_casts(12)
        P.realize(esG)
        esR3.close()

        if upto == "A":
            t1 = dma("pool", "qo", out.ap()[0:128, :], qaT[:, 0:TOK])
            t2 = dma("pool", "qo", out.ap()[128:256, :], ckvT[:, 0:2048])
            t3 = dma("pool", "qo", out.ap()[256:321, :], kpe_aug[0:65, 0:2048])
            P.emit("sp", lambda h: h.nop(), waits=qall("qo"))
            P.realize(esG)
            return nc

        esR4 = contextlib.ExitStack()
        qbw_h = [sbuf(esR4, f"qbw_h{i}", (128, 6 * 128), BF16) for i in range(2)]
        m1_256 = sbuf(esR4, "m1_256", (128, 256), BF16)
        m1_512 = sbuf(esR4, "m1_512", (128, 512), BF16)
        m2_sb = sbuf(esR4, "m2_sb", (128, 8 * 512), BF16)
        cq_sb = sbuf(esR4, "cq_sb", (ROPE, TOK), BF16)
        sq_sb = sbuf(esR4, "sq_sb", (ROPE, TOK), BF16)
        mswap_sb = sbuf(esR4, "mswap_sb", (128, 64), BF16)
        mshift_sb = sbuf(esR4, "mshift_sb", (128, 64), BF16)
        qraw = sbuf(esR4, "qraw", (128, TOK), BF16)
        qpe_aug = sbuf(esR4, "qpe_aug", (65, TOK), BF16)
        knope_h = [sbuf(esR4, f"knope_h{i}", (64, S), BF16) for i in range(2)]
        v2 = sbuf(esR4, "v2", (128, NKT * 2 * VD), BF16)
        tb1 = sbuf(esR4, "tb1", (64, 512), F32)
        tb2 = sbuf(esR4, "tb2", (64, 512), F32)
        mstats = sbuf(esR4, "mstats", (128, NT * 8), F32)
        zstats = sbuf(esR4, "zstats", (128, NT * 8), F32)
        mins = sbuf(esR4, "mins", (128, NT), F32)
        minbf = sbuf(esR4, "minbf", (128, NT), BF16)
        minr = sbuf(esR4, "minr", (128, NT), F32)
        t8a = sbuf(esR4, "t8a", (128, NT * 8), F32)
        zw8a = sbuf(esR4, "zw8a", (128, NT * 8), F32)
        zca = sbuf(esR4, "zca", (128, NT), F32)
        zia = sbuf(esR4, "zia", (128, NT), F32)
        zrow = xnT[0:1, 0:TOK]
        attnT = [sbuf(esR4, f"attnT{i}", (128, 512), BF16) for i in range(4)]
        escr = attnT[0]
        zbc = xn[:, 0:512]

        dma("pool", "qg", m1_256[:], ap("mask1_256"))
        dma("pool", "qg", m1_512[:], ap("mask1_512"))
        dma("pool", "qg", m2_sb[:].rearrange("p (c n) -> p c n", c=8),
            ap("mask2").rearrange("c p n -> p c n"))
        dma("pool", "qg", cq_sb[:], ap("cq"))
        dma("pool", "qg", sq_sb[:], ap("sq"))
        dma("pool", "qg", mswap_sb[:], ap("mswapT"))
        dma("pool", "qg", mshift_sb[:], ap("mshiftT"))
        t_bconst = qall("qg")

        t_qbw = []
        for h_ in range(2):
            t_qbw.append(dma("pool", "qg",
                             qbw_h[h_][:].rearrange("p (c n) -> p c n", c=6),
                             ap("qbw_h")[h_]))

        def p1_tiles(i):
            ext = (2 * i + 2) * 128
            full, remn = divmod(ext, 512)
            tl = [(j * 512, 512) for j in range(full)]
            if remn:
                tl.append((full * 512, remn))
            return tl

        att_rel = [None, None, None, None]
        for h_ in range(nheads):
            hh = h_ % 2
            buf = qbw_h[h_ % 2]
            emit_casts(4)
            P.emit("dve", lambda h: h.memset(qpe_aug[64:65, :], 0.0), inc="dve",
                   waits=[P.tok("pe")])
            t_qpz = P.tok("dve")
            t_qlast = None
            for nb in range(4):
                qsl = slice(nb * 512, (nb + 1) * 512)
                pq = ps[nb]
                for c in range(6):
                    last = c == 5
                    P.emit("pe", lambda h, c=c, nb=nb, pq=pq, buf=buf: h.matmul(
                        pq[:, :], buf[0:QLC[c], c * 128:(c + 1) * 128],
                        qaT[0:QLC[c], c * TOK + nb * 512:c * TOK + (nb + 1) * 512],
                        start=(c == 0), stop=last),
                        inc="pe" if last else None,
                        waits=([t_qbw[h_], rel.get(nb)] + t_bconst)
                        if c == 0 else ())
                t_q = P.tok("pe")
                t_qlast = t_q
                t_qr = P.emit("act", lambda h, qsl=qsl, pq=pq: h.mul(
                    qraw[:, qsl], pq[:, :], SCALE), inc="act",
                    waits=[t_q, P.tok("dve")])
                rel[nb] = t_qr
                P.emit("pe", lambda h, qsl=qsl: h.matmul(
                    ps[4][0:64, :], mswap_sb[:], qraw[:, qsl], start=True, stop=True),
                    inc="pe", waits=[t_qr, rel.get(4)])
                P.emit("pe", lambda h, qsl=qsl: h.matmul(
                    ps[5][0:64, :], mshift_sb[:], qraw[:, qsl], start=True, stop=True),
                    inc="pe", waits=[rel.get(5)])
                t_rot = P.tok("pe")
                t_d1 = P.emit("dve", lambda h, qsl=qsl: h.tensor_mul(
                    tb1[:], ps[4][0:64, :], sq_sb[:, qsl]), inc="dve",
                    waits=[t_rot] + t_bconst)
                rel[4] = t_d1
                t_d2 = P.emit("dve", lambda h, qsl=qsl: h.tensor_mul(
                    tb2[:], ps[5][0:64, :], cq_sb[:, qsl]), inc="dve")
                rel[5] = t_d2
                P.emit("dve", lambda h, qsl=qsl: h.tensor_add(
                    qpe_aug[0:64, qsl], tb1[:], tb2[:]), inc="dve",
                    waits=[t_d2, t_qpz])
            t_qf = P.tok("dve")
            if upto == "QF":
                dma("pool", "qo", out.ap()[256:384, :], qraw[:, :], waits=[t_qf])
                dma("pool", "qo", out.ap()[384:449, :], qpe_aug[0:65, :], waits=[t_qf])
                P.emit("sp", lambda h: h.nop(), waits=qall("qo"))
                P.realize(esG)
                return nc
            if h_ + 2 < nheads:
                t_qbw.append(dma("pool", "qg",
                                 qbw_h[h_ % 2][:].rearrange("p (c n) -> p c n", c=6),
                                 ap("qbw_h")[h_ + 2], waits=[t_qlast]))

            if hh == 0:
                for j in range(8):
                    ksl = slice(j * 512, (j + 1) * 512)
                    for hp in range(2):
                        bank = 4 + ((2 * j + hp) % 2)
                        pk = ps[bank]
                        for c in range(2):
                            last = c == 1
                            P.emit("pe", lambda h, c=c, j=j, pk=pk, h_=h_, hp=hp:
                                   h.matmul(
                                pk[0:64, :],
                                kvbn_sb[:, c * NH * NOPE + (h_ + hp) * NOPE:
                                        c * NH * NOPE + (h_ + hp + 1) * NOPE],
                                ckvT[:, c * S + j * 512:c * S + (j + 1) * 512],
                                start=(c == 0), stop=last),
                                inc="pe" if last else None,
                                waits=([rel.get(bank)] + t_bconst) if c == 0 else ())
                        t_k = P.tok("pe")
                        t_cp = P.emit("act", lambda h, ksl=ksl, pk=pk, hp=hp: h.copy(
                            knope_h[hp][:, ksl], pk[0:64, :]), inc="act", waits=[t_k])
                        rel[bank] = t_cp
            if hh == 0:
                for tk in range(NKT):
                    bank = 4 + (tk % 2)
                    pk = ps[bank]
                    for c in range(2):
                        last = c == 1
                        P.emit("pe", lambda h, c=c, tk=tk, pk=pk, h_=h_: h.matmul(
                            pk[:, 0:2 * VD],
                            ckvT[:, c * S + tk * 128:c * S + (tk + 1) * 128],
                            kvbv_sb[:, c * NH * VD + h_ * VD:
                                    c * NH * VD + (h_ + 2) * VD],
                            start=(c == 0), stop=last),
                            inc="pe" if last else None,
                            waits=([rel.get(bank)] + t_bconst) if c == 0 else ())
                    t_v = P.tok("pe")
                    t_cp = P.emit("act", lambda h, tk=tk, pk=pk: h.copy(
                        v2[:, tk * 2 * VD:(tk + 1) * 2 * VD], pk[:, 0:2 * VD]),
                        inc="act", waits=[t_v])
                    rel[bank] = t_cp
            t_kvb = P.tok("act")

            for qt in range(NT):
                tiles = p1_tiles(qt)
                nE = len(tiles)
                c0 = qt * 8
                for ti, (k0, w) in enumerate(tiles):
                    bank = ti % 4
                    pq = ps[bank]
                    P.emit("pe", lambda h, k0=k0, w=w, pq=pq, qt=qt, hh=hh: h.matmul(
                        pq[:, 0:w], qraw[0:64, qt * 128:(qt + 1) * 128],
                        knope_h[hh][:, k0:k0 + w],
                        start=True, stop=False),
                        waits=[t_qf, t_kvb, t_kpe, rel.get(bank)])
                    P.emit("pe", lambda h, k0=k0, w=w, pq=pq, qt=qt: h.matmul(
                        pq[:, 0:w], qpe_aug[0:65, qt * 128:(qt + 1) * 128],
                        kpe_aug[0:65, k0:k0 + w], start=False, stop=True), inc="pe")
                    t_s = P.tok("pe")
                    if ti == nE - 1:
                        msk = m1_256 if w == 256 else m1_512
                        t_s = P.emit("dve", lambda h, w=w, pq=pq, msk=msk:
                                     h.tensor_add(pq[:, 0:w], pq[:, 0:w], msk[:, 0:w]),
                                     inc="dve", waits=[t_s] + t_bconst)
                    t_mk = P.emit("dve", lambda h, w=w, pq=pq, ci=c0 + ti:
                                  h.reduce_max(mstats[:, ci:ci + 1], pq[:, 0:w],
                                               axis=AX.X, negate=True), inc="dve",
                                  waits=[t_s])
                    t_e = P.emit("act", lambda h, w=w, pq=pq, ci=c0 + ti: h.activation(
                        escr[:, 0:w], pq[:, 0:w], AF.Exp,
                        bias=mstats[:, ci:ci + 1],
                        accum_out=zstats[:, ci:ci + 1]), inc="act",
                        waits=[t_mk, att_rel[0]])
                    rel[bank] = t_e
                t_z = P.tok("act")
                P.emit("dve", lambda h, nE=nE, c0=c0, qt=qt: h.tensor_reduce(
                    mins[:, qt:qt + 1], mstats[:, c0:c0 + nE], AX.X, ALU.min),
                    inc="dve", waits=[t_z])
                P.emit("dve", lambda h, qt=qt: h.tensor_copy(
                    minbf[:, qt:qt + 1], mins[:, qt:qt + 1]), inc="dve",
                    waits=[P.tok("dve")])
                t_mr = P.emit("dve", lambda h, qt=qt: h.tensor_copy(
                    minr[:, qt:qt + 1], minbf[:, qt:qt + 1]), inc="dve",
                    waits=[P.tok("dve")])
                t_t8 = P.emit("act", lambda h, nE=nE, c0=c0, qt=qt: h.activation(
                    t8a[:, c0:c0 + nE], mstats[:, c0:c0 + nE], AF.Exp,
                    bias=minr[:, qt:qt + 1], scale=-1.0), inc="act", waits=[t_mr])
                P.emit("dve", lambda h, nE=nE, c0=c0: h.tensor_mul(
                    zw8a[:, c0:c0 + nE], t8a[:, c0:c0 + nE], zstats[:, c0:c0 + nE]),
                    inc="dve", waits=[t_t8])
                P.emit("dve", lambda h, nE=nE, c0=c0, qt=qt: h.reduce_sum(
                    zca[:, qt:qt + 1], zw8a[:, c0:c0 + nE], axis=AX.X), inc="dve",
                    waits=[P.tok("dve")])
                t_zi = P.emit("dve", lambda h, qt=qt: h.reciprocal(
                    zia[:, qt:qt + 1], zca[:, qt:qt + 1]), inc="dve",
                    waits=[P.tok("dve")])
                bank = 6 + (qt % 2)
                t_tr = P.emit("pe", lambda h, qt=qt, bank=bank: h.transpose(
                    ps[bank][0:1, 0:128], minr[:, qt:qt + 1], id_f32[:]), inc="pe",
                    waits=[t_mr, rel.get(bank)])
                t_mrow = P.emit("act", lambda h, qt=qt, bank=bank: h.copy(
                    qpe_aug[64:65, qt * 128:(qt + 1) * 128], ps[bank][0:1, 0:128]),
                    inc="act", waits=[t_tr])
                t_tr2 = P.emit("pe", lambda h, qt=qt, bank=bank: h.transpose(
                    ps[bank][0:1, 128:256], zia[:, qt:qt + 1], id_f32[:]), inc="pe",
                    waits=[t_zi, t_mrow])
                t_zrow = P.emit("act", lambda h, qt=qt, bank=bank: h.copy(
                    zrow[0:1, qt * 128:(qt + 1) * 128], ps[bank][0:1, 128:256]),
                    inc="act", waits=[t_tr2])
                rel[bank] = t_zrow
            t_p1 = P.tok("act")

            for qb in range(4):
                nkt = 8 * qb + 8
                pend = []
                t_av = None

                def emit_av(kt, t_e, nkt=None):
                    return P.emit("pe", lambda h, kt=kt, nkt=nkt, hh=hh: h.matmul(
                        ps[4][:, :],
                        v2[:, kt * 2 * VD + hh * VD:kt * 2 * VD + (hh + 1) * VD],
                        attnT[kt % 4][:], start=(kt == 0), stop=(kt == nkt - 1)),
                        inc="pe", waits=[t_e] + ([rel.get(4)] if kt == 0 else []))

                for kt in range(nkt):
                    bank = kt % 4
                    pq = ps[bank]
                    P.emit("pe", lambda h, kt=kt, pq=pq, qb=qb, hh=hh: h.matmul(
                        pq[:, :], knope_h[hh][:, kt * 128:(kt + 1) * 128],
                        qraw[0:64, qb * 512:(qb + 1) * 512], start=True, stop=False),
                        waits=[t_p1, rel.get(bank)])
                    P.emit("pe", lambda h, kt=kt, pq=pq, qb=qb: h.matmul(
                        pq[:, :], kpe_aug[0:65, kt * 128:(kt + 1) * 128],
                        qpe_aug[0:65, qb * 512:(qb + 1) * 512],
                        start=False, stop=True), inc="pe")
                    t_s = P.tok("pe")
                    dd = kt - 8 * qb
                    if dd >= 0:
                        t_s = P.emit("dve", lambda h, pq=pq, dd=dd: h.tensor_add(
                            pq[:, :], pq[:, :], m2_sb[:, dd * 512:(dd + 1) * 512]),
                            inc="dve", waits=[t_s] + t_bconst)
                    t_e = P.emit("act", lambda h, pq=pq, kt=kt: h.activation(
                        attnT[kt % 4][:], pq[:, :], AF.Exp), inc="act",
                        waits=[t_s, att_rel[kt % 4]])
                    rel[bank] = t_e
                    pend.append((kt, t_e))
                    if len(pend) > 2:
                        kt0, te0 = pend.pop(0)
                        att_rel[kt0 % 4] = emit_av(kt0, te0, nkt)
                for kt0, te0 in pend:
                    att_rel[kt0 % 4] = emit_av(kt0, te0, nkt)
                t_av = P.tok("pe")
                t_zb = P.emit("pe", lambda h, qb=qb: h.matmul(
                    ps[5][:, :], ones_bf[0:1, :], zrow[0:1, qb * 512:(qb + 1) * 512],
                    start=True, stop=True), inc="pe", waits=[rel.get(5)])
                t_zc = P.emit("act", lambda h: h.copy(zbc[:], ps[5][:, :]), inc="act",
                              waits=[t_zb])
                rel[5] = t_zc
                t_o = P.emit("dve", lambda h, h_=h_, qb=qb: h.tensor_mul(
                    o_all[:, h_ * TOK + qb * 512:h_ * TOK + (qb + 1) * 512],
                    ps[4][:, :], zbc[:]), inc="dve", waits=[t_av, t_zc])
                rel[4] = t_o
        P.realize(esG)
        esR4.close()
        esR2.close()

        if upto == "B":
            for h_ in range(nheads):
                dma("pool", "qo", out.ap()[h_ * 128:(h_ + 1) * 128, :],
                    o_all[:, h_ * TOK:(h_ + 1) * TOK])
            if nheads == 1:
                dma("pool", "qo", out.ap()[256:384, :], qraw[:, :])
                dma("pool", "qo", out.ap()[384:449, :], qpe_aug[0:65, :])
                dma("pool", "qo", out.ap()[512:576, :], knope_h[0][:, 0:2048])
                dma("pool", "qo", out.ap()[640:768, :], v2[:, 0:2048])
                dma("pool", "qo", out.ap()[800:801, :], zrow[0:1, :])
                dma("pool", "qo", out.ap()[832:960, :], qaT[:, TOK:2 * TOK])
                dma("pool", "qo", out.ap()[960:1002, :], qaT[0:42, 5 * TOK:6 * TOK])
                dma("pool", "qo", out.ap()[1024:1152, 0:768], qbw_h[0][:, :])
            P.emit("sp", lambda h: h.nop(), waits=qall("qo"))
            P.realize(esG)
            return nc

        esR5 = contextlib.ExitStack()
        ow_sb = sbuf(esR5, "ow_sb", (128, KC * D), BF16)
        xtC = [sbuf(esR5, f"xtC{i}", (128, D), F32) for i in range(2)]
        x1t = [sbuf(esR5, f"x1t{i}", (128, D), F32) for i in range(2)]
        for c in range(KC):
            dma("pool", "qg",
                ow_sb[:].rearrange("p (c n) -> p c n", c=KC)[:, c:c + 1, :],
                ap("ow_t")[:, c:c + 1, :])
        t_ow = qall("qg")
        xrelC = [None, None]
        x1rel = [None, None]
        for tt in range(NT):
            xb = xtC[tt % 2]
            t_x = dma("sp", "qs", xb[:], ap("xq")[tt * 128:(tt + 1) * 128, :],
                      waits=[xrelC[tt % 2]])
            for h_ in range(NH):
                for hcg in range(4):
                    P.emit("pe", lambda h, h_=h_, hcg=hcg, tt=tt: h.matmul(
                        ps[hcg][:, :],
                        o_all[:, h_ * TOK + tt * 128:h_ * TOK + (tt + 1) * 128],
                        ow_sb[:, h_ * D + hcg * 512:h_ * D + (hcg + 1) * 512],
                        start=(h_ == 0), stop=(h_ == NH - 1)),
                        inc="pe" if (h_ == NH - 1 and hcg == 3) else None,
                        waits=(t_ow + [rel.get(i) for i in range(4)])
                        if (h_ == 0 and hcg == 0) else ())
            t_o = P.tok("pe")
            x1b = x1t[tt % 2]
            t_a = None
            for hcg in range(4):
                t_a = P.emit("dve", lambda h, hcg=hcg, x1b=x1b, xb=xb: h.tensor_add(
                    x1b[:, hcg * 512:(hcg + 1) * 512], ps[hcg][:, :],
                    xb[:, hcg * 512:(hcg + 1) * 512]), inc="dve",
                    waits=[t_o, t_x, x1rel[tt % 2]])
            for i in range(4):
                rel[i] = t_a
            xrelC[tt % 2] = t_a
            t_st = dma("sp", "qs", x1d.ap()[tt * 128:(tt + 1) * 128, :], x1b[:],
                       waits=[t_a])
            x1rel[tt % 2] = t_st
        t_x1d = qall("qs")
        P.realize(esG)
        esR5.close()
        esR1.close()

        if upto == "C":
            for tt in range(NT):
                dma("sp", "qo", out.ap()[tt * 128:(tt + 1) * 128, :],
                    x1d.ap()[tt * 128:(tt + 1) * 128, :], waits=t_x1d)
            P.emit("sp", lambda h: h.nop(), waits=qall("qo"))
            P.realize(esG)
            return nc

        esR6 = contextlib.ExitStack()
        x1nT = sbuf(esR6, "x1nT", (128, KC * 1024), BF16)
        z_e = sbuf(esR6, "z_e", (128, IT_MOE * 1024), BF16)
        wgl = [sbuf(esR6, f"wgl{i}", (128, KC * 128), BF16) for i in range(2)]
        wul = [sbuf(esR6, f"wul{i}", (128, KC * 128), BF16) for i in range(2)]
        wdr = [sbuf(esR6, f"wdr{i}", (128, IT_MOE * 1024), BF16) for i in range(2)]
        xg = sbuf(esR6, "xg", (128, 4 * D), BF16)
        xgT = [sbuf(esR6, f"xgT{i}", (128, KC * 512), BF16) for i in range(2)]
        stage = [sbuf(esR6, f"stage{i}", (128, D), F32) for i in range(2)]
        combg = [sbuf(esR6, f"combg{i}", (128, 4 * E), F32) for i in range(2)]
        idxsb = [sbuf(esR6, f"idxsb{i}", (128, 4), I32) for i in range(2)]
        bigi = sbuf(esR6, "bigi", (128, 64), I32)
        rowid = sbuf(esR6, "rowid", (128, 16), I32)
        lgmx = sbuf(esR6, "lgmx", (128, 1), F32)
        pun = sbuf(esR6, "pun", (128, 16), F32)
        rank = sbuf(esR6, "rank", (128, 16), F32)
        tcmp = sbuf(esR6, "tcmp", (128, 16), F32)
        mask4l = [sbuf(esR6, f"mask4_{i}", (128, 16), F32) for i in range(2)]
        combl = [sbuf(esR6, f"comb_{i}", (128, 16), F32) for i in range(2)]
        den = sbuf(esR6, "den", (128, 1), F32)
        dinv = sbuf(esR6, "dinv", (128, 1), F32)
        sg = sbuf(esR6, "sg", (128, 8), F32)
        silu_s = sbuf(esR6, "silu_s", (128, 512), BF16)
        tmul = sbuf(esR6, "tmul", (128, 512), BF16)
        lx_sb = sbuf(esR6, "lx_sb", (128, 128), F32)
        CAP = 512
        onesc = sbuf(esR6, "onesc", (128, 1), F32)
        onesr = sbuf(esR6, "onesr", (1, 128), F32)
        base_r = sbuf(esR6, "base_r", (1, 16), F32)
        slotf = sbuf(esR6, "slotf", (128, 16), F32)
        sltmp = sbuf(esR6, "sltmp", (128, 16), F32)
        sltmp2 = sbuf(esR6, "sltmp2", (128, 16), F32)
        ovf = sbuf(esR6, "ovf", (128, 16), F32)
        slot_il = [sbuf(esR6, f"slot_i{i}", (128, 16), I32) for i in range(2)]

        dma("sp", "qs", lx_sb[:], ap("lx"))
        t_lxq = qall("qs")
        rbc_cap = esR6.enter_context(nc.gpsimd.register("rbc_cap"))
        rbc_tok = esR6.enter_context(nc.gpsimd.register("rbc_tok"))
        P.emit("pool", lambda h: h.reg_mov(rbc_cap, CAP - 1))
        P.emit("pool", lambda h: h.reg_mov(rbc_tok, TOK - 1))
        P.emit("dve", lambda h: h.memset(onesc[:], 1.0), inc="dve")
        P.emit("dve", lambda h: h.memset(onesr[:], 1.0), inc="dve")
        t_ones = P.tok("dve")
        P.emit("pool", lambda h: h.memset(bigi[:], 2 ** 30), inc="pool")
        P.emit("pool", lambda h: h.iota(rowid[:], [[128, 16]], base=0,
                                        channel_multiplier=1), inc="pool")
        t_rowid = P.tok("pool")

        def idma(fn, waits, accum=False):
            s_name = f"qg{qrot['qg']}"
            qrot["qg"] = (qrot["qg"] + 1) % QROT
            return P.emit("pool", fn, inc=s_name, inck=16, waits=waits)

        addch = None
        stage_rel = [None, None]
        gath_rel = None
        xgt_rel = [None, None]
        idx_rel = [None, None]
        cg_rel = [None, None]
        for hf in range(2):
            toff = hf * 8
            P.emit("dve", lambda h: h.memset(base_r[:], 0.0), inc="dve",
                   waits=[P.tok("pe")])
            t_base = P.tok("dve")
            for tt8 in range(8):
                tt = toff + tt8
                xb = stage[tt8 % 2]
                t_x = dma("sp", "qs", xb[:], x1d.ap()[tt * 128:(tt + 1) * 128, :],
                          waits=t_x1d + [stage_rel[tt8 % 2], addch])
                t_xn = emit_ln(xb, t_x, extra=[P.tok("pe")])
                stage_rel[tt8 % 2] = t_xn
                dma("sp", "qs", x1nd.ap()[tt * 128:(tt + 1) * 128, :], xn[:],
                    waits=[t_xn])
                emit_transposes(
                    xn, KC, None,
                    lambda c, tt8=tt8: x1nT[:, c * 1024 + tt8 * 128:
                                            c * 1024 + (tt8 + 1) * 128], t_xn)
            t_nT = P.tok("act")
            t_x1nd = qall("qs")

            t_idxinit = dma("pool", "qg",
                            idxd.ap()[hf].rearrange("e c -> (e c)")
                            .rearrange("(p g) -> p g", p=128),
                            bigi[:], waits=[t_rowid, addch])

            m4rel = [None, None]
            cbrel = [None, None]
            sirel = [None, None]
            for tt8 in range(8):
                tt = toff + tt8
                mask4 = mask4l[tt8 % 2]
                comb = combl[tt8 % 2]
                slot_i = slot_il[tt8 % 2]
                for c in range(KC):
                    P.emit("pe", lambda h, c=c, tt8=tt8: h.matmul(
                        ps[0][:, 0:E],
                        x1nT[:, c * 1024 + tt8 * 128:c * 1024 + (tt8 + 1) * 128],
                        router_sb[:, c * E:(c + 1) * E],
                        start=(c == 0), stop=(c == KC - 1)),
                        inc="pe" if c == KC - 1 else None,
                        waits=[t_nT, rel.get(0)] if c == 0 else ())
                t_l = P.tok("pe")
                P.emit("dve", lambda h: h.reduce_max(lgmx[:], ps[0][:, 0:E],
                                                     axis=AX.X, negate=True),
                       inc="dve", waits=[t_l, P.tok("act")])
                t_nm = P.tok("dve")
                t_p = P.emit("act", lambda h: h.activation(
                    pun[:], ps[0][:, 0:E], AF.Exp, bias=lgmx[:]), inc="act",
                    waits=[t_nm])
                rel[0] = t_p
                P.emit("dve", lambda h: h.memset(rank[:], 0.0), inc="dve",
                       waits=[t_p])
                for e2 in range(E):
                    P.emit("dve", lambda h, e2=e2: h.tensor_scalar(
                        tcmp[:], pun[:], pun[:, e2:e2 + 1], None, ALU.is_lt),
                        inc="dve", waits=[P.tok("dve")])
                    P.emit("dve", lambda h: h.tensor_add(rank[:], rank[:], tcmp[:]),
                           inc="dve", waits=[P.tok("dve")])
                t_m4 = P.emit("dve", lambda h, mask4=mask4: h.tensor_scalar(
                    mask4[:], rank[:], 3.0, None, ALU.is_le), inc="dve",
                    waits=[P.tok("dve"), m4rel[tt8 % 2]])
                P.emit("dve", lambda h, comb=comb, mask4=mask4: h.tensor_mul(
                    comb[:], pun[:], mask4[:]),
                    inc="dve", waits=[t_m4, cbrel[tt8 % 2]])
                P.emit("dve", lambda h, comb=comb: h.reduce_sum(
                    den[:], comb[:], axis=AX.X), inc="dve", waits=[P.tok("dve")])
                P.emit("dve", lambda h: h.reciprocal(dinv[:], den[:]), inc="dve",
                       waits=[P.tok("dve")])
                t_cb = P.emit("dve", lambda h, comb=comb: h.tensor_scalar(
                    comb[:], comb[:], dinv[:], None, ALU.mult), inc="dve",
                    waits=[P.tok("dve")])
                cbrel[tt8 % 2] = dma("sp", "qs",
                                     combd.ap()[tt * 128:(tt + 1) * 128, :],
                                     comb[:], waits=[t_cb])
                P.emit("pe", lambda h, mask4=mask4: h.matmul(
                    ps[2][:, 0:E], lx_sb[:], mask4[:], start=True, stop=False),
                    waits=[t_m4, rel.get(2)] + t_lxq)
                P.emit("pe", lambda h: h.matmul(
                    ps[2][:, 0:E], onesr[:], base_r[:], start=False, stop=True),
                    inc="pe", waits=[t_ones, t_base])
                t_sl = P.tok("pe")
                P.emit("pe", lambda h, mask4=mask4: h.matmul(
                    ps[3][0:1, 0:E], onesc[:], mask4[:], start=True, stop=True),
                    inc="pe", waits=[rel.get(3), t_ones])
                t_cnt = P.tok("pe")
                m4rel[tt8 % 2] = t_cnt
                P.emit("dve", lambda h: h.tensor_add(base_r[:], base_r[:],
                                                     ps[3][0:1, 0:E]), inc="dve",
                       waits=[t_cnt])
                t_base = P.tok("dve")
                rel[3] = t_base
                P.emit("dve", lambda h, mask4=mask4: h.tensor_scalar(
                    sltmp[:], mask4[:], -1e9, 1e9, ALU.mult, ALU.add), inc="dve",
                    waits=[t_m4])
                P.emit("dve", lambda h: h.tensor_scalar(
                    ovf[:], ps[2][:, 0:E], float(CAP) - 0.5, None, ALU.is_ge),
                    inc="dve", waits=[t_sl])
                P.emit("dve", lambda h: h.tensor_scalar(
                    ovf[:], ovf[:], 1e9, None, ALU.mult), inc="dve",
                    waits=[P.tok("dve")])
                P.emit("dve", lambda h: h.tensor_scalar(
                    slotf[:], ps[2][:, 0:E], 512.0, None, ALU.is_ge), inc="dve",
                    waits=[P.tok("dve")])
                P.emit("dve", lambda h: h.scalar_tensor_tensor(
                    sltmp2[:], slotf[:], -512.0, ps[2][:, 0:E], ALU.mult, ALU.add),
                    inc="dve", waits=[P.tok("dve")])
                P.emit("dve", lambda h: h.tensor_scalar(
                    slotf[:], sltmp2[:], 256.0, None, ALU.is_ge), inc="dve",
                    waits=[P.tok("dve")])
                P.emit("dve", lambda h: h.scalar_tensor_tensor(
                    sltmp2[:], slotf[:], -256.0, sltmp2[:], ALU.mult, ALU.add),
                    inc="dve", waits=[P.tok("dve")])
                P.emit("dve", lambda h: h.tensor_scalar(
                    slotf[:], sltmp2[:], 128.0, None, ALU.is_ge), inc="dve",
                    waits=[P.tok("dve")])
                P.emit("dve", lambda h: h.scalar_tensor_tensor(
                    sltmp2[:], slotf[:], -128.0, sltmp2[:], ALU.mult, ALU.add),
                    inc="dve", waits=[P.tok("dve")])
                P.emit("dve", lambda h: h.tensor_tensor(
                    slotf[:], ps[2][:, 0:E], sltmp2[:], ALU.subtract), inc="dve",
                    waits=[P.tok("dve")])
                t_r0 = P.emit("dve", lambda h: h.tensor_scalar(
                    slotf[:], slotf[:], 1.0 / 128.0, None, ALU.mult), inc="dve",
                    waits=[P.tok("dve")])
                rel[2] = t_r0
                P.emit("dve", lambda h: h.tensor_scalar(
                    sltmp2[:], sltmp2[:], 4.0, None, ALU.mult), inc="dve",
                    waits=[t_r0])
                P.emit("dve", lambda h: h.tensor_add(slotf[:], slotf[:], sltmp2[:]),
                       inc="dve", waits=[P.tok("dve")])
                P.emit("dve", lambda h, mask4=mask4: h.tensor_mul(
                    slotf[:], slotf[:], mask4[:]),
                    inc="dve", waits=[P.tok("dve")])
                P.emit("dve", lambda h: h.tensor_add(slotf[:], slotf[:], sltmp[:]),
                       inc="dve", waits=[P.tok("dve")])
                P.emit("dve", lambda h: h.tensor_add(slotf[:], slotf[:], ovf[:]),
                       inc="dve", waits=[P.tok("dve")])
                t_si = P.emit("dve", lambda h, slot_i=slot_i: h.tensor_copy(
                    slot_i[:], slotf[:]),
                    inc="dve", waits=[P.tok("dve"), sirel[tt8 % 2]])
                t_sc_last = None
                for e2 in range(E):
                    t_sc_last = idma(lambda h, e2=e2, tt8=tt8, hf=hf,
                                     slot_i=slot_i: h.indirect_dma_start(
                        idxd.ap().rearrange("a e c -> (a e c)")
                        .rearrange("(x u) -> x u", u=1),
                        IndirectOffsetOnAxis(slot_i[:, e2:e2 + 1], 0),
                        rowid[:, toff + tt8:toff + tt8 + 1], None,
                        element_offset=(hf * E + e2) * CAP,
                        bounds_check=rbc_cap, oob_is_err=False),
                        waits=[t_si, t_idxinit])
                sirel[tt8 % 2] = t_sc_last
            t_scat = qall("qg")
            t_combd = qall("qs")

            for tt8 in range(8):
                for c in range(KC):
                    P.emit("pe", lambda h, c=c, tt8=tt8: h.matmul(
                        ps[1][:, 0:1],
                        x1nT[:, c * 1024 + tt8 * 128:c * 1024 + (tt8 + 1) * 128],
                        shgw_sb[:, c:c + 1], start=(c == 0), stop=(c == KC - 1)),
                        inc="pe" if c == KC - 1 else None,
                        waits=[rel.get(1)] if c == 0 else ())
                t_sgm = P.tok("pe")
                t_sg = P.emit("act", lambda h, tt8=tt8: h.activation(
                    sg[:, tt8:tt8 + 1], ps[1][:, 0:1], AF.Sigmoid), inc="act",
                    waits=[t_sgm])
                rel[1] = t_sg
            t_sgall = P.tok("act")

            def load_gu2(kind, idx, buf, wtok):
                if kind == "moe":
                    e_, it = idx
                    ta = dma("sp", "qs", wgl[buf][:], wg_bf.ap()[e_, it],
                             waits=[wtok, cast_tok[("g", e_)]])
                    tb = dma("sp", "qs", wul[buf][:], wu_bf.ap()[e_, it],
                             waits=[cast_tok[("u", e_)]])
                else:
                    it = idx
                    ta = dma("sp", "qs", wgl[buf][:], shg_bf.ap()[it],
                             waits=[wtok, cast_tok[("shg", it // IT_MOE)]])
                    tb = dma("sp", "qs", wul[buf][:], shu_bf.ap()[it],
                             waits=[cast_tok[("shu", it // IT_MOE)]])
                return [ta, tb]

            def load_wd2(kind, idx, buf, wtok):
                if kind == "moe":
                    e_, hh2 = idx
                    return [dma("act", "qa", wdr[buf][:], wd_bf.ap()[e_, hh2],
                                waits=[wtok, cast_tok[("d", e_)]])]
                grp, hh2 = idx
                return [dma("act", "qa", wdr[buf][:], shd_bf.ap()[grp, hh2],
                            waits=[wtok, cast_tok[("shd", grp)]])]

            gu_seq = ([("moe", (e_, it)) for e_ in range(E) for it in range(IT_MOE)]
                      + [("sh", g * IT_MOE + it) for g in range(4)
                         for it in range(IT_MOE)])
            wd_seq = ([("moe", (e_, hh2)) for e_ in range(E) for hh2 in range(2)]
                      + [("sh", (g, hh2)) for g in range(4) for hh2 in range(2)])
            t_wgd = {}
            t_wdd = {}
            for gi in range(2):
                t_wgd[gi] = load_gu2(gu_seq[gi][0], gu_seq[gi][1], gi % 2,
                                     P.tok("pe"))
            for di in range(2):
                t_wdd[di] = load_wd2(wd_seq[di][0], wd_seq[di][1], di % 2,
                                     P.tok("pe"))
            gi_c = [0]
            di_c = [0]
            t_idx = {}
            t_gath = {}

            def emit_gather(e_, pb):
                t_idx[e_] = dma("sp", "qs", idxsb[pb][:],
                                idxd.ap()[hf, e_].rearrange("(p g) -> p g", p=128),
                                waits=t_scat + [idx_rel[pb]])
                gtk = []
                for g_ in range(4):
                    gtk.append(idma(lambda h, g_=g_, pb=pb: h.indirect_dma_start(
                        xg[:, g_ * D:(g_ + 1) * D], None,
                        x1nd.ap(), IndirectOffsetOnAxis(idxsb[pb][:, g_:g_ + 1], 0),
                        bounds_check=rbc_tok, oob_is_err=False),
                        waits=[t_idx[e_], gath_rel] + t_x1nd))
                    gtk.append(idma(lambda h, g_=g_, pb=pb: h.indirect_dma_start(
                        combg[pb][:, g_ * E:(g_ + 1) * E], None,
                        combd.ap(), IndirectOffsetOnAxis(idxsb[pb][:, g_:g_ + 1], 0),
                        bounds_check=rbc_tok, oob_is_err=False),
                        waits=[t_idx[e_], cg_rel[pb]] + t_combd))
                t_gath[e_] = gtk

            emit_gather(0, 0)
            for e_ in range(E):
                pb = e_ % 2
                t_tr_last = None
                for g_ in range(4):
                    for c in range(KC):
                        bank = 6 + (c % 2)
                        t_t = P.emit("pe", lambda h, g_=g_, c=c, bank=bank:
                                     h.transpose(
                            ps[bank].bitcast(BF16)[:, 0:128],
                            xg[:, g_ * D + c * 128:g_ * D + (c + 1) * 128],
                            id_bf[:]), inc="pe",
                            waits=[rel.get(bank), xgt_rel[pb]] +
                            (t_gath[e_] if (g_ == 0 and c == 0) else []))
                        t_tr_last = P.emit("act", lambda h, g_=g_, c=c, bank=bank,
                                           pb=pb: h.copy(
                            xgT[pb][:, c * 512 + g_ * 128:c * 512 + (g_ + 1) * 128],
                            ps[bank].bitcast(BF16)[:, 0:128]), inc="act",
                            waits=[t_t])
                        rel[bank] = t_tr_last
                gath_rel = P.tok("pe")
                if e_ + 1 < E:
                    emit_gather(e_ + 1, (e_ + 1) % 2)
                for it in range(IT_MOE):
                    gi = gi_c[0]
                    buf = gi % 2
                    bs = [0, 1] if (gi % 2 == 0) else [2, 3]
                    t_w = t_wgd[gi]
                    for c in range(KC):
                        P.emit("pe", lambda h, c=c, bs=bs, buf=buf, pb=pb: h.matmul(
                            ps[bs[0]][:, :], wgl[buf][:, c * 128:(c + 1) * 128],
                            xgT[pb][:, c * 512:(c + 1) * 512],
                            start=(c == 0), stop=(c == KC - 1)),
                            waits=(t_w + [t_tr_last] + [rel.get(i) for i in bs])
                            if c == 0 else ())
                        P.emit("pe", lambda h, c=c, bs=bs, buf=buf, pb=pb: h.matmul(
                            ps[bs[1]][:, :], wul[buf][:, c * 128:(c + 1) * 128],
                            xgT[pb][:, c * 512:(c + 1) * 512],
                            start=(c == 0), stop=(c == KC - 1)),
                            inc="pe" if c == KC - 1 else None)
                    t_gu = P.tok("pe")
                    if gi + 2 < len(gu_seq):
                        t_wgd[gi + 2] = load_gu2(gu_seq[gi + 2][0],
                                                 gu_seq[gi + 2][1], buf, t_gu)
                    t_si2 = P.emit("act", lambda h, bs=bs: h.activation(
                        silu_s[:], ps[bs[0]][:, :], AF.Silu), inc="act",
                        waits=[t_gu, P.tok("dve")])
                    rel[bs[0]] = t_si2
                    P.emit("dve", lambda h, bs=bs: h.tensor_copy(
                        tmul[:], ps[bs[1]][:, :]), inc="dve",
                        waits=[t_gu, P.tok("act")])
                    t_tm = P.tok("dve")
                    rel[bs[1]] = t_tm
                    P.emit("dve", lambda h, it=it: h.tensor_mul(
                        z_e[:, it * 1024:it * 1024 + 512], tmul[:], silu_s[:]),
                        inc="dve", waits=[t_tm, t_si2])
                    gi_c[0] += 1
                t_z = P.tok("dve")
                xgt_rel[pb] = t_z
                for hh2 in range(2):
                    di = di_c[0]
                    dbuf = di % 2
                    t_wdt = t_wdd[di]
                    t_d = None
                    for st in range(4):
                        dbs = [4 + (st % 2) * 2, 5 + (st % 2) * 2]
                        for it in range(IT_MOE):
                            for hq in range(2):
                                bank = dbs[hq]
                                P.emit("pe", lambda h, it=it, st=st, hq=hq,
                                       dbuf=dbuf, bank=bank: h.matmul(
                                    ps[bank][:, :],
                                    z_e[:, it * 1024 + st * 128:
                                        it * 1024 + (st + 1) * 128],
                                    wdr[dbuf][:, it * 1024 + hq * 512:
                                              it * 1024 + hq * 512 + 512],
                                    start=(it == 0), stop=(it == IT_MOE - 1)),
                                    inc="pe" if (it == IT_MOE - 1 and hq == 1)
                                    else None,
                                    waits=([t_z] + t_wdt +
                                           [rel.get(b) for b in dbs])
                                    if (it == 0 and hq == 0) else ())
                        t_d = P.tok("pe")
                        stg = stage[st % 2]
                        t_ev = None
                        for hq in range(2):
                            t_ev = P.emit("dve", lambda h, hq=hq, st=st, e_=e_,
                                          dbs=dbs, stg=stg, pb=pb: h.tensor_scalar(
                                stg[:, hq * 512:hq * 512 + 512], ps[dbs[hq]][:, :],
                                combg[pb][:, st * E + e_:st * E + e_ + 1], None,
                                ALU.mult), inc="dve",
                                waits=[t_d, stage_rel[st % 2]])
                            rel[dbs[hq]] = t_ev
                        addch = idma(lambda h, hh2=hh2, stg=stg, pb=pb, st=st:
                                     h.indirect_dma_start(
                            x1d.ap(),
                            IndirectOffsetOnAxis(idxsb[pb][:, st:st + 1], 0),
                            stg[:, 0:1024], None,
                            element_offset=hh2 * 1024,
                            compute_op=ALU.add, bounds_check=rbc_tok,
                            oob_is_err=False),
                            waits=[t_ev, addch])
                        stage_rel[st % 2] = addch
                    di_c[0] += 1
                    if di + 2 < len(wd_seq):
                        t_wdd[di + 2] = load_wd2(wd_seq[di + 2][0],
                                                 wd_seq[di + 2][1], dbuf, t_d)
                idx_rel[pb] = addch
                cg_rel[pb] = P.tok("dve")

            for grp in range(4):
                for itg in range(IT_MOE):
                    gi = gi_c[0]
                    buf = gi % 2
                    bs = [0, 1, 2, 3] if (gi % 2 == 0) else [4, 5, 6, 7]
                    t_w = t_wgd[gi]
                    for c in range(KC):
                        for bi, (pp, wsb, xo) in enumerate((
                                (ps[bs[0]], wgl[buf], 0), (ps[bs[1]], wgl[buf], 512),
                                (ps[bs[2]], wul[buf], 0), (ps[bs[3]], wul[buf], 512))):
                            P.emit("pe", lambda h, c=c, pp=pp, wsb=wsb, xo=xo:
                                   h.matmul(pp[:, :], wsb[:, c * 128:(c + 1) * 128],
                                            x1nT[:, c * 1024 + xo:c * 1024 + xo + 512],
                                            start=(c == 0), stop=(c == KC - 1)),
                                   inc="pe" if (c == KC - 1 and bi == 3) else None,
                                   waits=(t_w + [t_nT] + [rel.get(i) for i in bs])
                                   if (c == 0 and bi == 0) else ())
                    t_gu = P.tok("pe")
                    if gi + 2 < len(gu_seq):
                        t_wgd[gi + 2] = load_gu2(gu_seq[gi + 2][0],
                                                 gu_seq[gi + 2][1], buf, t_gu)
                    for nb in range(2):
                        zsl = slice(itg * 1024 + nb * 512, itg * 1024 + (nb + 1) * 512)
                        pg = (ps[bs[0]], ps[bs[1]])[nb]
                        pu = (ps[bs[2]], ps[bs[3]])[nb]
                        t_si2 = P.emit("act", lambda h, pg=pg: h.activation(
                            silu_s[:], pg[:, :], AF.Silu), inc="act",
                            waits=[t_gu, P.tok("dve")])
                        rel[bs[nb]] = t_si2
                        P.emit("dve", lambda h, pu=pu: h.tensor_copy(
                            tmul[:], pu[:, :]), inc="dve",
                            waits=[t_gu, P.tok("act")])
                        t_tm = P.tok("dve")
                        rel[bs[2 + nb]] = t_tm
                        P.emit("dve", lambda h, zsl=zsl: h.tensor_mul(
                            z_e[:, zsl], tmul[:], silu_s[:]), inc="dve",
                            waits=[t_tm, t_si2])
                    gi_c[0] += 1
                t_z = P.tok("dve")
                for hh2 in range(2):
                    di = di_c[0]
                    dbuf = di % 2
                    t_wdt = t_wdd[di]
                    t_d = None
                    for pair in range(4):
                        dbs = ([0, 1, 2, 3] if (pair % 2 == (di % 2))
                               else [4, 5, 6, 7])
                        for it in range(IT_MOE):
                            for sub in range(2):
                                tt8 = pair * 2 + sub
                                for hq in range(2):
                                    bank = dbs[sub * 2 + hq]
                                    P.emit("pe", lambda h, it=it, tt8=tt8, hq=hq,
                                           dbuf=dbuf, bank=bank: h.matmul(
                                        ps[bank][:, :],
                                        z_e[:, it * 1024 + tt8 * 128:
                                            it * 1024 + (tt8 + 1) * 128],
                                        wdr[dbuf][:, it * 1024 + hq * 512:
                                                  it * 1024 + hq * 512 + 512],
                                        start=(it == 0), stop=(it == IT_MOE - 1)),
                                        inc="pe" if (it == IT_MOE - 1 and sub == 1
                                                     and hq == 1) else None,
                                        waits=([t_z] + t_wdt +
                                               [rel.get(b) for b in dbs])
                                        if (it == 0 and sub == 0 and hq == 0)
                                        else ())
                        t_d = P.tok("pe")
                        for sub in range(2):
                            tt8 = pair * 2 + sub
                            stg = stage[sub]
                            t_ev = None
                            for hq in range(2):
                                bank = dbs[sub * 2 + hq]
                                t_ev = P.emit("dve", lambda h, tt8=tt8, bank=bank,
                                              hq=hq, stg=stg: h.tensor_scalar(
                                    stg[:, hq * 512:hq * 512 + 512], ps[bank][:, :],
                                    sg[:, tt8:tt8 + 1], None, ALU.mult),
                                    inc="dve",
                                    waits=[t_d, t_sgall, stage_rel[sub]])
                                rel[bank] = t_ev
                            addch = P.emit("pool", lambda h, tt8=tt8, hh2=hh2,
                                           stg=stg, toff=toff: h.dma_start(
                                x1d.ap()[(toff + tt8) * 128:(toff + tt8 + 1) * 128,
                                         hh2 * 1024:(hh2 + 1) * 1024],
                                stg[:, 0:1024], accum_op=ALU.add),
                                inc=f"qg{qrot['qg']}", inck=16,
                                waits=[t_ev, addch])
                            qrot["qg"] = (qrot["qg"] + 1) % QROT
                            stage_rel[sub] = addch
                    di_c[0] += 1
                    if di + 2 < len(wd_seq):
                        t_wdd[di + 2] = load_wd2(wd_seq[di + 2][0],
                                                 wd_seq[di + 2][1], dbuf, t_d)
            for tt8 in range(8):
                tt = toff + tt8
                dma("sp", "qo", out.ap()[tt * 128:(tt + 1) * 128, :],
                    x1d.ap()[tt * 128:(tt + 1) * 128, :], waits=[addch])
            P.realize(esG)
        esR6.close()

        P.emit("sp", lambda h: h.nop(), waits=qall("qo"))
        P.realize(esG)
    return nc


def kernel(**inputs):
    pr, percore = _host_prepare(inputs)
    nc = build_nc()
    shared = {k: np.ascontiguousarray(v, dtype=np.float32) for k, v in pr.items()}
    in_maps = []
    for c in range(8):
        m = dict(shared)
        for k, v in percore[c].items():
            m[k] = np.ascontiguousarray(v, dtype=np.float32)
        in_maps.append(m)
    res = run_bass_kernel_spmd(nc, in_maps, core_ids=list(range(8)))
    full = np.empty((B, S, D), dtype=np.float32)
    for c in range(8):
        b, p = c // 2, c % 2
        o = np.asarray(res.results[c]["out"])
        for i in range(NT):
            g = 2 * i + p
            full[b, g * 128:(g + 1) * 128, :] = o[i * 128:(i + 1) * 128, :]
    return full


# version 09 (final)
# speedup vs baseline: 1.8740x; 1.1718x over previous
import contextlib
import math
import numpy as np
import concourse.bass as bass
import concourse.mybir as mybir
from concourse.bass import IndirectOffsetOnAxis
from concourse.bass_utils import run_bass_kernel_spmd

F32 = mybir.dt.float32
BF16 = mybir.dt.bfloat16
I32 = mybir.dt.int32
AF = mybir.ActivationFunctionType
ALU = mybir.AluOpType
AX = mybir.AxisListType

B, S, D, NH = 4, 4096, 2048, 16
ROPE, NOPE, VD = 64, 64, 128
QH = ROPE + NOPE
QL = D // 3
KVL = D // 8
THETA = 128000.0
SCALE = 1.0 / math.sqrt(QH)
EPS_LN = 1e-5
EPS_RMS = 1e-6
E = 16
I_MOE, I_SH = 1408, 5632
NT = 16
TOK = NT * 128
NKT = 32
CAP = 384
CAPG = CAP // 128
IT_MOE = I_MOE // 128
IT_SH = I_SH // 128
KC = D // 128
QLC = [128, 128, 128, 128, 128, 42]
NEG = -30000.0

ENGS = ("pe", "dve", "act", "pool", "sp")


def _rope_tables():
    inv = 1.0 / THETA ** (np.arange(0, ROPE, 2, dtype=np.float64) / ROPE)
    pos = np.arange(S, dtype=np.float64)
    fr = np.outer(inv, pos)
    cos = np.cos(np.concatenate([fr, fr], 0))
    sin = np.sin(np.concatenate([fr, fr], 0))
    sin_signed = sin.copy()
    sin_signed[:32] *= -1.0
    return cos.astype(np.float32), sin_signed.astype(np.float32)


_PERM = np.concatenate([np.arange(0, 64, 2), np.arange(1, 64, 2)])


def _tile_k(w):
    k, n = w.shape
    return np.ascontiguousarray(w.reshape(k // 128, 128, n).transpose(1, 0, 2))


def _tile_cols(w, ct):
    k, n = w.shape
    return np.ascontiguousarray(
        w.reshape(k // 128, 128, n // ct, ct).transpose(2, 1, 0, 3))


def _host_prepare(inp):
    cos, sin = _rope_tables()
    pr = {}
    pr["qaw_t"] = _tile_k(inp["L0_attn_q_a_w"])
    pr["lnq"] = inp["L0_attn_q_a_ln_w"].reshape(1, QL)
    qbw = np.array(inp["L0_attn_q_b_w"], dtype=np.float32)
    for h in range(NH):
        base = h * QH + NOPE
        qbw[:, base:base + ROPE] = qbw[:, base + _PERM]
    qbh = np.zeros((NH, 128, 6, 128), np.float32)
    for h in range(NH):
        for c in range(6):
            w = QLC[c]
            qbh[h, :w, c, :] = qbw[c * 128:c * 128 + w, h * QH:(h + 1) * QH]
    pr["qbw_h"] = qbh
    kvaw = np.array(inp["L0_attn_kv_a_w"], dtype=np.float32)
    kvaw[:, KVL:] = kvaw[:, KVL + _PERM]
    pr["kvaw_t"] = _tile_k(kvaw)
    pr["lnkv"] = inp["L0_attn_kv_a_ln_w"].reshape(1, KVL)
    kvb3 = np.asarray(inp["L0_attn_kv_b_w"]).reshape(KVL, NH, NOPE + VD)
    pr["kvb_nope"] = np.ascontiguousarray(kvb3[:, :, :NOPE].reshape(KVL, NH * NOPE))
    pr["kvb_v"] = np.ascontiguousarray(kvb3[:, :, NOPE:].reshape(KVL, NH * VD))
    pr["ow_t"] = _tile_k(inp["L0_attn_o_w"])
    pr["router_t"] = _tile_k(np.ascontiguousarray(np.asarray(inp["L0_ffn_router_w"]).T))
    pr["wg_t"] = np.stack([_tile_cols(inp["L0_ffn_w_gate"][e], 128) for e in range(E)])
    pr["wu_t"] = np.stack([_tile_cols(inp["L0_ffn_w_up"][e], 128) for e in range(E)])
    pr["wd_t"] = np.stack([
        np.ascontiguousarray(np.asarray(inp["L0_ffn_w_down"][e], np.float32)
                             .reshape(IT_MOE, 128, 2, 1024).transpose(2, 1, 0, 3)
                             .reshape(2, 128, IT_MOE * 1024)) for e in range(E)])
    pr["shg_t"] = _tile_cols(inp["L0_ffn_sh_gate"], 128)
    pr["shu_t"] = _tile_cols(inp["L0_ffn_sh_up"], 128)
    pr["shd_t"] = np.ascontiguousarray(
        np.asarray(inp["L0_ffn_sh_down"], np.float32)
        .reshape(4, IT_MOE, 128, 2, 1024).transpose(0, 3, 2, 1, 4)
        .reshape(4, 2, 128, IT_MOE * 1024))
    pr["shgw_t"] = _tile_k(inp["L0_ffn_sh_gate_w"])
    pr["id128"] = np.eye(128, dtype=np.float32)
    pr["ones_row"] = np.ones((1, 128), np.float32)
    mswap = np.zeros((64, 128), np.float32)
    for m in range(32):
        mswap[m, 96 + m] = 1.0
        mswap[32 + m, 64 + m] = 1.0
    pr["mswapT"] = np.ascontiguousarray(mswap.T)
    mshift = np.zeros((64, 128), np.float32)
    for m in range(64):
        mshift[m, 64 + m] = 1.0
    pr["mshiftT"] = np.ascontiguousarray(mshift.T)
    r64 = np.zeros((64, 64), np.float32)
    for m in range(32):
        r64[m, 32 + m] = 1.0
        r64[32 + m, m] = 1.0
    pr["r64T"] = np.ascontiguousarray(r64.T)
    pr["ck"] = np.ascontiguousarray(cos)
    pr["sk"] = np.ascontiguousarray(sin)
    pr["lx"] = np.triu(np.ones((128, 128), np.float32), k=1)
    oneh = np.zeros((E, E * 128), np.float32)
    for e in range(E):
        oneh[e, e * 128:(e + 1) * 128] = 1.0
    pr["onehot"] = oneh
    percore = []
    x = np.ascontiguousarray(np.asarray(inp["L0_attn_x"], dtype=np.float32))
    for c in range(8):
        b, p = c // 2, c % 2
        d = {}
        slots = np.array([2 * i + p for i in range(NT)])
        rows = (slots[:, None] * 128 + np.arange(128)[None, :]).reshape(-1)
        d["xq"] = np.ascontiguousarray(x[b][rows])
        d["xk"] = x[b]
        d["cq"] = np.ascontiguousarray(cos[:, rows])
        d["sq"] = np.ascontiguousarray(sin[:, rows])
        for W in (256, 512, 768, 1024):
            cc, rr = np.meshgrid(np.arange(W), np.arange(128))
            d[f"mask1_{W}"] = np.where(cc - rr > W - 256 + 128 * p, NEG, 0.0
                                       ).astype(np.float32)
        m2 = np.zeros((9, 128, 512), np.float32)
        for dd in range(8):
            for t in range(4):
                cc, rr = np.meshgrid(np.arange(128), np.arange(128))
                m2[dd][:, t * 128:(t + 1) * 128] = np.where(
                    (dd - 2 * t - p) * 128 + rr > cc, NEG, 0.0)
        m2[8][:] = NEG
        d["mask2"] = m2
        percore.append(d)
    return pr, percore


class Prog:
    def __init__(self, nc):
        self.nc = nc
        self.items = {e: [] for e in ENGS}
        self.sem = {}
        self.cnt = {}
        self.water = {e: {} for e in ENGS}

    def new_sem(self, name, es):
        self.sem[name] = es.enter_context(self.nc.semaphore(name))
        self.cnt[name] = 0

    def tok(self, name):
        return (name, self.cnt[name])

    paranoid = ()

    def emit(self, eng, fn, inc=None, inck=1, waits=()):
        if self.paranoid:
            waits = list(waits) + [self.tok(s) for s in self.paranoid]
        wl = []
        for tkn in waits:
            if tkn is None:
                continue
            s, v = tkn
            if v <= 0:
                continue
            w = self.water[eng]
            if w.get(s, 0) >= v:
                continue
            w[s] = v
            wl.append((self.sem[s], v))
        tk = None
        sem_h = None
        if inc is not None:
            self.cnt[inc] += inck
            tk = (inc, self.cnt[inc])
            sem_h = self.sem[inc]

        def wrapped(h, fn=fn, wl=wl, sem_h=sem_h, inck=inck):
            for (sh, v) in wl:
                h.wait_ge(sh, v)
            inst = fn(h)
            if sem_h is not None:
                inst.then_inc(sem_h, inck)
        self.items[eng].append(wrapped)
        return tk

    def realize(self, es):
        with self.nc.Block() as block:
            for eng, deco in (("pe", block.tensor), ("dve", block.vector),
                              ("act", block.scalar), ("pool", block.gpsimd),
                              ("sp", block.sync)):
                fns = self.items[eng]
                if fns:
                    deco(lambda h, fns=list(fns): [fn(h) for fn in fns])
        self.items = {e: [] for e in ENGS}
        for e in ENGS:
            for s in self.sem:
                if not s.startswith("q"):
                    self.water[e][s] = self.cnt[s]


def build_nc(upto="full", nheads=NH):
    nc = bass.Bass("TRN2", target_bir_lowering=False)
    dr = {}

    def din(name, shape):
        dr[name] = nc.dram_tensor(name, list(shape), F32, kind="ExternalInput")

    din("xq", (TOK, D)); din("xk", (S, D))
    din("qaw_t", (128, KC, QL)); din("lnq", (1, QL)); din("qbw_h", (NH, 128, 6, 128))
    din("kvaw_t", (128, KC, KVL + ROPE)); din("lnkv", (1, KVL))
    din("kvb_nope", (KVL, NH * NOPE)); din("kvb_v", (KVL, NH * VD))
    din("ow_t", (128, KC, D)); din("router_t", (128, KC, E))
    din("wg_t", (E, IT_MOE, 128, KC, 128)); din("wu_t", (E, IT_MOE, 128, KC, 128))
    din("wd_t", (E, 2, 128, IT_MOE * 1024))
    din("shg_t", (IT_SH, 128, KC, 128)); din("shu_t", (IT_SH, 128, KC, 128))
    din("shd_t", (4, 2, 128, IT_MOE * 1024)); din("shgw_t", (128, KC, 1))
    din("id128", (128, 128)); din("ones_row", (1, 128))
    din("mswapT", (128, 64)); din("mshiftT", (128, 64)); din("r64T", (64, 64))
    din("ck", (ROPE, S)); din("sk", (ROPE, S))
    din("cq", (ROPE, TOK)); din("sq", (ROPE, TOK))
    din("mask1_256", (128, 256)); din("mask1_512", (128, 512))
    din("mask1_768", (128, 768)); din("mask1_1024", (128, 1024))
    din("mask2", (9, 128, 512)); din("onehot", (E, E * 128)); din("lx", (128, 128))
    out = nc.dram_tensor("out", [TOK, D], F32, kind="ExternalOutput")
    x1d = nc.dram_tensor("x1d", [TOK, D], F32, kind="Internal")
    wg_bf = nc.dram_tensor("wg_bf", [E, IT_MOE, 128, KC * 128], BF16, kind="Internal")
    wu_bf = nc.dram_tensor("wu_bf", [E, IT_MOE, 128, KC * 128], BF16, kind="Internal")
    wd_bf = nc.dram_tensor("wd_bf", [E, 2, 128, IT_MOE * 1024], BF16, kind="Internal")
    shg_bf = nc.dram_tensor("shg_bf", [IT_SH, 128, KC * 128], BF16, kind="Internal")
    shu_bf = nc.dram_tensor("shu_bf", [IT_SH, 128, KC * 128], BF16, kind="Internal")
    shd_bf = nc.dram_tensor("shd_bf", [4, 2, 128, IT_MOE * 1024], BF16, kind="Internal")
    x1nd = nc.dram_tensor("x1nd", [TOK, D], BF16, kind="Internal")
    combd = nc.dram_tensor("combd", [TOK, E], F32, kind="Internal")
    idxd = nc.dram_tensor("idxd", [2, E, CAP], I32, kind="Internal")
    ap = lambda n: dr[n].ap()

    P = Prog(nc)
    esG = contextlib.ExitStack()
    with esG:
        for s in ("pe", "dve", "act", "pool"):
            P.new_sem(s, esG)
        QROT = 8
        for q in ("qg", "qs", "qo", "qa"):
            for i in range(QROT):
                P.new_sem(f"{q}{i}", esG)
        qrot = {"qg": 0, "qs": 0, "qo": 0, "qa": 0}

        def qall(q):
            return [P.tok(f"{q}{i}") for i in range(QROT)]
        ps = [esG.enter_context(nc.psum_tensor(f"ps{i}", [128, 512], F32))
              for i in range(8)]
        rel = {}

        def sbuf(es, name, shp, dt=F32, side="right"):
            return es.enter_context(nc.sbuf_tensor(name, list(shp), dt, side=side))

        def dma(eng, q, dst, src, waits=()):
            s_name = f"{q}{qrot[q]}"
            qrot[q] = (qrot[q] + 1) % QROT
            return P.emit(eng, lambda h, d=dst, s=src: h.dma_start(d, s),
                          inc=s_name, inck=16, waits=waits)

        cast_jobs = []
        cast_tok = {}
        for e_ in range(E):
            cast_jobs.append((("g", e_), wg_bf.ap()[e_], ap("wg_t")[e_]))
            cast_jobs.append((("u", e_), wu_bf.ap()[e_], ap("wu_t")[e_]))
            cast_jobs.append((("d", e_), wd_bf.ap()[e_], ap("wd_t")[e_]))
        for g_ in range(4):
            cast_jobs.append((("shg", g_), shg_bf.ap()[g_ * IT_MOE:(g_ + 1) * IT_MOE],
                              ap("shg_t")[g_ * IT_MOE:(g_ + 1) * IT_MOE]))
            cast_jobs.append((("shu", g_), shu_bf.ap()[g_ * IT_MOE:(g_ + 1) * IT_MOE],
                              ap("shu_t")[g_ * IT_MOE:(g_ + 1) * IT_MOE]))
            cast_jobs.append((("shd", g_), shd_bf.ap()[g_], ap("shd_t")[g_]))

        def emit_casts(n):
            for _ in range(n):
                if not cast_jobs:
                    return
                key, dst, srcap = cast_jobs.pop(0)
                cast_tok[key] = dma("pool", "qg", dst, srcap)

        L = lambda name, shp, dt=F32: sbuf(esG, name, shp, dt, side="left")
        id_bf = L("id_bf", (128, 128), BF16)
        id_f32 = L("id_f32", (128, 128), F32)
        ones_bf = L("ones_bf", (1, 128), BF16)
        router_sb = L("router_sb", (128, KC * E), BF16)
        shgw_sb = L("shgw_sb", (128, KC), BF16)
        xn = L("xn", (128, D), BF16)
        xnT = L("xnT", (128, KC * 128), BF16)
        stats = L("stats", (128, 24), F32)
        mv = L("mv", (128, 2), F32)
        sd_t = L("sd_t", (128, 1), F32)
        rstd = L("rstd", (128, 1), F32)
        sq1 = L("sq1", (128, 1), F32)
        sq2 = L("sq2", (128, 1), F32)
        sqs = L("sqs", (128, 1), F32)

        esR1 = contextlib.ExitStack()
        o_all = sbuf(esR1, "o_all", (128, NH * TOK), BF16)
        esR2 = contextlib.ExitStack()
        qaT = sbuf(esR2, "qaT", (128, 6 * TOK), BF16)
        ckvT = sbuf(esR2, "ckvT", (128, 2 * S), BF16)
        kpe_aug = sbuf(esR2, "kpe_aug", (65, S), BF16)
        kvbn_sb = sbuf(esR2, "kvbn_sb", (128, 2 * NH * NOPE), BF16)
        kvbv_sb = sbuf(esR2, "kvbv_sb", (128, 2 * NH * VD), BF16)

        esR3 = contextlib.ExitStack()
        qaw_sb = sbuf(esR3, "qaw_sb", (128, KC * QL), BF16)
        kva_sb = sbuf(esR3, "kva_sb", (128, KC * (KVL + ROPE)), BF16)
        lnq_b = sbuf(esR3, "lnq_b", (128, QL), BF16)
        lnkv_b = sbuf(esR3, "lnkv_b", (128, KVL), BF16)
        lnrow = sbuf(esR3, "lnrow", (1, QL), BF16)
        ck_sb = sbuf(esR3, "ck_sb", (ROPE, S), BF16)
        sk_sb = sbuf(esR3, "sk_sb", (ROPE, S), BF16)
        r64_sb = sbuf(esR3, "r64_sb", (64, 64), BF16)
        xtile = [sbuf(esR3, f"xtileA{i}", (128, D), F32) for i in range(2)]
        qa_n = sbuf(esR3, "qa_n", (128, QL), BF16)
        ckv_n = sbuf(esR3, "ckv_n", (128, KVL), BF16)
        kpe_tm = sbuf(esR3, "kpe_tm", (128, ROPE), BF16)
        sqscr = sbuf(esR3, "sqscr", (128, 512), BF16)
        tp1 = sbuf(esR3, "tp1", (64, 512), F32)
        tp2 = sbuf(esR3, "tp2", (64, 512), F32)

        dma("pool", "qg", id_bf[:], ap("id128"))
        dma("sp", "qs", id_f32[:], ap("id128"))
        dma("pool", "qg", ones_bf[:], ap("ones_row"))
        dma("pool", "qg", router_sb[:].rearrange("p (c n) -> p c n", c=KC), ap("router_t"))
        dma("pool", "qg", shgw_sb[:].rearrange("p (c n) -> p c n", c=KC), ap("shgw_t"))
        dma("pool", "qg", kvbn_sb[:].rearrange("p (c n) -> p c n", c=2),
            ap("kvb_nope").rearrange("(c p) n -> p c n", p=128))
        dma("pool", "qg", kvbv_sb[:].rearrange("p (c n) -> p c n", c=2),
            ap("kvb_v").rearrange("(c p) n -> p c n", p=128))
        dma("pool", "qg", ck_sb[:], ap("ck"))
        dma("pool", "qg", sk_sb[:], ap("sk"))
        dma("pool", "qg", r64_sb[:], ap("r64T"))
        t_wconst = qall("qg")
        t_qaw = dma("pool", "qg", qaw_sb[:].rearrange("p (c n) -> p c n", c=KC), ap("qaw_t"))
        t_kva = dma("pool", "qg", kva_sb[:].rearrange("p (c n) -> p c n", c=KC), ap("kvaw_t"))
        t_l1 = dma("pool", "qg", lnrow[0:1, 0:QL], ap("lnq"))

        def bcast_row(width, dst, t_row):
            t_cp = None
            for j in range((width + 511) // 512):
                w = min(512, width - j * 512)
                t_mm = P.emit("pe", lambda h, j=j, w=w: h.matmul(
                    ps[5][:, 0:w], ones_bf[0:1, :], lnrow[0:1, j * 512:j * 512 + w],
                    start=True, stop=True), inc="pe", waits=[t_row, rel.get(5)])
                t_cp = P.emit("act", lambda h, j=j, w=w, dst=dst: h.copy(
                    dst[:, j * 512:j * 512 + w], ps[5][:, 0:w]), inc="act",
                    waits=[t_mm])
                rel[5] = t_cp
            return t_cp

        t_lnqb = bcast_row(QL, lnq_b, t_l1)
        t_l2 = dma("pool", "qg", lnrow[0:1, 0:KVL], ap("lnkv"), waits=[rel[5]])
        t_lnkb = bcast_row(KVL, lnkv_b, t_l2)

        def emit_ln(xbuf, t_x, extra=()):
            for sl in range(4):
                P.emit("dve", lambda h, sl=sl, xb=xbuf: h.bn_stats(
                    stats[:, sl * 6:(sl + 1) * 6], xb[:, sl * 512:(sl + 1) * 512]),
                    inc="dve", waits=[t_x] if sl == 0 else ())
            P.emit("dve", lambda h: h.bn_aggr(mv[:], stats[:]), inc="dve",
                   waits=[P.tok("dve")])
            P.emit("dve", lambda h: h.tensor_scalar(sq1[:], mv[:, 1:2], EPS_LN,
                                                    None, ALU.add), inc="dve",
                   waits=[P.tok("dve")])
            P.emit("act", lambda h: h.activation(sd_t[:], sq1[:], AF.Sqrt), inc="act",
                   waits=[P.tok("dve")])
            P.emit("dve", lambda h: h.reciprocal(rstd[:], sd_t[:]), inc="dve",
                   waits=[P.tok("act")])
            P.emit("dve", lambda h, xb=xbuf: h.tensor_scalar(
                xn[:], xb[:], mv[:, 0:1], rstd[:], ALU.subtract, ALU.mult),
                inc="dve", waits=[P.tok("dve")] + list(extra))
            return P.tok("dve")

        def emit_transposes(src, n, widths, dst_fn, t_src):
            t_last = None
            for c in range(n):
                w = widths[c] if widths else 128
                bank = 6 + (c % 2)
                t_t = P.emit("pe", lambda h, c=c, w=w, bank=bank, src=src: h.transpose(
                    ps[bank].bitcast(BF16)[0:w, 0:128], src[:, c * 128:c * 128 + w],
                    id_bf[:]), inc="pe", waits=[t_src, rel.get(bank)])
                t_last = P.emit("act", lambda h, d=dst_fn(c), w=w, bank=bank: h.copy(
                    d, ps[bank].bitcast(BF16)[0:w, 0:128]), inc="act", waits=[t_t])
                rel[bank] = t_last
            return t_last

        xrel = [None, None]
        for tt in range(NT):
            xb = xtile[tt % 2]
            t_x = dma("sp", "qs", xb[:], ap("xq")[tt * 128:(tt + 1) * 128, :],
                      waits=[xrel[tt % 2]])
            t_xn = emit_ln(xb, t_x, extra=[P.tok("pe")])
            xrel[tt % 2] = t_xn
            t_xT = emit_transposes(xn, KC, None,
                                   lambda c: xnT[:, c * 128:(c + 1) * 128], t_xn)
            for c in range(KC):
                last = c == KC - 1
                P.emit("pe", lambda h, c=c: h.matmul(
                    ps[0][:, :], xnT[:, c * 128:(c + 1) * 128],
                    qaw_sb[:, c * QL:c * QL + 512], start=(c == 0), stop=last),
                    waits=[t_xT, t_qaw, rel.get(0), rel.get(1)] if c == 0 else ())
                P.emit("pe", lambda h, c=c: h.matmul(
                    ps[1][:, 0:170], xnT[:, c * 128:(c + 1) * 128],
                    qaw_sb[:, c * QL + 512:(c + 1) * QL], start=(c == 0), stop=last),
                    inc="pe" if last else None)
            t_qa = P.tok("pe")
            P.emit("act", lambda h: h.activation(sqscr[:, 0:512], ps[0][:, :],
                                                 AF.Square, accum_out=sq1[:]),
                   inc="act", waits=[t_qa, P.tok("dve")])
            P.emit("act", lambda h: h.activation(sqscr[:, 0:170], ps[1][:, 0:170],
                                                 AF.Square, accum_out=sq2[:]),
                   inc="act")
            P.emit("dve", lambda h: h.tensor_add(sqs[:], sq1[:], sq2[:]), inc="dve",
                   waits=[P.tok("act")])
            P.emit("dve", lambda h: h.tensor_scalar(sqs[:], sqs[:], 1.0 / QL, EPS_RMS,
                                                    ALU.mult, ALU.add), inc="dve",
                   waits=[P.tok("dve")])
            P.emit("act", lambda h: h.activation(sd_t[:], sqs[:], AF.Sqrt), inc="act",
                   waits=[P.tok("dve")])
            P.emit("dve", lambda h: h.reciprocal(rstd[:], sd_t[:]), inc="dve",
                   waits=[P.tok("act")])
            P.emit("dve", lambda h: h.tensor_scalar(qa_n[:, 0:512], ps[0][:, :],
                                                    rstd[:], None, ALU.mult),
                   inc="dve", waits=[P.tok("dve")])
            t_ev = P.emit("dve", lambda h: h.tensor_scalar(
                qa_n[:, 512:QL], ps[1][:, 0:170], rstd[:], None, ALU.mult), inc="dve")
            rel[0] = t_ev
            rel[1] = t_ev
            t_qan = P.emit("dve", lambda h: h.tensor_mul(qa_n[:], qa_n[:], lnq_b[:]),
                           inc="dve", waits=[t_ev, t_lnqb])
            emit_transposes(qa_n, 6, QLC,
                            lambda c, tt=tt: qaT[0:QLC[c],
                                                 c * TOK + tt * 128:c * TOK + (tt + 1) * 128],
                            t_qan)
        P.realize(esG)

        for tk in range(NKT):
            xb = xtile[tk % 2]
            t_x = dma("sp", "qs", xb[:], ap("xk")[tk * 128:(tk + 1) * 128, :],
                      waits=[xrel[tk % 2]])
            t_xn = emit_ln(xb, t_x, extra=[P.tok("pe")])
            xrel[tk % 2] = t_xn
            t_xT = emit_transposes(xn, KC, None,
                                   lambda c: xnT[:, c * 128:(c + 1) * 128], t_xn)
            for c in range(KC):
                last = c == KC - 1
                P.emit("pe", lambda h, c=c: h.matmul(
                    ps[0][:, 0:KVL + ROPE], xnT[:, c * 128:(c + 1) * 128],
                    kva_sb[:, c * 320:(c + 1) * 320], start=(c == 0), stop=last),
                    inc="pe" if last else None,
                    waits=[t_xT, t_kva, rel.get(0)] if c == 0 else ())
            t_kv = P.tok("pe")
            P.emit("act", lambda h: h.activation(sqscr[:, 0:KVL], ps[0][:, 0:KVL],
                                                 AF.Square, accum_out=sqs[:]),
                   inc="act", waits=[t_kv, P.tok("dve")])
            P.emit("dve", lambda h: h.tensor_scalar(sqs[:], sqs[:], 1.0 / KVL,
                                                    EPS_RMS, ALU.mult, ALU.add),
                   inc="dve", waits=[P.tok("act")])
            P.emit("act", lambda h: h.activation(sd_t[:], sqs[:], AF.Sqrt), inc="act",
                   waits=[P.tok("dve")])
            P.emit("dve", lambda h: h.reciprocal(rstd[:], sd_t[:]), inc="dve",
                   waits=[P.tok("act")])
            t_craw = P.emit("dve", lambda h: h.tensor_scalar(
                ckv_n[:], ps[0][:, 0:KVL], rstd[:], None, ALU.mult),
                inc="dve", waits=[P.tok("dve")])
            t_cn = P.emit("dve", lambda h: h.tensor_mul(ckv_n[:], ckv_n[:], lnkv_b[:]),
                          inc="dve", waits=[P.tok("dve"), t_lnkb])
            t_kp = P.emit("act", lambda h: h.copy(kpe_tm[:], ps[0][:, KVL:KVL + ROPE]),
                          inc="act", waits=[t_kv, t_craw])
            rel[0] = t_kp
            emit_transposes(ckv_n, 2, None,
                            lambda c, tk=tk: ckvT[:, c * S + tk * 128:c * S + (tk + 1) * 128],
                            t_cn)
            bank = 6
            t_t = P.emit("pe", lambda h: h.transpose(
                ps[bank].bitcast(BF16)[0:ROPE, 0:128], kpe_tm[:], id_bf[:]),
                inc="pe", waits=[t_kp, rel.get(bank)])
            t_cpy = P.emit("act", lambda h, tk=tk: h.copy(
                kpe_aug[0:64, tk * 128:(tk + 1) * 128],
                ps[bank].bitcast(BF16)[0:ROPE, 0:128]), inc="act", waits=[t_t])
            rel[bank] = t_cpy
        t_kperaw = P.tok("act")

        P.emit("dve", lambda h: h.memset(kpe_aug[64:65, :], 1.0), inc="dve",
               waits=[t_kperaw])
        for j in range(8):
            slc = slice(j * 512, (j + 1) * 512)
            bank = j % 2
            t_m = P.emit("pe", lambda h, slc=slc, bank=bank: h.matmul(
                ps[bank][0:64, :], r64_sb[:], kpe_aug[0:64, slc],
                start=True, stop=True), inc="pe",
                waits=[t_kperaw, rel.get(bank), P.tok("dve")] + t_wconst)
            P.emit("dve", lambda h, slc=slc, bank=bank: h.tensor_mul(
                tp1[:], ps[bank][0:64, :], sk_sb[:, slc]), inc="dve", waits=[t_m])
            P.emit("dve", lambda h, slc=slc: h.tensor_mul(
                tp2[:], kpe_aug[0:64, slc], ck_sb[:, slc]), inc="dve")
            t_add = P.emit("dve", lambda h, slc=slc: h.tensor_add(
                kpe_aug[0:64, slc], tp1[:], tp2[:]), inc="dve", waits=[P.tok("dve")])
            rel[bank] = t_add
        t_kpe = P.tok("dve")
        emit_casts(12)
        P.realize(esG)
        esR3.close()

        if upto == "A":
            t1 = dma("pool", "qo", out.ap()[0:128, :], qaT[:, 0:TOK])
            t2 = dma("pool", "qo", out.ap()[128:256, :], ckvT[:, 0:2048])
            t3 = dma("pool", "qo", out.ap()[256:321, :], kpe_aug[0:65, 0:2048])
            P.emit("sp", lambda h: h.nop(), waits=qall("qo"))
            P.realize(esG)
            return nc

        esR4 = contextlib.ExitStack()
        qbw_h = [sbuf(esR4, f"qbw_h{i}", (128, 6 * 128), BF16) for i in range(2)]
        m1full = sbuf(esR4, "m1full", (128, 1024), BF16)
        m2_sb = sbuf(esR4, "m2_sb", (128, 8 * 512), BF16)
        cq_sb = sbuf(esR4, "cq_sb", (128, TOK), BF16)
        sq_sb = sbuf(esR4, "sq_sb", (128, TOK), BF16)
        mswap_sb = sbuf(esR4, "mswap_sb", (128, 64), BF16)
        mshift_sb = sbuf(esR4, "mshift_sb", (128, 64), BF16)
        qraw = sbuf(esR4, "qraw", (128, TOK), BF16)
        kf_h = [sbuf(esR4, f"kf_h{i}", (128, S), BF16) for i in range(2)]
        v2 = sbuf(esR4, "v2", (128, NKT * 2 * VD), BF16)
        tb12 = sbuf(esR4, "tb12", (128, 1024), F32)
        tbh = tb12[:, 0:512]
        tbh2 = tb12[:, 512:1024]
        mstats = sbuf(esR4, "mstats", (128, NT * 8), F32)
        zstats = sbuf(esR4, "zstats", (128, NT * 8), F32)
        mins = sbuf(esR4, "mins", (128, NT), F32)
        minbf = sbuf(esR4, "minbf", (128, NT), BF16)
        minr = sbuf(esR4, "minr", (128, NT), F32)
        t8a = sbuf(esR4, "t8a", (128, NT * 8), F32)
        zca = sbuf(esR4, "zca", (128, NT), F32)
        zia = sbuf(esR4, "zia", (128, NT), F32)
        zrow = xnT[0:1, 0:TOK]
        zrow2 = kpe_aug[0:1, 0:TOK]
        mrow = sbuf(esR4, "mrow", (1, TOK), BF16)
        zexp = sbuf(esR4, "zexp", (1, 1024), BF16)
        mqn32 = sbuf(esR4, "mqn32", (1, 1), F32)
        mqp32 = sbuf(esR4, "mqp32", (1, 1), F32)
        mqnbf = sbuf(esR4, "mqnbf", (1, 1), BF16)
        bias_qb = sbuf(esR4, "bias_qb", (128, 1), F32)
        attnT = [sbuf(esR4, f"attnT{i}", (128, 512), BF16) for i in range(4)]
        escr = attnT[0]

        dma("pool", "qg", m1full[:], ap("mask1_1024"))
        dma("pool", "qg", m2_sb[:].rearrange("p (c n) -> p c n", c=8),
            ap("mask2")[0:8].rearrange("c p n -> p c n"))
        dma("pool", "qg", cq_sb[64:128, :], ap("cq"))
        dma("pool", "qg", sq_sb[64:128, :], ap("sq"))
        dma("pool", "qg", mswap_sb[:], ap("mswapT"))
        dma("pool", "qg", mshift_sb[:], ap("mshiftT"))
        t_bconst = qall("qg")

        t_qbw = []
        for h_ in range(2):
            t_qbw.append(dma("pool", "qg",
                             qbw_h[h_][:].rearrange("p (c n) -> p c n", c=6),
                             ap("qbw_h")[h_]))

        def p1_tiles(i):
            ext = (2 * i + 2) * 128
            full, remn = divmod(ext, 512)
            tl = [(j * 512, 512) for j in range(full)]
            if remn:
                tl.append((full * 512, remn))
            return tl

        for hp in range(2):
            dma("sp", "qs", kf_h[hp][64:128, :], kpe_aug[0:64, :],
                waits=[t_kpe])
        t_kfd = qall("qs")

        att_rel = [None, None, None, None]
        kf_rel = [None, None]
        zbc_rel = [None, None]
        for h_ in range(nheads):
            hh = h_ % 2
            buf = qbw_h[h_ % 2]
            emit_casts(4)
            t_qlast = None
            for nb in range(4):
                qsl = slice(nb * 512, (nb + 1) * 512)
                pq = ps[nb]
                for c in range(6):
                    last = c == 5
                    P.emit("pe", lambda h, c=c, nb=nb, pq=pq, buf=buf: h.matmul(
                        pq[:, :], buf[0:QLC[c], c * 128:(c + 1) * 128],
                        qaT[0:QLC[c], c * TOK + nb * 512:c * TOK + (nb + 1) * 512],
                        start=(c == 0), stop=last),
                        inc="pe" if last else None,
                        waits=([t_qbw[h_], rel.get(nb)] + t_bconst)
                        if c == 0 else ())
                t_q = P.tok("pe")
                t_qlast = t_q
                t_qr = P.emit("act", lambda h, qsl=qsl, pq=pq: h.mul(
                    qraw[:, qsl], pq[:, :], SCALE), inc="act",
                    waits=[t_q, P.tok("dve")])
                rel[nb] = t_qr
                P.emit("pe", lambda h, qsl=qsl: h.matmul(
                    ps[4][64:128, :], mswap_sb[:], qraw[:, qsl],
                    start=True, stop=True, tile_position=(0, 64)),
                    inc="pe", waits=[t_qr, rel.get(4)])
                P.emit("pe", lambda h, qsl=qsl: h.matmul(
                    ps[5][64:128, :], mshift_sb[:], qraw[:, qsl],
                    start=True, stop=True, tile_position=(0, 64)),
                    inc="pe", waits=[rel.get(5)])
                t_rot = P.tok("pe")
                t_d1 = P.emit("dve", lambda h, qsl=qsl: h.tensor_mul(
                    tbh[64:128, :], ps[4][64:128, :], sq_sb[64:128, qsl]),
                    inc="dve", waits=[t_rot] + t_bconst)
                rel[4] = t_d1
                t_d2 = P.emit("dve", lambda h, qsl=qsl: h.tensor_mul(
                    tbh2[64:128, :], ps[5][64:128, :], cq_sb[64:128, qsl]),
                    inc="dve")
                rel[5] = t_d2
                P.emit("dve", lambda h, qsl=qsl: h.tensor_add(
                    qraw[64:128, qsl], tbh[64:128, :], tbh2[64:128, :]),
                    inc="dve", waits=[t_d2, t_qr])
            t_qf = P.tok("dve")
            if h_ + 2 < nheads:
                t_qbw.append(dma("pool", "qg",
                                 qbw_h[h_ % 2][:].rearrange("p (c n) -> p c n", c=6),
                                 ap("qbw_h")[h_ + 2], waits=[t_qlast]))

            if hh == 0:
                for j in range(8):
                    ksl = slice(j * 512, (j + 1) * 512)
                    for hp in range(2):
                        bank = 4 + ((2 * j + hp) % 2)
                        pk = ps[bank]
                        for c in range(2):
                            last = c == 1
                            P.emit("pe", lambda h, c=c, j=j, pk=pk, h_=h_, hp=hp:
                                   h.matmul(
                                pk[0:64, :],
                                kvbn_sb[:, c * NH * NOPE + (h_ + hp) * NOPE:
                                        c * NH * NOPE + (h_ + hp + 1) * NOPE],
                                ckvT[:, c * S + j * 512:c * S + (j + 1) * 512],
                                start=(c == 0), stop=last),
                                inc="pe" if last else None,
                                waits=([rel.get(bank), kf_rel[hp]] + t_bconst)
                                if c == 0 else ())
                        t_k = P.tok("pe")
                        t_cp = P.emit("act", lambda h, ksl=ksl, pk=pk, hp=hp: h.copy(
                            kf_h[hp][0:64, ksl], pk[0:64, :]), inc="act",
                            waits=[t_k])
                        rel[bank] = t_cp
                for tk in range(NKT):
                    bank = 4 + (tk % 2)
                    pk = ps[bank]
                    for c in range(2):
                        last = c == 1
                        P.emit("pe", lambda h, c=c, tk=tk, pk=pk, h_=h_: h.matmul(
                            pk[:, 0:2 * VD],
                            ckvT[:, c * S + tk * 128:c * S + (tk + 1) * 128],
                            kvbv_sb[:, c * NH * VD + h_ * VD:
                                    c * NH * VD + (h_ + 2) * VD],
                            start=(c == 0), stop=last),
                            inc="pe" if last else None,
                            waits=([rel.get(bank)] + t_bconst) if c == 0 else ())
                    t_v = P.tok("pe")
                    t_cp = P.emit("act", lambda h, tk=tk, pk=pk: h.copy(
                        v2[:, tk * 2 * VD:(tk + 1) * 2 * VD], pk[:, 0:2 * VD]),
                        inc="act", waits=[t_v])
                    rel[bank] = t_cp
            t_kvb = P.tok("act")

            for qt in range(NT):
                tiles = p1_tiles(qt)
                nE = len(tiles)
                c0 = qt * 8
                for ti, (k0, w) in enumerate(tiles):
                    bank = ti % 4
                    pq = ps[bank]
                    P.emit("pe", lambda h, k0=k0, w=w, pq=pq, qt=qt, hh=hh:
                           h.matmul(
                        pq[:, 0:w], qraw[:, qt * 128:(qt + 1) * 128],
                        kf_h[hh][:, k0:k0 + w], start=True, stop=True),
                        inc="pe",
                        waits=[t_qf, t_kvb, rel.get(bank)] + t_kfd)
                    t_s = P.tok("pe")
                    if ti == nE - 1:
                        t_s = P.emit("dve", lambda h, w=w, pq=pq:
                                     h.tensor_add(pq[:, 0:w], pq[:, 0:w],
                                                  m1full[:, 1024 - w:1024]),
                                     inc="dve", waits=[t_s] + t_bconst)
                    t_mk = P.emit("dve", lambda h, w=w, pq=pq, ci=c0 + ti:
                                  h.reduce_max(mstats[:, ci:ci + 1], pq[:, 0:w],
                                               axis=AX.X, negate=True), inc="dve",
                                  waits=[t_s])
                    t_e = P.emit("act", lambda h, w=w, pq=pq, ci=c0 + ti:
                                 h.activation(
                        escr[:, 0:w], pq[:, 0:w], AF.Exp,
                        bias=mstats[:, ci:ci + 1],
                        accum_out=zstats[:, ci:ci + 1]), inc="act",
                        waits=[t_mk, att_rel[0]])
                    rel[bank] = t_e
                t_z = P.tok("act")
                P.emit("dve", lambda h, nE=nE, c0=c0, qt=qt: h.tensor_reduce(
                    mins[:, qt:qt + 1], mstats[:, c0:c0 + nE], AX.X, ALU.min),
                    inc="dve", waits=[t_z])
                P.emit("dve", lambda h, qt=qt: h.tensor_copy(
                    minbf[:, qt:qt + 1], mins[:, qt:qt + 1]), inc="dve",
                    waits=[P.tok("dve")])
                t_mr = P.emit("dve", lambda h, qt=qt: h.tensor_copy(
                    minr[:, qt:qt + 1], minbf[:, qt:qt + 1]), inc="dve",
                    waits=[P.tok("dve")])
                t_t8 = P.emit("act", lambda h, nE=nE, c0=c0, qt=qt: h.activation(
                    t8a[:, c0:c0 + nE], mstats[:, c0:c0 + nE], AF.Exp,
                    bias=minr[:, qt:qt + 1], scale=-1.0), inc="act", waits=[t_mr])
                P.emit("dve", lambda h, nE=nE, c0=c0: h.tensor_mul(
                    t8a[:, c0:c0 + nE], t8a[:, c0:c0 + nE], zstats[:, c0:c0 + nE]),
                    inc="dve", waits=[t_t8])
                P.emit("dve", lambda h, nE=nE, c0=c0, qt=qt: h.reduce_sum(
                    zca[:, qt:qt + 1], t8a[:, c0:c0 + nE], axis=AX.X), inc="dve",
                    waits=[P.tok("dve")])
                t_zi = P.emit("dve", lambda h, qt=qt: h.reciprocal(
                    zia[:, qt:qt + 1], zca[:, qt:qt + 1]), inc="dve",
                    waits=[P.tok("dve")])
                bank = 6 + (qt % 2)
                t_tr = P.emit("pe", lambda h, qt=qt, bank=bank: h.transpose(
                    ps[bank][0:1, 0:128], minr[:, qt:qt + 1], id_f32[:]), inc="pe",
                    waits=[t_mr, rel.get(bank)])
                t_mrow = P.emit("act", lambda h, qt=qt, bank=bank: h.copy(
                    mrow[0:1, qt * 128:(qt + 1) * 128], ps[bank][0:1, 0:128]),
                    inc="act", waits=[t_tr])
                t_tr2 = P.emit("pe", lambda h, qt=qt, bank=bank: h.transpose(
                    ps[bank][0:1, 128:256], zia[:, qt:qt + 1], id_f32[:]), inc="pe",
                    waits=[t_zi, t_mrow])
                t_zrow = P.emit("act", lambda h, qt=qt, bank=bank: h.copy(
                    zrow[0:1, qt * 128:(qt + 1) * 128], ps[bank][0:1, 128:256]),
                    inc="act", waits=[t_tr2])
                rel[bank] = t_zrow
            t_p1 = P.tok("act")

            P.emit("dve", lambda h: h.tensor_reduce(
                mqn32[0:1, 0:1], mrow[0:1, 0:TOK], AX.X, ALU.min), inc="dve",
                waits=[t_p1])
            P.emit("dve", lambda h: h.tensor_copy(mqnbf[:], mqn32[:]),
                   inc="dve", waits=[P.tok("dve")])
            P.emit("dve", lambda h: h.tensor_scalar(
                mqp32[:], mqn32[:], -1.0, None, ALU.mult), inc="dve",
                waits=[P.tok("dve")])
            t_mq = P.tok("dve")
            t_bq = P.emit("pe", lambda h: h.matmul(
                ps[6][0:128, 0:1], ones_bf[0:1, :], mqnbf[0:1, 0:1],
                start=True, stop=True), inc="pe", waits=[t_mq, rel.get(6)])
            t_bqc = P.emit("act", lambda h: h.copy(bias_qb[:], ps[6][:, 0:1]),
                           inc="act", waits=[t_bq])
            rel[6] = t_bqc
            t_zr2 = None
            for ch in range(2):
                cs = slice(ch * 1024, (ch + 1) * 1024)
                P.emit("act", lambda h, cs=cs: h.activation(
                    zexp[0:1, :], mrow[0:1, cs], AF.Exp,
                    bias=mqp32[0:1, 0:1]), inc="act",
                    waits=[t_mq] + ([t_zr2] if t_zr2 else []))
                t_zr2 = P.emit("dve", lambda h, cs=cs: h.tensor_mul(
                    zrow2[0:1, cs], zrow[0:1, cs], zexp[0:1, :]),
                    inc="dve", waits=[P.tok("act")])

            for qb in range(4):
                qsl = slice(qb * 512, (qb + 1) * 512)
                nkt = 8 * qb + 8
                obank = 4 + (qb % 2)
                zbank = 6 + (qb % 2)
                pend = []

                def emit_av(kt, t_e, nkt, hh, obank):
                    return P.emit("pe", lambda h, kt=kt, hh=hh, nkt=nkt,
                                  obank=obank: h.matmul(
                        ps[obank][:, :],
                        v2[:, kt * 2 * VD + hh * VD:kt * 2 * VD + (hh + 1) * VD],
                        attnT[kt % 4][:, :],
                        start=(kt == 0), stop=(kt == nkt - 1)),
                        inc="pe",
                        waits=[t_e] + ([rel.get(obank)] if kt == 0 else []))

                for kt in range(nkt):
                    bank = kt % 4
                    pq = ps[bank]
                    P.emit("pe", lambda h, kt=kt, pq=pq, qsl=qsl, hh=hh: h.matmul(
                        pq[:, :], kf_h[hh][:, kt * 128:(kt + 1) * 128],
                        qraw[:, qsl], start=True, stop=True), inc="pe",
                        waits=[rel.get(bank)])
                    t_s = P.tok("pe")
                    dd = kt - 8 * qb
                    if dd >= 0:
                        t_s = P.emit("dve", lambda h, pq=pq, dd=dd:
                                     h.tensor_add(
                            pq[:, :], pq[:, :],
                            m2_sb[:, dd * 512:(dd + 1) * 512]),
                            inc="dve", waits=[t_s] + t_bconst)
                    t_e = P.emit("act", lambda h, pq=pq, kt=kt: h.activation(
                        attnT[kt % 4][:], pq[:, :], AF.Exp, bias=bias_qb[:]),
                        inc="act", waits=[t_s, att_rel[kt % 4], t_bqc])
                    rel[bank] = t_e
                    pend.append((kt, t_e))
                    if len(pend) > 2:
                        kt0, te0 = pend.pop(0)
                        att_rel[kt0 % 4] = emit_av(kt0, te0, nkt, hh, obank)
                for kt0, te0 in pend:
                    att_rel[kt0 % 4] = emit_av(kt0, te0, nkt, hh, obank)
                t_av = P.tok("pe")
                zslot = xn[:, (qb % 2) * 512:(qb % 2) * 512 + 512]
                t_zb = P.emit("pe", lambda h, qsl=qsl, zbank=zbank: h.matmul(
                    ps[zbank][:, :], ones_bf[0:1, :], zrow2[0:1, qsl],
                    start=True, stop=True), inc="pe",
                    waits=[rel.get(zbank), t_zr2])
                t_zc = P.emit("act", lambda h, zslot=zslot, zbank=zbank: h.copy(
                    zslot, ps[zbank][:, :]), inc="act",
                    waits=[t_zb, zbc_rel[qb % 2]])
                rel[zbank] = t_zc
                t_o = P.emit("dve", lambda h, h_=h_, qb=qb, zslot=zslot,
                             obank=obank: h.tensor_mul(
                    o_all[:, h_ * TOK + qb * 512:h_ * TOK + (qb + 1) * 512],
                    ps[obank][:, :], zslot), inc="dve",
                    waits=[t_av, t_zc])
                rel[obank] = t_o
                zbc_rel[qb % 2] = t_o
            kf_rel[hh] = P.tok("pe")
        P.realize(esG)
        esR4.close()
        esR2.close()

        if upto == "B":
            for h_ in range(nheads):
                dma("pool", "qo", out.ap()[h_ * 128:(h_ + 1) * 128, :],
                    o_all[:, h_ * TOK:(h_ + 1) * TOK])
            if nheads == 1:
                dma("pool", "qo", out.ap()[256:384, :], qraw[:, :])
                dma("pool", "qo", out.ap()[384:512, :], kf_h[0][:, 0:2048])
                dma("pool", "qo", out.ap()[512:640, :], kf_h[0][:, 2048:4096])
                dma("pool", "qo", out.ap()[640:768, :], v2[:, 0:2048])
                dma("pool", "qo", out.ap()[800:801, :], zrow[0:1, :])
                dma("pool", "qo", out.ap()[801:802, :], mrow[0:1, :])
                dma("pool", "qo", out.ap()[832:960, :], qaT[:, TOK:2 * TOK])
                dma("pool", "qo", out.ap()[960:1002, :], qaT[0:42, 5 * TOK:6 * TOK])
                dma("pool", "qo", out.ap()[1024:1152, 0:768], qbw_h[0][:, :])
            P.emit("sp", lambda h: h.nop(), waits=qall("qo"))
            P.realize(esG)
            return nc

        esR5 = contextlib.ExitStack()
        ow_sb = sbuf(esR5, "ow_sb", (128, KC * D), BF16)
        xtC = [sbuf(esR5, f"xtC{i}", (128, D), F32) for i in range(2)]
        x1t = [sbuf(esR5, f"x1t{i}", (128, D), F32) for i in range(2)]
        for c in range(KC):
            dma("pool", "qg",
                ow_sb[:].rearrange("p (c n) -> p c n", c=KC)[:, c:c + 1, :],
                ap("ow_t")[:, c:c + 1, :])
        t_ow = qall("qg")
        xrelC = [None, None]
        x1rel = [None, None]
        for tt in range(NT):
            xb = xtC[tt % 2]
            t_x = dma("sp", "qs", xb[:], ap("xq")[tt * 128:(tt + 1) * 128, :],
                      waits=[xrelC[tt % 2]])
            for h_ in range(NH):
                for hcg in range(4):
                    P.emit("pe", lambda h, h_=h_, hcg=hcg, tt=tt: h.matmul(
                        ps[hcg][:, :],
                        o_all[:, h_ * TOK + tt * 128:h_ * TOK + (tt + 1) * 128],
                        ow_sb[:, h_ * D + hcg * 512:h_ * D + (hcg + 1) * 512],
                        start=(h_ == 0), stop=(h_ == NH - 1)),
                        inc="pe" if (h_ == NH - 1 and hcg == 3) else None,
                        waits=(t_ow + [rel.get(i) for i in range(4)])
                        if (h_ == 0 and hcg == 0) else ())
            t_o = P.tok("pe")
            x1b = x1t[tt % 2]
            t_a = None
            for hcg in range(4):
                t_a = P.emit("dve", lambda h, hcg=hcg, x1b=x1b, xb=xb: h.tensor_add(
                    x1b[:, hcg * 512:(hcg + 1) * 512], ps[hcg][:, :],
                    xb[:, hcg * 512:(hcg + 1) * 512]), inc="dve",
                    waits=[t_o, t_x, x1rel[tt % 2]])
            for i in range(4):
                rel[i] = t_a
            xrelC[tt % 2] = t_a
            t_st = dma("sp", "qs", x1d.ap()[tt * 128:(tt + 1) * 128, :], x1b[:],
                       waits=[t_a])
            x1rel[tt % 2] = t_st
        t_x1d = qall("qs")
        P.realize(esG)
        esR5.close()
        esR1.close()

        if upto == "C":
            for tt in range(NT):
                dma("sp", "qo", out.ap()[tt * 128:(tt + 1) * 128, :],
                    x1d.ap()[tt * 128:(tt + 1) * 128, :], waits=t_x1d)
            P.emit("sp", lambda h: h.nop(), waits=qall("qo"))
            P.realize(esG)
            return nc

        esR6 = contextlib.ExitStack()
        x1nT = sbuf(esR6, "x1nT", (128, KC * 1024), BF16)
        z_e = sbuf(esR6, "z_e", (128, IT_MOE * 1024), BF16)
        wgl = [sbuf(esR6, f"wgl{i}", (128, KC * 128), BF16) for i in range(2)]
        wul = [sbuf(esR6, f"wul{i}", (128, KC * 128), BF16) for i in range(2)]
        wdr = [sbuf(esR6, f"wdr{i}", (128, IT_MOE * 1024), BF16) for i in range(2)]
        xg = sbuf(esR6, "xg", (128, CAPG * D), BF16)
        xgT = [sbuf(esR6, f"xgT{i}", (128, KC * CAP), BF16) for i in range(2)]
        stage = [sbuf(esR6, f"stage{i}", (128, D), F32) for i in range(2)]
        combg = [sbuf(esR6, f"combg{i}", (128, CAPG * E), F32) for i in range(2)]
        idxsb = [sbuf(esR6, f"idxsb{i}", (128, CAPG), I32) for i in range(2)]
        bigi = sbuf(esR6, "bigi", (128, E * CAP // 128), I32)
        rowid = sbuf(esR6, "rowid", (128, 16), I32)
        lgmx = sbuf(esR6, "lgmx", (128, 1), F32)
        pun = sbuf(esR6, "pun", (128, 16), F32)
        rank = sbuf(esR6, "rank", (128, 16), F32)
        tcmp = sbuf(esR6, "tcmp", (128, 16), F32)
        mask4l = [sbuf(esR6, f"mask4_{i}", (128, 16), F32) for i in range(2)]
        combl = [sbuf(esR6, f"comb_{i}", (128, 16), F32) for i in range(2)]
        den = sbuf(esR6, "den", (128, 1), F32)
        dinv = sbuf(esR6, "dinv", (128, 1), F32)
        sg = sbuf(esR6, "sg", (128, 8), F32)
        silu_s = sbuf(esR6, "silu_s", (128, 512), BF16)
        tmul = sbuf(esR6, "tmul", (128, 512), BF16)
        lx_sb = sbuf(esR6, "lx_sb", (128, 128), F32)
        onesc = sbuf(esR6, "onesc", (128, 1), F32)
        onesr = sbuf(esR6, "onesr", (1, 128), F32)
        base_r = sbuf(esR6, "base_r", (1, 16), F32)
        slotf = sbuf(esR6, "slotf", (128, 16), F32)
        sltmp = sbuf(esR6, "sltmp", (128, 16), F32)
        sltmp2 = sbuf(esR6, "sltmp2", (128, 16), F32)
        ovf = sbuf(esR6, "ovf", (128, 16), F32)
        slot_il = [sbuf(esR6, f"slot_i{i}", (128, 16), I32) for i in range(2)]

        dma("sp", "qs", lx_sb[:], ap("lx"))
        t_lxq = qall("qs")
        rbc_cap = esR6.enter_context(nc.gpsimd.register("rbc_cap"))
        rbc_tok = esR6.enter_context(nc.gpsimd.register("rbc_tok"))
        P.emit("pool", lambda h: h.reg_mov(rbc_cap, CAP - 1))
        P.emit("pool", lambda h: h.reg_mov(rbc_tok, TOK - 1))
        P.emit("dve", lambda h: h.memset(onesc[:], 1.0), inc="dve")
        P.emit("dve", lambda h: h.memset(onesr[:], 1.0), inc="dve")
        t_ones = P.tok("dve")
        P.emit("pool", lambda h: h.memset(bigi[:], 2 ** 30), inc="pool")
        P.emit("pool", lambda h: h.iota(rowid[:], [[128, 16]], base=0,
                                        channel_multiplier=1), inc="pool")
        t_rowid = P.tok("pool")

        def idma(fn, waits, accum=False):
            s_name = f"qg{qrot['qg']}"
            qrot["qg"] = (qrot["qg"] + 1) % QROT
            return P.emit("pool", fn, inc=s_name, inck=16, waits=waits)

        addch = None
        stage_rel = [None, None]
        gath_rel = None
        xgt_rel = [None, None]
        idx_rel = [None, None]
        cg_rel = [None, None]
        for hf in range(2):
            toff = hf * 8
            P.emit("dve", lambda h: h.memset(base_r[:], 0.0), inc="dve",
                   waits=[P.tok("pe")])
            t_base = P.tok("dve")
            for tt8 in range(8):
                tt = toff + tt8
                xb = stage[tt8 % 2]
                t_x = dma("sp", "qs", xb[:], x1d.ap()[tt * 128:(tt + 1) * 128, :],
                          waits=t_x1d + [stage_rel[tt8 % 2], addch])
                t_xn = emit_ln(xb, t_x, extra=[P.tok("pe")])
                stage_rel[tt8 % 2] = t_xn
                dma("sp", "qs", x1nd.ap()[tt * 128:(tt + 1) * 128, :], xn[:],
                    waits=[t_xn])
                emit_transposes(
                    xn, KC, None,
                    lambda c, tt8=tt8: x1nT[:, c * 1024 + tt8 * 128:
                                            c * 1024 + (tt8 + 1) * 128], t_xn)
            t_nT = P.tok("act")
            t_x1nd = qall("qs")

            t_idxinit = dma("pool", "qg",
                            idxd.ap()[hf].rearrange("e c -> (e c)")
                            .rearrange("(p g) -> p g", p=128),
                            bigi[:], waits=[t_rowid, addch])

            m4rel = [None, None]
            cbrel = [None, None]
            sirel = [None, None]
            for tt8 in range(8):
                tt = toff + tt8
                mask4 = mask4l[tt8 % 2]
                comb = combl[tt8 % 2]
                slot_i = slot_il[tt8 % 2]
                for c in range(KC):
                    P.emit("pe", lambda h, c=c, tt8=tt8: h.matmul(
                        ps[0][:, 0:E],
                        x1nT[:, c * 1024 + tt8 * 128:c * 1024 + (tt8 + 1) * 128],
                        router_sb[:, c * E:(c + 1) * E],
                        start=(c == 0), stop=(c == KC - 1)),
                        inc="pe" if c == KC - 1 else None,
                        waits=[t_nT, rel.get(0)] if c == 0 else ())
                t_l = P.tok("pe")
                P.emit("dve", lambda h: h.reduce_max(lgmx[:], ps[0][:, 0:E],
                                                     axis=AX.X, negate=True),
                       inc="dve", waits=[t_l, P.tok("act")])
                t_nm = P.tok("dve")
                t_p = P.emit("act", lambda h: h.activation(
                    pun[:], ps[0][:, 0:E], AF.Exp, bias=lgmx[:]), inc="act",
                    waits=[t_nm])
                rel[0] = t_p
                P.emit("dve", lambda h: h.memset(rank[:], 0.0), inc="dve",
                       waits=[t_p])
                for e2 in range(E):
                    P.emit("dve", lambda h, e2=e2: h.tensor_scalar(
                        tcmp[:], pun[:], pun[:, e2:e2 + 1], None, ALU.is_lt),
                        inc="dve", waits=[P.tok("dve")])
                    P.emit("dve", lambda h: h.tensor_add(rank[:], rank[:], tcmp[:]),
                           inc="dve", waits=[P.tok("dve")])
                t_m4 = P.emit("dve", lambda h, mask4=mask4: h.tensor_scalar(
                    mask4[:], rank[:], 3.0, None, ALU.is_le), inc="dve",
                    waits=[P.tok("dve"), m4rel[tt8 % 2]])
                P.emit("dve", lambda h, comb=comb, mask4=mask4: h.tensor_mul(
                    comb[:], pun[:], mask4[:]),
                    inc="dve", waits=[t_m4, cbrel[tt8 % 2]])
                P.emit("dve", lambda h, comb=comb: h.reduce_sum(
                    den[:], comb[:], axis=AX.X), inc="dve", waits=[P.tok("dve")])
                P.emit("dve", lambda h: h.reciprocal(dinv[:], den[:]), inc="dve",
                       waits=[P.tok("dve")])
                t_cb = P.emit("dve", lambda h, comb=comb: h.tensor_scalar(
                    comb[:], comb[:], dinv[:], None, ALU.mult), inc="dve",
                    waits=[P.tok("dve")])
                cbrel[tt8 % 2] = dma("sp", "qs",
                                     combd.ap()[tt * 128:(tt + 1) * 128, :],
                                     comb[:], waits=[t_cb])
                P.emit("pe", lambda h, mask4=mask4: h.matmul(
                    ps[2][:, 0:E], lx_sb[:], mask4[:], start=True, stop=False),
                    waits=[t_m4, rel.get(2)] + t_lxq)
                P.emit("pe", lambda h: h.matmul(
                    ps[2][:, 0:E], onesr[:], base_r[:], start=False, stop=True),
                    inc="pe", waits=[t_ones, t_base])
                t_sl = P.tok("pe")
                P.emit("pe", lambda h, mask4=mask4: h.matmul(
                    ps[3][0:1, 0:E], onesc[:], mask4[:], start=True, stop=True),
                    inc="pe", waits=[rel.get(3), t_ones])
                t_cnt = P.tok("pe")
                m4rel[tt8 % 2] = t_cnt
                P.emit("dve", lambda h: h.tensor_add(base_r[:], base_r[:],
                                                     ps[3][0:1, 0:E]), inc="dve",
                       waits=[t_cnt])
                t_base = P.tok("dve")
                rel[3] = t_base
                P.emit("dve", lambda h, mask4=mask4: h.tensor_scalar(
                    sltmp[:], mask4[:], -1e9, 1e9, ALU.mult, ALU.add), inc="dve",
                    waits=[t_m4])
                P.emit("dve", lambda h: h.tensor_scalar(
                    ovf[:], ps[2][:, 0:E], float(CAP) - 0.5, None, ALU.is_ge),
                    inc="dve", waits=[t_sl])
                P.emit("dve", lambda h: h.tensor_scalar(
                    ovf[:], ovf[:], 1e9, None, ALU.mult), inc="dve",
                    waits=[P.tok("dve")])
                P.emit("dve", lambda h: h.tensor_scalar(
                    slotf[:], ps[2][:, 0:E], 512.0, None, ALU.is_ge), inc="dve",
                    waits=[P.tok("dve")])
                P.emit("dve", lambda h: h.scalar_tensor_tensor(
                    sltmp2[:], slotf[:], -512.0, ps[2][:, 0:E], ALU.mult, ALU.add),
                    inc="dve", waits=[P.tok("dve")])
                P.emit("dve", lambda h: h.tensor_scalar(
                    slotf[:], sltmp2[:], 256.0, None, ALU.is_ge), inc="dve",
                    waits=[P.tok("dve")])
                P.emit("dve", lambda h: h.scalar_tensor_tensor(
                    sltmp2[:], slotf[:], -256.0, sltmp2[:], ALU.mult, ALU.add),
                    inc="dve", waits=[P.tok("dve")])
                P.emit("dve", lambda h: h.tensor_scalar(
                    slotf[:], sltmp2[:], 128.0, None, ALU.is_ge), inc="dve",
                    waits=[P.tok("dve")])
                P.emit("dve", lambda h: h.scalar_tensor_tensor(
                    sltmp2[:], slotf[:], -128.0, sltmp2[:], ALU.mult, ALU.add),
                    inc="dve", waits=[P.tok("dve")])
                P.emit("dve", lambda h: h.tensor_tensor(
                    slotf[:], ps[2][:, 0:E], sltmp2[:], ALU.subtract), inc="dve",
                    waits=[P.tok("dve")])
                t_r0 = P.emit("dve", lambda h: h.tensor_scalar(
                    slotf[:], slotf[:], 1.0 / 128.0, None, ALU.mult), inc="dve",
                    waits=[P.tok("dve")])
                rel[2] = t_r0
                P.emit("dve", lambda h: h.tensor_scalar(
                    sltmp2[:], sltmp2[:], float(CAPG), None, ALU.mult), inc="dve",
                    waits=[t_r0])
                P.emit("dve", lambda h: h.tensor_add(slotf[:], slotf[:], sltmp2[:]),
                       inc="dve", waits=[P.tok("dve")])
                P.emit("dve", lambda h, mask4=mask4: h.tensor_mul(
                    slotf[:], slotf[:], mask4[:]),
                    inc="dve", waits=[P.tok("dve")])
                P.emit("dve", lambda h: h.tensor_add(slotf[:], slotf[:], sltmp[:]),
                       inc="dve", waits=[P.tok("dve")])
                P.emit("dve", lambda h: h.tensor_add(slotf[:], slotf[:], ovf[:]),
                       inc="dve", waits=[P.tok("dve")])
                t_si = P.emit("dve", lambda h, slot_i=slot_i: h.tensor_copy(
                    slot_i[:], slotf[:]),
                    inc="dve", waits=[P.tok("dve"), sirel[tt8 % 2]])
                t_sc_last = None
                for e2 in range(E):
                    t_sc_last = idma(lambda h, e2=e2, tt8=tt8, hf=hf,
                                     slot_i=slot_i: h.indirect_dma_start(
                        idxd.ap().rearrange("a e c -> (a e c)")
                        .rearrange("(x u) -> x u", u=1),
                        IndirectOffsetOnAxis(slot_i[:, e2:e2 + 1], 0),
                        rowid[:, toff + tt8:toff + tt8 + 1], None,
                        element_offset=(hf * E + e2) * CAP,
                        bounds_check=rbc_cap, oob_is_err=False),
                        waits=[t_si, t_idxinit])
                sirel[tt8 % 2] = t_sc_last
            t_scat = qall("qg")
            t_combd = qall("qs")

            for tt8 in range(8):
                for c in range(KC):
                    P.emit("pe", lambda h, c=c, tt8=tt8: h.matmul(
                        ps[1][:, 0:1],
                        x1nT[:, c * 1024 + tt8 * 128:c * 1024 + (tt8 + 1) * 128],
                        shgw_sb[:, c:c + 1], start=(c == 0), stop=(c == KC - 1)),
                        inc="pe" if c == KC - 1 else None,
                        waits=[rel.get(1)] if c == 0 else ())
                t_sgm = P.tok("pe")
                t_sg = P.emit("act", lambda h, tt8=tt8: h.activation(
                    sg[:, tt8:tt8 + 1], ps[1][:, 0:1], AF.Sigmoid), inc="act",
                    waits=[t_sgm])
                rel[1] = t_sg
            t_sgall = P.tok("act")

            def load_gu2(kind, idx, buf, wtok):
                if kind == "moe":
                    e_, it = idx
                    ta = dma("sp", "qs", wgl[buf][:], wg_bf.ap()[e_, it],
                             waits=[wtok, cast_tok[("g", e_)]])
                    tb = dma("sp", "qs", wul[buf][:], wu_bf.ap()[e_, it],
                             waits=[cast_tok[("u", e_)]])
                else:
                    it = idx
                    ta = dma("sp", "qs", wgl[buf][:], shg_bf.ap()[it],
                             waits=[wtok, cast_tok[("shg", it // IT_MOE)]])
                    tb = dma("sp", "qs", wul[buf][:], shu_bf.ap()[it],
                             waits=[cast_tok[("shu", it // IT_MOE)]])
                return [ta, tb]

            def load_wd2(kind, idx, buf, wtok):
                if kind == "moe":
                    e_, hh2 = idx
                    return [dma("act", "qa", wdr[buf][:], wd_bf.ap()[e_, hh2],
                                waits=[wtok, cast_tok[("d", e_)]])]
                grp, hh2 = idx
                return [dma("act", "qa", wdr[buf][:], shd_bf.ap()[grp, hh2],
                            waits=[wtok, cast_tok[("shd", grp)]])]

            gu_seq = ([("moe", (e_, it)) for e_ in range(E) for it in range(IT_MOE)]
                      + [("sh", g * IT_MOE + it) for g in range(4)
                         for it in range(IT_MOE)])
            wd_seq = ([("moe", (e_, hh2)) for e_ in range(E) for hh2 in range(2)]
                      + [("sh", (g, hh2)) for g in range(4) for hh2 in range(2)])
            t_wgd = {}
            t_wdd = {}
            for gi in range(2):
                t_wgd[gi] = load_gu2(gu_seq[gi][0], gu_seq[gi][1], gi % 2,
                                     P.tok("pe"))
            for di in range(2):
                t_wdd[di] = load_wd2(wd_seq[di][0], wd_seq[di][1], di % 2,
                                     P.tok("pe"))
            gi_c = [0]
            di_c = [0]
            t_idx = {}
            t_gath = {}

            def emit_gather(e_, pb):
                t_idx[e_] = dma("sp", "qs", idxsb[pb][:],
                                idxd.ap()[hf, e_].rearrange("(p g) -> p g", p=128),
                                waits=t_scat + [idx_rel[pb]])
                gtk = []
                for g_ in range(CAPG):
                    gtk.append(idma(lambda h, g_=g_, pb=pb: h.indirect_dma_start(
                        xg[:, g_ * D:(g_ + 1) * D], None,
                        x1nd.ap(), IndirectOffsetOnAxis(idxsb[pb][:, g_:g_ + 1], 0),
                        bounds_check=rbc_tok, oob_is_err=False),
                        waits=[t_idx[e_], gath_rel] + t_x1nd))
                    gtk.append(idma(lambda h, g_=g_, pb=pb: h.indirect_dma_start(
                        combg[pb][:, g_ * E:(g_ + 1) * E], None,
                        combd.ap(), IndirectOffsetOnAxis(idxsb[pb][:, g_:g_ + 1], 0),
                        bounds_check=rbc_tok, oob_is_err=False),
                        waits=[t_idx[e_], cg_rel[pb]] + t_combd))
                t_gath[e_] = gtk

            emit_gather(0, 0)
            for e_ in range(E):
                pb = e_ % 2
                t_tr_last = None
                for g_ in range(CAPG):
                    for c in range(KC):
                        bank = 6 + (c % 2)
                        t_t = P.emit("pe", lambda h, g_=g_, c=c, bank=bank:
                                     h.transpose(
                            ps[bank].bitcast(BF16)[:, 0:128],
                            xg[:, g_ * D + c * 128:g_ * D + (c + 1) * 128],
                            id_bf[:]), inc="pe",
                            waits=[rel.get(bank), xgt_rel[pb]] +
                            (t_gath[e_] if (g_ == 0 and c == 0) else []))
                        t_tr_last = P.emit("act", lambda h, g_=g_, c=c, bank=bank,
                                           pb=pb: h.copy(
                            xgT[pb][:, c * CAP + g_ * 128:c * CAP + (g_ + 1) * 128],
                            ps[bank].bitcast(BF16)[:, 0:128]), inc="act",
                            waits=[t_t])
                        rel[bank] = t_tr_last
                gath_rel = P.tok("pe")
                if e_ + 1 < E:
                    emit_gather(e_ + 1, (e_ + 1) % 2)
                for it in range(IT_MOE):
                    gi = gi_c[0]
                    buf = gi % 2
                    bs = [0, 1] if (gi % 2 == 0) else [2, 3]
                    t_w = t_wgd[gi]
                    for c in range(KC):
                        P.emit("pe", lambda h, c=c, bs=bs, buf=buf, pb=pb: h.matmul(
                            ps[bs[0]][:, 0:CAP], wgl[buf][:, c * 128:(c + 1) * 128],
                            xgT[pb][:, c * CAP:(c + 1) * CAP],
                            start=(c == 0), stop=(c == KC - 1)),
                            waits=(t_w + [t_tr_last] + [rel.get(i) for i in bs])
                            if c == 0 else ())
                        P.emit("pe", lambda h, c=c, bs=bs, buf=buf, pb=pb: h.matmul(
                            ps[bs[1]][:, 0:CAP], wul[buf][:, c * 128:(c + 1) * 128],
                            xgT[pb][:, c * CAP:(c + 1) * CAP],
                            start=(c == 0), stop=(c == KC - 1)),
                            inc="pe" if c == KC - 1 else None)
                    t_gu = P.tok("pe")
                    if gi + 2 < len(gu_seq):
                        t_wgd[gi + 2] = load_gu2(gu_seq[gi + 2][0],
                                                 gu_seq[gi + 2][1], buf, t_gu)
                    t_si2 = P.emit("act", lambda h, bs=bs: h.activation(
                        silu_s[:, 0:CAP], ps[bs[0]][:, 0:CAP], AF.Silu), inc="act",
                        waits=[t_gu, P.tok("dve")])
                    rel[bs[0]] = t_si2
                    P.emit("dve", lambda h, bs=bs: h.tensor_copy(
                        tmul[:, 0:CAP], ps[bs[1]][:, 0:CAP]), inc="dve",
                        waits=[t_gu, P.tok("act")])
                    t_tm = P.tok("dve")
                    rel[bs[1]] = t_tm
                    P.emit("dve", lambda h, it=it: h.tensor_mul(
                        z_e[:, it * 1024:it * 1024 + CAP], tmul[:, 0:CAP],
                        silu_s[:, 0:CAP]),
                        inc="dve", waits=[t_tm, t_si2])
                    gi_c[0] += 1
                t_z = P.tok("dve")
                xgt_rel[pb] = t_z
                for hh2 in range(2):
                    di = di_c[0]
                    dbuf = di % 2
                    t_wdt = t_wdd[di]
                    t_d = None
                    for st in range(CAPG):
                        dbs = [4 + (st % 2) * 2, 5 + (st % 2) * 2]
                        for it in range(IT_MOE):
                            for hq in range(2):
                                bank = dbs[hq]
                                P.emit("pe", lambda h, it=it, st=st, hq=hq,
                                       dbuf=dbuf, bank=bank: h.matmul(
                                    ps[bank][:, :],
                                    z_e[:, it * 1024 + st * 128:
                                        it * 1024 + (st + 1) * 128],
                                    wdr[dbuf][:, it * 1024 + hq * 512:
                                              it * 1024 + hq * 512 + 512],
                                    start=(it == 0), stop=(it == IT_MOE - 1)),
                                    inc="pe" if (it == IT_MOE - 1 and hq == 1)
                                    else None,
                                    waits=([t_z] + t_wdt +
                                           [rel.get(b) for b in dbs])
                                    if (it == 0 and hq == 0) else ())
                        t_d = P.tok("pe")
                        stg = stage[st % 2]
                        t_ev = None
                        for hq in range(2):
                            t_ev = P.emit("dve", lambda h, hq=hq, st=st, e_=e_,
                                          dbs=dbs, stg=stg, pb=pb: h.tensor_scalar(
                                stg[:, hq * 512:hq * 512 + 512], ps[dbs[hq]][:, :],
                                combg[pb][:, st * E + e_:st * E + e_ + 1], None,
                                ALU.mult), inc="dve",
                                waits=[t_d, stage_rel[st % 2]])
                            rel[dbs[hq]] = t_ev
                        addch = idma(lambda h, hh2=hh2, stg=stg, pb=pb, st=st:
                                     h.indirect_dma_start(
                            x1d.ap(),
                            IndirectOffsetOnAxis(idxsb[pb][:, st:st + 1], 0),
                            stg[:, 0:1024], None,
                            element_offset=hh2 * 1024,
                            compute_op=ALU.add, bounds_check=rbc_tok,
                            oob_is_err=False),
                            waits=[t_ev, addch])
                        stage_rel[st % 2] = addch
                    di_c[0] += 1
                    if di + 2 < len(wd_seq):
                        t_wdd[di + 2] = load_wd2(wd_seq[di + 2][0],
                                                 wd_seq[di + 2][1], dbuf, t_d)
                idx_rel[pb] = addch
                cg_rel[pb] = P.tok("dve")

            for grp in range(4):
                for itg in range(IT_MOE):
                    gi = gi_c[0]
                    buf = gi % 2
                    bs = [0, 1, 2, 3] if (gi % 2 == 0) else [4, 5, 6, 7]
                    t_w = t_wgd[gi]
                    for c in range(KC):
                        for bi, (pp, wsb, xo) in enumerate((
                                (ps[bs[0]], wgl[buf], 0), (ps[bs[1]], wgl[buf], 512),
                                (ps[bs[2]], wul[buf], 0), (ps[bs[3]], wul[buf], 512))):
                            P.emit("pe", lambda h, c=c, pp=pp, wsb=wsb, xo=xo:
                                   h.matmul(pp[:, :], wsb[:, c * 128:(c + 1) * 128],
                                            x1nT[:, c * 1024 + xo:c * 1024 + xo + 512],
                                            start=(c == 0), stop=(c == KC - 1)),
                                   inc="pe" if (c == KC - 1 and bi == 3) else None,
                                   waits=(t_w + [t_nT] + [rel.get(i) for i in bs])
                                   if (c == 0 and bi == 0) else ())
                    t_gu = P.tok("pe")
                    if gi + 2 < len(gu_seq):
                        t_wgd[gi + 2] = load_gu2(gu_seq[gi + 2][0],
                                                 gu_seq[gi + 2][1], buf, t_gu)
                    for nb in range(2):
                        zsl = slice(itg * 1024 + nb * 512, itg * 1024 + (nb + 1) * 512)
                        pg = (ps[bs[0]], ps[bs[1]])[nb]
                        pu = (ps[bs[2]], ps[bs[3]])[nb]
                        t_si2 = P.emit("act", lambda h, pg=pg: h.activation(
                            silu_s[:], pg[:, :], AF.Silu), inc="act",
                            waits=[t_gu, P.tok("dve")])
                        rel[bs[nb]] = t_si2
                        P.emit("dve", lambda h, pu=pu: h.tensor_copy(
                            tmul[:], pu[:, :]), inc="dve",
                            waits=[t_gu, P.tok("act")])
                        t_tm = P.tok("dve")
                        rel[bs[2 + nb]] = t_tm
                        P.emit("dve", lambda h, zsl=zsl: h.tensor_mul(
                            z_e[:, zsl], tmul[:], silu_s[:]), inc="dve",
                            waits=[t_tm, t_si2])
                    gi_c[0] += 1
                t_z = P.tok("dve")
                for hh2 in range(2):
                    di = di_c[0]
                    dbuf = di % 2
                    t_wdt = t_wdd[di]
                    t_d = None
                    for pair in range(4):
                        dbs = ([0, 1, 2, 3] if (pair % 2 == (di % 2))
                               else [4, 5, 6, 7])
                        for it in range(IT_MOE):
                            for sub in range(2):
                                tt8 = pair * 2 + sub
                                for hq in range(2):
                                    bank = dbs[sub * 2 + hq]
                                    P.emit("pe", lambda h, it=it, tt8=tt8, hq=hq,
                                           dbuf=dbuf, bank=bank: h.matmul(
                                        ps[bank][:, :],
                                        z_e[:, it * 1024 + tt8 * 128:
                                            it * 1024 + (tt8 + 1) * 128],
                                        wdr[dbuf][:, it * 1024 + hq * 512:
                                                  it * 1024 + hq * 512 + 512],
                                        start=(it == 0), stop=(it == IT_MOE - 1)),
                                        inc="pe" if (it == IT_MOE - 1 and sub == 1
                                                     and hq == 1) else None,
                                        waits=([t_z] + t_wdt +
                                               [rel.get(b) for b in dbs])
                                        if (it == 0 and sub == 0 and hq == 0)
                                        else ())
                        t_d = P.tok("pe")
                        for sub in range(2):
                            tt8 = pair * 2 + sub
                            stg = stage[sub]
                            t_ev = None
                            for hq in range(2):
                                bank = dbs[sub * 2 + hq]
                                t_ev = P.emit("dve", lambda h, tt8=tt8, bank=bank,
                                              hq=hq, stg=stg: h.tensor_scalar(
                                    stg[:, hq * 512:hq * 512 + 512], ps[bank][:, :],
                                    sg[:, tt8:tt8 + 1], None, ALU.mult),
                                    inc="dve",
                                    waits=[t_d, t_sgall, stage_rel[sub]])
                                rel[bank] = t_ev
                            addch = P.emit("pool", lambda h, tt8=tt8, hh2=hh2,
                                           stg=stg, toff=toff: h.dma_start(
                                x1d.ap()[(toff + tt8) * 128:(toff + tt8 + 1) * 128,
                                         hh2 * 1024:(hh2 + 1) * 1024],
                                stg[:, 0:1024], accum_op=ALU.add),
                                inc=f"qg{qrot['qg']}", inck=16,
                                waits=[t_ev, addch])
                            qrot["qg"] = (qrot["qg"] + 1) % QROT
                            stage_rel[sub] = addch
                    di_c[0] += 1
                    if di + 2 < len(wd_seq):
                        t_wdd[di + 2] = load_wd2(wd_seq[di + 2][0],
                                                 wd_seq[di + 2][1], dbuf, t_d)
            for tt8 in range(8):
                tt = toff + tt8
                dma("sp", "qo", out.ap()[tt * 128:(tt + 1) * 128, :],
                    x1d.ap()[tt * 128:(tt + 1) * 128, :], waits=[addch])
            P.realize(esG)
        esR6.close()

        P.emit("sp", lambda h: h.nop(), waits=qall("qo"))
        P.realize(esG)
    return nc


def kernel(**inputs):
    pr, percore = _host_prepare(inputs)
    nc = build_nc()
    shared = {k: np.ascontiguousarray(v, dtype=np.float32) for k, v in pr.items()}
    in_maps = []
    for c in range(8):
        m = dict(shared)
        for k, v in percore[c].items():
            m[k] = np.ascontiguousarray(v, dtype=np.float32)
        in_maps.append(m)
    res = run_bass_kernel_spmd(nc, in_maps, core_ids=list(range(8)))
    full = np.empty((B, S, D), dtype=np.float32)
    for c in range(8):
        b, p = c // 2, c % 2
        o = np.asarray(res.results[c]["out"])
        for i in range(NT):
            g = 2 * i + p
            full[b, g * 128:(g + 1) * 128, :] = o[i * 128:(i + 1) * 128, :]
    return full
